# version 01
# baseline (speedup 1.0000x reference)
import contextlib
import numpy as np
import ml_dtypes
import concourse.bass as bass
import concourse.mybir as mybir
from concourse.bass_utils import run_bass_kernel_spmd

F32 = mybir.dt.float32
BF16 = mybir.dt.bfloat16
I32 = mybir.dt.int32
AF = mybir.ActivationFunctionType
ALU = mybir.AluOpType
AX = mybir.AxisListType
BF = ml_dtypes.bfloat16

NCORES = 8
B, L, DM = 4, 4096, 2048
TOK = 2048
NKT = DM // 128
EPS_LN = 1e-5
H, DK, NH, KCONV, DV = 6, 256, 2, 4, 512
KDIM, VDIM = H * DK, H * DV
INTER = 8192

ENGINES = ("sync", "scalar", "vector", "tensor", "gpsimd")


class Sched:

    AUTOSYNC = ("vector", "scalar", "gpsimd")

    def __init__(self):
        self.ops = {e: [] for e in ENGINES}
        self.cnt_c = {e: 0 for e in ENGINES}
        self.dch = {}
        self.last_use = {}

    def emit(self, e, fn, inc=0):
        if e in self.AUTOSYNC and self.cnt_c[e] > 0:
            self.ops[e].append(("wait", ("C", e), self.cnt_c[e]))
        self.ops[e].append(("op", fn, e))
        self.cnt_c[e] += 1
        return (("C", e), self.cnt_c[e])

    def dma(self, e, out, in_, ch=None, **kw):
        ch = ch if ch is not None else ("eng", e)
        self.dch[ch] = self.dch.get(ch, 0) + 16
        self.ops[e].append(
            ("dma", lambda g, out=out, in_=in_, kw=kw: g.dma_start(out=out, in_=in_, **kw), ch))
        return (("D", ch), self.dch[ch])

    def wait(self, e, tok):
        if tok is None:
            return
        key, val = tok
        if key == ("C", e):
            return
        self.ops[e].append(("wait", key, val))

    def wait_self(self, e, tok):
        key, val = tok
        self.ops[e].append(("wait", key, val))

    def chan_total(self, ch):
        return (("D", ch), self.dch.get(ch, 0))

    def use(self, key, tok):
        if tok is not None:
            self.last_use[key] = tok

    def guard(self, e, key):
        self.wait(e, self.last_use.get(key))

    def realize(self, nc):
        sems = {}
        with contextlib.ExitStack() as stack:
            for e in ENGINES:
                sems[("C", e)] = stack.enter_context(nc.semaphore(f"c_{e}"))
            for i, ch in enumerate(self.dch):
                sems[("D", ch)] = stack.enter_context(nc.semaphore(f"dm{i}"))
            with nc.Block() as block:
                for e in ENGINES:
                    if not self.ops[e]:
                        continue

                    def make_body(eng_name, oplist):
                        def body(eng):
                            hi = {}
                            for item in oplist:
                                if item[0] == "wait":
                                    _, key, val = item
                                    if val <= 0 or hi.get(key, -1) >= val:
                                        continue
                                    hi[key] = val
                                    eng.wait_ge(sems[key], val)
                                elif item[0] == "dma":
                                    _, fn, ch = item
                                    fn(eng).then_inc(sems[("D", ch)], 16)
                                else:
                                    _, fn, en = item
                                    fn(eng).then_inc(sems[("C", en)], 1)
                        return body

                    getattr(block, e)(make_body(e, self.ops[e]))
        return sems


def hadamard(n):
    h = np.array([[1.0]], np.float32)
    while h.shape[0] < n:
        h = np.block([[h, h], [h, -h]])
    return h


def build_stage_a():
    nc = bass.Bass("TRN2", target_bir_lowering=False)
    s = Sched()
    xb = nc.dram_tensor("xb", [L, DM], F32, kind="ExternalInput").ap()
    wdelta = nc.dram_tensor("wdelta", [DM, 128], F32, kind="ExternalInput").ap()
    wdt = nc.dram_tensor("wdt", [128, 1024], F32, kind="ExternalInput").ap()
    bdt = nc.dram_tensor("bdt", [128, 8], F32, kind="ExternalInput").ap()
    wbc = nc.dram_tensor("wbc", [DM, 32], F32, kind="ExternalInput").ap()
    acols = nc.dram_tensor("acols", [128, 128], F32, kind="ExternalInput").ap()
    dcols = nc.dram_tensor("dcols", [128, 8], F32, kind="ExternalInput").ap()
    selbc = nc.dram_tensor("selbc", [32, 32 * 128], F32, kind="ExternalInput").ap()
    constsa = nc.dram_tensor("constsa", [128, 4], F32, kind="ExternalInput").ap()
    yhi = nc.dram_tensor("yhi", [1024, L], BF16, kind="ExternalOutput").ap()
    ylo = nc.dram_tensor("ylo", [1024, L], BF16, kind="ExternalOutput").ap()
    xlntm_d = nc.dram_tensor("xlntm", [L, DM], BF16, kind="Internal").ap()

    sb = lambda n, shp, dt: nc.alloc_sbuf_tensor(n, shp, dt).ap()
    ps = lambda n, shp, dt: nc.alloc_psum_tensor(n, shp, dt).ap()

    xin = sb("xin", [128, DM], F32)
    sq_junk = sb("sq_junk", [128, DM], BF16)
    xlnbf = sb("xlnbf", [128, DM], BF16)
    stats = sb("stats", [128, 8], F32)
    xlt = sb("xlt", [128, 2 * 8 * 512], BF16)
    xstream = sb("xstream", [128, 2 * 8 * 512], BF16)
    xhalf32 = sb("xhalf32", [128, 8 * 512], F32)
    wd_bf = sb("wd_bf", [128, NKT * 128], BF16)
    wdt_bf = sb("wdt_bf", [128, 1024], BF16)
    wbc_bf = sb("wbc_bf", [128, NKT * 32], BF16)
    wld = sb("wld", [128, 2048], F32)
    bdt_sb = sb("bdt_sb", [128, 8], F32)
    acol_sb = sb("acol_sb", [128, 128], F32)
    dcol_sb = sb("dcol_sb", [128, 8], F32)
    selbc_sb = sb("selbc_sb", [32, 32 * 128], F32)
    cca = sb("cca", [128, 4], F32)
    bpcp32 = sb("bpcp32", [32, 512], F32)
    dpT = sb("dpT", [128, 512], BF16)
    spz = sb("spz", [128, 8 * 512], F32)
    spa = sb("spa", [128, 512], F32)
    spe = sb("spe", [128, 512], F32)
    spl = sb("spl", [128, 512], F32)
    spr = sb("spr", [128, 512], F32)
    uT = sb("uT", [128, 8 * 512], F32)
    NSL = 3
    abuf = sb("abuf", [128, NSL * 512], F32)
    bbuf = sb("bbuf", [128, NSL * 512], F32)
    hbuf = sb("hbuf", [128, NSL * 512], F32)
    t1buf = sb("t1buf", [128, NSL * 512], F32)
    yacc = sb("yacc", [128, 8 * 512], F32)
    yout = sb("yout", [128, 8 * 512], F32)
    yhi_sb = sb("yhi_sb", [128, 512], BF16)
    yhi32 = sb("yhi32", [128, 512], F32)
    ylo_sb = sb("ylo_sb", [128, 512], BF16)
    state = sb("state", [128, 8 * 16], F32)

    p_dp = ps("p_dp", [128, 512], F32)
    p_z = [ps("p_z0", [128, 512], F32), ps("p_z1", [128, 512], F32)]
    p_bc = [ps("p_bc0", [128, 2 * 512], F32), ps("p_bc1", [128, 2 * 512], F32)]
    p_bpcp = ps("p_bpcp", [32, 512], F32)

    tok = None
    for k in range(NKT):
        tok = s.dma("sync", wld[:, k * 128:(k + 1) * 128], wdelta[k * 128:(k + 1) * 128, :], ch="wld")
    s.wait("vector", tok)
    t_wd = s.emit("vector", lambda v: v.tensor_copy(wd_bf[:], wld[:, 0:2048]), inc=1)
    s.guard("sync", "wld")
    s.use("wld", t_wd)
    s.wait("sync", t_wd)
    tok = s.dma("sync", wld[:, 0:1024], wdt[:], ch="wld")
    s.wait("vector", tok)
    t_wdt = s.emit("vector", lambda v: v.tensor_copy(wdt_bf[:], wld[:, 0:1024]), inc=1)
    s.wait("sync", t_wdt)
    for k in range(NKT):
        tok = s.dma("sync", wld[:, k * 32:(k + 1) * 32], wbc[k * 128:(k + 1) * 128, :], ch="wld")
    s.wait("vector", tok)
    t_wbc = s.emit("vector", lambda v: v.tensor_copy(wbc_bf[:], wld[:, 0:NKT * 32]), inc=1)
    s.dma("sync", bdt_sb[:], bdt[:], ch="consts")
    s.dma("sync", acol_sb[:], acols[:], ch="consts")
    s.dma("sync", dcol_sb[:], dcols[:], ch="consts")
    s.dma("sync", selbc_sb[:], selbc[:], ch="consts")
    s.dma("sync", cca[:], constsa[:], ch="consts")
    t_consts = t_cca = s.chan_total("consts")

    t_state0 = s.emit("vector", lambda v: v.memset(state[:], 0.0), inc=1)

    t_store = None
    t_ln = None
    for tt in range(L // 128):
        if t_ln is not None:
            s.wait("sync", t_ln)
            s.wait("sync", t_sq)
        t_ld = s.dma("sync", xin[:], xb[tt * 128:(tt + 1) * 128, :], ch="xin")
        s.wait("vector", t_ld)
        s.emit("vector", lambda v: v.reduce_sum(stats[:, 0:1], xin[:], axis=AX.X))
        s.wait("scalar", t_ld)
        t_sq = s.emit("scalar", lambda sc: sc.activation(
            out=sq_junk[:], in_=xin[:], func=AF.Square, accum_out=stats[:, 1:2]), inc=1)
        s.wait("vector", t_sq)
        s.emit("vector", lambda v: v.tensor_scalar(
            out=stats[:, 2:3], in0=stats[:, 0:1], scalar1=1.0 / DM, scalar2=None, op0=ALU.mult))
        s.emit("vector", lambda v: v.tensor_scalar(
            out=stats[:, 5:6], in0=stats[:, 1:2], scalar1=1.0 / DM, scalar2=None, op0=ALU.mult))
        s.emit("vector", lambda v: v.tensor_tensor(
            out=stats[:, 3:4], in0=stats[:, 2:3], in1=stats[:, 2:3], op=ALU.mult))
        t_var = s.emit("vector", lambda v: v.tensor_tensor(
            out=stats[:, 3:4], in0=stats[:, 5:6], in1=stats[:, 3:4], op=ALU.subtract), inc=1)
        s.wait("scalar", t_var)
        if tt == 0:
            s.wait("scalar", t_cca)
        s.emit("scalar", lambda sc: sc.activation(
            out=stats[:, 4:5], in_=stats[:, 3:4], func=AF.Ln, bias=cca[:, 0:1]))
        t_rstd = s.emit("scalar", lambda sc: sc.activation(
            out=stats[:, 4:5], in_=stats[:, 4:5], func=AF.Exp, scale=cca[:, 2:3]), inc=1)
        s.wait("vector", t_rstd)
        if t_store is not None:
            s.wait("vector", t_store)
        t_ln = s.emit("vector", lambda v: v.tensor_scalar(
            out=xlnbf[:], in0=xin[:], scalar1=stats[:, 2:3], scalar2=stats[:, 4:5],
            op0=ALU.subtract, op1=ALU.mult), inc=1)
        s.wait("gpsimd", t_ln)
        t_store = s.dma("gpsimd", xlntm_d[tt * 128:(tt + 1) * 128, :], xlnbf[:], ch="xlnst")

    s.wait("scalar", t_store)

    NCH = L // 512
    for c in range(NCH):
        par = c % 2

        def xl(k, c=c, par=par):
            if k < 8:
                return xlt[:, (par * 8 + k) * 512:(par * 8 + k + 1) * 512]
            return xstream[:, (par * 8 + (k - 8)) * 512:(par * 8 + (k - 8) + 1) * 512]

        t_xlt = None
        for k in range(8):
            s.guard("scalar", ("xlt", par, k))
            t_xlt = s.dma("scalar", xlt[:, (par * 8 + k) * 512:(par * 8 + k + 1) * 512],
                          xlntm_d[c * 512:(c + 1) * 512, k * 128:(k + 1) * 128],
                          transpose=True, ch=("xlt", par))
        t_xs = None
        for k in range(8):
            s.guard("scalar", ("xstream", par, k))
            t_xs = s.dma("scalar", xstream[:, (par * 8 + k) * 512:(par * 8 + k + 1) * 512],
                         xlntm_d[c * 512:(c + 1) * 512, (8 + k) * 128:(9 + k) * 128],
                         transpose=True, ch=("xs", par))

        s.wait("tensor", t_xlt)
        s.wait("tensor", t_xs)
        s.wait("tensor", t_wd)
        s.guard("tensor", "p_dp")
        t_mm = None
        for k in range(NKT):
            ap = xl(k)
            t_mm = s.emit("tensor", lambda t, k=k, ap=ap: t.matmul(
                p_dp[:], wd_bf[:, k * 128:(k + 1) * 128], ap,
                start=(k == 0), stop=(k == NKT - 1)), inc=(1 if k == NKT - 1 else 0))
        s.wait("vector", t_mm)
        s.guard("vector", "dpT")
        t_dp = s.emit("vector", lambda v: v.tensor_copy(dpT[:], p_dp[:]), inc=1)
        s.use("p_dp", t_dp)

        s.wait("tensor", t_wbc)
        s.guard("tensor", "p_bpcp")
        for k in range(NKT):
            ap = xl(k)
            t_mmb = s.emit("tensor", lambda t, k=k, ap=ap: t.matmul(
                p_bpcp[:], wbc_bf[:, k * 32:(k + 1) * 32], ap,
                start=(k == 0), stop=(k == NKT - 1)), inc=(1 if k == NKT - 1 else 0))
        s.wait("vector", t_mmb)
        s.guard("vector", "bpcp32")
        t_bpcp = s.emit("vector", lambda v: v.tensor_copy(bpcp32[:], p_bpcp[:]), inc=1)
        s.use("p_bpcp", t_bpcp)
        for k in range(8):
            s.use(("xstream", par, k), t_mmb)

        s.wait("vector", t_xlt)
        s.guard("vector", "xhalf32")
        t_xh = None
        for k in range(8):
            ap = xl(k)
            t_xh = s.emit("vector", lambda v, k=k, ap=ap: v.tensor_copy(
                xhalf32[:, k * 512:(k + 1) * 512], ap), inc=(1 if k == 7 else 0))
        for k in range(8):
            s.use(("xlt", par, k), t_xh)

        t_u = {}
        t_delta = {}
        for d in range(8):
            pz = p_z[d % 2]
            s.wait("tensor", t_dp)
            s.wait("tensor", t_wdt)
            s.guard("tensor", ("p_z", d % 2))
            t_mm = s.emit("tensor", lambda t, d=d, pz=pz: t.matmul(
                pz[:], wdt_bf[:, d * 128:(d + 1) * 128], dpT[:],
                start=True, stop=True), inc=1)
            if d == 7:
                s.use("dpT", t_mm)
            s.wait("vector", t_mm)
            s.guard("vector", ("spz", d))
            t_zb = s.emit("vector", lambda v, d=d, pz=pz: v.tensor_scalar(
                out=spz[:, d * 512:(d + 1) * 512], in0=pz[:], scalar1=bdt_sb[:, d:d + 1],
                scalar2=None, op0=ALU.add), inc=1)
            s.use(("p_z", d % 2), t_zb)
            s.wait("scalar", t_zb)
            s.emit("scalar", lambda sc, d=d: sc.activation(
                out=spa[:], in_=spz[:, d * 512:(d + 1) * 512], func=AF.Abs))
            s.emit("scalar", lambda sc: sc.activation(out=spe[:], in_=spa[:], func=AF.Exp, scale=cca[:, 3:4]))
            t_l = s.emit("scalar", lambda sc: sc.activation(out=spl[:], in_=spe[:], func=AF.Ln, bias=cca[:, 1:2]), inc=1)
            s.emit("vector", lambda v, d=d: v.tensor_scalar(
                out=spr[:], in0=spz[:, d * 512:(d + 1) * 512], scalar1=0.0, scalar2=None, op0=ALU.max))
            s.wait("vector", t_l)
            t_delta[d] = s.emit("vector", lambda v, d=d: v.tensor_tensor(
                out=spz[:, d * 512:(d + 1) * 512], in0=spr[:], in1=spl[:], op=ALU.add), inc=1)
            s.use("spl", t_delta[d])
            t_u[d] = s.emit("vector", lambda v, d=d: v.tensor_tensor(
                out=uT[:, d * 512:(d + 1) * 512], in0=spz[:, d * 512:(d + 1) * 512],
                in1=xhalf32[:, d * 512:(d + 1) * 512], op=ALU.mult), inc=1)

        s.guard("vector", "yacc")
        s.emit("vector", lambda v: v.memset(yacc[:], 0.0))
        if c == 0:
            s.wait("vector", t_state0)
            s.wait("scalar", t_consts)

        slot = 0
        for n in range(16):
            pbc = p_bc[n % 2]
            s.wait("tensor", t_bpcp)
            s.guard("tensor", ("p_bc", n % 2, 0))
            s.guard("tensor", ("p_bc", n % 2, 1))
            t_b1 = s.emit("tensor", lambda t, n=n, pbc=pbc: t.matmul(
                pbc[:, 0:512], selbc_sb[:, n * 128:(n + 1) * 128], bpcp32[:],
                start=True, stop=True), inc=1)
            t_b2 = s.emit("tensor", lambda t, n=n, pbc=pbc: t.matmul(
                pbc[:, 512:1024], selbc_sb[:, (16 + n) * 128:(17 + n) * 128], bpcp32[:],
                start=True, stop=True), inc=1)
            if n == 15:
                s.use("bpcp32", t_b2)
            for d in range(8):
                col = d * 16 + n
                bb = slot * 512
                s.wait("scalar", t_delta[d])
                s.guard("scalar", ("abuf", slot))
                t_ab = s.emit("scalar", lambda sc, d=d, col=col, bb=bb: sc.activation(
                    out=abuf[:, bb:bb + 512], in_=spz[:, d * 512:(d + 1) * 512],
                    func=AF.Exp, scale=acol_sb[:, col:col + 1]), inc=1)
                s.use(("spz", d), t_ab)
                s.wait("vector", t_b1)
                s.wait("vector", t_u[d])
                s.guard("vector", ("bbuf", slot))
                t_bb = s.emit("vector", lambda v, d=d, bb=bb, pbc=pbc: v.tensor_tensor(
                    out=bbuf[:, bb:bb + 512], in0=uT[:, d * 512:(d + 1) * 512],
                    in1=pbc[:, 0:512], op=ALU.mult), inc=1)
                s.wait("vector", t_ab)
                s.guard("vector", ("hbuf", slot))
                t_h = s.emit("vector", lambda v, col=col, bb=bb: v.tensor_tensor_scan(
                    out=hbuf[:, bb:bb + 512], data0=abuf[:, bb:bb + 512],
                    data1=bbuf[:, bb:bb + 512], initial=state[:, col:col + 1],
                    op0=ALU.mult, op1=ALU.add), inc=1)
                s.use(("abuf", slot), t_h)
                s.use(("bbuf", slot), t_h)
                s.emit("vector", lambda v, col=col, bb=bb: v.tensor_copy(
                    state[:, col:col + 1], hbuf[:, bb + 511:bb + 512]))
                s.wait("vector", t_b2)
                s.emit("vector", lambda v, bb=bb, pbc=pbc: v.tensor_tensor(
                    out=t1buf[:, bb:bb + 512], in0=hbuf[:, bb:bb + 512],
                    in1=pbc[:, 512:1024], op=ALU.mult))
                t_yy = s.emit("vector", lambda v, d=d, bb=bb: v.tensor_tensor(
                    out=yacc[:, d * 512:(d + 1) * 512], in0=yacc[:, d * 512:(d + 1) * 512],
                    in1=t1buf[:, bb:bb + 512], op=ALU.add), inc=1)
                s.use(("hbuf", slot), t_yy)
                if d == 7:
                    s.use(("p_bc", n % 2, 0), t_yy)
                    s.use(("p_bc", n % 2, 1), t_yy)
                slot = (slot + 1) % NSL
        for d in range(8):
            s.guard("vector", ("yout", d))
            t_yf = s.emit("vector", lambda v, d=d: v.scalar_tensor_tensor(
                out=yout[:, d * 512:(d + 1) * 512], in0=xhalf32[:, d * 512:(d + 1) * 512],
                scalar=dcol_sb[:, d:d + 1], in1=yacc[:, d * 512:(d + 1) * 512],
                op0=ALU.mult, op1=ALU.add), inc=1)
            if d == 7:
                s.use("xhalf32", t_yf)
                s.use("yacc", t_yf)
            s.guard("vector", "yhis")
            s.emit("vector", lambda v, d=d: v.tensor_copy(
                yhi_sb[:], yout[:, d * 512:(d + 1) * 512]))
            s.emit("vector", lambda v: v.tensor_copy(yhi32[:], yhi_sb[:]))
            t_lo = s.emit("vector", lambda v, d=d: v.tensor_tensor(
                out=ylo_sb[:], in0=yout[:, d * 512:(d + 1) * 512], in1=yhi32[:],
                op=ALU.subtract))
            s.wait("gpsimd", t_lo)
            t_s1 = s.dma("gpsimd", yhi[d * 128:(d + 1) * 128, c * 512:(c + 1) * 512], yhi_sb[:], ch="yst")
            t_s2 = s.dma("gpsimd", ylo[d * 128:(d + 1) * 128, c * 512:(c + 1) * 512], ylo_sb[:], ch="yst")
            s.use("yhis", t_s2)

    s.wait_self("gpsimd", s.chan_total(("eng", "gpsimd")))
    return nc, s



def build_stage_b(debug=False):
    nc = bass.Bass("TRN2", target_bir_lowering=False)
    s = Sched()
    xrows = nc.dram_tensor("xrows", [TOK, DM], F32, kind="ExternalInput").ap()
    yhi = nc.dram_tensor("yhi", [DM, TOK], BF16, kind="ExternalInput").ap()
    ylo = nc.dram_tensor("ylo", [DM, TOK], BF16, kind="ExternalInput").ap()
    w2up = nc.dram_tensor("w2up", [DM, INTER], BF16, kind="ExternalInput").ap()
    w2gate = nc.dram_tensor("w2gate", [DM, INTER], BF16, kind="ExternalInput").ap()
    w2down = nc.dram_tensor("w2down", [INTER, DM], BF16, kind="ExternalInput").ap()
    scu = nc.dram_tensor("scu", [128, 64], F32, kind="ExternalInput").ap()
    scg = nc.dram_tensor("scg", [128, 64], F32, kind="ExternalInput").ap()
    constsb = nc.dram_tensor("constsb", [128, 2], F32, kind="ExternalInput").ap()
    x2o_d = nc.dram_tensor("x2o", [TOK, DM], F32, kind="ExternalOutput").ap()
    xln_d = nc.dram_tensor("xln_sc", [TOK, DM], BF16, kind="Internal").ap()
    if debug:
        dbg_x1 = nc.dram_tensor("dbg_x1", [512, DM], F32, kind="ExternalOutput").ap()
        dbg_xln = nc.dram_tensor("dbg_xln", [512, DM], BF16, kind="ExternalOutput").ap()
        dbg_z = nc.dram_tensor("dbg_z", [INTER, 512], BF16, kind="ExternalOutput").ap()

    sb = lambda n, shp, dt: nc.alloc_sbuf_tensor(n, shp, dt).ap()
    ps = lambda n, shp, dt: nc.alloc_psum_tensor(n, shp, dt).ap()

    x1 = sb("x1", [128, 4 * DM], F32)
    yts = sb("yts", [128, 2 * DM], BF16)
    xlnbf = sb("xlnbf", [128, DM], BF16)
    sq_junk = sb("sq_junk", [128, DM], BF16)
    stats = sb("stats", [128, 8], F32)
    ccb128 = sb("ccb128", [128, 2], F32)
    xlnT = sb("xlnT", [128, 16 * 512], BF16)
    zbf = sb("zbf", [128, 64 * 512], BF16)
    wup_sb = sb("wup_sb", [128, 2 * 2048], BF16)
    wgt_sb = sb("wgt_sb", [128, 2 * 2048], BF16)
    wdn_sb = sb("wdn_sb", [128, 2 * 16 * 512], BF16)
    up_sb = sb("up_sb", [128, 2 * 512], F32)
    gs_sb = sb("gs_sb", [128, 2 * 512], F32)
    x2s = sb("x2s", [128, 2 * 512], F32)
    scu_sb = sb("scu_sb", [128, 64], F32)
    scg_sb = sb("scg_sb", [128, 64], F32)

    p_up = [ps("p_up0", [128, 512], F32), ps("p_up1", [128, 512], F32)]
    p_gt = [ps("p_gt0", [128, 512], F32), ps("p_gt1", [128, 512], F32)]
    p_dn = [p_up[0], p_up[1], p_gt[0], p_gt[1]]

    s.dma("sync", scu_sb[:], scu[:], ch="consts")
    s.dma("sync", scg_sb[:], scg[:], ch="consts")
    s.dma("sync", ccb128[:], constsb[:], ch="consts")
    t_sc = t_cc128 = s.chan_total("consts")

    NCH = TOK // 512
    for c in range(NCH):
        t_lnst = {}
        for tt in range(4):
            s.guard("sync", ("x1", tt))
            t_x = s.dma("sync", x1[:, tt * DM:(tt + 1) * DM],
                        xrows[c * 512 + tt * 128: c * 512 + (tt + 1) * 128, :], ch=("x1", tt))
            par = tt % 2
            s.guard("scalar", ("yts", par))
            t_yh = s.dma("scalar", yts[:, par * DM:(par + 1) * DM],
                         yhi[:, c * 512 + tt * 128: c * 512 + (tt + 1) * 128], transpose=True,
                         ch=("yts", par))
            s.wait("vector", t_x)
            s.wait("vector", t_yh)
            t_a1 = s.emit("vector", lambda v, tt=tt, par=par: v.tensor_tensor(
                out=x1[:, tt * DM:(tt + 1) * DM], in0=x1[:, tt * DM:(tt + 1) * DM],
                in1=yts[:, par * DM:(par + 1) * DM], op=ALU.add))
            s.use(("yts", par), t_a1)
            s.guard("scalar", ("yts", par))
            t_yl = s.dma("scalar", yts[:, par * DM:(par + 1) * DM],
                         ylo[:, c * 512 + tt * 128: c * 512 + (tt + 1) * 128], transpose=True,
                         ch=("yts", par))
            s.wait("vector", t_yl)
            t_a2 = s.emit("vector", lambda v, tt=tt, par=par: v.tensor_tensor(
                out=x1[:, tt * DM:(tt + 1) * DM], in0=x1[:, tt * DM:(tt + 1) * DM],
                in1=yts[:, par * DM:(par + 1) * DM], op=ALU.add))
            s.use(("yts", par), t_a2)
            s.emit("vector", lambda v, tt=tt: v.reduce_sum(
                stats[:, 0:1], x1[:, tt * DM:(tt + 1) * DM], axis=AX.X))
            s.wait("scalar", t_a2)
            t_sq = s.emit("scalar", lambda sc, tt=tt: sc.activation(
                out=sq_junk[:], in_=x1[:, tt * DM:(tt + 1) * DM], func=AF.Square,
                accum_out=stats[:, 1:2]))
            s.wait("vector", t_sq)
            s.emit("vector", lambda v: v.tensor_scalar(
                out=stats[:, 2:3], in0=stats[:, 0:1], scalar1=1.0 / DM, scalar2=None, op0=ALU.mult))
            s.emit("vector", lambda v: v.tensor_scalar(
                out=stats[:, 5:6], in0=stats[:, 1:2], scalar1=1.0 / DM, scalar2=None, op0=ALU.mult))
            s.emit("vector", lambda v: v.tensor_tensor(
                out=stats[:, 3:4], in0=stats[:, 2:3], in1=stats[:, 2:3], op=ALU.mult))
            t_var = s.emit("vector", lambda v: v.tensor_tensor(
                out=stats[:, 3:4], in0=stats[:, 5:6], in1=stats[:, 3:4], op=ALU.subtract))
            s.wait("scalar", t_var)
            if c == 0 and tt == 0:
                s.wait("scalar", t_cc128)
            s.emit("scalar", lambda sc: sc.activation(
                out=stats[:, 4:5], in_=stats[:, 3:4], func=AF.Ln, bias=ccb128[:, 0:1]))
            t_rstd = s.emit("scalar", lambda sc: sc.activation(
                out=stats[:, 4:5], in_=stats[:, 4:5], func=AF.Exp, scale=ccb128[:, 1:2]))
            s.wait("vector", t_rstd)
            s.guard("vector", "xlnbf")
            t_ln = s.emit("vector", lambda v, tt=tt: v.tensor_scalar(
                out=xlnbf[:], in0=x1[:, tt * DM:(tt + 1) * DM], scalar1=stats[:, 2:3],
                scalar2=stats[:, 4:5], op0=ALU.subtract, op1=ALU.mult))
            s.wait("gpsimd", t_ln)
            t_xst = s.dma("gpsimd", xln_d[c * 512 + tt * 128: c * 512 + (tt + 1) * 128, :], xlnbf[:],
                          ch="xlnst")
            s.use("xlnbf", t_xst)
            t_lnst[tt] = t_xst

        if debug and c == 0:
            s.wait("gpsimd", s.chan_total("xlnst"))
            for tt in range(4):
                s.dma("gpsimd", dbg_x1[tt * 128:(tt + 1) * 128, :], x1[:, tt * DM:(tt + 1) * DM], ch="dbg")
            t_dmp = s.dma("gpsimd", dbg_xln[:], xln_d[0:512, :], ch="dbg")
            t_dt = s.chan_total("dbg")
            for e in ("sync", "scalar", "vector", "tensor"):
                s.wait(e, t_dt)
        t_allst = s.chan_total("xlnst")
        for k in range(NKT):
            s.guard("scalar", ("xlnT", k))
            s.wait("scalar", t_allst)
            s.dma("scalar", xlnT[:, k * 512:(k + 1) * 512],
                  xln_d[c * 512:(c + 1) * 512, k * 128:(k + 1) * 128], transpose=True, ch="xlnT")
        t_xt_all = s.chan_total("xlnT")

        for i in range(64):
            par = i % 2
            s.guard("sync", ("wup", par))
            t_wu = s.dma("sync",
                         wup_sb[:, par * 2048:(par + 1) * 2048].rearrange("p (k f) -> p k f", k=16),
                         w2up[:, i * 128:(i + 1) * 128].rearrange("(k p) f -> p k f", p=128),
                         ch=("wup", par))
            s.guard("sync", ("wgt", par))
            t_wg = s.dma("sync",
                         wgt_sb[:, par * 2048:(par + 1) * 2048].rearrange("p (k f) -> p k f", k=16),
                         w2gate[:, i * 128:(i + 1) * 128].rearrange("(k p) f -> p k f", p=128),
                         ch=("wgt", par))
            s.wait("tensor", t_wu)
            s.wait("tensor", t_wg)
            s.guard("tensor", ("p_up", par))
            s.guard("tensor", ("p_gt", par))
            s.wait("tensor", t_xt_all)
            for k in range(NKT):
                s.emit("tensor", lambda t, i=i, k=k, par=par: t.matmul(
                    p_up[par][:], wup_sb[:, par * 2048 + k * 128: par * 2048 + (k + 1) * 128],
                    xlnT[:, k * 512:(k + 1) * 512], start=(k == 0), stop=(k == NKT - 1)))
            t_mm2 = None
            for k in range(NKT):
                t_mm2 = s.emit("tensor", lambda t, i=i, k=k, par=par: t.matmul(
                    p_gt[par][:], wgt_sb[:, par * 2048 + k * 128: par * 2048 + (k + 1) * 128],
                    xlnT[:, k * 512:(k + 1) * 512], start=(k == 0), stop=(k == NKT - 1)))
            s.use(("wup", par), t_mm2)
            s.use(("wgt", par), t_mm2)
            if i >= 62:
                for k in range(NKT):
                    s.use(("xlnT", k), t_mm2)
            s.wait("vector", t_mm2)
            s.wait("vector", t_sc)
            t_up = s.emit("vector", lambda v, i=i, par=par: v.tensor_scalar(
                out=up_sb[:, par * 512:(par + 1) * 512], in0=p_up[par][:],
                scalar1=scu_sb[:, i:i + 1], scalar2=None, op0=ALU.mult))
            s.use(("p_up", par), t_up)
            s.wait("scalar", t_mm2)
            t_gs = s.emit("scalar", lambda sc, i=i, par=par: sc.activation(
                out=gs_sb[:, par * 512:(par + 1) * 512], in_=p_gt[par][:],
                func=AF.Silu, scale=scg_sb[:, i:i + 1]))
            s.use(("p_gt", par), t_gs)
            s.wait("vector", t_gs)
            s.guard("vector", ("zrd", i))
            t_z = s.emit("vector", lambda v, i=i, par=par: v.tensor_tensor(
                out=zbf[:, i * 512:(i + 1) * 512], in0=up_sb[:, par * 512:(par + 1) * 512],
                in1=gs_sb[:, par * 512:(par + 1) * 512], op=ALU.mult))
            s.use(("zmm", i), t_z)

        if debug and c == 0:
            t_dz = None
            for i in range(64):
                s.wait("gpsimd", s.last_use.get(("zmm", i)))
                t_dz = s.dma("gpsimd", dbg_z[i * 128:(i + 1) * 128, :], zbf[:, i * 512:(i + 1) * 512])
            s.wait("vector", t_dz)
            s.wait("tensor", t_dz)
        for oc in range(4):
            t_dn = {}
            for ig in range(4):
                wpar = ig % 2
                s.guard("sync", ("wdn", wpar))
                t_wd = s.dma("sync",
                             wdn_sb[:, wpar * 8192:(wpar + 1) * 8192].rearrange("p (k f) -> p k f", k=16),
                             w2down[ig * 2048:(ig + 1) * 2048, oc * 512:(oc + 1) * 512]
                             .rearrange("(k p) f -> p k f", p=128))
                s.wait("tensor", t_wd)
                for tt in range(4):
                    if ig == 0:
                        s.guard("tensor", ("p_up", tt) if tt < 2 else ("p_gt", tt - 2))
                    for ii in range(16):
                        i = ig * 16 + ii
                        s.wait("tensor", s.last_use.get(("zmm", i)))
                        t_dn[tt] = s.emit("tensor", lambda t, tt=tt, i=i, ii=ii, wpar=wpar, ig=ig: t.matmul(
                            p_dn[tt][:], zbf[:, i * 512 + tt * 128: i * 512 + (tt + 1) * 128],
                            wdn_sb[:, wpar * 8192 + ii * 512:wpar * 8192 + (ii + 1) * 512],
                            start=(ig == 0 and ii == 0), stop=(ig == 3 and ii == 15)))
                s.use(("wdn", wpar), t_dn[3])
            for tt in range(4):
                if oc == 3:
                    for i in range(64):
                        s.use(("zrd", i), t_dn[3])
                s.wait("vector", t_dn[tt])
                ppar = tt % 2
                s.guard("vector", ("x2s", ppar))
                t_x2 = s.emit("vector", lambda v, tt=tt, oc=oc, ppar=ppar: v.tensor_tensor(
                    out=x2s[:, ppar * 512:(ppar + 1) * 512], in0=p_dn[tt][:],
                    in1=x1[:, tt * DM + oc * 512: tt * DM + (oc + 1) * 512], op=ALU.add))
                s.use(("p_up", tt) if tt < 2 else ("p_gt", tt - 2), t_x2)
                if oc == 3:
                    s.use(("x1", tt), t_x2)
                s.wait("gpsimd", t_x2)
                t_so = s.dma("gpsimd",
                             x2o_d[c * 512 + tt * 128: c * 512 + (tt + 1) * 128, oc * 512:(oc + 1) * 512],
                             x2s[:, ppar * 512:(ppar + 1) * 512], ch="x2st")
                s.use(("x2s", ppar), t_so)

    s.wait_self("gpsimd", s.chan_total("x2st"))
    return nc, s



def build_stage_c():
    nc = bass.Bass("TRN2", target_bir_lowering=False)
    s = Sched()
    x2r = nc.dram_tensor("x2rows", [TOK + 16, DM], F32, kind="ExternalInput").ap()
    wq = nc.dram_tensor("wq", [DM, KDIM], BF16, kind="ExternalInput").ap()
    wk0 = nc.dram_tensor("wk0", [DM, KDIM], BF16, kind="ExternalInput").ap()
    wk1 = nc.dram_tensor("wk1", [DM, KDIM], BF16, kind="ExternalInput").ap()
    wv0 = nc.dram_tensor("wv0", [DM, VDIM], BF16, kind="ExternalInput").ap()
    wv1 = nc.dram_tensor("wv1", [DM, VDIM], BF16, kind="ExternalInput").ap()
    wb01 = nc.dram_tensor("wb01", [DM, 12], BF16, kind="ExternalInput").ap()
    wa = nc.dram_tensor("wa", [DM, 6], BF16, kind="ExternalInput").ap()
    wg = nc.dram_tensor("wg", [DM, VDIM], BF16, kind="ExternalInput").ap()
    cw = nc.dram_tensor("cw", [128, 84 * 4], F32, kind="ExternalInput").ap()
    alogdt = nc.dram_tensor("alogdt", [6, 2], F32, kind="ExternalInput").ap()
    constsc = nc.dram_tensor("constsc", [128, 6], F32, kind="ExternalInput").ap()
    identc = nc.dram_tensor("identc", [128, 128], F32, kind="ExternalInput").ap()
    Ks = nc.dram_tensor("Ks", [2 * TOK, KDIM], BF16, kind="ExternalOutput").ap()
    Qs = nc.dram_tensor("Qs", [2 * TOK, KDIM], BF16, kind="ExternalOutput").ap()
    Vs = nc.dram_tensor("Vs", [2 * TOK, VDIM], BF16, kind="ExternalOutput").ap()
    bgz = nc.dram_tensor("bgz", [2 * TOK, 12], F32, kind="ExternalOutput").ap()
    Kse = Ks.rearrange("(t two) f -> two t f", two=2)
    Qse = Qs.rearrange("(t two) f -> two t f", two=2)
    Vse = Vs.rearrange("(t two) f -> two t f", two=2)
    bgze = bgz.rearrange("(t two) f -> two t f", two=2)
    gg_d = nc.dram_tensor("gg", [TOK, VDIM], BF16, kind="ExternalOutput").ap()
    xln_d = nc.dram_tensor("xln_sc", [TOK + 16, DM], BF16, kind="Internal").ap()
    qc_d = nc.dram_tensor("qc_sc", [KDIM, TOK], BF16, kind="Internal").ap()
    k0c_d = nc.dram_tensor("k0c_sc", [KDIM, TOK], BF16, kind="Internal").ap()
    k1c_d = nc.dram_tensor("k1c_sc", [KDIM, TOK], BF16, kind="Internal").ap()
    v0c_d = nc.dram_tensor("v0c_sc", [VDIM, TOK], BF16, kind="Internal").ap()
    v1c_d = nc.dram_tensor("v1c_sc", [VDIM, TOK], BF16, kind="Internal").ap()

    sb = lambda n, shp, dt: nc.alloc_sbuf_tensor(n, shp, dt).ap()
    ps = lambda n, shp, dt: nc.alloc_psum_tensor(n, shp, dt).ap()

    xin = sb("xin", [128, DM], F32)
    sq_junk = sb("sq_junk", [128, DM], BF16)
    xlnbf = sb("xlnbf", [128, DM], BF16)
    stats = sb("stats", [128, 8], F32)
    cc = sb("cc", [128, 6], F32)
    hT = sb("hT", [128, 16 * 512], BF16)
    wproj = sb("wproj", [128, 2 * 2048], BF16)
    wsmall = sb("wsmall", [128, 16 * 18], BF16)
    projc = sb("projc", [128, 2 * 520], F32)
    carry = sb("carry", [128, 84 * 4], F32)
    convo = sb("convo", [128, 2 * 512], F32)
    cw_sb = sb("cw_sb", [128, 84 * 4], F32)
    silub = sb("silub", [128, 2 * 512], BF16)
    wmov = sb("wmov", [128, 2 * 16 * 512], BF16)
    ggsb = sb("ggsb", [128, 2 * 512], BF16)
    negA = sb("negA", [6, 2], F32)
    bsp = sb("bsp", [128, 5 * 512], F32)
    bgt = sb("bgt", [128, 2 * 12], F32)
    zero6 = sb("zero6", [128, 6], F32)
    idc = sb("idc", [128, 128], F32)
    idcb = sb("idcb", [16, 16], BF16)
    tq = sb("tq", [128, KDIM], BF16)
    tk0 = sb("tk0", [128, KDIM], BF16)
    tk1 = sb("tk1", [128, KDIM], BF16)
    tv0 = sb("tv0", [128, VDIM], BF16)
    tv1 = sb("tv1", [128, VDIM], BF16)
    tmpsq = sb("tmpsq", [128, KDIM], F32)
    nsq = sb("nsq", [128, 18], F32)
    outq = sb("outq", [128, KDIM], BF16)
    outk = sb("outk", [128, KDIM], BF16)
    zeroq = sb("zeroq", [128, KDIM], BF16)

    p_pr = [ps("p_pr0", [128, 512], F32), ps("p_pr1", [128, 512], F32)]
    p_gg = [ps("p_gg0", [128, 512], F32), ps("p_gg1", [128, 512], F32)]
    p_ba = ps("p_ba", [128, 512], F32)
    p_aa = ps("p_aa", [128, 512], F32)
    p_bt = ps("p_bt", [128, 128], F32)
    p_btb = ps("p_btb", [128, 128], BF16)

    s.dma("sync", cw_sb[:], cw[:], ch="consts")
    s.dma("sync", cc[:], constsc[:], ch="consts")
    s.dma("sync", negA[:], alogdt[:], ch="consts")
    s.dma("sync", idc[:], identc[:], ch="consts")
    s.dma("gpsimd", idcb[:], identc[0:16, 0:16], ch="consts")
    t_consts = s.chan_total("consts")
    t_z6 = s.emit("vector", lambda v: v.memset(zero6[:], 0.0))
    t_zq = s.emit("vector", lambda v: v.memset(zeroq[:], 0.0))
    s.wait("scalar", t_consts)
    t_expA = s.emit("scalar", lambda sc: sc.activation(
        out=negA[0:6, 0:1], in_=negA[0:6, 0:1], func=AF.Exp))
    s.wait("vector", t_expA)
    t_negA = s.emit("vector", lambda v: v.tensor_scalar(
        out=negA[0:6, 0:1], in0=negA[0:6, 0:1], scalar1=-1.0, scalar2=None, op0=ALU.mult))
    for k in range(NKT):
        s.dma("sync", wsmall[:, k * 18:k * 18 + 12], wb01[k * 128:(k + 1) * 128, :], ch="wsm")
        s.dma("sync", wsmall[:, k * 18 + 12:k * 18 + 18], wa[k * 128:(k + 1) * 128, :], ch="wsm")
    t_wsm = s.chan_total("wsm")

    def ln_tile(rows, src_lo, dst_lo, first):
        nonlocal t_store, t_ln, t_sq
        if t_ln is not None:
            s.wait("sync", t_ln)
            s.wait("sync", t_sq)
        t_ld = s.dma("sync", xin[0:rows, :], x2r[src_lo:src_lo + rows, :], ch="xin")
        s.wait("vector", t_ld)
        s.emit("vector", lambda v: v.reduce_sum(stats[0:rows, 0:1], xin[0:rows, :], axis=AX.X))
        s.wait("scalar", t_ld)
        t_sq = s.emit("scalar", lambda sc: sc.activation(
            out=sq_junk[0:rows, :], in_=xin[0:rows, :], func=AF.Square, accum_out=stats[0:rows, 1:2]))
        s.wait("vector", t_sq)
        s.emit("vector", lambda v: v.tensor_scalar(
            out=stats[0:rows, 2:3], in0=stats[0:rows, 0:1], scalar1=1.0 / DM, scalar2=None, op0=ALU.mult))
        s.emit("vector", lambda v: v.tensor_scalar(
            out=stats[0:rows, 5:6], in0=stats[0:rows, 1:2], scalar1=1.0 / DM, scalar2=None, op0=ALU.mult))
        s.emit("vector", lambda v: v.tensor_tensor(
            out=stats[0:rows, 3:4], in0=stats[0:rows, 2:3], in1=stats[0:rows, 2:3], op=ALU.mult))
        t_var = s.emit("vector", lambda v: v.tensor_tensor(
            out=stats[0:rows, 3:4], in0=stats[0:rows, 5:6], in1=stats[0:rows, 3:4], op=ALU.subtract))
        s.wait("scalar", t_var)
        if first:
            s.wait("scalar", t_consts)
        s.emit("scalar", lambda sc: sc.activation(
            out=stats[0:rows, 4:5], in_=stats[0:rows, 3:4], func=AF.Ln, bias=cc[0:rows, 0:1]))
        t_rstd = s.emit("scalar", lambda sc: sc.activation(
            out=stats[0:rows, 4:5], in_=stats[0:rows, 4:5], func=AF.Exp, scale=cc[0:rows, 1:2]))
        s.wait("vector", t_rstd)
        if t_store is not None:
            s.wait("vector", t_store)
        t_ln = s.emit("vector", lambda v: v.tensor_scalar(
            out=xlnbf[0:rows, :], in0=xin[0:rows, :], scalar1=stats[0:rows, 2:3],
            scalar2=stats[0:rows, 4:5], op0=ALU.subtract, op1=ALU.mult))
        s.wait("gpsimd", t_ln)
        t_store = s.dma("gpsimd", xln_d[dst_lo:dst_lo + rows, :], xlnbf[0:rows, :], ch="xlnst")

    t_store = t_ln = t_sq = None
    for tt in range(16):
        ln_tile(128, 16 + tt * 128, 16 + tt * 128, tt == 0)
    ln_tile(16, 0, 0, False)
    t_allln = s.chan_total("xlnst")

    ftiles = []
    for i in range(12):
        ftiles.append((wq, qc_d, i))
    for i in range(12):
        ftiles.append((wk0, k0c_d, i))
    for i in range(12):
        ftiles.append((wk1, k1c_d, i))
    for i in range(24):
        ftiles.append((wv0, v0c_d, i))
    for i in range(24):
        ftiles.append((wv1, v1c_d, i))

    s.wait("sync", t_allln)
    t_hl = s.dma("sync", xlnbf[0:16, :], xln_d[0:16, :], ch="xin")
    t_hx_all = None
    for k in range(NKT):
        s.wait("tensor", t_hl)
        s.wait("tensor", t_consts)
        s.guard("tensor", "p_btb")
        t_tp = s.emit("tensor", lambda t, k=k: t.transpose(
            p_btb[:, 0:16], xlnbf[0:16, k * 128:(k + 1) * 128], idcb[0:16, 0:16]))
        s.wait("vector", t_tp)
        t_hx_all = s.emit("vector", lambda v, k=k: v.tensor_copy(
            hT[:, k * 512:k * 512 + 16], p_btb[:, 0:16]))
        s.use("p_btb", t_hx_all)
    for fi, (wap, scd_, fw) in enumerate(ftiles):
        par = fi % 2
        s.guard("sync", ("wpr", par))
        t_w = s.dma("sync",
                    wproj[:, par * 2048:(par + 1) * 2048].rearrange("p (k f) -> p k f", k=16),
                    wap[:, fw * 128:(fw + 1) * 128].rearrange("(k p) f -> p k f", p=128),
                    ch=("wpr", par))
        s.wait("tensor", t_w)
        s.wait("tensor", t_hx_all)
        s.guard("tensor", ("p_pr", par))
        t_mm = None
        for k in range(NKT):
            t_mm = s.emit("tensor", lambda t, k=k, par=par: t.matmul(
                p_pr[par][:, 0:16], wproj[:, par * 2048 + k * 128: par * 2048 + (k + 1) * 128],
                hT[:, k * 512:k * 512 + 16], start=(k == 0), stop=(k == NKT - 1)))
        s.use(("wpr", par), t_mm)
        for k in range(NKT):
            s.use(("hT", k), t_mm)
        s.wait("vector", t_mm)
        t_cv = s.emit("vector", lambda v, fi=fi, par=par: v.tensor_copy(
            carry[:, fi * 4:fi * 4 + 3], p_pr[par][:, 13:16]))
        s.use(("p_pr", par), t_cv)

    NCH = TOK // 512
    for c in range(NCH):
        for k in range(NKT):
            s.guard("scalar", ("hT", k))
            s.wait("scalar", t_allln)
            s.dma("scalar", hT[:, k * 512:(k + 1) * 512],
                  xln_d[16 + c * 512:16 + (c + 1) * 512, k * 128:(k + 1) * 128],
                  transpose=True, ch="hT")
        t_hT = s.chan_total("hT")

        for fi, (wap, scd_, fw) in enumerate(ftiles):
            par = fi % 2
            s.guard("sync", ("wpr", par))
            t_w = s.dma("sync",
                        wproj[:, par * 2048:(par + 1) * 2048].rearrange("p (k f) -> p k f", k=16),
                        wap[:, fw * 128:(fw + 1) * 128].rearrange("(k p) f -> p k f", p=128),
                        ch=("wpr", par))
            s.wait("tensor", t_w)
            s.wait("tensor", t_hT)
            s.guard("tensor", ("p_pr", par))
            t_mm = None
            for k in range(NKT):
                t_mm = s.emit("tensor", lambda t, k=k, par=par: t.matmul(
                    p_pr[par][:], wproj[:, par * 2048 + k * 128: par * 2048 + (k + 1) * 128],
                    hT[:, k * 512:(k + 1) * 512], start=(k == 0), stop=(k == NKT - 1)))
            s.use(("wpr", par), t_mm)
            s.wait("vector", t_mm)
            s.guard("vector", ("projc", par))
            s.emit("vector", lambda v, par=par: v.tensor_copy(
                projc[:, par * 520 + 3: par * 520 + 515], p_pr[par][:]))
            s.use(("p_pr", par), s.emit("vector", lambda v, fi=fi, par=par: v.tensor_copy(
                projc[:, par * 520: par * 520 + 3], carry[:, fi * 4:fi * 4 + 3])))
            s.emit("vector", lambda v, fi=fi, par=par: v.tensor_copy(
                carry[:, fi * 4:fi * 4 + 3], projc[:, par * 520 + 512: par * 520 + 515]))
            s.wait("vector", t_consts)
            s.emit("vector", lambda v, fi=fi, par=par: v.tensor_scalar(
                out=convo[:, par * 512:(par + 1) * 512],
                in0=projc[:, par * 520: par * 520 + 512],
                scalar1=cw_sb[:, fi * 4 + 0: fi * 4 + 1], scalar2=None, op0=ALU.mult))
            t_cv1 = None
            for j in range(1, 4):
                t_cv1 = s.emit("vector", lambda v, fi=fi, par=par, j=j: v.scalar_tensor_tensor(
                    out=convo[:, par * 512:(par + 1) * 512],
                    in0=projc[:, par * 520 + j: par * 520 + j + 512],
                    scalar=cw_sb[:, fi * 4 + j: fi * 4 + j + 1],
                    in1=convo[:, par * 512:(par + 1) * 512],
                    op0=ALU.mult, op1=ALU.add))
            s.use(("projc", par), t_cv1)
            s.wait("scalar", t_cv1)
            s.guard("scalar", ("silub", par))
            t_si = s.emit("scalar", lambda sc, par=par: sc.activation(
                out=silub[:, par * 512:(par + 1) * 512], in_=convo[:, par * 512:(par + 1) * 512],
                func=AF.Silu))
            s.wait("gpsimd", t_si)
            t_ss = s.dma("gpsimd", scd_[fw * 128:(fw + 1) * 128, c * 512:(c + 1) * 512],
                         silub[:, par * 512:(par + 1) * 512], ch="convst")
            s.use(("silub", par), t_ss)

        for fc in range(6):
            par = fc % 2
            s.guard("sync", ("wmov", par))
            t_wm = s.dma("sync", wmov[:, par * 8192:(par + 1) * 8192]
                         .rearrange("p (k f) -> p k f", k=16),
                         wg[:, fc * 512:(fc + 1) * 512].rearrange("(k p) f -> p k f", p=128),
                         ch=("wmov", par))
            for tt in range(4):
                ppar = tt % 2
                s.wait("tensor", t_wm)
                s.wait("tensor", t_hT)
                s.guard("tensor", ("p_gg", ppar))
                t_mm = None
                for k in range(NKT):
                    t_mm = s.emit("tensor", lambda t, k=k, tt=tt, par=par, ppar=ppar: t.matmul(
                        p_gg[ppar][:], hT[:, k * 512 + tt * 128: k * 512 + (tt + 1) * 128],
                        wmov[:, par * 8192 + k * 512:par * 8192 + (k + 1) * 512],
                        start=(k == 0), stop=(k == NKT - 1)))
                if tt == 3:
                    s.use(("wmov", par), t_mm)
                s.wait("vector", t_mm)
                s.guard("vector", ("ggsb", ppar))
                t_ge = s.emit("vector", lambda v, ppar=ppar: v.tensor_copy(
                    ggsb[:, ppar * 512:(ppar + 1) * 512], p_gg[ppar][:]))
                s.use(("p_gg", ppar), t_ge)
                s.wait("gpsimd", t_ge)
                t_gs = s.dma("gpsimd", gg_d[c * 512 + tt * 128: c * 512 + (tt + 1) * 128,
                                            fc * 512:(fc + 1) * 512],
                             ggsb[:, ppar * 512:(ppar + 1) * 512], ch="ggst")
                s.use(("ggsb", ppar), t_gs)

        s.wait("tensor", t_wsm)
        s.guard("tensor", "p_ba")
        t_mmb = None
        for k in range(NKT):
            t_mmb = s.emit("tensor", lambda t, k=k: t.matmul(
                p_ba[0:12, :], wsmall[:, k * 18:k * 18 + 12],
                hT[:, k * 512:(k + 1) * 512], start=(k == 0), stop=(k == NKT - 1)))
        s.guard("tensor", "p_aa")
        t_mma = None
        for k in range(NKT):
            t_mma = s.emit("tensor", lambda t, k=k: t.matmul(
                p_aa[0:6, :], wsmall[:, k * 18 + 12:k * 18 + 18],
                hT[:, k * 512:(k + 1) * 512], start=(k == 0), stop=(k == NKT - 1)))
        s.wait("scalar", t_mmb)
        s.guard("scalar", "bsp0")
        t_bs = s.emit("scalar", lambda sc: sc.activation(
            out=bsp[0:12, 0:512], in_=p_ba[0:12, :], func=AF.Sigmoid))
        s.use("p_ba", t_bs)
        s.wait("vector", t_mma)
        s.wait("vector", t_negA)
        t_a0 = s.emit("vector", lambda v: v.tensor_scalar(
            out=bsp[0:6, 512:1024], in0=p_aa[0:6, :], scalar1=negA[0:6, 1:2],
            scalar2=None, op0=ALU.add))
        s.use("p_aa", t_a0)
        s.wait("scalar", t_a0)
        s.emit("scalar", lambda sc: sc.activation(
            out=bsp[0:6, 1024:1536], in_=bsp[0:6, 512:1024], func=AF.Abs))
        s.emit("scalar", lambda sc: sc.activation(
            out=bsp[0:6, 1024:1536], in_=bsp[0:6, 1024:1536], func=AF.Exp, scale=cc[0:6, 3:4]))
        t_l = s.emit("scalar", lambda sc: sc.activation(
            out=bsp[0:6, 1024:1536], in_=bsp[0:6, 1024:1536], func=AF.Ln, bias=cc[0:6, 2:3]))
        s.emit("vector", lambda v: v.tensor_scalar(
            out=bsp[0:6, 1536:2048], in0=bsp[0:6, 512:1024], scalar1=0.0, scalar2=None,
            op0=ALU.max))
        s.wait("vector", t_l)
        s.emit("vector", lambda v: v.tensor_tensor(
            out=bsp[0:6, 1536:2048], in0=bsp[0:6, 1536:2048], in1=bsp[0:6, 1024:1536], op=ALU.add))
        t_g = s.emit("vector", lambda v: v.tensor_scalar(
            out=bsp[0:6, 1536:2048], in0=bsp[0:6, 1536:2048], scalar1=negA[0:6, 0:1],
            scalar2=None, op0=ALU.mult))
        for bb in range(4):
            s.wait("tensor", t_bs)
            s.wait("tensor", t_g)
            s.guard("tensor", "p_bt")
            t_t1 = s.emit("tensor", lambda t, bb=bb: t.transpose(
                p_bt[:, 0:12], bsp[0:12, bb * 128:(bb + 1) * 128], idc[0:12, 0:12]))
            s.wait("vector", t_t1)
            s.guard("vector", "bgeo")
            t_e1 = s.emit("vector", lambda v: v.tensor_copy(bgt[:, 0:12], p_bt[:, 0:12]))
            s.use("p_bt", t_e1)
            s.guard("tensor", "p_bt")
            t_t2 = s.emit("tensor", lambda t, bb=bb: t.transpose(
                p_bt[:, 0:6], bsp[0:6, 1536 + bb * 128:1536 + (bb + 1) * 128], idc[0:6, 0:6]))
            s.wait("vector", t_t2)
            t_e2 = s.emit("vector", lambda v: v.tensor_copy(bgt[:, 12:18], p_bt[:, 0:6]))
            s.use("p_bt", t_e2)
            s.guard("vector", "bgeo")
            s.emit("vector", lambda v: v.tensor_copy(bgt[:, 18:24], bgt[:, 6:12]))
            t_asm = s.emit("vector", lambda v: v.tensor_copy(bgt[:, 6:12], bgt[:, 12:18]))
            s.wait("gpsimd", t_asm)
            base = c * 512 + bb * 128
            t_w1 = s.dma("gpsimd", bgze[0, base:base + 128, 0:12], bgt[:, 0:12], ch="bgst")
            t_w2 = s.dma("gpsimd", bgze[1, base:base + 128, 0:6], bgt[:, 18:24], ch="bgst")
            s.wait("gpsimd", t_z6)
            t_w3 = s.dma("gpsimd", bgze[1, base:base + 128, 6:12], zero6[:, 0:6], ch="bgst")
            s.use("bgeo", t_w3)

    t_allconv = s.chan_total("convst")
    for tt2 in range(TOK // 128):
        t0 = tt2 * 128
        s.guard("scalar", "tq")
        s.wait("scalar", t_allconv)
        t_lq = s.dma("scalar", tq[:], qc_d[:, t0:t0 + 128], transpose=True, ch="tq")
        s.guard("scalar", "tk0")
        t_lk0 = s.dma("scalar", tk0[:], k0c_d[:, t0:t0 + 128], transpose=True, ch="tk0")
        s.guard("scalar", "tk1")
        t_lk1 = s.dma("scalar", tk1[:], k1c_d[:, t0:t0 + 128], transpose=True, ch="tk1")
        s.guard("scalar", "tv0")
        t_lv0 = s.dma("scalar", tv0[:], v0c_d[:, t0:t0 + 128], transpose=True, ch="tv0")
        s.guard("scalar", "tv1")
        t_lv1 = s.dma("scalar", tv1[:], v1c_d[:, t0:t0 + 128], transpose=True, ch="tv1")
        for h in range(6):
            s.wait("scalar", t_lq)
            s.emit("scalar", lambda sc, h=h: sc.activation(
                out=sq_junk[:, 0:256], in_=tq[:, h * 256:(h + 1) * 256], func=AF.Square,
                accum_out=nsq[:, h:h + 1]))
            s.wait("scalar", t_lk0)
            s.emit("scalar", lambda sc, h=h: sc.activation(
                out=sq_junk[:, 0:256], in_=tk0[:, h * 256:(h + 1) * 256], func=AF.Square,
                accum_out=nsq[:, 6 + h:7 + h]))
            s.wait("scalar", t_lk1)
            t_sqq = s.emit("scalar", lambda sc, h=h: sc.activation(
                out=sq_junk[:, 0:256], in_=tk1[:, h * 256:(h + 1) * 256], func=AF.Square,
                accum_out=nsq[:, 12 + h:13 + h]))
        s.wait("scalar", t_sqq)
        t_ln1 = s.emit("scalar", lambda sc: sc.activation(
            out=nsq[:, 0:18], in_=nsq[:, 0:18], func=AF.Ln, bias=cc[:, 4:5]))
        t_rn = s.emit("scalar", lambda sc: sc.activation(
            out=nsq[:, 0:18], in_=nsq[:, 0:18], func=AF.Exp, scale=cc[:, 1:2]))
        s.wait("vector", t_rn)
        t_qs = s.emit("vector", lambda v: v.tensor_scalar(
            out=nsq[:, 0:6], in0=nsq[:, 0:6], scalar1=float(DK) ** -0.5, scalar2=None,
            op0=ALU.mult))
        s.guard("vector", "outq")
        t_oq = None
        for h in range(6):
            t_oq = s.emit("vector", lambda v, h=h: v.tensor_scalar(
                out=outq[:, h * 256:(h + 1) * 256], in0=tq[:, h * 256:(h + 1) * 256],
                scalar1=nsq[:, h:h + 1], scalar2=None, op0=ALU.mult))
        s.guard("vector", "outk")
        for h in range(6):
            t_kk = s.emit("vector", lambda v, h=h: v.tensor_scalar(
                out=outk[:, h * 256:(h + 1) * 256], in0=tk0[:, h * 256:(h + 1) * 256],
                scalar1=nsq[:, 6 + h:7 + h], scalar2=None, op0=ALU.mult))
        s.wait("gpsimd", t_kk)
        t_sk0 = s.dma("gpsimd", Kse[0, t0:t0 + 128, :], outk[:], ch="kst")
        s.use("outk", t_sk0)
        s.guard("vector", "outk")
        for h in range(6):
            t_kk = s.emit("vector", lambda v, h=h: v.tensor_scalar(
                out=outk[:, h * 256:(h + 1) * 256], in0=tk1[:, h * 256:(h + 1) * 256],
                scalar1=nsq[:, 12 + h:13 + h], scalar2=None, op0=ALU.mult))
        s.wait("gpsimd", t_kk)
        t_sk1 = s.dma("gpsimd", Kse[1, t0:t0 + 128, :], outk[:], ch="kst")
        s.use("outk", t_sk1)
        s.wait("gpsimd", t_oq)
        t_sq1 = s.dma("gpsimd", Qse[1, t0:t0 + 128, :], outq[:], ch="qst")
        s.use("outq", t_sq1)
        s.wait("gpsimd", t_zq)
        s.dma("gpsimd", Qse[0, t0:t0 + 128, :], zeroq[:], ch="qst")
        s.wait("gpsimd", t_lv0)
        t_sv0 = s.dma("gpsimd", Vse[0, t0:t0 + 128, :], tv0[:], ch="vst")
        s.use("tv0", t_sv0)
        s.wait("gpsimd", t_lv1)
        t_sv1 = s.dma("gpsimd", Vse[1, t0:t0 + 128, :], tv1[:], ch="vst")
        s.use("tv1", t_sv1)
        s.use("tq", t_sq1)
        s.use("tk0", t_sk0)
        s.use("tk1", t_sk1)

    for chn in ("ggst", "bgst", "kst", "qst", "vst"):
        s.wait_self("gpsimd", s.chan_total(chn))
    return nc, s



def build_stage_d(nch=None, nh=6, upto=99):
    nc = bass.Bass("TRN2", target_bir_lowering=False)
    s = Sched()
    Ks = nc.dram_tensor("Ks", [2 * TOK, KDIM], BF16, kind="ExternalInput").ap()
    Qs = nc.dram_tensor("Qs", [2 * TOK, KDIM], BF16, kind="ExternalInput").ap()
    Vs = nc.dram_tensor("Vs", [2 * TOK, VDIM], BF16, kind="ExternalInput").ap()
    bgz = nc.dram_tensor("bgz", [2 * TOK, 12], F32, kind="ExternalInput").ap()
    Sinit = nc.dram_tensor("Sinit", [KDIM, DV], F32, kind="ExternalInput").ap()
    mup = nc.dram_tensor("mup", [128, 128], F32, kind="ExternalInput").ap()
    mlo = nc.dram_tensor("mlo", [128, 128], F32, kind="ExternalInput").ap()
    identf = nc.dram_tensor("identf", [128, 128], F32, kind="ExternalInput").ap()
    identb = nc.dram_tensor("identb", [128, 128], BF16, kind="ExternalInput").ap()
    ihalf = nc.dram_tensor("ihalf", [128, 512], F32, kind="ExternalInput").ap()
    onesd = nc.dram_tensor("onesd", [6, 128], F32, kind="ExternalInput").ap()
    Sfin = nc.dram_tensor("Sfin", [KDIM, DV], F32, kind="ExternalOutput").ap()
    O_d = nc.dram_tensor("O", [2 * TOK, VDIM], BF16, kind="ExternalOutput").ap()

    sb = lambda n, shp, dt: nc.alloc_sbuf_tensor(n, shp, dt).ap()
    ps = lambda n, shp, dt: nc.alloc_psum_tensor(n, shp, dt).ap()

    Sb = [sb("Sb0", [128, 6 * 2 * 512], BF16), sb("Sb1", [128, 6 * 2 * 512], BF16)]
    Sld = sb("Sld", [128, 512], F32)
    bgld = sb("bgld", [128, 12], F32)
    grow = sb("grow", [6, 5 * 128], F32)
    onesr = sb("onesr", [6, 128], F32)
    gcols = sb("gcols", [128, 18], F32)
    lamC = sb("lamC", [128, 6], F32)
    Kt = sb("Kt", [128, 256], BF16)
    Vt = sb("Vt", [128, 512], BF16)
    Qt = sb("Qt", [128, 256], BF16)
    KTt = sb("KTt", [128, 256], BF16)
    QTt = sb("QTt", [128, 256], BF16)
    gcb = sb("gcb", [128, 128], F32)
    grb = sb("grb", [128, 128], F32)
    lrb = sb("lrb", [128, 128], F32)
    bcb = sb("bcb", [128, 128], F32)
    brb = sb("brb", [128, 128], F32)
    mupt = sb("mupt", [128, 128], F32)
    mlot = sb("mlot", [128, 128], F32)
    idf = sb("idf", [128, 128], F32)
    idb = sb("idb", [128, 128], BF16)
    ihf = sb("ihf", [128, 512], F32)
    dmi = sb("dmi", [128, 128], F32)
    dms = sb("dms", [128, 128], F32)
    dmt = sb("dmt", [128, 128], F32)
    dst_ = sb("dst_", [128, 128], F32)
    lsb = sb("lsb", [128, 128], F32)
    nsb = sb("nsb", [128, 128], BF16)
    ntsb = sb("ntsb", [128, 128], BF16)
    tmpa = sb("tmpa", [128, 128], F32)
    pw = sb("pw", [128, 2 * 128], BF16)
    qpowt = sb("qpowt", [128, 6 * 128], BF16)
    bvt = sb("bvt", [128, 512], BF16)
    bkt = sb("bkt", [128, 256], BF16)
    yw = sb("yw", [128, 512], BF16)
    yr = sb("yr", [128, 256], BF16)
    adt = sb("adt", [128, 128], BF16)
    aqkT = sb("aqkT", [128, 128], F32)
    ket = sb("ket", [128, 256], BF16)
    wet = sb("wet", [128, 512], BF16)
    gtt = sb("gtt", [128, 256], BF16)
    ftt = sb("ftt", [128, 2 * 256], BF16)
    ut = sb("ut", [128, 2 * 512], BF16)
    oev = sb("oev", [128, 512], BF16)
    scol = sb("scol", [128, 8], F32)
    sfo = sb("sfo", [128, 512], F32)

    p_a = ps("p_a", [128, 512], F32)
    p_b = ps("p_b", [128, 512], F32)
    p_w = ps("p_w", [128, 512], F32)
    p_r = ps("p_r", [128, 256], F32)
    p_o = ps("p_o", [128, 512], F32)
    p_s = [ps("p_s0", [128, 512], F32), ps("p_s1", [128, 512], F32)]
    p_g = ps("p_g", [128, 128], F32)

    s.dma("sync", mupt[:], mup[:], ch="consts")
    s.dma("sync", mlot[:], mlo[:], ch="consts")
    s.dma("sync", idf[:], identf[:], ch="consts")
    s.dma("sync", idb[:], identb[:], ch="consts")
    s.dma("sync", ihf[:], ihalf[:], ch="consts")
    s.dma("sync", onesr[:], onesd[:], ch="consts")
    t_consts = s.chan_total("consts")

    for h in range(6):
        for half in range(2):
            t_sl = s.dma("sync", Sld[:], Sinit[h * 256 + half * 128: h * 256 + (half + 1) * 128, :],
                         ch="sinit")
            s.wait("vector", t_sl)
            t_sc = s.emit("vector", lambda v, h=h, half=half: v.tensor_copy(
                Sb[1][:, (h * 2 + half) * 512:(h * 2 + half + 1) * 512], Sld[:]))
            s.wait("sync", t_sc)

    NCH = nch if nch is not None else (2 * TOK) // 128
    t_chain = {}
    for c in range(NCH):
        par = c % 2
        sold = Sb[(c + 1) % 2]
        snew = Sb[c % 2]
        s.guard("sync", "bgld_v")
        s.guard("sync", "bgld_t")
        t_bg = s.dma("sync", bgld[:], bgz[c * 128:(c + 1) * 128, :], ch="bgld")
        s.wait("tensor", t_bg)
        s.wait("vector", t_bg)
        s.wait("tensor", t_consts)
        s.guard("tensor", "p_g")
        t_tp = s.emit("tensor", lambda t: t.transpose(p_g[0:6, :], bgld[:, 6:12], idf[:]))
        s.use("bgld_t", t_tp)
        s.wait("vector", t_tp)
        s.guard("vector", "grow_s")
        s.guard("vector", "grow_t")
        t_ev = s.emit("vector", lambda v: v.tensor_copy(grow[:, 0:128], p_g[0:6, :]))
        s.use("p_g", t_ev)
        t_gam = s.emit("vector", lambda v: v.tensor_tensor_scan(
            out=grow[:, 128:256], data0=onesr[:], data1=grow[:, 0:128], initial=0.0,
            op0=ALU.mult, op1=ALU.add))
        s.wait("scalar", t_gam)
        t_lam = s.emit("scalar", lambda sc: sc.activation(
            out=grow[:, 256:384], in_=grow[:, 128:256], func=AF.Exp))
        t_er = s.emit("scalar", lambda sc: sc.activation(
            out=grow[:, 384:512], in_=grow[:, 128:256], func=AF.Exp, scale=-1.0,
            bias=grow[:, 255:256]))
        s.use("grow_s", t_er)
        for ii, (lo, tok) in enumerate(((128, t_gam), (256, t_lam), (384, t_er))):
            s.wait("tensor", tok)
            s.guard("tensor", "p_g")
            t_tp2 = s.emit("tensor", lambda t, lo=lo: t.transpose(
                p_g[:, 0:6], grow[:, lo:lo + 128], idf[0:6, 0:6]))
            s.use("grow_t", t_tp2)
            s.wait("vector", t_tp2)
            s.guard("vector", ("gcols", ii))
            t_gc = s.emit("vector", lambda v, ii=ii: v.tensor_copy(
                gcols[:, ii * 6:(ii + 1) * 6], p_g[:, 0:6]))
            s.use("p_g", t_gc)
        t_lc = s.emit("vector", lambda v: v.tensor_tensor(
            out=lamC[:], in0=gcols[:, 6:12], in1=gcols[:, 12:18], op=ALU.mult))

        for h in range(nh):
            s.guard("sync", "Kt")
            t_K = s.dma("sync", Kt[:], Ks[c * 128:(c + 1) * 128, h * 256:(h + 1) * 256], ch="ldK")
            s.guard("sync", "Vt")
            t_V = s.dma("sync", Vt[:], Vs[c * 128:(c + 1) * 128, h * 512:(h + 1) * 512], ch="ldV")
            s.guard("scalar", "KTt")
            t_KT = None
            for half in range(2):
                t_KT = s.dma("scalar", KTt[:, half * 128:(half + 1) * 128],
                             Ks[c * 128:(c + 1) * 128, h * 256 + half * 128: h * 256 + (half + 1) * 128],
                             transpose=True, ch="ldKT")
            s.guard("scalar", "QTt")
            t_QT = None
            for half in range(2):
                t_QT = s.dma("scalar", QTt[:, half * 128:(half + 1) * 128],
                             Qs[c * 128:(c + 1) * 128, h * 256 + half * 128: h * 256 + (half + 1) * 128],
                             transpose=True, ch="ldQT")
            if upto < 1:
                continue
            s.wait("vector", t_gc)
            s.guard("vector", "gcb")
            t_gcb = s.emit("vector", lambda v, h=h: v.tensor_scalar(
                out=gcb[:], in0=mupt[:], scalar1=0.0, scalar2=gcols[:, h:h + 1],
                op0=ALU.mult, op1=ALU.add))
            s.wait("tensor", t_gcb)
            s.guard("tensor", "p_g")
            t_tgrb = s.emit("tensor", lambda t: t.transpose(p_g[:], gcb[:], idf[:]))
            s.wait("vector", t_tgrb)
            s.guard("vector", "grb")
            t_grb = s.emit("vector", lambda v: v.tensor_copy(grb[:], p_g[:]))
            s.use("p_g", t_grb)
            t_bcb = s.emit("vector", lambda v, h=h: v.tensor_scalar(
                out=bcb[:], in0=mupt[:], scalar1=0.0, scalar2=bgld[:, h:h + 1],
                op0=ALU.mult, op1=ALU.add))
            s.wait("tensor", t_bcb)
            s.guard("tensor", "p_g")
            t_tbrb = s.emit("tensor", lambda t: t.transpose(p_g[:], bcb[:], idf[:]))
            s.wait("vector", t_tbrb)
            s.guard("vector", "brb")
            t_brb = s.emit("vector", lambda v: v.tensor_copy(brb[:], p_g[:]))
            s.use("p_g", t_brb)
            s.wait("scalar", t_grb)
            s.guard("scalar", "lrb")
            t_lrb = s.emit("scalar", lambda sc: sc.activation(out=lrb[:], in_=grb[:], func=AF.Exp))
            if upto < 2:
                continue
            s.guard("vector", "dmi")
            t_v1 = s.emit("vector", lambda v, h=h: v.scalar_tensor_tensor(
                out=dmi[:], in0=grb[:], scalar=gcols[:, h:h + 1], in1=mupt[:],
                op0=ALU.subtract, op1=ALU.add))
            s.wait("scalar", t_v1)
            t_dmi = s.emit("scalar", lambda sc: sc.activation(
                out=dmi[:], in_=dmi[:], func=AF.Exp, scale=-1.0))
            s.wait("vector", t_dmi)
            t_dms = s.emit("vector", lambda v: v.tensor_tensor(
                out=dms[:], in0=dmi[:], in1=idf[:], op=ALU.subtract))
            s.guard("vector", "dmt")
            t_v2 = s.emit("vector", lambda v, h=h: v.scalar_tensor_tensor(
                out=dmt[:], in0=grb[:], scalar=gcols[:, h:h + 1], in1=mlot[:],
                op0=ALU.subtract, op1=ALU.subtract))
            s.wait("scalar", t_v2)
            t_dmt = s.emit("scalar", lambda sc: sc.activation(
                out=dmt[:], in_=dmt[:], func=AF.Exp))
            s.wait("vector", t_dmt)
            t_dst = s.emit("vector", lambda v: v.tensor_tensor(
                out=dst_[:], in0=dmt[:], in1=idf[:], op=ALU.subtract))
            if upto < 3:
                continue
            s.wait("tensor", t_KT)
            s.guard("tensor", "p_a")
            t_mL = None
            for half in range(2):
                t_mL = s.emit("tensor", lambda t, half=half: t.matmul(
                    p_a[:, 0:128], KTt[:, half * 128:(half + 1) * 128],
                    KTt[:, half * 128:(half + 1) * 128], start=(half == 0), stop=(half == 1)))
            s.wait("tensor", t_QT)
            s.guard("tensor", "p_b")
            t_mA = None
            for half in range(2):
                t_mA = s.emit("tensor", lambda t, half=half: t.matmul(
                    p_b[:, 0:128], KTt[:, half * 128:(half + 1) * 128],
                    QTt[:, half * 128:(half + 1) * 128], start=(half == 0), stop=(half == 1)))
            s.use("KTt", t_mA)
            s.use("QTt", t_mA)
            s.wait("vector", t_mL)
            s.guard("vector", "lsb")
            t_lsb = s.emit("vector", lambda v: v.tensor_copy(lsb[:], p_a[:, 0:128]))
            s.use("p_a", t_lsb)
            s.wait("vector", t_mA)
            s.guard("vector", "aqkT")
            t_aqkT = s.emit("vector", lambda v: v.tensor_copy(aqkT[:], p_b[:, 0:128]))
            s.use("p_b", t_aqkT)
            if upto < 4:
                continue
            s.wait("vector", t_dms)
            s.guard("vector", "nsb")
            t_n = s.emit("vector", lambda v, h=h: v.scalar_tensor_tensor(
                out=nsb[:], in0=lsb[:], scalar=bgld[:, h:h + 1], in1=dms[:],
                op0=ALU.mult, op1=ALU.mult))
            s.wait("vector", t_dst)
            s.guard("vector", "tmpa")
            s.emit("vector", lambda v: v.tensor_tensor(
                out=tmpa[:], in0=lsb[:], in1=dst_[:], op=ALU.mult))
            s.guard("vector", "ntsb")
            t_nt = s.emit("vector", lambda v: v.tensor_tensor(
                out=ntsb[:], in0=tmpa[:], in1=brb[:], op=ALU.mult))
            s.guard("vector", "adt")
            t_adt = s.emit("vector", lambda v: v.tensor_tensor(
                out=adt[:], in0=aqkT[:], in1=dmt[:], op=ALU.mult))
            t_bl = s.emit("vector", lambda v, h=h: v.tensor_tensor(
                out=scol[:, 0:1], in0=bgld[:, h:h + 1], in1=gcols[:, 6 + h:7 + h], op=ALU.mult))
            s.wait("vector", t_V)
            s.guard("vector", "bvt")
            t_bv = s.emit("vector", lambda v, h=h: v.tensor_scalar(
                out=bvt[:], in0=Vt[:], scalar1=bgld[:, h:h + 1], scalar2=None, op0=ALU.mult))
            s.use("Vt", t_bv)
            if h == nh - 1:
                s.use("bgld_v", t_bv)
            s.wait("vector", t_K)
            s.guard("vector", "bkt")
            t_bk = s.emit("vector", lambda v: v.tensor_scalar(
                out=bkt[:], in0=Kt[:], scalar1=scol[:, 0:1], scalar2=None, op0=ALU.mult))
            s.guard("vector", "yw")
            t_yw = s.emit("vector", lambda v: v.tensor_copy(yw[:], bvt[:]))
            s.guard("vector", "yr")
            t_yr = s.emit("vector", lambda v: v.tensor_copy(yr[:], bkt[:]))
            if upto < 5:
                continue
            qpow = qpowt
            t_p_prev = s.emit("vector", lambda v: v.tensor_copy(pw[:, 0:128], nsb[:]))
            t_q_prev = s.emit("vector", lambda v: v.tensor_copy(pw[:, 128:256], ntsb[:]))
            t_qs = {0: t_q_prev}
            for lev in range(1, 7):
                s.wait("tensor", t_p_prev)
                s.wait("tensor", t_q_prev)
                s.guard("tensor", "p_b")
                t_mp = s.emit("tensor", lambda t: t.matmul(
                    p_b[:, 0:128], pw[:, 128:256], pw[:, 0:128], start=True, stop=True))
                t_mq = s.emit("tensor", lambda t: t.matmul(
                    p_b[:, 128:256], pw[:, 0:128], pw[:, 128:256], start=True, stop=True))
                s.wait("vector", t_mq)
                t_p_prev = s.emit("vector", lambda v: v.tensor_copy(pw[:, 0:128], p_b[:, 0:128]))
                t_q_prev = s.emit("vector", lambda v, lev=lev: v.tensor_copy(
                    qpow[:, (lev - 1) * 128:lev * 128], p_b[:, 128:256]))
                s.use("p_b", t_q_prev)
                t_q_prev = s.emit("vector", lambda v, lev=lev: v.tensor_copy(
                    pw[:, 128:256], qpow[:, (lev - 1) * 128:lev * 128]))
                t_qs[lev] = t_q_prev
            if upto < 6:
                continue
            for lev in range(6, 0, -1):
                s.wait("tensor", t_qs[lev])
                s.wait("tensor", t_yw)
                s.wait("tensor", t_yr)
                s.guard("tensor", "p_w")
                s.guard("tensor", "p_r")
                t_mw = s.emit("tensor", lambda t, lev=lev: t.matmul(
                    p_w[:], qpow[:, (lev - 1) * 128:lev * 128], yw[:], start=True, stop=False))
                t_mw = s.emit("tensor", lambda t: t.matmul(
                    p_w[:], idb[:], yw[:], start=False, stop=True))
                t_mr = s.emit("tensor", lambda t, lev=lev: t.matmul(
                    p_r[:], qpow[:, (lev - 1) * 128:lev * 128], yr[:], start=True, stop=False))
                t_mr = s.emit("tensor", lambda t: t.matmul(
                    p_r[:], idb[:], yr[:], start=False, stop=True))
                s.wait("vector", t_mw)
                t_yw = s.emit("vector", lambda v: v.tensor_copy(yw[:], p_w[:]))
                s.use("p_w", t_yw)
                s.wait("vector", t_mr)
                t_yr = s.emit("vector", lambda v: v.tensor_copy(yr[:], p_r[:]))
                s.use("p_r", t_yr)
            s.wait("tensor", t_n)
            s.wait("tensor", t_yw)
            s.wait("tensor", t_yr)
            s.guard("tensor", "p_w")
            s.guard("tensor", "p_r")
            t_mw = s.emit("tensor", lambda t: t.matmul(p_w[:], ntsb[:], yw[:], start=True, stop=True))
            t_mr = s.emit("tensor", lambda t: t.matmul(p_r[:], ntsb[:], yr[:], start=True, stop=True))
            s.wait("vector", t_mw)
            t_yw = s.emit("vector", lambda v: v.tensor_tensor(
                out=yw[:], in0=yw[:], in1=p_w[:], op=ALU.subtract))
            s.use("p_w", t_yw)
            s.wait("vector", t_mr)
            t_yr = s.emit("vector", lambda v: v.tensor_tensor(
                out=yr[:], in0=yr[:], in1=p_r[:], op=ALU.subtract))
            s.use("p_r", t_yr)
            if upto < 7:
                continue
            s.guard("vector", "wet")
            t_we = s.emit("vector", lambda v, h=h: v.tensor_scalar(
                out=wet[:], in0=yw[:], scalar1=gcols[:, 12 + h:13 + h], scalar2=None, op0=ALU.mult))
            s.guard("vector", "ket")
            t_ke = s.emit("vector", lambda v, h=h: v.tensor_scalar(
                out=ket[:], in0=Kt[:], scalar1=gcols[:, 12 + h:13 + h], scalar2=None, op0=ALU.mult))
            if h == nh - 1:
                for ii2 in range(3):
                    s.use(("gcols", ii2), t_ke)
                s.use("lamC_rd", t_ke)
            for half in range(2):
                s.wait("tensor", t_we)
                s.guard("tensor", "p_w")
                t_mu = s.emit("tensor", lambda t, half=half: t.matmul(
                    p_w[:], Kt[:, half * 128:(half + 1) * 128], wet[:], start=True, stop=True))
                s.wait("vector", t_mu)
                t_u = s.emit("vector", lambda v, half=half: v.tensor_copy(
                    ut[:, half * 512:(half + 1) * 512], p_w[:]))
                s.use("p_w", t_u)
                if half == 1:
                    s.use("Kt", t_mu)
            for half in range(2):
                s.wait("tensor", t_ke)
                s.wait("tensor", t_yr)
                s.guard("tensor", "p_r")
                s.guard("tensor", "p_a")
                t_mf = s.emit("tensor", lambda t, half=half: t.matmul(
                    p_a[:, 0:256], yr[:, half * 128:(half + 1) * 128], ket[:],
                    start=True, stop=True))
                s.wait("vector", t_mf)
                s.wait("vector", t_lc)
                t_ft = s.emit("vector", lambda v, half=half, h=h: v.scalar_tensor_tensor(
                    out=ftt[:, half * 256:(half + 1) * 256], in0=ihf[:, half * 256:(half + 1) * 256],
                    scalar=lamC[:, h:h + 1], in1=p_a[:, 0:256], op0=ALU.mult, op1=ALU.subtract))
                s.use("p_a", t_ft)
            for half in range(2):
                s.wait("tensor", t_adt)
                s.wait("tensor", t_yr)
                s.guard("tensor", "p_a")
                t_mg = s.emit("tensor", lambda t, half=half: t.matmul(
                    p_a[:, 0:128], yr[:, half * 128:(half + 1) * 128], adt[:],
                    start=True, stop=True))
                s.wait("vector", t_mg)
                s.wait("vector", t_lrb)
                s.guard("vector", "tmpa")
                s.emit("vector", lambda v, half=half: v.tensor_tensor(
                    out=tmpa[:], in0=QTt[:, half * 128:(half + 1) * 128], in1=lrb[:], op=ALU.mult))
                t_gt = s.emit("vector", lambda v, half=half: v.tensor_tensor(
                    out=gtt[:, half * 128:(half + 1) * 128], in0=tmpa[:], in1=p_a[:, 0:128],
                    op=ALU.subtract))
                s.use("p_a", t_gt)
                if half == 1:
                    s.use("QTt", t_gt)
            if upto < 8:
                continue
            s.wait("tensor", t_adt)
            s.wait("tensor", t_yw)
            s.guard("tensor", "p_o")
            s.emit("tensor", lambda t: t.matmul(p_o[:], adt[:], yw[:], start=True, stop=False))
            t_chn_prev = t_chain.get((c - 1, h))
            if t_chn_prev is not None:
                s.wait("tensor", t_chn_prev)
            s.wait("tensor", t_gt)
            t_mo = None
            for half in range(2):
                t_mo = s.emit("tensor", lambda t, half=half, h=h, sold=sold: t.matmul(
                    p_o[:], gtt[:, half * 128:(half + 1) * 128],
                    sold[:, (h * 2 + half) * 512:(h * 2 + half + 1) * 512],
                    start=False, stop=(half == 1)))
            s.wait("vector", t_mo)
            s.guard("vector", "oev")
            t_oev = s.emit("vector", lambda v: v.tensor_copy(oev[:], p_o[:]))
            s.use("p_o", t_oev)
            s.wait("gpsimd", t_oev)
            t_ost = s.dma("gpsimd", O_d[c * 128:(c + 1) * 128, h * 512:(h + 1) * 512], oev[:],
                          ch="ost")
            s.use("oev", t_ost)
            if upto < 9:
                t_chain[(c, h)] = t_chain.get((c - 1, h))
                continue
            import os as _os
            _cv = int(_os.environ.get("CHAINVAR", "0"))
            for ihalf_ in range(2):
                pS = p_s[ihalf_]
                s.guard("tensor", ("p_s", ihalf_))
                s.wait("tensor", t_ft)
                t_mc = None
                for khalf in range(2):
                    t_mc = s.emit("tensor", lambda t, ihalf_=ihalf_, khalf=khalf, h=h, pS=pS,
                                  sold=sold: t.matmul(
                        pS[:], ftt[:, khalf * 256 + ihalf_ * 128: khalf * 256 + (ihalf_ + 1) * 128],
                        sold[:, (h * 2 + khalf) * 512:(h * 2 + khalf + 1) * 512],
                        start=(khalf == 0), stop=(_cv == 1)))
                if _cv != 1:
                    s.wait("tensor", t_u)
                    t_mc = s.emit("tensor", lambda t, ihalf_=ihalf_, pS=pS: t.matmul(
                        pS[:], idb[:], ut[:, ihalf_ * 512:(ihalf_ + 1) * 512], start=False, stop=True))
                s.wait("vector", t_mc)
                t_sn = s.emit("vector", lambda v, ihalf_=ihalf_, h=h, pS=pS, snew=snew: v.tensor_copy(
                    snew[:, (h * 2 + ihalf_) * 512:(h * 2 + ihalf_ + 1) * 512], pS[:]))
                s.use(("p_s", ihalf_), t_sn)
                t_chain[(c, h)] = t_sn

    import os as _os2
    for h in range(nh if _os2.environ.get("NOSF") != "1" else 0):
        for half in range(2):
            if t_chain.get((NCH - 1, h)) is None:
                continue
            s.wait("vector", t_chain[(NCH - 1, h)])
            s.guard("vector", "sfo")
            t_sf = s.emit("vector", lambda v, h=h, half=half, Sb=Sb, NCH=NCH: v.tensor_copy(
                sfo[:], Sb[(NCH - 1) % 2][:, (h * 2 + half) * 512:(h * 2 + half + 1) * 512]))
            s.wait("gpsimd", t_sf)
            t_sfd = s.dma("gpsimd", Sfin[h * 256 + half * 128: h * 256 + (half + 1) * 128, :], sfo[:],
                  ch="sfst")
            s.use("sfo", t_sfd)
    s.wait_self("gpsimd", s.chan_total("ost"))
    s.wait_self("gpsimd", s.chan_total("sfst"))
    return nc, s



def build_stage_e(debug=False):
    nc = bass.Bass("TRN2", target_bir_lowering=False)
    s = Sched()
    Od = nc.dram_tensor("Od", [2 * TOK, VDIM], BF16, kind="ExternalInput").ap()
    gg_d = nc.dram_tensor("gg", [TOK, VDIM], BF16, kind="ExternalInput").ap()
    x2r = nc.dram_tensor("x2rows", [TOK, DM], F32, kind="ExternalInput").ap()
    wo = nc.dram_tensor("wo", [VDIM, DM], BF16, kind="ExternalInput").ap()
    onwb = nc.dram_tensor("onwb", [128, VDIM], BF16, kind="ExternalInput").ap()
    gTw = nc.dram_tensor("gT", [DM, INTER], BF16, kind="ExternalInput").ap()
    uTw = nc.dram_tensor("uT", [DM, INTER], BF16, kind="ExternalInput").ap()
    dTw = nc.dram_tensor("dT", [INTER, DM], BF16, kind="ExternalInput").ap()
    gsc = nc.dram_tensor("gsc", [128, 64], F32, kind="ExternalInput").ap()
    usc = nc.dram_tensor("usc", [128, 64], F32, kind="ExternalInput").ap()
    constse = nc.dram_tensor("constse", [128, 6], F32, kind="ExternalInput").ap()
    xout = nc.dram_tensor("xout", [TOK, DM], F32, kind="ExternalOutput").ap()
    gat_d = nc.dram_tensor("gat_sc", [TOK, VDIM], BF16, kind="Internal").ap()
    if debug:
        dbg_x3 = nc.dram_tensor("dbg_x3", [512, DM], F32, kind="ExternalOutput").ap()
        dbg_z = nc.dram_tensor("dbg_z", [INTER, 512], BF16, kind="ExternalOutput").ap()
        dbg_gat = nc.dram_tensor("dbg_gat", [TOK, VDIM], BF16, kind="ExternalOutput").ap()
    xln_d = nc.dram_tensor("xln3_sc", [TOK, DM], BF16, kind="Internal").ap()
    z_d = nc.dram_tensor("z_sc", [INTER, TOK], BF16, kind="Internal").ap()
    Ode = Od.rearrange("(t two) f -> two t f", two=2)

    sb = lambda n, shp, dt: nc.alloc_sbuf_tensor(n, shp, dt).ap()
    ps = lambda n, shp, dt: nc.alloc_psum_tensor(n, shp, dt).ap()

    ot = sb("ot", [128, VDIM], BF16)
    ggt = sb("ggt", [128, VDIM], BF16)
    sgg = sb("sgg", [128, VDIM], BF16)
    onw_sb = sb("onw_sb", [128, VDIM], BF16)
    sqj = sb("sqj", [128, 512], BF16)
    nsq = sb("nsq", [128, 8], F32)
    cc = sb("cc", [128, 6], F32)
    gatb = sb("gatb", [128, VDIM], BF16)
    x2t = sb("x2t", [128, 2 * 512], F32)
    x3t = sb("x3t", [128, 4 * DM], F32)
    gatT = sb("gatT", [128, 24 * 512], BF16)
    wo_sb = sb("wo_sb", [128, 24 * 512], BF16)
    xlnbf = sb("xlnbf", [128, DM], BF16)
    sq_junk = sb("sq_junk", [128, DM], BF16)
    stats = sb("stats", [128, 8], F32)
    xlnT = sb("xlnT", [128, 16 * 512], BF16)
    zsb = sb("zsb", [128, 2 * 512], BF16)
    zld = sb("zld", [128, 2 * 512], BF16)
    wg_sb = sb("wg_sb", [128, 2 * 2048], BF16)
    wu_sb = sb("wu_sb", [128, 2 * 2048], BF16)
    wd_sb = sb("wd_sb", [128, 16 * 512], BF16)
    up_sb = sb("up_sb", [128, 2 * 512], F32)
    gs_sb = sb("gs_sb", [128, 2 * 512], F32)
    gsc_sb = sb("gsc_sb", [128, 64], F32)
    usc_sb = sb("usc_sb", [128, 64], F32)
    x4s = sb("x4s", [128, 2 * 512], F32)

    p_wo = [ps("p_wo0", [128, 512], F32), ps("p_wo1", [128, 512], F32)]
    p_g = [ps("p_gA", [128, 512], F32), ps("p_gB", [128, 512], F32)]
    p_u = [ps("p_uA", [128, 512], F32), ps("p_uB", [128, 512], F32)]
    p_dn = [p_g[0], p_g[1], p_u[0], p_u[1]]
    dn_keys = [("p_g", 0), ("p_g", 1), ("p_u", 0), ("p_u", 1)]

    s.dma("sync", onw_sb[:], onwb[:], ch="consts")
    s.dma("sync", cc[:], constse[:], ch="consts")
    s.dma("sync", gsc_sb[:], gsc[:], ch="consts")
    s.dma("sync", usc_sb[:], usc[:], ch="consts")
    t_consts = s.chan_total("consts")

    NCH = TOK // 512
    for c in range(NCH):
        for tt in range(4):
            t0 = c * 512 + tt * 128
            s.guard("sync", "ot")
            t_o = s.dma("sync", ot[:], Ode[1, t0:t0 + 128, :], ch="oldc")
            s.guard("sync", "ggt")
            t_g = s.dma("sync", ggt[:], gg_d[t0:t0 + 128, :], ch="ggld")
            s.wait("scalar", t_o)
            for h in range(6):
                s.emit("scalar", lambda sc, h=h: sc.activation(
                    out=sqj[:, 0:512], in_=ot[:, h * 512:(h + 1) * 512], func=AF.Square,
                    accum_out=nsq[:, h:h + 1]))
            s.wait("scalar", t_consts)
            s.emit("scalar", lambda sc: sc.activation(
                out=nsq[:, 0:6], in_=nsq[:, 0:6], func=AF.Ln, scale=cc[:, 2:3], bias=cc[:, 0:1]))
            t_rms = s.emit("scalar", lambda sc: sc.activation(
                out=nsq[:, 0:6], in_=nsq[:, 0:6], func=AF.Exp, scale=cc[:, 1:2]))
            s.wait("scalar", t_g)
            s.guard("scalar", "sgg")
            t_sg = s.emit("scalar", lambda sc: sc.activation(
                out=sgg[:], in_=ggt[:], func=AF.Silu))
            s.use("ggt", t_sg)
            s.wait("vector", t_sg)
            s.wait("vector", t_consts)
            t_m1 = s.emit("vector", lambda v: v.tensor_tensor(
                out=sgg[:], in0=sgg[:], in1=onw_sb[:], op=ALU.mult))
            s.use("sgg", t_m1)
            s.wait("vector", t_rms)
            s.guard("vector", "gatb")
            for h in range(6):
                s.emit("vector", lambda v, h=h: v.tensor_scalar(
                    out=gatb[:, h * 512:(h + 1) * 512], in0=ot[:, h * 512:(h + 1) * 512],
                    scalar1=nsq[:, h:h + 1], scalar2=None, op0=ALU.mult))
            t_gb = s.emit("vector", lambda v: v.tensor_tensor(
                out=gatb[:], in0=gatb[:], in1=sgg[:], op=ALU.mult))
            s.use("ot", t_gb)
            s.wait("gpsimd", t_gb)
            t_gst = s.dma("gpsimd", gat_d[t0:t0 + 128, :], gatb[:], ch="gatst")
            s.use("gatb", t_gst)
    t_gat_all = s.chan_total("gatst")
    if debug:
        s.wait("gpsimd", t_gat_all)
        s.dma("gpsimd", dbg_gat[:], gat_d[:], ch="dbg")

    for c in range(NCH):
        for k in range(24):
            s.guard("scalar", ("gatT", k))
            s.wait("scalar", t_gat_all)
            s.dma("scalar", gatT[:, k * 512:(k + 1) * 512],
                  gat_d[c * 512:(c + 1) * 512, k * 128:(k + 1) * 128], transpose=True, ch="gatT")
        t_gT = s.chan_total("gatT")
        t_x3f = {}
        for oc in range(4):
            s.guard("sync", "wo_sb")
            t_wo = s.dma("sync",
                         wo_sb.rearrange("p (k f) -> p k f", k=24),
                         wo[:, oc * 512:(oc + 1) * 512].rearrange("(k p) f -> p k f", p=128),
                         ch="wold")
            t_wo_used = None
            for tt in range(4):
                ppar = tt % 2
                s.guard("sync", ("x2t", ppar))
                t_x2 = s.dma("sync", x2t[:, ppar * 512:(ppar + 1) * 512],
                             x2r[c * 512 + tt * 128:c * 512 + (tt + 1) * 128,
                                 oc * 512:(oc + 1) * 512], ch=("x2l", ppar))
                s.wait("tensor", t_wo)
                s.wait("tensor", t_gT)
                s.guard("tensor", ("p_wo", ppar))
                t_mm = None
                for k in range(24):
                    t_mm = s.emit("tensor", lambda t, k=k, tt=tt, ppar=ppar: t.matmul(
                        p_wo[ppar][:], gatT[:, k * 512 + tt * 128:k * 512 + (tt + 1) * 128],
                        wo_sb[:, k * 512:(k + 1) * 512], start=(k == 0), stop=(k == 23)))
                t_wo_used = t_mm
                if oc == 3 and tt == 3:
                    for k in range(24):
                        s.use(("gatT", k), t_mm)
                s.wait("vector", t_mm)
                s.wait("vector", t_x2)
                s.guard("vector", ("x3s", tt))
                s.guard("vector", ("x3rd", tt))
                t_x3 = s.emit("vector", lambda v, tt=tt, oc=oc, ppar=ppar: v.tensor_tensor(
                    out=x3t[:, tt * DM + oc * 512:tt * DM + (oc + 1) * 512],
                    in0=p_wo[ppar][:], in1=x2t[:, ppar * 512:(ppar + 1) * 512], op=ALU.add))
                s.use(("p_wo", ppar), t_x3)
                s.use(("x2t", ppar), t_x3)
                if oc == 3:
                    t_x3f[tt] = t_x3
            s.use("wo_sb", t_wo_used)
        if debug and c == 0:
            for tt in range(4):
                s.wait("gpsimd", t_x3f[tt])
                s.dma("gpsimd", dbg_x3[tt * 128:(tt + 1) * 128, :], x3t[:, tt * DM:(tt + 1) * DM],
                      ch="dbg")
            t_dt_ = s.chan_total("dbg")
            for e in ("sync", "scalar", "vector", "tensor"):
                s.wait(e, t_dt_)
        t_lnst = {}
        for tt in range(4):
            s.emit("vector", lambda v, tt=tt: v.reduce_sum(
                stats[:, 0:1], x3t[:, tt * DM:(tt + 1) * DM], axis=AX.X))
            s.wait("scalar", t_x3f[tt])
            t_sq = s.emit("scalar", lambda sc, tt=tt: sc.activation(
                out=sq_junk[:], in_=x3t[:, tt * DM:(tt + 1) * DM], func=AF.Square,
                accum_out=stats[:, 1:2]))
            s.use(("x3s", tt), t_sq)
            s.wait("vector", t_sq)
            s.emit("vector", lambda v: v.tensor_scalar(
                out=stats[:, 2:3], in0=stats[:, 0:1], scalar1=1.0 / DM, scalar2=None, op0=ALU.mult))
            s.emit("vector", lambda v: v.tensor_scalar(
                out=stats[:, 5:6], in0=stats[:, 1:2], scalar1=1.0 / DM, scalar2=None, op0=ALU.mult))
            s.emit("vector", lambda v: v.tensor_tensor(
                out=stats[:, 3:4], in0=stats[:, 2:3], in1=stats[:, 2:3], op=ALU.mult))
            t_var = s.emit("vector", lambda v: v.tensor_tensor(
                out=stats[:, 3:4], in0=stats[:, 5:6], in1=stats[:, 3:4], op=ALU.subtract))
            s.wait("scalar", t_var)
            s.emit("scalar", lambda sc: sc.activation(
                out=stats[:, 4:5], in_=stats[:, 3:4], func=AF.Ln, bias=cc[:, 0:1]))
            t_rstd = s.emit("scalar", lambda sc: sc.activation(
                out=stats[:, 4:5], in_=stats[:, 4:5], func=AF.Exp, scale=cc[:, 1:2]))
            s.wait("vector", t_rstd)
            s.guard("vector", "xlnbf")
            t_ln = s.emit("vector", lambda v, tt=tt: v.tensor_scalar(
                out=xlnbf[:], in0=x3t[:, tt * DM:(tt + 1) * DM], scalar1=stats[:, 2:3],
                scalar2=stats[:, 4:5], op0=ALU.subtract, op1=ALU.mult))
            s.wait("gpsimd", t_ln)
            t_xst = s.dma("gpsimd", xln_d[c * 512 + tt * 128:c * 512 + (tt + 1) * 128, :],
                          xlnbf[:], ch="xlnst")
            s.use("xlnbf", t_xst)
            t_lnst[tt] = t_xst
        t_allst = s.chan_total("xlnst")
        for k in range(NKT):
            s.guard("scalar", ("xlnT", k))
            s.wait("scalar", t_allst)
            s.dma("scalar", xlnT[:, k * 512:(k + 1) * 512],
                  xln_d[c * 512:(c + 1) * 512, k * 128:(k + 1) * 128], transpose=True, ch="xlnT")
        t_xt_all = s.chan_total("xlnT")

        for i in range(64):
            par = i % 2
            s.guard("sync", ("wg", par))
            t_wg = s.dma("sync",
                         wg_sb[:, par * 2048:(par + 1) * 2048].rearrange("p (k f) -> p k f", k=16),
                         gTw[:, i * 128:(i + 1) * 128].rearrange("(k p) f -> p k f", p=128),
                         ch=("wg", par))
            s.guard("sync", ("wu", par))
            t_wu = s.dma("sync",
                         wu_sb[:, par * 2048:(par + 1) * 2048].rearrange("p (k f) -> p k f", k=16),
                         uTw[:, i * 128:(i + 1) * 128].rearrange("(k p) f -> p k f", p=128),
                         ch=("wu", par))
            s.wait("tensor", t_wg)
            s.wait("tensor", t_wu)
            s.wait("tensor", t_xt_all)
            s.guard("tensor", ("p_g", par))
            s.guard("tensor", ("p_u", par))
            for k in range(NKT):
                s.emit("tensor", lambda t, i=i, k=k, par=par: t.matmul(
                    p_g[par][:], wg_sb[:, par * 2048 + k * 128:par * 2048 + (k + 1) * 128],
                    xlnT[:, k * 512:(k + 1) * 512], start=(k == 0), stop=(k == NKT - 1)))
            t_mm2 = None
            for k in range(NKT):
                t_mm2 = s.emit("tensor", lambda t, i=i, k=k, par=par: t.matmul(
                    p_u[par][:], wu_sb[:, par * 2048 + k * 128:par * 2048 + (k + 1) * 128],
                    xlnT[:, k * 512:(k + 1) * 512], start=(k == 0), stop=(k == NKT - 1)))
            s.use(("wg", par), t_mm2)
            s.use(("wu", par), t_mm2)
            if i >= 62:
                for k in range(NKT):
                    s.use(("xlnT", k), t_mm2)
            s.wait("scalar", t_mm2)
            s.guard("scalar", ("gs", par))
            t_gs = s.emit("scalar", lambda sc, i=i, par=par: sc.activation(
                out=gs_sb[:, par * 512:(par + 1) * 512], in_=p_g[par][:],
                func=AF.Silu, scale=gsc_sb[:, i:i + 1]))
            s.use(("p_g", par), t_gs)
            s.wait("vector", t_mm2)
            s.guard("vector", ("up", par))
            t_up = s.emit("vector", lambda v, i=i, par=par: v.tensor_scalar(
                out=up_sb[:, par * 512:(par + 1) * 512], in0=p_u[par][:],
                scalar1=usc_sb[:, i:i + 1], scalar2=None, op0=ALU.mult))
            s.use(("p_u", par), t_up)
            s.wait("vector", t_gs)
            s.guard("vector", ("zsb", par))
            t_z = s.emit("vector", lambda v, par=par: v.tensor_tensor(
                out=zsb[:, par * 512:(par + 1) * 512], in0=up_sb[:, par * 512:(par + 1) * 512],
                in1=gs_sb[:, par * 512:(par + 1) * 512], op=ALU.mult))
            s.use(("gs", par), t_z)
            s.use(("up", par), t_z)
            s.wait("gpsimd", t_z)
            t_zst = s.dma("gpsimd", z_d[i * 128:(i + 1) * 128, c * 512:(c + 1) * 512],
                          zsb[:, par * 512:(par + 1) * 512], ch="zst")
            s.use(("zsb", par), t_zst)
        t_z_all = s.chan_total("zst")
        if debug and c == 0:
            s.wait("gpsimd", t_z_all)
            s.dma("gpsimd", dbg_z[:, :], z_d[:, 0:512], ch="dbg")
            t_dt2 = s.chan_total("dbg")
            for e in ("sync", "scalar", "vector", "tensor"):
                s.wait(e, t_dt2)

        for oc in range(4):
            s.guard("sync", "wd_sb")
            t_wd = s.dma("sync",
                         wd_sb.rearrange("p (k f) -> p k f", k=16),
                         dTw[0:2048, oc * 512:(oc + 1) * 512].rearrange("(k p) f -> p k f", p=128),
                         ch="wdld")
            t_dn = {}
            for ig in range(4):
                if ig > 0:
                    s.guard("sync", "wd_sb")
                    t_wd = s.dma("sync",
                                 wd_sb.rearrange("p (k f) -> p k f", k=16),
                                 dTw[ig * 2048:(ig + 1) * 2048, oc * 512:(oc + 1) * 512]
                                 .rearrange("(k p) f -> p k f", p=128), ch="wdld")
                for ii in range(16):
                    i = ig * 16 + ii
                    zpar = ii % 2
                    s.guard("sync", ("zld", zpar))
                    s.wait("sync", t_z_all)
                    t_zl = s.dma("sync", zld[:, zpar * 512:(zpar + 1) * 512],
                                 z_d[i * 128:(i + 1) * 128, c * 512:(c + 1) * 512],
                                 ch=("zld", zpar))
                    s.wait("tensor", t_zl)
                    s.wait("tensor", t_wd)
                    t_last_z = None
                    for tt in range(4):
                        if ig == 0 and ii == 0:
                            s.guard("tensor", dn_keys[tt])
                        t_dn[tt] = s.emit("tensor", lambda t, tt=tt, ii=ii, zpar=zpar, ig=ig: t.matmul(
                            p_dn[tt][:], zld[:, zpar * 512 + tt * 128:zpar * 512 + (tt + 1) * 128],
                            wd_sb[:, ii * 512:(ii + 1) * 512],
                            start=(ig == 0 and ii == 0), stop=(ig == 3 and ii == 15)))
                    s.use(("zld", zpar), t_dn[3])
                s.use("wd_sb", t_dn[3])
            for tt in range(4):
                ppar = tt % 2
                s.wait("vector", t_dn[tt])
                s.guard("vector", ("x4s", ppar))
                t_x4 = s.emit("vector", lambda v, tt=tt, oc=oc, ppar=ppar: v.tensor_tensor(
                    out=x4s[:, ppar * 512:(ppar + 1) * 512], in0=p_dn[tt][:],
                    in1=x3t[:, tt * DM + oc * 512:tt * DM + (oc + 1) * 512], op=ALU.add))
                s.use(dn_keys[tt], t_x4)
                if oc == 3:
                    s.use(("x3rd", tt), t_x4)
                s.wait("gpsimd", t_x4)
                t_xo = s.dma("gpsimd",
                             xout[c * 512 + tt * 128:c * 512 + (tt + 1) * 128,
                                  oc * 512:(oc + 1) * 512],
                             x4s[:, ppar * 512:(ppar + 1) * 512], ch="xost")
                s.use(("x4s", ppar), t_xo)

    s.wait_self("gpsimd", s.chan_total("xost"))
    return nc, s



def build_prep():
    nc = bass.Bass("TRN2", target_bir_lowering=False)
    s = Sched()
    wup_r = nc.dram_tensor("wup_r", [1024, DM], F32, kind="ExternalInput").ap()
    wgt_r = nc.dram_tensor("wgt_r", [1024, DM], F32, kind="ExternalInput").ap()
    wdn_r = nc.dram_tensor("wdn_r", [256, INTER], F32, kind="ExternalInput").ap()
    gq_r = nc.dram_tensor("gq_r", [1024, DM], I32, kind="ExternalInput").ap()
    uq_r = nc.dram_tensor("uq_r", [1024, DM], I32, kind="ExternalInput").ap()
    dq_r = nc.dram_tensor("dq_r", [256, INTER], I32, kind="ExternalInput").ap()
    suug = nc.dram_tensor("suug", [128, 16], F32, kind="ExternalInput").ap()
    sudp = nc.dram_tensor("sudp", [128, 64], F32, kind="ExternalInput").ap()
    wsdp_c = nc.dram_tensor("wsdp_c", [128, 2], F32, kind="ExternalInput").ap()
    dsc = nc.dram_tensor("dsc", [128, 2], F32, kind="ExternalInput").ap()
    h128p = nc.dram_tensor("h128p", [128, 128], BF16, kind="ExternalInput").ap()
    h128n = nc.dram_tensor("h128n", [128, 128], BF16, kind="ExternalInput").ap()
    aw = {}
    AT_SHAPES = {"wq": (DM, KDIM), "wk0": (DM, KDIM), "wk1": (DM, KDIM),
                 "wv0": (DM, VDIM), "wv1": (DM, VDIM), "wg": (DM, VDIM),
                 "wo": (VDIM, DM), "wb01": (DM, 12), "wa": (DM, 6)}
    for nm, (r, cdim) in AT_SHAPES.items():
        aw[nm] = (nc.dram_tensor(nm + "_r", [r // 8, cdim], F32, kind="ExternalInput").ap(),
                  nc.dram_tensor(nm + "_o", [r // 8, cdim], BF16, kind="ExternalOutput").ap())
    w2up_o = nc.dram_tensor("w2up_o", [DM, 1024], BF16, kind="ExternalOutput").ap()
    w2gt_o = nc.dram_tensor("w2gt_o", [DM, 1024], BF16, kind="ExternalOutput").ap()
    w2dn_o = nc.dram_tensor("w2dn_o", [INTER, 256], BF16, kind="ExternalOutput").ap()
    gT_o = nc.dram_tensor("gT_o", [DM, 1024], BF16, kind="ExternalOutput").ap()
    uT_o = nc.dram_tensor("uT_o", [DM, 1024], BF16, kind="ExternalOutput").ap()
    dT_o = nc.dram_tensor("dT_o", [INTER, 256], BF16, kind="ExternalOutput").ap()
    upb_d = nc.dram_tensor("upb_d", [1024, DM], BF16, kind="Internal").ap()
    gtb_d = nc.dram_tensor("gtb_d", [1024, DM], BF16, kind="Internal").ap()
    dnb_d = nc.dram_tensor("dnb_d", [256, INTER], BF16, kind="Internal").ap()
    gqb_d = nc.dram_tensor("gqb_d", [1024, DM], BF16, kind="Internal").ap()
    uqb_d = nc.dram_tensor("uqb_d", [1024, DM], BF16, kind="Internal").ap()
    dqb_d = nc.dram_tensor("dqb_d", [256, INTER], BF16, kind="Internal").ap()

    sb = lambda n, shp, dt: nc.alloc_sbuf_tensor(n, shp, dt).ap()
    ps = lambda n, shp, dt: nc.alloc_psum_tensor(n, shp, dt).ap()
    hp = sb("hp", [128, 128], BF16)
    hn = sb("hn", [128, 128], BF16)
    su16 = sb("su16", [128, 16], F32)
    su64 = sb("su64", [128, 64], F32)
    wsd = sb("wsd", [128, 2], F32)
    dss = sb("dss", [128, 2], F32)
    ld32 = sb("ld32", [128, INTER], F32)
    ldi = sb("ldi", [128, INTER], BF16)
    wt = sb("wt", [128, 2 * 1024], BF16)
    wtd = sb("wtd", [128, 2 * 256], BF16)
    evb = sb("evb", [128, 2 * 512], BF16)
    p_t = [ps("p_t0", [128, 512], F32), ps("p_t1", [128, 512], F32)]

    s.dma("sync", hp[:], h128p[:], ch="consts")
    s.dma("sync", hn[:], h128n[:], ch="consts")
    s.dma("sync", su16[:], suug[:], ch="consts")
    s.dma("sync", su64[:], sudp[:], ch="consts")
    s.dma("sync", wsd[:], wsdp_c[:], ch="consts")
    s.dma("sync", dss[:], dsc[:], ch="consts")
    t_consts = s.chan_total("consts")
    H16 = hadamard(16)
    H64 = hadamard(64)

    for nm, (src, dst) in aw.items():
        s.dma("gpsimd", dst[:], src[:], ch="cast")

    s.dma("gpsimd", upb_d[:], wup_r[:], ch="cast")
    s.dma("gpsimd", gtb_d[:], wgt_r[:], ch="cast")
    for t2 in range(2):
        t_l = s.dma("sync", ld32[:], wdn_r[t2 * 128:(t2 + 1) * 128, :], ch="ld32")
        s.wait("vector", t_l)
        s.guard("vector", ("ldi", 0))
        t_cv = s.emit("vector", lambda v, t2=t2: v.tensor_scalar(
            out=ldi[:], in0=ld32[:], scalar1=wsd[:, t2:t2 + 1], scalar2=None, op0=ALU.mult))
        s.wait("vector", t_consts)
        s.wait("sync", t_cv)
        s.wait("gpsimd", t_cv)
        t_st = s.dma("gpsimd", dnb_d[t2 * 128:(t2 + 1) * 128, :], ldi[:], ch="dnst")
        s.use(("ldi", 0), t_st)
    t_ic = {}
    for nm, src, dstd in (("g", gq_r, gqb_d), ("u", uq_r, uqb_d)):
        t_ic[nm] = s.dma("gpsimd", dstd[:], src[:], ch="icast")
    t_icd = s.dma("gpsimd", dqb_d[:], dq_r[:], ch="icast")
    t_ic_all = s.chan_total("icast")
    for nm, dstd, outd in (("g", gqb_d, gT_o), ("u", uqb_d, uT_o)):
        for t2 in range(8):
            s.guard("sync", "ldi2")
            s.wait("sync", t_ic_all)
            t_l = s.dma("sync", ldi[:, 0:DM], dstd[t2 * 128:(t2 + 1) * 128, :], ch="ldi2c")
            s.wait("vector", t_l)
            t_sh = s.emit("vector", lambda v: v.tensor_scalar(
                out=ldi[:, 0:DM], in0=ldi[:, 0:DM], scalar1=128.0, scalar2=None, op0=ALU.subtract))
            s.use("ldi2", t_sh)
            s.wait("gpsimd", t_sh)
            t_st = s.dma("gpsimd", dstd[t2 * 128:(t2 + 1) * 128, :], ldi[:, 0:DM], ch="shst")
            s.use("ldi2", t_st)
    for t2 in range(2):
        s.guard("sync", "ldi2")
        s.wait("sync", t_ic_all)
        t_l = s.dma("sync", ldi[:], dqb_d[t2 * 128:(t2 + 1) * 128, :], ch="ldi2c")
        s.wait("vector", t_l)
        t_sh = s.emit("vector", lambda v, t2=t2: v.tensor_scalar(
            out=ldi[:], in0=ldi[:], scalar1=128.0, scalar2=dss[:, t2:t2 + 1],
            op0=ALU.subtract, op1=ALU.mult))
        s.use("ldi2", t_sh)
        s.wait("gpsimd", t_sh)
        t_st = s.dma("gpsimd", dqb_d[t2 * 128:(t2 + 1) * 128, :], ldi[:], ch="shst")
        s.use("ldi2", t_st)
    t_sh_all = s.chan_total("shst")
    t_cast_all = s.chan_total("cast")
    t_dn_all = s.chan_total("dnst")

    for dstd, outd in ((gqb_d, gT_o), (uqb_d, uT_o)):
        for hb in range(16):
            s.guard("scalar", ("wt", hb % 2))
            s.wait("scalar", t_sh_all)
            t_t = s.dma("scalar", wt[:, (hb % 2) * 1024:(hb % 2 + 1) * 1024],
                        dstd[:, hb * 128:(hb + 1) * 128], transpose=True, ch=("wt", hb % 2))
            s.wait("gpsimd", t_t)
            t_st = s.dma("gpsimd", outd[hb * 128:(hb + 1) * 128, :],
                         wt[:, (hb % 2) * 1024:(hb % 2 + 1) * 1024], ch="outst")
            s.use(("wt", hb % 2), t_st)
    for ib in range(64):
        s.guard("scalar", ("wtd", ib % 2))
        s.wait("scalar", t_sh_all)
        t_t = s.dma("scalar", wtd[:, (ib % 2) * 256:(ib % 2 + 1) * 256],
                    dqb_d[:, ib * 128:(ib + 1) * 128], transpose=True, ch=("wtd", ib % 2))
        s.wait("gpsimd", t_t)
        t_st = s.dma("gpsimd", dT_o[ib * 128:(ib + 1) * 128, :],
                     wtd[:, (ib % 2) * 256:(ib % 2 + 1) * 256], ch="outst")
        s.use(("wtd", ib % 2), t_st)

    for srcd, outd, su in ((upb_d, w2up_o, su16), (gtb_d, w2gt_o, su16)):
        for ap_ in range(16):
            for ch_ in range(2):
                ppar = (ap_ * 2 + ch_) % 2
                s.guard("tensor", ("p_t", ppar))
                t_mm = None
                for a_ in range(16):
                    wpar = a_ % 2
                    s.guard("scalar", ("wtp", wpar))
                    s.wait("scalar", t_cast_all)
                    t_t = s.dma("scalar", wt[:, wpar * 1024 + 0:wpar * 1024 + 512],
                                srcd[ch_ * 512:(ch_ + 1) * 512, a_ * 128:(a_ + 1) * 128],
                                transpose=True, ch=("wtp", wpar))
                    s.wait("tensor", t_t)
                    s.wait("tensor", t_consts)
                    sgn = H16[ap_, a_]
                    t_mm = s.emit("tensor", lambda t, ppar=ppar, wpar=wpar, sgn=sgn, a_=a_: t.matmul(
                        p_t[ppar][:], hp[:] if sgn > 0 else hn[:],
                        wt[:, wpar * 1024 + 0:wpar * 1024 + 512],
                        start=(a_ == 0), stop=(a_ == 15)))
                    s.use(("wtp", wpar), t_mm)
                s.wait("vector", t_mm)
                s.guard("vector", ("evb", ppar))
                t_ev = s.emit("vector", lambda v, ap_=ap_, ppar=ppar, su=su: v.tensor_scalar(
                    out=evb[:, ppar * 512:(ppar + 1) * 512], in0=p_t[ppar][:],
                    scalar1=su[:, ap_:ap_ + 1], scalar2=None, op0=ALU.mult))
                s.use(("p_t", ppar), t_ev)
                s.wait("gpsimd", t_ev)
                t_st = s.dma("gpsimd", outd[ap_ * 128:(ap_ + 1) * 128, ch_ * 512:(ch_ + 1) * 512],
                             evb[:, ppar * 512:(ppar + 1) * 512], ch="outst")
                s.use(("evb", ppar), t_st)
    for ap_ in range(64):
        ppar = ap_ % 2
        s.guard("tensor", ("p_t", ppar))
        t_mm = None
        for a_ in range(64):
            wpar = a_ % 2
            s.guard("scalar", ("wtdp", wpar))
            s.wait("scalar", t_dn_all)
            t_t = s.dma("scalar", wtd[:, wpar * 256:(wpar + 1) * 256],
                        dnb_d[:, a_ * 128:(a_ + 1) * 128], transpose=True, ch=("wtdp", wpar))
            s.wait("tensor", t_t)
            sgn = H64[ap_, a_]
            t_mm = s.emit("tensor", lambda t, ppar=ppar, wpar=wpar, sgn=sgn, a_=a_: t.matmul(
                p_t[ppar][:, 0:256], hp[:] if sgn > 0 else hn[:],
                wtd[:, wpar * 256:(wpar + 1) * 256],
                start=(a_ == 0), stop=(a_ == 63)))
            s.use(("wtdp", wpar), t_mm)
        s.wait("vector", t_mm)
        s.guard("vector", ("evb", ppar))
        t_ev = s.emit("vector", lambda v, ap_=ap_, ppar=ppar: v.tensor_scalar(
            out=evb[:, ppar * 512:ppar * 512 + 256], in0=p_t[ppar][:, 0:256],
            scalar1=su64[:, ap_:ap_ + 1], scalar2=None, op0=ALU.mult))
        s.use(("p_t", ppar), t_ev)
        s.wait("gpsimd", t_ev)
        t_st = s.dma("gpsimd", w2dn_o[ap_ * 128:(ap_ + 1) * 128, :],
                     evb[:, ppar * 512:ppar * 512 + 256], ch="outst")
        s.use(("evb", ppar), t_st)

    s.wait_self("gpsimd", s.chan_total("outst"))
    s.wait_self("gpsimd", s.chan_total("cast"))
    return nc, s


def _stage_a_inputs(inputs):
    x = np.ascontiguousarray(np.asarray(inputs["L0_attn_x"], np.float32))
    W_delta = np.asarray(inputs["L0_attn_W_delta"], np.float32)
    W_dt = np.asarray(inputs["L0_attn_W_dt"], np.float32)
    b_dt = np.asarray(inputs["L0_attn_b_dt"], np.float32)
    W_B = np.asarray(inputs["L0_attn_W_B"], np.float32)
    W_C = np.asarray(inputs["L0_attn_W_C"], np.float32)
    A = -np.exp(np.asarray(inputs["L0_attn_A_log"], np.float32))
    Dp = np.asarray(inputs["L0_attn_D"], np.float32)

    selbc = np.zeros((32, 32 * 128), np.float32)
    for n in range(16):
        selbc[n, n * 128:(n + 1) * 128] = 1.0
        selbc[16 + n, (16 + n) * 128:(17 + n) * 128] = 1.0
    wbc = np.concatenate([W_B, W_C], axis=1).astype(np.float32)

    maps = []
    for c in range(NCORES):
        b, j = divmod(c, 2)
        perm = np.concatenate([np.arange(j * 1024, (j + 1) * 1024),
                               np.arange((1 - j) * 1024, (2 - j) * 1024)])
        own = perm[:1024]
        constsa = np.zeros((128, 4), np.float32)
        constsa[:, 0] = EPS_LN
        constsa[:, 1] = 1.0
        constsa[:, 2] = -0.5
        constsa[:, 3] = -1.0
        maps.append({
            "constsa": constsa,
            "xb": np.ascontiguousarray(x[b][:, perm]),
            "wdelta": np.ascontiguousarray(W_delta[perm, :]),
            "wdt": np.ascontiguousarray(W_dt[:, own]),
            "bdt": np.ascontiguousarray(b_dt[own].reshape(8, 128).T),
            "wbc": np.ascontiguousarray(wbc[perm, :]),
            "acols": np.ascontiguousarray(
                A[own].reshape(8, 128, 16).transpose(1, 0, 2).reshape(128, 8 * 16)),
            "dcols": np.ascontiguousarray(Dp[own].reshape(8, 128).T),
            "selbc": selbc,
        })
    return maps


def _stage_b_maps(inputs, yhiF, yloF, w2upT, w2gateT, w2downT):
    x = np.asarray(inputs["L0_attn_x"], np.float32)
    wsug = np.asarray(inputs["L0_ffn_Wscale_ug"], np.float32)
    scu = (wsug[:INTER] / np.sqrt(DM)).reshape(64, 128).T.astype(np.float32)
    scg = (wsug[INTER:] / np.sqrt(DM)).reshape(64, 128).T.astype(np.float32)
    constsb = np.zeros((128, 2), np.float32)
    constsb[:, 0] = EPS_LN
    constsb[:, 1] = -0.5
    maps = []
    for c in range(NCORES):
        b, j = divmod(c, 2)
        sl = slice(j * TOK, (j + 1) * TOK)
        maps.append({
            "xrows": np.ascontiguousarray(x[b, sl, :]),
            "yhi": np.ascontiguousarray(yhiF[b][:, sl]),
            "ylo": np.ascontiguousarray(yloF[b][:, sl]),
            "w2up": w2upT, "w2gate": w2gateT, "w2down": w2downT,
            "scu": scu, "scg": scg, "constsb": constsb,
        })
    return maps


def run_stage_b(inputs, yhiF, yloF, w2upT, w2gateT, w2downT):
    nc, s = build_stage_b()
    s.realize(nc)
    maps = _stage_b_maps(inputs, yhiF, yloF, w2upT, w2gateT, w2downT)
    res = run_bass_kernel_spmd(nc, maps, core_ids=list(range(NCORES)))
    x2 = np.zeros((B, L, DM), np.float32)
    for c in range(NCORES):
        b, j = divmod(c, 2)
        x2[b][j * TOK:(j + 1) * TOK, :] = res.results[c]["x2o"]
    return x2


def _stage_c_maps(inputs, x2, wbf):
    conv_q = np.asarray(inputs["L1_attn_conv_q_w"], np.float32)
    conv_k = np.asarray(inputs["L1_attn_conv_k_w"], np.float32)
    conv_v = np.asarray(inputs["L1_attn_conv_v_w"], np.float32)
    cw = np.zeros((128, 84 * 4), np.float32)
    fi = 0
    for src, n in ((conv_q, 12), (conv_k[0], 12), (conv_k[1], 12),
                   (conv_v[0], 24), (conv_v[1], 24)):
        for i in range(n):
            cw[:, fi * 4:(fi + 1) * 4] = src[i * 128:(i + 1) * 128, :]
            fi += 1
    alogdt = np.stack([np.asarray(inputs["L1_attn_A_log"], np.float32),
                       np.asarray(inputs["L1_attn_dt_bias"], np.float32)], axis=1)
    constsc = np.zeros((128, 6), np.float32)
    constsc[:, 0] = EPS_LN
    constsc[:, 1] = -0.5
    constsc[:, 2] = 1.0
    constsc[:, 3] = -1.0
    constsc[:, 4] = 1e-6
    ident = np.eye(128, dtype=np.float32)
    maps = []
    for c in range(NCORES):
        b, j = divmod(c, 2)
        x2rows = np.zeros((TOK + 16, DM), np.float32)
        x2rows[16:] = x2[b, j * TOK:(j + 1) * TOK, :]
        if j == 1:
            x2rows[:16] = x2[b, TOK - 16:TOK, :]
        maps.append({
            "x2rows": x2rows,
            "wq": wbf["wq"], "wk0": wbf["wk0"], "wk1": wbf["wk1"],
            "wv0": wbf["wv0"], "wv1": wbf["wv1"], "wb01": wbf["wb01"],
            "wa": wbf["wa"], "wg": wbf["wg"],
            "cw": cw, "alogdt": alogdt, "constsc": constsc, "identc": ident,
        })
    return maps


def run_stage_c(inputs, x2, wbf):
    nc, s = build_stage_c()
    s.realize(nc)
    maps = _stage_c_maps(inputs, x2, wbf)
    res = run_bass_kernel_spmd(nc, maps, core_ids=list(range(NCORES)))
    return [res.results[c] for c in range(NCORES)]


def _stage_d_consts():
    r = np.arange(128)
    mup = (1e9 * (r[None, :] > r[:, None])).astype(np.float32)
    mlo = (1e9 * (r[None, :] < r[:, None])).astype(np.float32)
    identf = np.eye(128, dtype=np.float32)
    identb = np.eye(128, dtype=BF)
    ihalf = np.zeros((128, 512), np.float32)
    ihalf[r, r] = 1.0
    ihalf[r, 256 + 128 + r] = 1.0
    onesd = np.ones((6, 128), np.float32)
    return {"mup": mup, "mlo": mlo, "identf": identf, "identb": identb,
            "ihalf": ihalf, "onesd": onesd}


def run_stage_d(coutс_or_list, sinits):
    nc, s = build_stage_d()
    s.realize(nc)
    consts = _stage_d_consts()
    maps = []
    for c in range(NCORES):
        r = coutс_or_list[c]
        maps.append({
            "Ks": np.asarray(r["Ks"]), "Qs": np.asarray(r["Qs"]),
            "Vs": np.asarray(r["Vs"]), "bgz": np.asarray(r["bgz"]),
            "Sinit": sinits[c], **consts,
        })
    res = run_bass_kernel_spmd(nc, maps, core_ids=list(range(NCORES)))
    return [res.results[c] for c in range(NCORES)]


def _stage_e_maps(inputs, x2, outsD, outsC, webf):
    onw = np.asarray(inputs["L1_attn_o_norm_w"], np.float32)
    onwb = np.broadcast_to(np.tile(onw, 6)[None, :], (128, VDIM)).astype(BF)
    gsc_ = np.asarray(inputs["L1_ffn_gate_scale"], np.float32).reshape(64, 128).T.copy()
    usc_ = np.asarray(inputs["L1_ffn_up_scale"], np.float32).reshape(64, 128).T.copy()
    constse = np.zeros((128, 6), np.float32)
    constse[:, 0] = EPS_LN
    constse[:, 1] = -0.5
    constse[:, 2] = 1.0 / DV
    maps = []
    for c in range(NCORES):
        b, j = divmod(c, 2)
        sl = slice(j * TOK, (j + 1) * TOK)
        maps.append({
            "Od": np.asarray(outsD[c]["O"]),
            "gg": np.asarray(outsC[c]["gg"]),
            "x2rows": np.ascontiguousarray(x2[b, sl, :]),
            "wo": webf["wo"], "gT": webf["gT"], "uT": webf["uT"], "dT": webf["dT"],
            "onwb": onwb, "gsc": gsc_, "usc": usc_, "constse": constse,
        })
    return maps


def run_stage_e(inputs, x2, outsD, outsC, webf):
    nc, s = build_stage_e()
    s.realize(nc)
    maps = _stage_e_maps(inputs, x2, outsD, outsC, webf)
    res = run_bass_kernel_spmd(nc, maps, core_ids=list(range(NCORES)))
    out = np.zeros((B, L, DM), np.float32)
    for c in range(NCORES):
        b, j = divmod(c, 2)
        out[b][j * TOK:(j + 1) * TOK, :] = res.results[c]["xout"]
    return out


def run_prep(inputs):
    nc, s = build_prep()
    s.realize(nc)
    H128 = hadamard(128)
    h128p = H128.astype(BF)
    h128n = (-H128).astype(BF)
    SU_ug = np.asarray(inputs["L0_ffn_SU_ug"], np.float32)
    SU_dp = np.asarray(inputs["L0_ffn_SU_dp"], np.float32)
    wsdp = np.asarray(inputs["L0_ffn_Wscale_dp"], np.float32)
    ds_ = np.asarray(inputs["L1_ffn_down_scale"], np.float32)
    W_up = np.asarray(inputs["L0_ffn_W_up"], np.float32)
    W_gate = np.asarray(inputs["L0_ffn_W_gate"], np.float32)
    W_down = np.asarray(inputs["L0_ffn_W_down"], np.float32)
    gq = np.asarray(inputs["L1_ffn_gate_w"], np.int32)
    uq = np.asarray(inputs["L1_ffn_up_w"], np.int32)
    dq = np.asarray(inputs["L1_ffn_down_w"], np.int32)
    at_full = {
        "wq": np.asarray(inputs["L1_attn_Wq"], np.float32),
        "wk0": np.asarray(inputs["L1_attn_Wk"], np.float32)[0],
        "wk1": np.asarray(inputs["L1_attn_Wk"], np.float32)[1],
        "wv0": np.asarray(inputs["L1_attn_Wv"], np.float32)[0],
        "wv1": np.asarray(inputs["L1_attn_Wv"], np.float32)[1],
        "wg": np.asarray(inputs["L1_attn_Wg"], np.float32),
        "wo": np.asarray(inputs["L1_attn_Wo"], np.float32),
        "wb01": np.concatenate([np.asarray(inputs["L1_attn_Wb"], np.float32)[0],
                                np.asarray(inputs["L1_attn_Wb"], np.float32)[1]], axis=1),
        "wa": np.asarray(inputs["L1_attn_Wa"], np.float32),
    }
    maps = []
    for c in range(NCORES):
        i1 = slice(c * 1024, (c + 1) * 1024)
        o1 = slice(c * 256, (c + 1) * 256)
        wsd = np.zeros((128, 2), np.float32)
        wsd[:, 0] = wsdp[c * 256:c * 256 + 128] / np.sqrt(INTER)
        wsd[:, 1] = wsdp[c * 256 + 128:(c + 1) * 256] / np.sqrt(INTER)
        dsc_ = np.zeros((128, 2), np.float32)
        dsc_[:, 0] = ds_[c * 256:c * 256 + 128]
        dsc_[:, 1] = ds_[c * 256 + 128:(c + 1) * 256]
        m = {
            "wup_r": np.ascontiguousarray(W_up[i1]),
            "wgt_r": np.ascontiguousarray(W_gate[i1]),
            "wdn_r": np.ascontiguousarray(W_down[o1]),
            "gq_r": np.ascontiguousarray(gq[i1]),
            "uq_r": np.ascontiguousarray(uq[i1]),
            "dq_r": np.ascontiguousarray(dq[o1]),
            "suug": np.ascontiguousarray(SU_ug.reshape(16, 128).T),
            "sudp": np.ascontiguousarray(SU_dp.reshape(64, 128).T),
            "wsdp_c": wsd, "dsc": dsc_,
            "h128p": h128p, "h128n": h128n,
        }
        for nm, arr in at_full.items():
            rsl = arr.shape[0] // 8
            m[nm + "_r"] = np.ascontiguousarray(arr[c * rsl:(c + 1) * rsl])
        maps.append(m)
    res = run_bass_kernel_spmd(nc, maps, core_ids=list(range(NCORES)))
    rs = res.results
    W = {}
    W["w2up"] = np.concatenate([np.asarray(rs[c]["w2up_o"]) for c in range(NCORES)], axis=1)
    W["w2gate"] = np.concatenate([np.asarray(rs[c]["w2gt_o"]) for c in range(NCORES)], axis=1)
    W["w2down"] = np.concatenate([np.asarray(rs[c]["w2dn_o"]) for c in range(NCORES)], axis=1)
    W["gT"] = np.concatenate([np.asarray(rs[c]["gT_o"]) for c in range(NCORES)], axis=1)
    W["uT"] = np.concatenate([np.asarray(rs[c]["uT_o"]) for c in range(NCORES)], axis=1)
    W["dT"] = np.concatenate([np.asarray(rs[c]["dT_o"]) for c in range(NCORES)], axis=1)
    for nm in at_full:
        W[nm] = np.concatenate([np.asarray(rs[c][nm + "_o"]) for c in range(NCORES)], axis=0)
    return W


def kernel(**inputs):
    W = run_prep(inputs)
    yhi, ylo = run_stage_a(inputs)
    x2 = run_stage_b(inputs, yhi, ylo, W["w2up"], W["w2gate"], W["w2down"])
    wbf = {k: W[k] for k in ("wq", "wk0", "wk1", "wv0", "wv1", "wb01", "wa", "wg")}
    outsC = run_stage_c(inputs, x2, wbf)
    zero_s = np.zeros((KDIM, DV), np.float32)
    outsD1 = run_stage_d_launch(outsC, [zero_s] * NCORES)
    sinits = [np.asarray(outsD1[c - 1]["Sfin"]) if c % 2 == 1 else zero_s
              for c in range(NCORES)]
    outsD2 = run_stage_d_launch(outsC, sinits)
    webf = {"wo": W["wo"], "gT": W["gT"], "uT": W["uT"], "dT": W["dT"]}
    out = run_stage_e(inputs, x2, outsD2, outsC, webf)
    return out


def run_stage_d_launch(outsC, sinits):
    return run_stage_d(outsC, sinits)


def run_stage_a(inputs):
    nc, s = build_stage_a()
    s.realize(nc)
    maps = _stage_a_inputs(inputs)
    res = run_bass_kernel_spmd(nc, maps, core_ids=list(range(NCORES)))
    yhi = np.zeros((B, DM, L), BF)
    ylo = np.zeros((B, DM, L), BF)
    for c in range(NCORES):
        b, j = divmod(c, 2)
        yhi[b, j * 1024:(j + 1) * 1024, :] = res.results[c]["yhi"]
        ylo[b, j * 1024:(j + 1) * 1024, :] = res.results[c]["ylo"]
    return yhi, ylo




# version 02
# speedup vs baseline: 1.1958x; 1.1958x over previous
import contextlib
import numpy as np
import ml_dtypes
import concourse.bass as bass
import concourse.mybir as mybir
from concourse.bass_utils import run_bass_kernel_spmd

F32 = mybir.dt.float32
BF16 = mybir.dt.bfloat16
I32 = mybir.dt.int32
AF = mybir.ActivationFunctionType
ALU = mybir.AluOpType
AX = mybir.AxisListType
BF = ml_dtypes.bfloat16

NCORES = 8
B, L, DM = 4, 4096, 2048
TOK = 2048
NKT = DM // 128
EPS_LN = 1e-5
H, DK, NH, KCONV, DV = 6, 256, 2, 4, 512
KDIM, VDIM = H * DK, H * DV
INTER = 8192

ENGINES = ("sync", "scalar", "vector", "tensor", "gpsimd")


class Sched:

    AUTOSYNC = ("vector", "scalar", "gpsimd")

    def __init__(self):
        self.ops = {e: [] for e in ENGINES}
        self.cnt_c = {e: 0 for e in ENGINES}
        self.dch = {}
        self.last_use = {}

    def emit(self, e, fn, inc=0):
        if e in self.AUTOSYNC and self.cnt_c[e] > 0:
            self.ops[e].append(("wait", ("C", e), self.cnt_c[e]))
        self.ops[e].append(("op", fn, e))
        self.cnt_c[e] += 1
        return (("C", e), self.cnt_c[e])

    def dma(self, e, out, in_, ch=None, **kw):
        ch = ch if ch is not None else ("eng", e)
        self.dch[ch] = self.dch.get(ch, 0) + 16
        self.ops[e].append(
            ("dma", lambda g, out=out, in_=in_, kw=kw: g.dma_start(out=out, in_=in_, **kw), ch))
        return (("D", ch), self.dch[ch])

    def wait(self, e, tok):
        if tok is None:
            return
        key, val = tok
        if key == ("C", e):
            return
        self.ops[e].append(("wait", key, val))

    def wait_self(self, e, tok):
        key, val = tok
        self.ops[e].append(("wait", key, val))

    def chan_total(self, ch):
        return (("D", ch), self.dch.get(ch, 0))

    def use(self, key, tok):
        if tok is not None:
            self.last_use[key] = tok

    def guard(self, e, key):
        self.wait(e, self.last_use.get(key))

    def realize(self, nc):
        sems = {}
        with contextlib.ExitStack() as stack:
            for e in ENGINES:
                sems[("C", e)] = stack.enter_context(nc.semaphore(f"c_{e}"))
            for i, ch in enumerate(self.dch):
                sems[("D", ch)] = stack.enter_context(nc.semaphore(f"dm{i}"))
            with nc.Block() as block:
                for e in ENGINES:
                    if not self.ops[e]:
                        continue

                    def make_body(eng_name, oplist):
                        def body(eng):
                            hi = {}
                            for item in oplist:
                                if item[0] == "wait":
                                    _, key, val = item
                                    if val <= 0 or hi.get(key, -1) >= val:
                                        continue
                                    hi[key] = val
                                    eng.wait_ge(sems[key], val)
                                elif item[0] == "dma":
                                    _, fn, ch = item
                                    fn(eng).then_inc(sems[("D", ch)], 16)
                                else:
                                    _, fn, en = item
                                    fn(eng).then_inc(sems[("C", en)], 1)
                        return body

                    getattr(block, e)(make_body(e, self.ops[e]))
        return sems


def hadamard(n):
    h = np.array([[1.0]], np.float32)
    while h.shape[0] < n:
        h = np.block([[h, h], [h, -h]])
    return h


def build_stage_a():
    nc = bass.Bass("TRN2", target_bir_lowering=False)
    s = Sched()
    xb = nc.dram_tensor("xb", [L, DM], F32, kind="ExternalInput").ap()
    wdelta = nc.dram_tensor("wdelta", [DM, 128], F32, kind="ExternalInput").ap()
    wdt = nc.dram_tensor("wdt", [128, 1024], F32, kind="ExternalInput").ap()
    bdt = nc.dram_tensor("bdt", [128, 8], F32, kind="ExternalInput").ap()
    wbc = nc.dram_tensor("wbc", [DM, 32], F32, kind="ExternalInput").ap()
    acols = nc.dram_tensor("acols", [128, 128], F32, kind="ExternalInput").ap()
    dcols = nc.dram_tensor("dcols", [128, 8], F32, kind="ExternalInput").ap()
    selbc = nc.dram_tensor("selbc", [32, 32 * 128], F32, kind="ExternalInput").ap()
    constsa = nc.dram_tensor("constsa", [128, 4], F32, kind="ExternalInput").ap()
    yhi = nc.dram_tensor("yhi", [1024, L], BF16, kind="ExternalOutput").ap()
    ylo = nc.dram_tensor("ylo", [1024, L], BF16, kind="ExternalOutput").ap()
    xlntm_d = nc.dram_tensor("xlntm", [L, DM], BF16, kind="Internal").ap()

    sb = lambda n, shp, dt: nc.alloc_sbuf_tensor(n, shp, dt).ap()
    ps = lambda n, shp, dt: nc.alloc_psum_tensor(n, shp, dt).ap()

    xin = sb("xin", [128, DM], F32)
    sq_junk = sb("sq_junk", [128, DM], BF16)
    xlnbf = sb("xlnbf", [128, DM], BF16)
    stats = sb("stats", [128, 8], F32)
    xlt = sb("xlt", [128, 2 * 8 * 512], BF16)
    xstream = sb("xstream", [128, 2 * 8 * 512], BF16)
    xhalf32 = sb("xhalf32", [128, 8 * 512], F32)
    wd_bf = sb("wd_bf", [128, NKT * 128], BF16)
    wdt_bf = sb("wdt_bf", [128, 1024], BF16)
    wbc_bf = sb("wbc_bf", [128, NKT * 32], BF16)
    wld = sb("wld", [128, 2048], F32)
    bdt_sb = sb("bdt_sb", [128, 8], F32)
    acol_sb = sb("acol_sb", [128, 128], F32)
    dcol_sb = sb("dcol_sb", [128, 8], F32)
    selbc_sb = sb("selbc_sb", [32, 32 * 128], F32)
    cca = sb("cca", [128, 4], F32)
    bpcp32 = sb("bpcp32", [32, 512], F32)
    dpT = sb("dpT", [128, 512], BF16)
    spz = sb("spz", [128, 8 * 512], F32)
    spa = sb("spa", [128, 512], F32)
    spe = sb("spe", [128, 512], F32)
    spl = sb("spl", [128, 512], F32)
    spr = sb("spr", [128, 512], F32)
    uT = sb("uT", [128, 8 * 512], F32)
    NSL = 3
    abuf = sb("abuf", [128, NSL * 512], F32)
    bbuf = sb("bbuf", [128, NSL * 512], F32)
    hbuf = sb("hbuf", [128, NSL * 512], F32)
    t1buf = sb("t1buf", [128, NSL * 512], F32)
    yacc = sb("yacc", [128, 8 * 512], F32)
    yout = sb("yout", [128, 8 * 512], F32)
    yhi_sb = sb("yhi_sb", [128, 512], BF16)
    yhi32 = sb("yhi32", [128, 512], F32)
    ylo_sb = sb("ylo_sb", [128, 512], BF16)
    state = sb("state", [128, 8 * 16], F32)

    p_dp = ps("p_dp", [128, 512], F32)
    p_z = [ps("p_z0", [128, 512], F32), ps("p_z1", [128, 512], F32)]
    p_bc = [ps("p_bc0", [128, 2 * 512], F32), ps("p_bc1", [128, 2 * 512], F32)]
    p_bpcp = ps("p_bpcp", [32, 512], F32)

    tok = None
    for k in range(NKT):
        tok = s.dma("sync", wld[:, k * 128:(k + 1) * 128], wdelta[k * 128:(k + 1) * 128, :], ch="wld")
    s.wait("vector", tok)
    t_wd = s.emit("vector", lambda v: v.tensor_copy(wd_bf[:], wld[:, 0:2048]), inc=1)
    s.guard("sync", "wld")
    s.use("wld", t_wd)
    s.wait("sync", t_wd)
    tok = s.dma("sync", wld[:, 0:1024], wdt[:], ch="wld")
    s.wait("vector", tok)
    t_wdt = s.emit("vector", lambda v: v.tensor_copy(wdt_bf[:], wld[:, 0:1024]), inc=1)
    s.wait("sync", t_wdt)
    for k in range(NKT):
        tok = s.dma("sync", wld[:, k * 32:(k + 1) * 32], wbc[k * 128:(k + 1) * 128, :], ch="wld")
    s.wait("vector", tok)
    t_wbc = s.emit("vector", lambda v: v.tensor_copy(wbc_bf[:], wld[:, 0:NKT * 32]), inc=1)
    s.dma("sync", bdt_sb[:], bdt[:], ch="consts")
    s.dma("sync", acol_sb[:], acols[:], ch="consts")
    s.dma("sync", dcol_sb[:], dcols[:], ch="consts")
    s.dma("sync", selbc_sb[:], selbc[:], ch="consts")
    s.dma("sync", cca[:], constsa[:], ch="consts")
    t_consts = t_cca = s.chan_total("consts")

    t_state0 = s.emit("vector", lambda v: v.memset(state[:], 0.0), inc=1)

    t_store = None
    t_ln = None
    for tt in range(L // 128):
        if t_ln is not None:
            s.wait("sync", t_ln)
            s.wait("sync", t_sq)
        t_ld = s.dma("sync", xin[:], xb[tt * 128:(tt + 1) * 128, :], ch="xin")
        s.wait("vector", t_ld)
        s.emit("vector", lambda v: v.reduce_sum(stats[:, 0:1], xin[:], axis=AX.X))
        s.wait("scalar", t_ld)
        t_sq = s.emit("scalar", lambda sc: sc.activation(
            out=sq_junk[:], in_=xin[:], func=AF.Square, accum_out=stats[:, 1:2]), inc=1)
        s.wait("vector", t_sq)
        s.emit("vector", lambda v: v.tensor_scalar(
            out=stats[:, 2:3], in0=stats[:, 0:1], scalar1=1.0 / DM, scalar2=None, op0=ALU.mult))
        s.emit("vector", lambda v: v.tensor_scalar(
            out=stats[:, 5:6], in0=stats[:, 1:2], scalar1=1.0 / DM, scalar2=None, op0=ALU.mult))
        s.emit("vector", lambda v: v.tensor_tensor(
            out=stats[:, 3:4], in0=stats[:, 2:3], in1=stats[:, 2:3], op=ALU.mult))
        t_var = s.emit("vector", lambda v: v.tensor_tensor(
            out=stats[:, 3:4], in0=stats[:, 5:6], in1=stats[:, 3:4], op=ALU.subtract), inc=1)
        s.wait("scalar", t_var)
        if tt == 0:
            s.wait("scalar", t_cca)
        s.emit("scalar", lambda sc: sc.activation(
            out=stats[:, 4:5], in_=stats[:, 3:4], func=AF.Ln, bias=cca[:, 0:1]))
        t_rstd = s.emit("scalar", lambda sc: sc.activation(
            out=stats[:, 4:5], in_=stats[:, 4:5], func=AF.Exp, scale=cca[:, 2:3]), inc=1)
        s.wait("vector", t_rstd)
        if t_store is not None:
            s.wait("vector", t_store)
        t_ln = s.emit("vector", lambda v: v.tensor_scalar(
            out=xlnbf[:], in0=xin[:], scalar1=stats[:, 2:3], scalar2=stats[:, 4:5],
            op0=ALU.subtract, op1=ALU.mult), inc=1)
        s.wait("gpsimd", t_ln)
        t_store = s.dma("gpsimd", xlntm_d[tt * 128:(tt + 1) * 128, :], xlnbf[:], ch="xlnst")

    s.wait("scalar", t_store)

    NCH = L // 512
    for c in range(NCH):
        par = c % 2

        def xl(k, c=c, par=par):
            if k < 8:
                return xlt[:, (par * 8 + k) * 512:(par * 8 + k + 1) * 512]
            return xstream[:, (par * 8 + (k - 8)) * 512:(par * 8 + (k - 8) + 1) * 512]

        t_xlt = None
        for k in range(8):
            s.guard("scalar", ("xlt", par, k))
            t_xlt = s.dma("scalar", xlt[:, (par * 8 + k) * 512:(par * 8 + k + 1) * 512],
                          xlntm_d[c * 512:(c + 1) * 512, k * 128:(k + 1) * 128],
                          transpose=True, ch=("xlt", par))
        t_xs = None
        for k in range(8):
            s.guard("scalar", ("xstream", par, k))
            t_xs = s.dma("scalar", xstream[:, (par * 8 + k) * 512:(par * 8 + k + 1) * 512],
                         xlntm_d[c * 512:(c + 1) * 512, (8 + k) * 128:(9 + k) * 128],
                         transpose=True, ch=("xs", par))

        s.wait("tensor", t_xlt)
        s.wait("tensor", t_xs)
        s.wait("tensor", t_wd)
        s.guard("tensor", "p_dp")
        t_mm = None
        for k in range(NKT):
            ap = xl(k)
            t_mm = s.emit("tensor", lambda t, k=k, ap=ap: t.matmul(
                p_dp[:], wd_bf[:, k * 128:(k + 1) * 128], ap,
                start=(k == 0), stop=(k == NKT - 1)), inc=(1 if k == NKT - 1 else 0))
        s.wait("vector", t_mm)
        s.guard("vector", "dpT")
        t_dp = s.emit("vector", lambda v: v.tensor_copy(dpT[:], p_dp[:]), inc=1)
        s.use("p_dp", t_dp)

        s.wait("tensor", t_wbc)
        s.guard("tensor", "p_bpcp")
        for k in range(NKT):
            ap = xl(k)
            t_mmb = s.emit("tensor", lambda t, k=k, ap=ap: t.matmul(
                p_bpcp[:], wbc_bf[:, k * 32:(k + 1) * 32], ap,
                start=(k == 0), stop=(k == NKT - 1)), inc=(1 if k == NKT - 1 else 0))
        s.wait("vector", t_mmb)
        s.guard("vector", "bpcp32")
        t_bpcp = s.emit("vector", lambda v: v.tensor_copy(bpcp32[:], p_bpcp[:]), inc=1)
        s.use("p_bpcp", t_bpcp)
        for k in range(8):
            s.use(("xstream", par, k), t_mmb)

        s.wait("vector", t_xlt)
        s.guard("vector", "xhalf32")
        t_xh = None
        for k in range(8):
            ap = xl(k)
            t_xh = s.emit("vector", lambda v, k=k, ap=ap: v.tensor_copy(
                xhalf32[:, k * 512:(k + 1) * 512], ap), inc=(1 if k == 7 else 0))
        for k in range(8):
            s.use(("xlt", par, k), t_xh)

        t_u = {}
        t_delta = {}
        for d in range(8):
            pz = p_z[d % 2]
            s.wait("tensor", t_dp)
            s.wait("tensor", t_wdt)
            s.guard("tensor", ("p_z", d % 2))
            t_mm = s.emit("tensor", lambda t, d=d, pz=pz: t.matmul(
                pz[:], wdt_bf[:, d * 128:(d + 1) * 128], dpT[:],
                start=True, stop=True), inc=1)
            if d == 7:
                s.use("dpT", t_mm)
            s.wait("vector", t_mm)
            s.guard("vector", ("spz", d))
            t_zb = s.emit("vector", lambda v, d=d, pz=pz: v.tensor_scalar(
                out=spz[:, d * 512:(d + 1) * 512], in0=pz[:], scalar1=bdt_sb[:, d:d + 1],
                scalar2=None, op0=ALU.add), inc=1)
            s.use(("p_z", d % 2), t_zb)
            s.wait("scalar", t_zb)
            s.emit("scalar", lambda sc, d=d: sc.activation(
                out=spa[:], in_=spz[:, d * 512:(d + 1) * 512], func=AF.Abs))
            s.emit("scalar", lambda sc: sc.activation(out=spe[:], in_=spa[:], func=AF.Exp, scale=cca[:, 3:4]))
            t_l = s.emit("scalar", lambda sc: sc.activation(out=spl[:], in_=spe[:], func=AF.Ln, bias=cca[:, 1:2]), inc=1)
            s.emit("vector", lambda v, d=d: v.tensor_scalar(
                out=spr[:], in0=spz[:, d * 512:(d + 1) * 512], scalar1=0.0, scalar2=None, op0=ALU.max))
            s.wait("vector", t_l)
            t_delta[d] = s.emit("vector", lambda v, d=d: v.tensor_tensor(
                out=spz[:, d * 512:(d + 1) * 512], in0=spr[:], in1=spl[:], op=ALU.add), inc=1)
            s.use("spl", t_delta[d])
            t_u[d] = s.emit("vector", lambda v, d=d: v.tensor_tensor(
                out=uT[:, d * 512:(d + 1) * 512], in0=spz[:, d * 512:(d + 1) * 512],
                in1=xhalf32[:, d * 512:(d + 1) * 512], op=ALU.mult), inc=1)

        s.guard("vector", "yacc")
        s.emit("vector", lambda v: v.memset(yacc[:], 0.0))
        if c == 0:
            s.wait("vector", t_state0)
            s.wait("scalar", t_consts)

        slot = 0
        for n in range(16):
            pbc = p_bc[n % 2]
            s.wait("tensor", t_bpcp)
            s.guard("tensor", ("p_bc", n % 2, 0))
            s.guard("tensor", ("p_bc", n % 2, 1))
            t_b1 = s.emit("tensor", lambda t, n=n, pbc=pbc: t.matmul(
                pbc[:, 0:512], selbc_sb[:, n * 128:(n + 1) * 128], bpcp32[:],
                start=True, stop=True), inc=1)
            t_b2 = s.emit("tensor", lambda t, n=n, pbc=pbc: t.matmul(
                pbc[:, 512:1024], selbc_sb[:, (16 + n) * 128:(17 + n) * 128], bpcp32[:],
                start=True, stop=True), inc=1)
            if n == 15:
                s.use("bpcp32", t_b2)
            for d in range(8):
                col = d * 16 + n
                bb = slot * 512
                s.wait("scalar", t_delta[d])
                s.guard("scalar", ("abuf", slot))
                t_ab = s.emit("scalar", lambda sc, d=d, col=col, bb=bb: sc.activation(
                    out=abuf[:, bb:bb + 512], in_=spz[:, d * 512:(d + 1) * 512],
                    func=AF.Exp, scale=acol_sb[:, col:col + 1]), inc=1)
                s.use(("spz", d), t_ab)
                s.wait("vector", t_b1)
                s.wait("vector", t_u[d])
                s.guard("vector", ("bbuf", slot))
                t_bb = s.emit("vector", lambda v, d=d, bb=bb, pbc=pbc: v.tensor_tensor(
                    out=bbuf[:, bb:bb + 512], in0=uT[:, d * 512:(d + 1) * 512],
                    in1=pbc[:, 0:512], op=ALU.mult), inc=1)
                s.wait("vector", t_ab)
                s.guard("vector", ("hbuf", slot))
                t_h = s.emit("vector", lambda v, col=col, bb=bb: v.tensor_tensor_scan(
                    out=hbuf[:, bb:bb + 512], data0=abuf[:, bb:bb + 512],
                    data1=bbuf[:, bb:bb + 512], initial=state[:, col:col + 1],
                    op0=ALU.mult, op1=ALU.add), inc=1)
                s.use(("abuf", slot), t_h)
                s.use(("bbuf", slot), t_h)
                s.emit("vector", lambda v, col=col, bb=bb: v.tensor_copy(
                    state[:, col:col + 1], hbuf[:, bb + 511:bb + 512]))
                s.wait("vector", t_b2)
                s.emit("vector", lambda v, bb=bb, pbc=pbc: v.tensor_tensor(
                    out=t1buf[:, bb:bb + 512], in0=hbuf[:, bb:bb + 512],
                    in1=pbc[:, 512:1024], op=ALU.mult))
                t_yy = s.emit("vector", lambda v, d=d, bb=bb: v.tensor_tensor(
                    out=yacc[:, d * 512:(d + 1) * 512], in0=yacc[:, d * 512:(d + 1) * 512],
                    in1=t1buf[:, bb:bb + 512], op=ALU.add), inc=1)
                s.use(("hbuf", slot), t_yy)
                if d == 7:
                    s.use(("p_bc", n % 2, 0), t_yy)
                    s.use(("p_bc", n % 2, 1), t_yy)
                slot = (slot + 1) % NSL
        for d in range(8):
            s.guard("vector", ("yout", d))
            t_yf = s.emit("vector", lambda v, d=d: v.scalar_tensor_tensor(
                out=yout[:, d * 512:(d + 1) * 512], in0=xhalf32[:, d * 512:(d + 1) * 512],
                scalar=dcol_sb[:, d:d + 1], in1=yacc[:, d * 512:(d + 1) * 512],
                op0=ALU.mult, op1=ALU.add), inc=1)
            if d == 7:
                s.use("xhalf32", t_yf)
                s.use("yacc", t_yf)
            s.guard("vector", "yhis")
            s.emit("vector", lambda v, d=d: v.tensor_copy(
                yhi_sb[:], yout[:, d * 512:(d + 1) * 512]))
            s.emit("vector", lambda v: v.tensor_copy(yhi32[:], yhi_sb[:]))
            t_lo = s.emit("vector", lambda v, d=d: v.tensor_tensor(
                out=ylo_sb[:], in0=yout[:, d * 512:(d + 1) * 512], in1=yhi32[:],
                op=ALU.subtract))
            s.wait("gpsimd", t_lo)
            t_s1 = s.dma("gpsimd", yhi[d * 128:(d + 1) * 128, c * 512:(c + 1) * 512], yhi_sb[:], ch="yst")
            t_s2 = s.dma("gpsimd", ylo[d * 128:(d + 1) * 128, c * 512:(c + 1) * 512], ylo_sb[:], ch="yst")
            s.use("yhis", t_s2)

    s.wait_self("gpsimd", s.chan_total(("eng", "gpsimd")))
    return nc, s



def build_stage_b(debug=False):
    nc = bass.Bass("TRN2", target_bir_lowering=False)
    s = Sched()
    xrows = nc.dram_tensor("xrows", [TOK, DM], F32, kind="ExternalInput").ap()
    yhi = nc.dram_tensor("yhi", [DM, TOK], BF16, kind="ExternalInput").ap()
    ylo = nc.dram_tensor("ylo", [DM, TOK], BF16, kind="ExternalInput").ap()
    w2up = nc.dram_tensor("w2up", [DM, INTER], BF16, kind="ExternalInput").ap()
    w2gate = nc.dram_tensor("w2gate", [DM, INTER], BF16, kind="ExternalInput").ap()
    w2down = nc.dram_tensor("w2down", [INTER, DM], BF16, kind="ExternalInput").ap()
    scu = nc.dram_tensor("scu", [128, 64], F32, kind="ExternalInput").ap()
    scg = nc.dram_tensor("scg", [128, 64], F32, kind="ExternalInput").ap()
    constsb = nc.dram_tensor("constsb", [128, 2], F32, kind="ExternalInput").ap()
    x2o_d = nc.dram_tensor("x2o", [TOK, DM], F32, kind="ExternalOutput").ap()
    xln_d = nc.dram_tensor("xln_sc", [TOK, DM], BF16, kind="Internal").ap()
    if debug:
        dbg_x1 = nc.dram_tensor("dbg_x1", [512, DM], F32, kind="ExternalOutput").ap()
        dbg_xln = nc.dram_tensor("dbg_xln", [512, DM], BF16, kind="ExternalOutput").ap()
        dbg_z = nc.dram_tensor("dbg_z", [INTER, 512], BF16, kind="ExternalOutput").ap()

    sb = lambda n, shp, dt: nc.alloc_sbuf_tensor(n, shp, dt).ap()
    ps = lambda n, shp, dt: nc.alloc_psum_tensor(n, shp, dt).ap()

    x1 = sb("x1", [128, 4 * DM], F32)
    yts = sb("yts", [128, 2 * DM], BF16)
    xlnbf = sb("xlnbf", [128, DM], BF16)
    sq_junk = sb("sq_junk", [128, DM], BF16)
    stats = sb("stats", [128, 8], F32)
    ccb128 = sb("ccb128", [128, 2], F32)
    xlnT = sb("xlnT", [128, 16 * 512], BF16)
    zbf = sb("zbf", [128, 64 * 512], BF16)
    wup_sb = sb("wup_sb", [128, 2 * 2048], BF16)
    wgt_sb = sb("wgt_sb", [128, 2 * 2048], BF16)
    wdn_sb = sb("wdn_sb", [128, 2 * 16 * 512], BF16)
    up_sb = sb("up_sb", [128, 2 * 512], F32)
    gs_sb = sb("gs_sb", [128, 2 * 512], F32)
    x2s = sb("x2s", [128, 2 * 512], F32)
    scu_sb = sb("scu_sb", [128, 64], F32)
    scg_sb = sb("scg_sb", [128, 64], F32)

    p_up = [ps("p_up0", [128, 512], F32), ps("p_up1", [128, 512], F32)]
    p_gt = [ps("p_gt0", [128, 512], F32), ps("p_gt1", [128, 512], F32)]
    p_dn = [p_up[0], p_up[1], p_gt[0], p_gt[1]]

    s.dma("sync", scu_sb[:], scu[:], ch="consts")
    s.dma("sync", scg_sb[:], scg[:], ch="consts")
    s.dma("sync", ccb128[:], constsb[:], ch="consts")
    t_sc = t_cc128 = s.chan_total("consts")

    NCH = TOK // 512
    for c in range(NCH):
        t_lnst = {}
        for tt in range(4):
            s.guard("sync", ("x1", tt))
            t_x = s.dma("sync", x1[:, tt * DM:(tt + 1) * DM],
                        xrows[c * 512 + tt * 128: c * 512 + (tt + 1) * 128, :], ch=("x1", tt))
            par = tt % 2
            s.guard("scalar", ("yts", par))
            t_yh = s.dma("scalar", yts[:, par * DM:(par + 1) * DM],
                         yhi[:, c * 512 + tt * 128: c * 512 + (tt + 1) * 128], transpose=True,
                         ch=("yts", par))
            s.wait("vector", t_x)
            s.wait("vector", t_yh)
            t_a1 = s.emit("vector", lambda v, tt=tt, par=par: v.tensor_tensor(
                out=x1[:, tt * DM:(tt + 1) * DM], in0=x1[:, tt * DM:(tt + 1) * DM],
                in1=yts[:, par * DM:(par + 1) * DM], op=ALU.add))
            s.use(("yts", par), t_a1)
            s.guard("scalar", ("yts", par))
            t_yl = s.dma("scalar", yts[:, par * DM:(par + 1) * DM],
                         ylo[:, c * 512 + tt * 128: c * 512 + (tt + 1) * 128], transpose=True,
                         ch=("yts", par))
            s.wait("vector", t_yl)
            t_a2 = s.emit("vector", lambda v, tt=tt, par=par: v.tensor_tensor(
                out=x1[:, tt * DM:(tt + 1) * DM], in0=x1[:, tt * DM:(tt + 1) * DM],
                in1=yts[:, par * DM:(par + 1) * DM], op=ALU.add))
            s.use(("yts", par), t_a2)
            s.emit("vector", lambda v, tt=tt: v.reduce_sum(
                stats[:, 0:1], x1[:, tt * DM:(tt + 1) * DM], axis=AX.X))
            s.wait("scalar", t_a2)
            t_sq = s.emit("scalar", lambda sc, tt=tt: sc.activation(
                out=sq_junk[:], in_=x1[:, tt * DM:(tt + 1) * DM], func=AF.Square,
                accum_out=stats[:, 1:2]))
            s.wait("vector", t_sq)
            s.emit("vector", lambda v: v.tensor_scalar(
                out=stats[:, 2:3], in0=stats[:, 0:1], scalar1=1.0 / DM, scalar2=None, op0=ALU.mult))
            s.emit("vector", lambda v: v.tensor_scalar(
                out=stats[:, 5:6], in0=stats[:, 1:2], scalar1=1.0 / DM, scalar2=None, op0=ALU.mult))
            s.emit("vector", lambda v: v.tensor_tensor(
                out=stats[:, 3:4], in0=stats[:, 2:3], in1=stats[:, 2:3], op=ALU.mult))
            t_var = s.emit("vector", lambda v: v.tensor_tensor(
                out=stats[:, 3:4], in0=stats[:, 5:6], in1=stats[:, 3:4], op=ALU.subtract))
            s.wait("scalar", t_var)
            if c == 0 and tt == 0:
                s.wait("scalar", t_cc128)
            s.emit("scalar", lambda sc: sc.activation(
                out=stats[:, 4:5], in_=stats[:, 3:4], func=AF.Ln, bias=ccb128[:, 0:1]))
            t_rstd = s.emit("scalar", lambda sc: sc.activation(
                out=stats[:, 4:5], in_=stats[:, 4:5], func=AF.Exp, scale=ccb128[:, 1:2]))
            s.wait("vector", t_rstd)
            s.guard("vector", "xlnbf")
            t_ln = s.emit("vector", lambda v, tt=tt: v.tensor_scalar(
                out=xlnbf[:], in0=x1[:, tt * DM:(tt + 1) * DM], scalar1=stats[:, 2:3],
                scalar2=stats[:, 4:5], op0=ALU.subtract, op1=ALU.mult))
            s.wait("gpsimd", t_ln)
            t_xst = s.dma("gpsimd", xln_d[c * 512 + tt * 128: c * 512 + (tt + 1) * 128, :], xlnbf[:],
                          ch="xlnst")
            s.use("xlnbf", t_xst)
            t_lnst[tt] = t_xst

        if debug and c == 0:
            s.wait("gpsimd", s.chan_total("xlnst"))
            for tt in range(4):
                s.dma("gpsimd", dbg_x1[tt * 128:(tt + 1) * 128, :], x1[:, tt * DM:(tt + 1) * DM], ch="dbg")
            t_dmp = s.dma("gpsimd", dbg_xln[:], xln_d[0:512, :], ch="dbg")
            t_dt = s.chan_total("dbg")
            for e in ("sync", "scalar", "vector", "tensor"):
                s.wait(e, t_dt)
        t_allst = s.chan_total("xlnst")
        for k in range(NKT):
            s.guard("scalar", ("xlnT", k))
            s.wait("scalar", t_allst)
            s.dma("scalar", xlnT[:, k * 512:(k + 1) * 512],
                  xln_d[c * 512:(c + 1) * 512, k * 128:(k + 1) * 128], transpose=True, ch="xlnT")
        t_xt_all = s.chan_total("xlnT")

        for i in range(64):
            par = i % 2
            s.guard("sync", ("wup", par))
            t_wu = s.dma("sync",
                         wup_sb[:, par * 2048:(par + 1) * 2048].rearrange("p (k f) -> p k f", k=16),
                         w2up[:, i * 128:(i + 1) * 128].rearrange("(k p) f -> p k f", p=128),
                         ch=("wup", par))
            s.guard("sync", ("wgt", par))
            t_wg = s.dma("sync",
                         wgt_sb[:, par * 2048:(par + 1) * 2048].rearrange("p (k f) -> p k f", k=16),
                         w2gate[:, i * 128:(i + 1) * 128].rearrange("(k p) f -> p k f", p=128),
                         ch=("wgt", par))
            s.wait("tensor", t_wu)
            s.wait("tensor", t_wg)
            s.guard("tensor", ("p_up", par))
            s.guard("tensor", ("p_gt", par))
            s.wait("tensor", t_xt_all)
            for k in range(NKT):
                s.emit("tensor", lambda t, i=i, k=k, par=par: t.matmul(
                    p_up[par][:], wup_sb[:, par * 2048 + k * 128: par * 2048 + (k + 1) * 128],
                    xlnT[:, k * 512:(k + 1) * 512], start=(k == 0), stop=(k == NKT - 1)))
            t_mm2 = None
            for k in range(NKT):
                t_mm2 = s.emit("tensor", lambda t, i=i, k=k, par=par: t.matmul(
                    p_gt[par][:], wgt_sb[:, par * 2048 + k * 128: par * 2048 + (k + 1) * 128],
                    xlnT[:, k * 512:(k + 1) * 512], start=(k == 0), stop=(k == NKT - 1)))
            s.use(("wup", par), t_mm2)
            s.use(("wgt", par), t_mm2)
            if i >= 62:
                for k in range(NKT):
                    s.use(("xlnT", k), t_mm2)
            s.wait("vector", t_mm2)
            s.wait("vector", t_sc)
            t_up = s.emit("vector", lambda v, i=i, par=par: v.tensor_scalar(
                out=up_sb[:, par * 512:(par + 1) * 512], in0=p_up[par][:],
                scalar1=scu_sb[:, i:i + 1], scalar2=None, op0=ALU.mult))
            s.use(("p_up", par), t_up)
            s.wait("scalar", t_mm2)
            t_gs = s.emit("scalar", lambda sc, i=i, par=par: sc.activation(
                out=gs_sb[:, par * 512:(par + 1) * 512], in_=p_gt[par][:],
                func=AF.Silu, scale=scg_sb[:, i:i + 1]))
            s.use(("p_gt", par), t_gs)
            s.wait("vector", t_gs)
            s.guard("vector", ("zrd", i))
            t_z = s.emit("vector", lambda v, i=i, par=par: v.tensor_tensor(
                out=zbf[:, i * 512:(i + 1) * 512], in0=up_sb[:, par * 512:(par + 1) * 512],
                in1=gs_sb[:, par * 512:(par + 1) * 512], op=ALU.mult))
            s.use(("zmm", i), t_z)

        if debug and c == 0:
            t_dz = None
            for i in range(64):
                s.wait("gpsimd", s.last_use.get(("zmm", i)))
                t_dz = s.dma("gpsimd", dbg_z[i * 128:(i + 1) * 128, :], zbf[:, i * 512:(i + 1) * 512])
            s.wait("vector", t_dz)
            s.wait("tensor", t_dz)
        for oc in range(4):
            t_dn = {}
            for ig in range(4):
                wpar = ig % 2
                s.guard("sync", ("wdn", wpar))
                t_wd = s.dma("sync",
                             wdn_sb[:, wpar * 8192:(wpar + 1) * 8192].rearrange("p (k f) -> p k f", k=16),
                             w2down[ig * 2048:(ig + 1) * 2048, oc * 512:(oc + 1) * 512]
                             .rearrange("(k p) f -> p k f", p=128))
                s.wait("tensor", t_wd)
                for tt in range(4):
                    if ig == 0:
                        s.guard("tensor", ("p_up", tt) if tt < 2 else ("p_gt", tt - 2))
                    for ii in range(16):
                        i = ig * 16 + ii
                        s.wait("tensor", s.last_use.get(("zmm", i)))
                        t_dn[tt] = s.emit("tensor", lambda t, tt=tt, i=i, ii=ii, wpar=wpar, ig=ig: t.matmul(
                            p_dn[tt][:], zbf[:, i * 512 + tt * 128: i * 512 + (tt + 1) * 128],
                            wdn_sb[:, wpar * 8192 + ii * 512:wpar * 8192 + (ii + 1) * 512],
                            start=(ig == 0 and ii == 0), stop=(ig == 3 and ii == 15)))
                s.use(("wdn", wpar), t_dn[3])
            for tt in range(4):
                if oc == 3:
                    for i in range(64):
                        s.use(("zrd", i), t_dn[3])
                s.wait("vector", t_dn[tt])
                ppar = tt % 2
                s.guard("vector", ("x2s", ppar))
                t_x2 = s.emit("vector", lambda v, tt=tt, oc=oc, ppar=ppar: v.tensor_tensor(
                    out=x2s[:, ppar * 512:(ppar + 1) * 512], in0=p_dn[tt][:],
                    in1=x1[:, tt * DM + oc * 512: tt * DM + (oc + 1) * 512], op=ALU.add))
                s.use(("p_up", tt) if tt < 2 else ("p_gt", tt - 2), t_x2)
                if oc == 3:
                    s.use(("x1", tt), t_x2)
                s.wait("gpsimd", t_x2)
                t_so = s.dma("gpsimd",
                             x2o_d[c * 512 + tt * 128: c * 512 + (tt + 1) * 128, oc * 512:(oc + 1) * 512],
                             x2s[:, ppar * 512:(ppar + 1) * 512], ch="x2st")
                s.use(("x2s", ppar), t_so)

    s.wait_self("gpsimd", s.chan_total("x2st"))
    return nc, s



def build_stage_c():
    nc = bass.Bass("TRN2", target_bir_lowering=False)
    s = Sched()
    x2r = nc.dram_tensor("x2rows", [TOK + 16, DM], F32, kind="ExternalInput").ap()
    wq = nc.dram_tensor("wq", [DM, KDIM], BF16, kind="ExternalInput").ap()
    wk0 = nc.dram_tensor("wk0", [DM, KDIM], BF16, kind="ExternalInput").ap()
    wk1 = nc.dram_tensor("wk1", [DM, KDIM], BF16, kind="ExternalInput").ap()
    wv0 = nc.dram_tensor("wv0", [DM, VDIM], BF16, kind="ExternalInput").ap()
    wv1 = nc.dram_tensor("wv1", [DM, VDIM], BF16, kind="ExternalInput").ap()
    wb01 = nc.dram_tensor("wb01", [DM, 12], BF16, kind="ExternalInput").ap()
    wa = nc.dram_tensor("wa", [DM, 6], BF16, kind="ExternalInput").ap()
    wg = nc.dram_tensor("wg", [DM, VDIM], BF16, kind="ExternalInput").ap()
    cw = nc.dram_tensor("cw", [128, 84 * 4], F32, kind="ExternalInput").ap()
    alogdt = nc.dram_tensor("alogdt", [6, 2], F32, kind="ExternalInput").ap()
    constsc = nc.dram_tensor("constsc", [128, 6], F32, kind="ExternalInput").ap()
    identc = nc.dram_tensor("identc", [128, 128], F32, kind="ExternalInput").ap()
    Ks = nc.dram_tensor("Ks", [2 * TOK, KDIM], BF16, kind="ExternalOutput").ap()
    Qs = nc.dram_tensor("Qs", [2 * TOK, KDIM], BF16, kind="ExternalOutput").ap()
    Vs = nc.dram_tensor("Vs", [2 * TOK, VDIM], BF16, kind="ExternalOutput").ap()
    bgz = nc.dram_tensor("bgz", [2 * TOK, 12], F32, kind="ExternalOutput").ap()
    Kse = Ks.rearrange("(t two) f -> two t f", two=2)
    Qse = Qs.rearrange("(t two) f -> two t f", two=2)
    Vse = Vs.rearrange("(t two) f -> two t f", two=2)
    bgze = bgz.rearrange("(t two) f -> two t f", two=2)
    gg_d = nc.dram_tensor("gg", [TOK, VDIM], BF16, kind="ExternalOutput").ap()
    xln_d = nc.dram_tensor("xln_sc", [TOK + 16, DM], BF16, kind="Internal").ap()
    qc_d = nc.dram_tensor("qc_sc", [KDIM, TOK], BF16, kind="Internal").ap()
    k0c_d = nc.dram_tensor("k0c_sc", [KDIM, TOK], BF16, kind="Internal").ap()
    k1c_d = nc.dram_tensor("k1c_sc", [KDIM, TOK], BF16, kind="Internal").ap()
    v0c_d = nc.dram_tensor("v0c_sc", [VDIM, TOK], BF16, kind="Internal").ap()
    v1c_d = nc.dram_tensor("v1c_sc", [VDIM, TOK], BF16, kind="Internal").ap()

    sb = lambda n, shp, dt: nc.alloc_sbuf_tensor(n, shp, dt).ap()
    ps = lambda n, shp, dt: nc.alloc_psum_tensor(n, shp, dt).ap()

    xin = sb("xin", [128, DM], F32)
    sq_junk = sb("sq_junk", [128, DM], BF16)
    xlnbf = sb("xlnbf", [128, DM], BF16)
    stats = sb("stats", [128, 8], F32)
    cc = sb("cc", [128, 6], F32)
    hT = sb("hT", [128, 16 * 512], BF16)
    wproj = sb("wproj", [128, 2 * 2048], BF16)
    wsmall = sb("wsmall", [128, 16 * 18], BF16)
    projc = sb("projc", [128, 2 * 520], F32)
    carry = sb("carry", [128, 84 * 4], F32)
    convo = sb("convo", [128, 2 * 512], F32)
    cw_sb = sb("cw_sb", [128, 84 * 4], F32)
    silub = sb("silub", [128, 2 * 512], BF16)
    wmov = sb("wmov", [128, 2 * 16 * 512], BF16)
    ggsb = sb("ggsb", [128, 2 * 512], BF16)
    negA = sb("negA", [6, 2], F32)
    bsp = sb("bsp", [128, 5 * 512], F32)
    bgt = sb("bgt", [128, 2 * 12], F32)
    zero6 = sb("zero6", [128, 6], F32)
    idc = sb("idc", [128, 128], F32)
    idcb = sb("idcb", [16, 16], BF16)
    tq = sb("tq", [128, KDIM], BF16)
    tk0 = sb("tk0", [128, KDIM], BF16)
    tk1 = sb("tk1", [128, KDIM], BF16)
    tv0 = sb("tv0", [128, VDIM], BF16)
    tv1 = sb("tv1", [128, VDIM], BF16)
    tmpsq = sb("tmpsq", [128, KDIM], F32)
    nsq = sb("nsq", [128, 18], F32)
    outq = sb("outq", [128, KDIM], BF16)
    outk = sb("outk", [128, KDIM], BF16)
    zeroq = sb("zeroq", [128, KDIM], BF16)

    p_pr = [ps("p_pr0", [128, 512], F32), ps("p_pr1", [128, 512], F32)]
    p_gg = [ps("p_gg0", [128, 512], F32), ps("p_gg1", [128, 512], F32)]
    p_ba = ps("p_ba", [128, 512], F32)
    p_aa = ps("p_aa", [128, 512], F32)
    p_bt = ps("p_bt", [128, 128], F32)
    p_btb = ps("p_btb", [128, 128], BF16)

    s.dma("sync", cw_sb[:], cw[:], ch="consts")
    s.dma("sync", cc[:], constsc[:], ch="consts")
    s.dma("sync", negA[:], alogdt[:], ch="consts")
    s.dma("sync", idc[:], identc[:], ch="consts")
    s.dma("gpsimd", idcb[:], identc[0:16, 0:16], ch="consts")
    t_consts = s.chan_total("consts")
    t_z6 = s.emit("vector", lambda v: v.memset(zero6[:], 0.0))
    t_zq = s.emit("vector", lambda v: v.memset(zeroq[:], 0.0))
    s.wait("scalar", t_consts)
    t_expA = s.emit("scalar", lambda sc: sc.activation(
        out=negA[0:6, 0:1], in_=negA[0:6, 0:1], func=AF.Exp))
    s.wait("vector", t_expA)
    t_negA = s.emit("vector", lambda v: v.tensor_scalar(
        out=negA[0:6, 0:1], in0=negA[0:6, 0:1], scalar1=-1.0, scalar2=None, op0=ALU.mult))
    for k in range(NKT):
        s.dma("sync", wsmall[:, k * 18:k * 18 + 12], wb01[k * 128:(k + 1) * 128, :], ch="wsm")
        s.dma("sync", wsmall[:, k * 18 + 12:k * 18 + 18], wa[k * 128:(k + 1) * 128, :], ch="wsm")
    t_wsm = s.chan_total("wsm")

    def ln_tile(rows, src_lo, dst_lo, first):
        nonlocal t_store, t_ln, t_sq
        if t_ln is not None:
            s.wait("sync", t_ln)
            s.wait("sync", t_sq)
        t_ld = s.dma("sync", xin[0:rows, :], x2r[src_lo:src_lo + rows, :], ch="xin")
        s.wait("vector", t_ld)
        s.emit("vector", lambda v: v.reduce_sum(stats[0:rows, 0:1], xin[0:rows, :], axis=AX.X))
        s.wait("scalar", t_ld)
        t_sq = s.emit("scalar", lambda sc: sc.activation(
            out=sq_junk[0:rows, :], in_=xin[0:rows, :], func=AF.Square, accum_out=stats[0:rows, 1:2]))
        s.wait("vector", t_sq)
        s.emit("vector", lambda v: v.tensor_scalar(
            out=stats[0:rows, 2:3], in0=stats[0:rows, 0:1], scalar1=1.0 / DM, scalar2=None, op0=ALU.mult))
        s.emit("vector", lambda v: v.tensor_scalar(
            out=stats[0:rows, 5:6], in0=stats[0:rows, 1:2], scalar1=1.0 / DM, scalar2=None, op0=ALU.mult))
        s.emit("vector", lambda v: v.tensor_tensor(
            out=stats[0:rows, 3:4], in0=stats[0:rows, 2:3], in1=stats[0:rows, 2:3], op=ALU.mult))
        t_var = s.emit("vector", lambda v: v.tensor_tensor(
            out=stats[0:rows, 3:4], in0=stats[0:rows, 5:6], in1=stats[0:rows, 3:4], op=ALU.subtract))
        s.wait("scalar", t_var)
        if first:
            s.wait("scalar", t_consts)
        s.emit("scalar", lambda sc: sc.activation(
            out=stats[0:rows, 4:5], in_=stats[0:rows, 3:4], func=AF.Ln, bias=cc[0:rows, 0:1]))
        t_rstd = s.emit("scalar", lambda sc: sc.activation(
            out=stats[0:rows, 4:5], in_=stats[0:rows, 4:5], func=AF.Exp, scale=cc[0:rows, 1:2]))
        s.wait("vector", t_rstd)
        if t_store is not None:
            s.wait("vector", t_store)
        t_ln = s.emit("vector", lambda v: v.tensor_scalar(
            out=xlnbf[0:rows, :], in0=xin[0:rows, :], scalar1=stats[0:rows, 2:3],
            scalar2=stats[0:rows, 4:5], op0=ALU.subtract, op1=ALU.mult))
        s.wait("gpsimd", t_ln)
        t_store = s.dma("gpsimd", xln_d[dst_lo:dst_lo + rows, :], xlnbf[0:rows, :], ch="xlnst")

    t_store = t_ln = t_sq = None
    for tt in range(16):
        ln_tile(128, 16 + tt * 128, 16 + tt * 128, tt == 0)
    ln_tile(16, 0, 0, False)
    t_allln = s.chan_total("xlnst")

    ftiles = []
    for i in range(12):
        ftiles.append((wq, qc_d, i))
    for i in range(12):
        ftiles.append((wk0, k0c_d, i))
    for i in range(12):
        ftiles.append((wk1, k1c_d, i))
    for i in range(24):
        ftiles.append((wv0, v0c_d, i))
    for i in range(24):
        ftiles.append((wv1, v1c_d, i))

    s.wait("sync", t_allln)
    t_hl = s.dma("sync", xlnbf[0:16, :], xln_d[0:16, :], ch="xin")
    t_hx_all = None
    for k in range(NKT):
        s.wait("tensor", t_hl)
        s.wait("tensor", t_consts)
        s.guard("tensor", "p_btb")
        t_tp = s.emit("tensor", lambda t, k=k: t.transpose(
            p_btb[:, 0:16], xlnbf[0:16, k * 128:(k + 1) * 128], idcb[0:16, 0:16]))
        s.wait("vector", t_tp)
        t_hx_all = s.emit("vector", lambda v, k=k: v.tensor_copy(
            hT[:, k * 512:k * 512 + 16], p_btb[:, 0:16]))
        s.use("p_btb", t_hx_all)
    for fi, (wap, scd_, fw) in enumerate(ftiles):
        par = fi % 2
        s.guard("sync", ("wpr", par))
        t_w = s.dma("sync",
                    wproj[:, par * 2048:(par + 1) * 2048].rearrange("p (k f) -> p k f", k=16),
                    wap[:, fw * 128:(fw + 1) * 128].rearrange("(k p) f -> p k f", p=128),
                    ch=("wpr", par))
        s.wait("tensor", t_w)
        s.wait("tensor", t_hx_all)
        s.guard("tensor", ("p_pr", par))
        t_mm = None
        for k in range(NKT):
            t_mm = s.emit("tensor", lambda t, k=k, par=par: t.matmul(
                p_pr[par][:, 0:16], wproj[:, par * 2048 + k * 128: par * 2048 + (k + 1) * 128],
                hT[:, k * 512:k * 512 + 16], start=(k == 0), stop=(k == NKT - 1)))
        s.use(("wpr", par), t_mm)
        for k in range(NKT):
            s.use(("hT", k), t_mm)
        s.wait("vector", t_mm)
        t_cv = s.emit("vector", lambda v, fi=fi, par=par: v.tensor_copy(
            carry[:, fi * 4:fi * 4 + 3], p_pr[par][:, 13:16]))
        s.use(("p_pr", par), t_cv)

    NCH = TOK // 512
    for c in range(NCH):
        for k in range(NKT):
            s.guard("scalar", ("hT", k))
            s.wait("scalar", t_allln)
            s.dma("scalar", hT[:, k * 512:(k + 1) * 512],
                  xln_d[16 + c * 512:16 + (c + 1) * 512, k * 128:(k + 1) * 128],
                  transpose=True, ch="hT")
        t_hT = s.chan_total("hT")

        for fi, (wap, scd_, fw) in enumerate(ftiles):
            par = fi % 2
            s.guard("sync", ("wpr", par))
            t_w = s.dma("sync",
                        wproj[:, par * 2048:(par + 1) * 2048].rearrange("p (k f) -> p k f", k=16),
                        wap[:, fw * 128:(fw + 1) * 128].rearrange("(k p) f -> p k f", p=128),
                        ch=("wpr", par))
            s.wait("tensor", t_w)
            s.wait("tensor", t_hT)
            s.guard("tensor", ("p_pr", par))
            t_mm = None
            for k in range(NKT):
                t_mm = s.emit("tensor", lambda t, k=k, par=par: t.matmul(
                    p_pr[par][:], wproj[:, par * 2048 + k * 128: par * 2048 + (k + 1) * 128],
                    hT[:, k * 512:(k + 1) * 512], start=(k == 0), stop=(k == NKT - 1)))
            s.use(("wpr", par), t_mm)
            s.wait("vector", t_mm)
            s.guard("vector", ("projc", par))
            s.emit("vector", lambda v, par=par: v.tensor_copy(
                projc[:, par * 520 + 3: par * 520 + 515], p_pr[par][:]))
            s.use(("p_pr", par), s.emit("vector", lambda v, fi=fi, par=par: v.tensor_copy(
                projc[:, par * 520: par * 520 + 3], carry[:, fi * 4:fi * 4 + 3])))
            s.emit("vector", lambda v, fi=fi, par=par: v.tensor_copy(
                carry[:, fi * 4:fi * 4 + 3], projc[:, par * 520 + 512: par * 520 + 515]))
            s.wait("vector", t_consts)
            s.emit("vector", lambda v, fi=fi, par=par: v.tensor_scalar(
                out=convo[:, par * 512:(par + 1) * 512],
                in0=projc[:, par * 520: par * 520 + 512],
                scalar1=cw_sb[:, fi * 4 + 0: fi * 4 + 1], scalar2=None, op0=ALU.mult))
            t_cv1 = None
            for j in range(1, 4):
                t_cv1 = s.emit("vector", lambda v, fi=fi, par=par, j=j: v.scalar_tensor_tensor(
                    out=convo[:, par * 512:(par + 1) * 512],
                    in0=projc[:, par * 520 + j: par * 520 + j + 512],
                    scalar=cw_sb[:, fi * 4 + j: fi * 4 + j + 1],
                    in1=convo[:, par * 512:(par + 1) * 512],
                    op0=ALU.mult, op1=ALU.add))
            s.use(("projc", par), t_cv1)
            s.wait("scalar", t_cv1)
            s.guard("scalar", ("silub", par))
            t_si = s.emit("scalar", lambda sc, par=par: sc.activation(
                out=silub[:, par * 512:(par + 1) * 512], in_=convo[:, par * 512:(par + 1) * 512],
                func=AF.Silu))
            s.wait("gpsimd", t_si)
            t_ss = s.dma("gpsimd", scd_[fw * 128:(fw + 1) * 128, c * 512:(c + 1) * 512],
                         silub[:, par * 512:(par + 1) * 512], ch="convst")
            s.use(("silub", par), t_ss)

        for fc in range(6):
            par = fc % 2
            s.guard("sync", ("wmov", par))
            t_wm = s.dma("sync", wmov[:, par * 8192:(par + 1) * 8192]
                         .rearrange("p (k f) -> p k f", k=16),
                         wg[:, fc * 512:(fc + 1) * 512].rearrange("(k p) f -> p k f", p=128),
                         ch=("wmov", par))
            for tt in range(4):
                ppar = tt % 2
                s.wait("tensor", t_wm)
                s.wait("tensor", t_hT)
                s.guard("tensor", ("p_gg", ppar))
                t_mm = None
                for k in range(NKT):
                    t_mm = s.emit("tensor", lambda t, k=k, tt=tt, par=par, ppar=ppar: t.matmul(
                        p_gg[ppar][:], hT[:, k * 512 + tt * 128: k * 512 + (tt + 1) * 128],
                        wmov[:, par * 8192 + k * 512:par * 8192 + (k + 1) * 512],
                        start=(k == 0), stop=(k == NKT - 1)))
                if tt == 3:
                    s.use(("wmov", par), t_mm)
                s.wait("vector", t_mm)
                s.guard("vector", ("ggsb", ppar))
                t_ge = s.emit("vector", lambda v, ppar=ppar: v.tensor_copy(
                    ggsb[:, ppar * 512:(ppar + 1) * 512], p_gg[ppar][:]))
                s.use(("p_gg", ppar), t_ge)
                s.wait("gpsimd", t_ge)
                t_gs = s.dma("gpsimd", gg_d[c * 512 + tt * 128: c * 512 + (tt + 1) * 128,
                                            fc * 512:(fc + 1) * 512],
                             ggsb[:, ppar * 512:(ppar + 1) * 512], ch="ggst")
                s.use(("ggsb", ppar), t_gs)

        s.wait("tensor", t_wsm)
        s.guard("tensor", "p_ba")
        t_mmb = None
        for k in range(NKT):
            t_mmb = s.emit("tensor", lambda t, k=k: t.matmul(
                p_ba[0:12, :], wsmall[:, k * 18:k * 18 + 12],
                hT[:, k * 512:(k + 1) * 512], start=(k == 0), stop=(k == NKT - 1)))
        s.guard("tensor", "p_aa")
        t_mma = None
        for k in range(NKT):
            t_mma = s.emit("tensor", lambda t, k=k: t.matmul(
                p_aa[0:6, :], wsmall[:, k * 18 + 12:k * 18 + 18],
                hT[:, k * 512:(k + 1) * 512], start=(k == 0), stop=(k == NKT - 1)))
        s.wait("scalar", t_mmb)
        s.guard("scalar", "bsp0")
        t_bs = s.emit("scalar", lambda sc: sc.activation(
            out=bsp[0:12, 0:512], in_=p_ba[0:12, :], func=AF.Sigmoid))
        s.use("p_ba", t_bs)
        s.wait("vector", t_mma)
        s.wait("vector", t_negA)
        t_a0 = s.emit("vector", lambda v: v.tensor_scalar(
            out=bsp[0:6, 512:1024], in0=p_aa[0:6, :], scalar1=negA[0:6, 1:2],
            scalar2=None, op0=ALU.add))
        s.use("p_aa", t_a0)
        s.wait("scalar", t_a0)
        s.emit("scalar", lambda sc: sc.activation(
            out=bsp[0:6, 1024:1536], in_=bsp[0:6, 512:1024], func=AF.Abs))
        s.emit("scalar", lambda sc: sc.activation(
            out=bsp[0:6, 1024:1536], in_=bsp[0:6, 1024:1536], func=AF.Exp, scale=cc[0:6, 3:4]))
        t_l = s.emit("scalar", lambda sc: sc.activation(
            out=bsp[0:6, 1024:1536], in_=bsp[0:6, 1024:1536], func=AF.Ln, bias=cc[0:6, 2:3]))
        s.emit("vector", lambda v: v.tensor_scalar(
            out=bsp[0:6, 1536:2048], in0=bsp[0:6, 512:1024], scalar1=0.0, scalar2=None,
            op0=ALU.max))
        s.wait("vector", t_l)
        s.emit("vector", lambda v: v.tensor_tensor(
            out=bsp[0:6, 1536:2048], in0=bsp[0:6, 1536:2048], in1=bsp[0:6, 1024:1536], op=ALU.add))
        t_g = s.emit("vector", lambda v: v.tensor_scalar(
            out=bsp[0:6, 1536:2048], in0=bsp[0:6, 1536:2048], scalar1=negA[0:6, 0:1],
            scalar2=None, op0=ALU.mult))
        for bb in range(4):
            s.wait("tensor", t_bs)
            s.wait("tensor", t_g)
            s.guard("tensor", "p_bt")
            t_t1 = s.emit("tensor", lambda t, bb=bb: t.transpose(
                p_bt[:, 0:12], bsp[0:12, bb * 128:(bb + 1) * 128], idc[0:12, 0:12]))
            s.wait("vector", t_t1)
            s.guard("vector", "bgeo")
            t_e1 = s.emit("vector", lambda v: v.tensor_copy(bgt[:, 0:12], p_bt[:, 0:12]))
            s.use("p_bt", t_e1)
            s.guard("tensor", "p_bt")
            t_t2 = s.emit("tensor", lambda t, bb=bb: t.transpose(
                p_bt[:, 0:6], bsp[0:6, 1536 + bb * 128:1536 + (bb + 1) * 128], idc[0:6, 0:6]))
            s.wait("vector", t_t2)
            t_e2 = s.emit("vector", lambda v: v.tensor_copy(bgt[:, 12:18], p_bt[:, 0:6]))
            s.use("p_bt", t_e2)
            s.guard("vector", "bgeo")
            s.emit("vector", lambda v: v.tensor_copy(bgt[:, 18:24], bgt[:, 6:12]))
            t_asm = s.emit("vector", lambda v: v.tensor_copy(bgt[:, 6:12], bgt[:, 12:18]))
            s.wait("gpsimd", t_asm)
            base = c * 512 + bb * 128
            t_w1 = s.dma("gpsimd", bgze[0, base:base + 128, 0:12], bgt[:, 0:12], ch="bgst")
            t_w2 = s.dma("gpsimd", bgze[1, base:base + 128, 0:6], bgt[:, 18:24], ch="bgst")
            s.wait("gpsimd", t_z6)
            t_w3 = s.dma("gpsimd", bgze[1, base:base + 128, 6:12], zero6[:, 0:6], ch="bgst")
            s.use("bgeo", t_w3)

    t_allconv = s.chan_total("convst")
    for tt2 in range(TOK // 128):
        t0 = tt2 * 128
        s.guard("scalar", "tq")
        s.wait("scalar", t_allconv)
        t_lq = s.dma("scalar", tq[:], qc_d[:, t0:t0 + 128], transpose=True, ch="tq")
        s.guard("scalar", "tk0")
        t_lk0 = s.dma("scalar", tk0[:], k0c_d[:, t0:t0 + 128], transpose=True, ch="tk0")
        s.guard("scalar", "tk1")
        t_lk1 = s.dma("scalar", tk1[:], k1c_d[:, t0:t0 + 128], transpose=True, ch="tk1")
        s.guard("scalar", "tv0")
        t_lv0 = s.dma("scalar", tv0[:], v0c_d[:, t0:t0 + 128], transpose=True, ch="tv0")
        s.guard("scalar", "tv1")
        t_lv1 = s.dma("scalar", tv1[:], v1c_d[:, t0:t0 + 128], transpose=True, ch="tv1")
        for h in range(6):
            s.wait("scalar", t_lq)
            s.emit("scalar", lambda sc, h=h: sc.activation(
                out=sq_junk[:, 0:256], in_=tq[:, h * 256:(h + 1) * 256], func=AF.Square,
                accum_out=nsq[:, h:h + 1]))
            s.wait("scalar", t_lk0)
            s.emit("scalar", lambda sc, h=h: sc.activation(
                out=sq_junk[:, 0:256], in_=tk0[:, h * 256:(h + 1) * 256], func=AF.Square,
                accum_out=nsq[:, 6 + h:7 + h]))
            s.wait("scalar", t_lk1)
            t_sqq = s.emit("scalar", lambda sc, h=h: sc.activation(
                out=sq_junk[:, 0:256], in_=tk1[:, h * 256:(h + 1) * 256], func=AF.Square,
                accum_out=nsq[:, 12 + h:13 + h]))
        s.wait("scalar", t_sqq)
        t_ln1 = s.emit("scalar", lambda sc: sc.activation(
            out=nsq[:, 0:18], in_=nsq[:, 0:18], func=AF.Ln, bias=cc[:, 4:5]))
        t_rn = s.emit("scalar", lambda sc: sc.activation(
            out=nsq[:, 0:18], in_=nsq[:, 0:18], func=AF.Exp, scale=cc[:, 1:2]))
        s.wait("vector", t_rn)
        t_qs = s.emit("vector", lambda v: v.tensor_scalar(
            out=nsq[:, 0:6], in0=nsq[:, 0:6], scalar1=float(DK) ** -0.5, scalar2=None,
            op0=ALU.mult))
        s.guard("vector", "outq")
        t_oq = None
        for h in range(6):
            t_oq = s.emit("vector", lambda v, h=h: v.tensor_scalar(
                out=outq[:, h * 256:(h + 1) * 256], in0=tq[:, h * 256:(h + 1) * 256],
                scalar1=nsq[:, h:h + 1], scalar2=None, op0=ALU.mult))
        s.guard("vector", "outk")
        for h in range(6):
            t_kk = s.emit("vector", lambda v, h=h: v.tensor_scalar(
                out=outk[:, h * 256:(h + 1) * 256], in0=tk0[:, h * 256:(h + 1) * 256],
                scalar1=nsq[:, 6 + h:7 + h], scalar2=None, op0=ALU.mult))
        s.wait("gpsimd", t_kk)
        t_sk0 = s.dma("gpsimd", Kse[0, t0:t0 + 128, :], outk[:], ch="kst")
        s.use("outk", t_sk0)
        s.guard("vector", "outk")
        for h in range(6):
            t_kk = s.emit("vector", lambda v, h=h: v.tensor_scalar(
                out=outk[:, h * 256:(h + 1) * 256], in0=tk1[:, h * 256:(h + 1) * 256],
                scalar1=nsq[:, 12 + h:13 + h], scalar2=None, op0=ALU.mult))
        s.wait("gpsimd", t_kk)
        t_sk1 = s.dma("gpsimd", Kse[1, t0:t0 + 128, :], outk[:], ch="kst")
        s.use("outk", t_sk1)
        s.wait("gpsimd", t_oq)
        t_sq1 = s.dma("gpsimd", Qse[1, t0:t0 + 128, :], outq[:], ch="qst")
        s.use("outq", t_sq1)
        s.wait("gpsimd", t_zq)
        s.dma("gpsimd", Qse[0, t0:t0 + 128, :], zeroq[:], ch="qst")
        s.wait("gpsimd", t_lv0)
        t_sv0 = s.dma("gpsimd", Vse[0, t0:t0 + 128, :], tv0[:], ch="vst")
        s.use("tv0", t_sv0)
        s.wait("gpsimd", t_lv1)
        t_sv1 = s.dma("gpsimd", Vse[1, t0:t0 + 128, :], tv1[:], ch="vst")
        s.use("tv1", t_sv1)
        s.use("tq", t_sq1)
        s.use("tk0", t_sk0)
        s.use("tk1", t_sk1)

    for chn in ("ggst", "bgst", "kst", "qst", "vst"):
        s.wait_self("gpsimd", s.chan_total(chn))
    return nc, s



def build_stage_d(nch=None, nh=6, upto=99, skip_o=False):
    nc = bass.Bass("TRN2", target_bir_lowering=False)
    s = Sched()
    Ks = nc.dram_tensor("Ks", [2 * TOK, KDIM], BF16, kind="ExternalInput").ap()
    Qs = nc.dram_tensor("Qs", [2 * TOK, KDIM], BF16, kind="ExternalInput").ap()
    Vs = nc.dram_tensor("Vs", [2 * TOK, VDIM], BF16, kind="ExternalInput").ap()
    bgz = nc.dram_tensor("bgz", [2 * TOK, 12], F32, kind="ExternalInput").ap()
    Sinit = nc.dram_tensor("Sinit", [KDIM, DV], F32, kind="ExternalInput").ap()
    mup = nc.dram_tensor("mup", [128, 128], F32, kind="ExternalInput").ap()
    mlo = nc.dram_tensor("mlo", [128, 128], F32, kind="ExternalInput").ap()
    identf = nc.dram_tensor("identf", [128, 128], F32, kind="ExternalInput").ap()
    identb = nc.dram_tensor("identb", [128, 128], BF16, kind="ExternalInput").ap()
    ihalf = nc.dram_tensor("ihalf", [128, 512], F32, kind="ExternalInput").ap()
    onesd = nc.dram_tensor("onesd", [6, 128], F32, kind="ExternalInput").ap()
    Sfin = nc.dram_tensor("Sfin", [KDIM, DV], F32, kind="ExternalOutput").ap()
    O_d = nc.dram_tensor("O", [2 * TOK, VDIM], BF16, kind="ExternalOutput").ap()

    sb = lambda n, shp, dt: nc.alloc_sbuf_tensor(n, shp, dt).ap()
    ps = lambda n, shp, dt: nc.alloc_psum_tensor(n, shp, dt).ap()

    Sb = [sb("Sb0", [128, 6 * 2 * 512], BF16), sb("Sb1", [128, 6 * 2 * 512], BF16)]
    Sld = sb("Sld", [128, 512], F32)
    bgld = sb("bgld", [128, 12], F32)
    grow = sb("grow", [6, 5 * 128], F32)
    onesr = sb("onesr", [6, 128], F32)
    gcols = sb("gcols", [128, 18], F32)
    lamC = sb("lamC", [128, 6], F32)
    Kt = sb("Kt", [128, 256], BF16)
    Vt = sb("Vt", [128, 512], BF16)
    Qt = sb("Qt", [128, 256], BF16)
    KTt = sb("KTt", [128, 256], BF16)
    QTt = sb("QTt", [128, 256], BF16)
    gcb = sb("gcb", [128, 128], F32)
    grb = sb("grb", [128, 128], F32)
    lrb = sb("lrb", [128, 128], F32)
    bcb = sb("bcb", [128, 128], F32)
    brb = sb("brb", [128, 128], F32)
    mupt = sb("mupt", [128, 128], F32)
    mlot = sb("mlot", [128, 128], F32)
    idf = sb("idf", [128, 128], F32)
    idb = sb("idb", [128, 128], BF16)
    ihf = sb("ihf", [128, 512], F32)
    dmi = sb("dmi", [128, 128], F32)
    dms = sb("dms", [128, 128], F32)
    dmt = sb("dmt", [128, 128], F32)
    dst_ = sb("dst_", [128, 128], F32)
    lsb = sb("lsb", [128, 128], F32)
    nsb = sb("nsb", [128, 128], BF16)
    ntsb = sb("ntsb", [128, 128], BF16)
    tmpa = sb("tmpa", [128, 128], F32)
    pw = sb("pw", [128, 2 * 128], BF16)
    qpowt = sb("qpowt", [128, 6 * 128], BF16)
    bvt = sb("bvt", [128, 512], BF16)
    bkt = sb("bkt", [128, 256], BF16)
    yw = sb("yw", [128, 512], BF16)
    yr = sb("yr", [128, 256], BF16)
    adt = sb("adt", [128, 128], BF16)
    aqkT = sb("aqkT", [128, 128], F32)
    ket = sb("ket", [128, 256], BF16)
    wet = sb("wet", [128, 512], BF16)
    gtt = sb("gtt", [128, 256], BF16)
    ftt = sb("ftt", [128, 2 * 256], BF16)
    ut = sb("ut", [128, 2 * 512], BF16)
    oev = sb("oev", [128, 512], BF16)
    scol = sb("scol", [128, 8], F32)
    sfo = sb("sfo", [128, 512], F32)

    p_a = ps("p_a", [128, 512], F32)
    p_b = ps("p_b", [128, 512], F32)
    p_w = ps("p_w", [128, 512], F32)
    p_r = ps("p_r", [128, 256], F32)
    p_o = ps("p_o", [128, 512], F32)
    p_s = [ps("p_s0", [128, 512], F32), ps("p_s1", [128, 512], F32)]
    p_g = ps("p_g", [128, 128], F32)

    s.dma("sync", mupt[:], mup[:], ch="consts")
    s.dma("sync", mlot[:], mlo[:], ch="consts")
    s.dma("sync", idf[:], identf[:], ch="consts")
    s.dma("sync", idb[:], identb[:], ch="consts")
    s.dma("sync", ihf[:], ihalf[:], ch="consts")
    s.dma("sync", onesr[:], onesd[:], ch="consts")
    t_consts = s.chan_total("consts")

    for h in range(6):
        for half in range(2):
            t_sl = s.dma("sync", Sld[:], Sinit[h * 256 + half * 128: h * 256 + (half + 1) * 128, :],
                         ch="sinit")
            s.wait("vector", t_sl)
            t_sc = s.emit("vector", lambda v, h=h, half=half: v.tensor_copy(
                Sb[1][:, (h * 2 + half) * 512:(h * 2 + half + 1) * 512], Sld[:]))
            s.wait("sync", t_sc)

    NCH = nch if nch is not None else (2 * TOK) // 128
    t_chain = {}
    for c in range(NCH):
        par = c % 2
        sold = Sb[(c + 1) % 2]
        snew = Sb[c % 2]
        s.guard("sync", "bgld_v")
        s.guard("sync", "bgld_t")
        t_bg = s.dma("sync", bgld[:], bgz[c * 128:(c + 1) * 128, :], ch="bgld")
        s.wait("tensor", t_bg)
        s.wait("vector", t_bg)
        s.wait("tensor", t_consts)
        s.guard("tensor", "p_g")
        t_tp = s.emit("tensor", lambda t: t.transpose(p_g[0:6, :], bgld[:, 6:12], idf[:]))
        s.use("bgld_t", t_tp)
        s.wait("vector", t_tp)
        s.guard("vector", "grow_s")
        s.guard("vector", "grow_t")
        t_ev = s.emit("vector", lambda v: v.tensor_copy(grow[:, 0:128], p_g[0:6, :]))
        s.use("p_g", t_ev)
        t_gam = s.emit("vector", lambda v: v.tensor_tensor_scan(
            out=grow[:, 128:256], data0=onesr[:], data1=grow[:, 0:128], initial=0.0,
            op0=ALU.mult, op1=ALU.add))
        s.wait("scalar", t_gam)
        t_lam = s.emit("scalar", lambda sc: sc.activation(
            out=grow[:, 256:384], in_=grow[:, 128:256], func=AF.Exp))
        t_er = s.emit("scalar", lambda sc: sc.activation(
            out=grow[:, 384:512], in_=grow[:, 128:256], func=AF.Exp, scale=-1.0,
            bias=grow[:, 255:256]))
        s.use("grow_s", t_er)
        for ii, (lo, tok) in enumerate(((128, t_gam), (256, t_lam), (384, t_er))):
            s.wait("tensor", tok)
            s.guard("tensor", "p_g")
            t_tp2 = s.emit("tensor", lambda t, lo=lo: t.transpose(
                p_g[:, 0:6], grow[:, lo:lo + 128], idf[0:6, 0:6]))
            s.use("grow_t", t_tp2)
            s.wait("vector", t_tp2)
            s.guard("vector", ("gcols", ii))
            t_gc = s.emit("vector", lambda v, ii=ii: v.tensor_copy(
                gcols[:, ii * 6:(ii + 1) * 6], p_g[:, 0:6]))
            s.use("p_g", t_gc)
        t_lc = s.emit("vector", lambda v: v.tensor_tensor(
            out=lamC[:], in0=gcols[:, 6:12], in1=gcols[:, 12:18], op=ALU.mult))

        for h in range(nh):
            s.guard("sync", "Kt")
            t_K = s.dma("sync", Kt[:], Ks[c * 128:(c + 1) * 128, h * 256:(h + 1) * 256], ch="ldK")
            s.guard("sync", "Vt")
            t_V = s.dma("sync", Vt[:], Vs[c * 128:(c + 1) * 128, h * 512:(h + 1) * 512], ch="ldV")
            s.guard("scalar", "KTt")
            t_KT = None
            for half in range(2):
                t_KT = s.dma("scalar", KTt[:, half * 128:(half + 1) * 128],
                             Ks[c * 128:(c + 1) * 128, h * 256 + half * 128: h * 256 + (half + 1) * 128],
                             transpose=True, ch="ldKT")
            t_QT = None
            if not skip_o:
                s.guard("scalar", "QTt")
                for half in range(2):
                    t_QT = s.dma("scalar", QTt[:, half * 128:(half + 1) * 128],
                                 Qs[c * 128:(c + 1) * 128, h * 256 + half * 128: h * 256 + (half + 1) * 128],
                                 transpose=True, ch="ldQT")
            if upto < 1:
                continue
            s.wait("vector", t_gc)
            s.guard("vector", "gcb")
            t_gcb = s.emit("vector", lambda v, h=h: v.tensor_scalar(
                out=gcb[:], in0=mupt[:], scalar1=0.0, scalar2=gcols[:, h:h + 1],
                op0=ALU.mult, op1=ALU.add))
            s.wait("tensor", t_gcb)
            s.guard("tensor", "p_g")
            t_tgrb = s.emit("tensor", lambda t: t.transpose(p_g[:], gcb[:], idf[:]))
            s.wait("vector", t_tgrb)
            s.guard("vector", "grb")
            t_grb = s.emit("vector", lambda v: v.tensor_copy(grb[:], p_g[:]))
            s.use("p_g", t_grb)
            t_bcb = s.emit("vector", lambda v, h=h: v.tensor_scalar(
                out=bcb[:], in0=mupt[:], scalar1=0.0, scalar2=bgld[:, h:h + 1],
                op0=ALU.mult, op1=ALU.add))
            s.wait("tensor", t_bcb)
            s.guard("tensor", "p_g")
            t_tbrb = s.emit("tensor", lambda t: t.transpose(p_g[:], bcb[:], idf[:]))
            s.wait("vector", t_tbrb)
            s.guard("vector", "brb")
            t_brb = s.emit("vector", lambda v: v.tensor_copy(brb[:], p_g[:]))
            s.use("p_g", t_brb)
            t_lrb = None
            if not skip_o:
                s.wait("scalar", t_grb)
                s.guard("scalar", "lrb")
                t_lrb = s.emit("scalar", lambda sc: sc.activation(out=lrb[:], in_=grb[:], func=AF.Exp))
            if upto < 2:
                continue
            s.guard("vector", "dmi")
            t_v1 = s.emit("vector", lambda v, h=h: v.scalar_tensor_tensor(
                out=dmi[:], in0=grb[:], scalar=gcols[:, h:h + 1], in1=mupt[:],
                op0=ALU.subtract, op1=ALU.add))
            s.wait("scalar", t_v1)
            t_dmi = s.emit("scalar", lambda sc: sc.activation(
                out=dmi[:], in_=dmi[:], func=AF.Exp, scale=-1.0))
            s.wait("vector", t_dmi)
            t_dms = s.emit("vector", lambda v: v.tensor_tensor(
                out=dms[:], in0=dmi[:], in1=idf[:], op=ALU.subtract))
            s.guard("vector", "dmt")
            t_v2 = s.emit("vector", lambda v, h=h: v.scalar_tensor_tensor(
                out=dmt[:], in0=grb[:], scalar=gcols[:, h:h + 1], in1=mlot[:],
                op0=ALU.subtract, op1=ALU.subtract))
            s.wait("scalar", t_v2)
            t_dmt = s.emit("scalar", lambda sc: sc.activation(
                out=dmt[:], in_=dmt[:], func=AF.Exp))
            s.wait("vector", t_dmt)
            t_dst = s.emit("vector", lambda v: v.tensor_tensor(
                out=dst_[:], in0=dmt[:], in1=idf[:], op=ALU.subtract))
            if upto < 3:
                continue
            s.wait("tensor", t_KT)
            s.guard("tensor", "p_a")
            t_mL = None
            for half in range(2):
                t_mL = s.emit("tensor", lambda t, half=half: t.matmul(
                    p_a[:, 0:128], KTt[:, half * 128:(half + 1) * 128],
                    KTt[:, half * 128:(half + 1) * 128], start=(half == 0), stop=(half == 1)))
            if not skip_o:
                s.wait("tensor", t_QT)
                s.guard("tensor", "p_b")
                t_mA = None
                for half in range(2):
                    t_mA = s.emit("tensor", lambda t, half=half: t.matmul(
                        p_b[:, 0:128], KTt[:, half * 128:(half + 1) * 128],
                        QTt[:, half * 128:(half + 1) * 128], start=(half == 0), stop=(half == 1)))
                s.use("KTt", t_mA)
                s.use("QTt", t_mA)
            s.wait("vector", t_mL)
            s.guard("vector", "lsb")
            t_lsb = s.emit("vector", lambda v: v.tensor_copy(lsb[:], p_a[:, 0:128]))
            s.use("p_a", t_lsb)
            if not skip_o:
                s.wait("vector", t_mA)
                s.guard("vector", "aqkT")
                t_aqkT = s.emit("vector", lambda v: v.tensor_copy(aqkT[:], p_b[:, 0:128]))
                s.use("p_b", t_aqkT)
            if upto < 4:
                continue
            s.wait("vector", t_dms)
            s.guard("vector", "nsb")
            t_n = s.emit("vector", lambda v, h=h: v.scalar_tensor_tensor(
                out=nsb[:], in0=lsb[:], scalar=bgld[:, h:h + 1], in1=dms[:],
                op0=ALU.mult, op1=ALU.mult))
            s.wait("vector", t_dst)
            s.guard("vector", "tmpa")
            s.emit("vector", lambda v: v.tensor_tensor(
                out=tmpa[:], in0=lsb[:], in1=dst_[:], op=ALU.mult))
            s.guard("vector", "ntsb")
            t_nt = s.emit("vector", lambda v: v.tensor_tensor(
                out=ntsb[:], in0=tmpa[:], in1=brb[:], op=ALU.mult))
            t_adt = None
            if not skip_o:
                s.guard("vector", "adt")
                t_adt = s.emit("vector", lambda v: v.tensor_tensor(
                    out=adt[:], in0=aqkT[:], in1=dmt[:], op=ALU.mult))
            t_bl = s.emit("vector", lambda v, h=h: v.tensor_tensor(
                out=scol[:, 0:1], in0=bgld[:, h:h + 1], in1=gcols[:, 6 + h:7 + h], op=ALU.mult))
            s.wait("vector", t_V)
            s.guard("vector", "bvt")
            t_bv = s.emit("vector", lambda v, h=h: v.tensor_scalar(
                out=bvt[:], in0=Vt[:], scalar1=bgld[:, h:h + 1], scalar2=None, op0=ALU.mult))
            s.use("Vt", t_bv)
            if h == nh - 1:
                s.use("bgld_v", t_bv)
            s.wait("vector", t_K)
            s.guard("vector", "bkt")
            t_bk = s.emit("vector", lambda v: v.tensor_scalar(
                out=bkt[:], in0=Kt[:], scalar1=scol[:, 0:1], scalar2=None, op0=ALU.mult))
            s.guard("vector", "yw")
            t_yw = s.emit("vector", lambda v: v.tensor_copy(yw[:], bvt[:]))
            s.guard("vector", "yr")
            t_yr = s.emit("vector", lambda v: v.tensor_copy(yr[:], bkt[:]))
            if upto < 5:
                continue
            qpow = qpowt
            t_p_prev = s.emit("vector", lambda v: v.tensor_copy(pw[:, 0:128], nsb[:]))
            t_q_prev = s.emit("vector", lambda v: v.tensor_copy(pw[:, 128:256], ntsb[:]))
            t_qs = {0: t_q_prev}
            for lev in range(1, 7):
                s.wait("tensor", t_p_prev)
                s.wait("tensor", t_q_prev)
                s.guard("tensor", "p_b")
                t_mp = s.emit("tensor", lambda t: t.matmul(
                    p_b[:, 0:128], pw[:, 128:256], pw[:, 0:128], start=True, stop=True))
                t_mq = s.emit("tensor", lambda t: t.matmul(
                    p_b[:, 128:256], pw[:, 0:128], pw[:, 128:256], start=True, stop=True))
                s.wait("vector", t_mq)
                t_p_prev = s.emit("vector", lambda v: v.tensor_copy(pw[:, 0:128], p_b[:, 0:128]))
                t_q_prev = s.emit("vector", lambda v, lev=lev: v.tensor_copy(
                    qpow[:, (lev - 1) * 128:lev * 128], p_b[:, 128:256]))
                s.use("p_b", t_q_prev)
                t_q_prev = s.emit("vector", lambda v, lev=lev: v.tensor_copy(
                    pw[:, 128:256], qpow[:, (lev - 1) * 128:lev * 128]))
                t_qs[lev] = t_q_prev
            if upto < 6:
                continue
            for lev in range(6, 0, -1):
                s.wait("tensor", t_qs[lev])
                s.wait("tensor", t_yw)
                s.wait("tensor", t_yr)
                s.guard("tensor", "p_w")
                s.guard("tensor", "p_r")
                t_mw = s.emit("tensor", lambda t, lev=lev: t.matmul(
                    p_w[:], qpow[:, (lev - 1) * 128:lev * 128], yw[:], start=True, stop=False))
                t_mw = s.emit("tensor", lambda t: t.matmul(
                    p_w[:], idb[:], yw[:], start=False, stop=True))
                t_mr = s.emit("tensor", lambda t, lev=lev: t.matmul(
                    p_r[:], qpow[:, (lev - 1) * 128:lev * 128], yr[:], start=True, stop=False))
                t_mr = s.emit("tensor", lambda t: t.matmul(
                    p_r[:], idb[:], yr[:], start=False, stop=True))
                s.wait("vector", t_mw)
                t_yw = s.emit("vector", lambda v: v.tensor_copy(yw[:], p_w[:]))
                s.use("p_w", t_yw)
                s.wait("vector", t_mr)
                t_yr = s.emit("vector", lambda v: v.tensor_copy(yr[:], p_r[:]))
                s.use("p_r", t_yr)
            s.wait("tensor", t_n)
            s.wait("tensor", t_yw)
            s.wait("tensor", t_yr)
            s.guard("tensor", "p_w")
            s.guard("tensor", "p_r")
            t_mw = s.emit("tensor", lambda t: t.matmul(p_w[:], ntsb[:], yw[:], start=True, stop=True))
            t_mr = s.emit("tensor", lambda t: t.matmul(p_r[:], ntsb[:], yr[:], start=True, stop=True))
            s.wait("vector", t_mw)
            t_yw = s.emit("vector", lambda v: v.tensor_tensor(
                out=yw[:], in0=yw[:], in1=p_w[:], op=ALU.subtract))
            s.use("p_w", t_yw)
            s.wait("vector", t_mr)
            t_yr = s.emit("vector", lambda v: v.tensor_tensor(
                out=yr[:], in0=yr[:], in1=p_r[:], op=ALU.subtract))
            s.use("p_r", t_yr)
            if upto < 7:
                continue
            s.guard("vector", "wet")
            t_we = s.emit("vector", lambda v, h=h: v.tensor_scalar(
                out=wet[:], in0=yw[:], scalar1=gcols[:, 12 + h:13 + h], scalar2=None, op0=ALU.mult))
            s.guard("vector", "ket")
            t_ke = s.emit("vector", lambda v, h=h: v.tensor_scalar(
                out=ket[:], in0=Kt[:], scalar1=gcols[:, 12 + h:13 + h], scalar2=None, op0=ALU.mult))
            if h == nh - 1:
                for ii2 in range(3):
                    s.use(("gcols", ii2), t_ke)
                s.use("lamC_rd", t_ke)
            for half in range(2):
                s.wait("tensor", t_we)
                s.guard("tensor", "p_w")
                t_mu = s.emit("tensor", lambda t, half=half: t.matmul(
                    p_w[:], Kt[:, half * 128:(half + 1) * 128], wet[:], start=True, stop=True))
                s.wait("vector", t_mu)
                t_u = s.emit("vector", lambda v, half=half: v.tensor_copy(
                    ut[:, half * 512:(half + 1) * 512], p_w[:]))
                s.use("p_w", t_u)
                if half == 1:
                    s.use("Kt", t_mu)
            for half in range(2):
                s.wait("tensor", t_ke)
                s.wait("tensor", t_yr)
                s.guard("tensor", "p_r")
                s.guard("tensor", "p_a")
                t_mf = s.emit("tensor", lambda t, half=half: t.matmul(
                    p_a[:, 0:256], yr[:, half * 128:(half + 1) * 128], ket[:],
                    start=True, stop=True))
                s.wait("vector", t_mf)
                s.wait("vector", t_lc)
                t_ft = s.emit("vector", lambda v, half=half, h=h: v.scalar_tensor_tensor(
                    out=ftt[:, half * 256:(half + 1) * 256], in0=ihf[:, half * 256:(half + 1) * 256],
                    scalar=lamC[:, h:h + 1], in1=p_a[:, 0:256], op0=ALU.mult, op1=ALU.subtract))
                s.use("p_a", t_ft)
            for half in (range(2) if not skip_o else ()):
                s.wait("tensor", t_adt)
                s.wait("tensor", t_yr)
                s.guard("tensor", "p_a")
                t_mg = s.emit("tensor", lambda t, half=half: t.matmul(
                    p_a[:, 0:128], yr[:, half * 128:(half + 1) * 128], adt[:],
                    start=True, stop=True))
                s.wait("vector", t_mg)
                s.wait("vector", t_lrb)
                s.guard("vector", "tmpa")
                s.emit("vector", lambda v, half=half: v.tensor_tensor(
                    out=tmpa[:], in0=QTt[:, half * 128:(half + 1) * 128], in1=lrb[:], op=ALU.mult))
                t_gt = s.emit("vector", lambda v, half=half: v.tensor_tensor(
                    out=gtt[:, half * 128:(half + 1) * 128], in0=tmpa[:], in1=p_a[:, 0:128],
                    op=ALU.subtract))
                s.use("p_a", t_gt)
                if half == 1:
                    s.use("QTt", t_gt)
            if upto < 8:
                continue
            if not skip_o:
                s.wait("tensor", t_adt)
                s.wait("tensor", t_yw)
                s.guard("tensor", "p_o")
                s.emit("tensor", lambda t: t.matmul(p_o[:], adt[:], yw[:], start=True, stop=False))
                t_chn_prev = t_chain.get((c - 1, h))
                if t_chn_prev is not None:
                    s.wait("tensor", t_chn_prev)
                s.wait("tensor", t_gt)
                t_mo = None
                for half in range(2):
                    t_mo = s.emit("tensor", lambda t, half=half, h=h, sold=sold: t.matmul(
                        p_o[:], gtt[:, half * 128:(half + 1) * 128],
                        sold[:, (h * 2 + half) * 512:(h * 2 + half + 1) * 512],
                        start=False, stop=(half == 1)))
                s.wait("vector", t_mo)
                s.guard("vector", "oev")
                t_oev = s.emit("vector", lambda v: v.tensor_copy(oev[:], p_o[:]))
                s.use("p_o", t_oev)
                s.wait("gpsimd", t_oev)
                t_ost = s.dma("gpsimd", O_d[c * 128:(c + 1) * 128, h * 512:(h + 1) * 512], oev[:],
                              ch="ost")
                s.use("oev", t_ost)
                if upto < 9:
                    t_chain[(c, h)] = t_chain.get((c - 1, h))
                    continue
            if skip_o:
                t_chn_prev = t_chain.get((c - 1, h))
                if t_chn_prev is not None:
                    s.wait("tensor", t_chn_prev)
            import os as _os
            _cv = int(_os.environ.get("CHAINVAR", "0"))
            for ihalf_ in range(2):
                pS = p_s[ihalf_]
                s.guard("tensor", ("p_s", ihalf_))
                s.wait("tensor", t_ft)
                t_mc = None
                for khalf in range(2):
                    t_mc = s.emit("tensor", lambda t, ihalf_=ihalf_, khalf=khalf, h=h, pS=pS,
                                  sold=sold: t.matmul(
                        pS[:], ftt[:, khalf * 256 + ihalf_ * 128: khalf * 256 + (ihalf_ + 1) * 128],
                        sold[:, (h * 2 + khalf) * 512:(h * 2 + khalf + 1) * 512],
                        start=(khalf == 0), stop=(_cv == 1)))
                if _cv != 1:
                    s.wait("tensor", t_u)
                    t_mc = s.emit("tensor", lambda t, ihalf_=ihalf_, pS=pS: t.matmul(
                        pS[:], idb[:], ut[:, ihalf_ * 512:(ihalf_ + 1) * 512], start=False, stop=True))
                s.wait("vector", t_mc)
                t_sn = s.emit("vector", lambda v, ihalf_=ihalf_, h=h, pS=pS, snew=snew: v.tensor_copy(
                    snew[:, (h * 2 + ihalf_) * 512:(h * 2 + ihalf_ + 1) * 512], pS[:]))
                s.use(("p_s", ihalf_), t_sn)
                t_chain[(c, h)] = t_sn

    import os as _os2
    for h in range(nh if _os2.environ.get("NOSF") != "1" else 0):
        for half in range(2):
            if t_chain.get((NCH - 1, h)) is None:
                continue
            s.wait("vector", t_chain[(NCH - 1, h)])
            s.guard("vector", "sfo")
            t_sf = s.emit("vector", lambda v, h=h, half=half, Sb=Sb, NCH=NCH: v.tensor_copy(
                sfo[:], Sb[(NCH - 1) % 2][:, (h * 2 + half) * 512:(h * 2 + half + 1) * 512]))
            s.wait("gpsimd", t_sf)
            t_sfd = s.dma("gpsimd", Sfin[h * 256 + half * 128: h * 256 + (half + 1) * 128, :], sfo[:],
                  ch="sfst")
            s.use("sfo", t_sfd)
    s.wait_self("gpsimd", s.chan_total("ost"))
    s.wait_self("gpsimd", s.chan_total("sfst"))
    return nc, s



def build_stage_e(debug=False):
    nc = bass.Bass("TRN2", target_bir_lowering=False)
    s = Sched()
    Od = nc.dram_tensor("Od", [2 * TOK, VDIM], BF16, kind="ExternalInput").ap()
    gg_d = nc.dram_tensor("gg", [TOK, VDIM], BF16, kind="ExternalInput").ap()
    x2r = nc.dram_tensor("x2rows", [TOK, DM], F32, kind="ExternalInput").ap()
    wo = nc.dram_tensor("wo", [VDIM, DM], BF16, kind="ExternalInput").ap()
    onwb = nc.dram_tensor("onwb", [128, VDIM], BF16, kind="ExternalInput").ap()
    gTw = nc.dram_tensor("gT", [DM, INTER], BF16, kind="ExternalInput").ap()
    uTw = nc.dram_tensor("uT", [DM, INTER], BF16, kind="ExternalInput").ap()
    dTw = nc.dram_tensor("dT", [INTER, DM], BF16, kind="ExternalInput").ap()
    gsc = nc.dram_tensor("gsc", [128, 64], F32, kind="ExternalInput").ap()
    usc = nc.dram_tensor("usc", [128, 64], F32, kind="ExternalInput").ap()
    constse = nc.dram_tensor("constse", [128, 6], F32, kind="ExternalInput").ap()
    xout = nc.dram_tensor("xout", [TOK, DM], F32, kind="ExternalOutput").ap()
    gat_d = nc.dram_tensor("gat_sc", [TOK, VDIM], BF16, kind="Internal").ap()
    if debug:
        dbg_x3 = nc.dram_tensor("dbg_x3", [512, DM], F32, kind="ExternalOutput").ap()
        dbg_z = nc.dram_tensor("dbg_z", [INTER, 512], BF16, kind="ExternalOutput").ap()
        dbg_gat = nc.dram_tensor("dbg_gat", [TOK, VDIM], BF16, kind="ExternalOutput").ap()
    xln_d = nc.dram_tensor("xln3_sc", [TOK, DM], BF16, kind="Internal").ap()
    z_d = nc.dram_tensor("z_sc", [INTER, TOK], BF16, kind="Internal").ap()
    Ode = Od.rearrange("(t two) f -> two t f", two=2)

    sb = lambda n, shp, dt: nc.alloc_sbuf_tensor(n, shp, dt).ap()
    ps = lambda n, shp, dt: nc.alloc_psum_tensor(n, shp, dt).ap()

    ot = sb("ot", [128, VDIM], BF16)
    ggt = sb("ggt", [128, VDIM], BF16)
    sgg = sb("sgg", [128, VDIM], BF16)
    onw_sb = sb("onw_sb", [128, VDIM], BF16)
    sqj = sb("sqj", [128, 512], BF16)
    nsq = sb("nsq", [128, 8], F32)
    cc = sb("cc", [128, 6], F32)
    gatb = sb("gatb", [128, VDIM], BF16)
    x2t = sb("x2t", [128, 2 * 512], F32)
    x3t = sb("x3t", [128, 4 * DM], F32)
    gatT = sb("gatT", [128, 24 * 512], BF16)
    wo_sb = sb("wo_sb", [128, 24 * 512], BF16)
    xlnbf = sb("xlnbf", [128, DM], BF16)
    sq_junk = sb("sq_junk", [128, DM], BF16)
    stats = sb("stats", [128, 8], F32)
    xlnT = sb("xlnT", [128, 16 * 512], BF16)
    zsb = sb("zsb", [128, 2 * 512], BF16)
    zld = sb("zld", [128, 2 * 512], BF16)
    wg_sb = sb("wg_sb", [128, 2 * 2048], BF16)
    wu_sb = sb("wu_sb", [128, 2 * 2048], BF16)
    wd_sb = sb("wd_sb", [128, 16 * 512], BF16)
    up_sb = sb("up_sb", [128, 2 * 512], F32)
    gs_sb = sb("gs_sb", [128, 2 * 512], F32)
    gsc_sb = sb("gsc_sb", [128, 64], F32)
    usc_sb = sb("usc_sb", [128, 64], F32)
    x4s = sb("x4s", [128, 2 * 512], F32)

    p_wo = [ps("p_wo0", [128, 512], F32), ps("p_wo1", [128, 512], F32)]
    p_g = [ps("p_gA", [128, 512], F32), ps("p_gB", [128, 512], F32)]
    p_u = [ps("p_uA", [128, 512], F32), ps("p_uB", [128, 512], F32)]
    p_dn = [p_g[0], p_g[1], p_u[0], p_u[1]]
    dn_keys = [("p_g", 0), ("p_g", 1), ("p_u", 0), ("p_u", 1)]

    s.dma("sync", onw_sb[:], onwb[:], ch="consts")
    s.dma("sync", cc[:], constse[:], ch="consts")
    s.dma("sync", gsc_sb[:], gsc[:], ch="consts")
    s.dma("sync", usc_sb[:], usc[:], ch="consts")
    t_consts = s.chan_total("consts")

    NCH = TOK // 512
    for c in range(NCH):
        for tt in range(4):
            t0 = c * 512 + tt * 128
            s.guard("sync", "ot")
            t_o = s.dma("sync", ot[:], Ode[1, t0:t0 + 128, :], ch="oldc")
            s.guard("sync", "ggt")
            t_g = s.dma("sync", ggt[:], gg_d[t0:t0 + 128, :], ch="ggld")
            s.wait("scalar", t_o)
            for h in range(6):
                s.emit("scalar", lambda sc, h=h: sc.activation(
                    out=sqj[:, 0:512], in_=ot[:, h * 512:(h + 1) * 512], func=AF.Square,
                    accum_out=nsq[:, h:h + 1]))
            s.wait("scalar", t_consts)
            s.emit("scalar", lambda sc: sc.activation(
                out=nsq[:, 0:6], in_=nsq[:, 0:6], func=AF.Ln, scale=cc[:, 2:3], bias=cc[:, 0:1]))
            t_rms = s.emit("scalar", lambda sc: sc.activation(
                out=nsq[:, 0:6], in_=nsq[:, 0:6], func=AF.Exp, scale=cc[:, 1:2]))
            s.wait("scalar", t_g)
            s.guard("scalar", "sgg")
            t_sg = s.emit("scalar", lambda sc: sc.activation(
                out=sgg[:], in_=ggt[:], func=AF.Silu))
            s.use("ggt", t_sg)
            s.wait("vector", t_sg)
            s.wait("vector", t_consts)
            t_m1 = s.emit("vector", lambda v: v.tensor_tensor(
                out=sgg[:], in0=sgg[:], in1=onw_sb[:], op=ALU.mult))
            s.use("sgg", t_m1)
            s.wait("vector", t_rms)
            s.guard("vector", "gatb")
            for h in range(6):
                s.emit("vector", lambda v, h=h: v.tensor_scalar(
                    out=gatb[:, h * 512:(h + 1) * 512], in0=ot[:, h * 512:(h + 1) * 512],
                    scalar1=nsq[:, h:h + 1], scalar2=None, op0=ALU.mult))
            t_gb = s.emit("vector", lambda v: v.tensor_tensor(
                out=gatb[:], in0=gatb[:], in1=sgg[:], op=ALU.mult))
            s.use("ot", t_gb)
            s.wait("gpsimd", t_gb)
            t_gst = s.dma("gpsimd", gat_d[t0:t0 + 128, :], gatb[:], ch="gatst")
            s.use("gatb", t_gst)
    t_gat_all = s.chan_total("gatst")
    if debug:
        s.wait("gpsimd", t_gat_all)
        s.dma("gpsimd", dbg_gat[:], gat_d[:], ch="dbg")

    for c in range(NCH):
        for k in range(24):
            s.guard("scalar", ("gatT", k))
            s.wait("scalar", t_gat_all)
            s.dma("scalar", gatT[:, k * 512:(k + 1) * 512],
                  gat_d[c * 512:(c + 1) * 512, k * 128:(k + 1) * 128], transpose=True, ch="gatT")
        t_gT = s.chan_total("gatT")
        t_x3f = {}
        for oc in range(4):
            s.guard("sync", "wo_sb")
            t_wo = s.dma("sync",
                         wo_sb.rearrange("p (k f) -> p k f", k=24),
                         wo[:, oc * 512:(oc + 1) * 512].rearrange("(k p) f -> p k f", p=128),
                         ch="wold")
            t_wo_used = None
            for tt in range(4):
                ppar = tt % 2
                s.guard("sync", ("x2t", ppar))
                t_x2 = s.dma("sync", x2t[:, ppar * 512:(ppar + 1) * 512],
                             x2r[c * 512 + tt * 128:c * 512 + (tt + 1) * 128,
                                 oc * 512:(oc + 1) * 512], ch=("x2l", ppar))
                s.wait("tensor", t_wo)
                s.wait("tensor", t_gT)
                s.guard("tensor", ("p_wo", ppar))
                t_mm = None
                for k in range(24):
                    t_mm = s.emit("tensor", lambda t, k=k, tt=tt, ppar=ppar: t.matmul(
                        p_wo[ppar][:], gatT[:, k * 512 + tt * 128:k * 512 + (tt + 1) * 128],
                        wo_sb[:, k * 512:(k + 1) * 512], start=(k == 0), stop=(k == 23)))
                t_wo_used = t_mm
                if oc == 3 and tt == 3:
                    for k in range(24):
                        s.use(("gatT", k), t_mm)
                s.wait("vector", t_mm)
                s.wait("vector", t_x2)
                s.guard("vector", ("x3s", tt))
                s.guard("vector", ("x3rd", tt))
                t_x3 = s.emit("vector", lambda v, tt=tt, oc=oc, ppar=ppar: v.tensor_tensor(
                    out=x3t[:, tt * DM + oc * 512:tt * DM + (oc + 1) * 512],
                    in0=p_wo[ppar][:], in1=x2t[:, ppar * 512:(ppar + 1) * 512], op=ALU.add))
                s.use(("p_wo", ppar), t_x3)
                s.use(("x2t", ppar), t_x3)
                if oc == 3:
                    t_x3f[tt] = t_x3
            s.use("wo_sb", t_wo_used)
        if debug and c == 0:
            for tt in range(4):
                s.wait("gpsimd", t_x3f[tt])
                s.dma("gpsimd", dbg_x3[tt * 128:(tt + 1) * 128, :], x3t[:, tt * DM:(tt + 1) * DM],
                      ch="dbg")
            t_dt_ = s.chan_total("dbg")
            for e in ("sync", "scalar", "vector", "tensor"):
                s.wait(e, t_dt_)
        t_lnst = {}
        for tt in range(4):
            s.emit("vector", lambda v, tt=tt: v.reduce_sum(
                stats[:, 0:1], x3t[:, tt * DM:(tt + 1) * DM], axis=AX.X))
            s.wait("scalar", t_x3f[tt])
            t_sq = s.emit("scalar", lambda sc, tt=tt: sc.activation(
                out=sq_junk[:], in_=x3t[:, tt * DM:(tt + 1) * DM], func=AF.Square,
                accum_out=stats[:, 1:2]))
            s.use(("x3s", tt), t_sq)
            s.wait("vector", t_sq)
            s.emit("vector", lambda v: v.tensor_scalar(
                out=stats[:, 2:3], in0=stats[:, 0:1], scalar1=1.0 / DM, scalar2=None, op0=ALU.mult))
            s.emit("vector", lambda v: v.tensor_scalar(
                out=stats[:, 5:6], in0=stats[:, 1:2], scalar1=1.0 / DM, scalar2=None, op0=ALU.mult))
            s.emit("vector", lambda v: v.tensor_tensor(
                out=stats[:, 3:4], in0=stats[:, 2:3], in1=stats[:, 2:3], op=ALU.mult))
            t_var = s.emit("vector", lambda v: v.tensor_tensor(
                out=stats[:, 3:4], in0=stats[:, 5:6], in1=stats[:, 3:4], op=ALU.subtract))
            s.wait("scalar", t_var)
            s.emit("scalar", lambda sc: sc.activation(
                out=stats[:, 4:5], in_=stats[:, 3:4], func=AF.Ln, bias=cc[:, 0:1]))
            t_rstd = s.emit("scalar", lambda sc: sc.activation(
                out=stats[:, 4:5], in_=stats[:, 4:5], func=AF.Exp, scale=cc[:, 1:2]))
            s.wait("vector", t_rstd)
            s.guard("vector", "xlnbf")
            t_ln = s.emit("vector", lambda v, tt=tt: v.tensor_scalar(
                out=xlnbf[:], in0=x3t[:, tt * DM:(tt + 1) * DM], scalar1=stats[:, 2:3],
                scalar2=stats[:, 4:5], op0=ALU.subtract, op1=ALU.mult))
            s.wait("gpsimd", t_ln)
            t_xst = s.dma("gpsimd", xln_d[c * 512 + tt * 128:c * 512 + (tt + 1) * 128, :],
                          xlnbf[:], ch="xlnst")
            s.use("xlnbf", t_xst)
            t_lnst[tt] = t_xst
        t_allst = s.chan_total("xlnst")
        for k in range(NKT):
            s.guard("scalar", ("xlnT", k))
            s.wait("scalar", t_allst)
            s.dma("scalar", xlnT[:, k * 512:(k + 1) * 512],
                  xln_d[c * 512:(c + 1) * 512, k * 128:(k + 1) * 128], transpose=True, ch="xlnT")
        t_xt_all = s.chan_total("xlnT")

        for i in range(64):
            par = i % 2
            s.guard("sync", ("wg", par))
            t_wg = s.dma("sync",
                         wg_sb[:, par * 2048:(par + 1) * 2048].rearrange("p (k f) -> p k f", k=16),
                         gTw[:, i * 128:(i + 1) * 128].rearrange("(k p) f -> p k f", p=128),
                         ch=("wg", par))
            s.guard("sync", ("wu", par))
            t_wu = s.dma("sync",
                         wu_sb[:, par * 2048:(par + 1) * 2048].rearrange("p (k f) -> p k f", k=16),
                         uTw[:, i * 128:(i + 1) * 128].rearrange("(k p) f -> p k f", p=128),
                         ch=("wu", par))
            s.wait("tensor", t_wg)
            s.wait("tensor", t_wu)
            s.wait("tensor", t_xt_all)
            s.guard("tensor", ("p_g", par))
            s.guard("tensor", ("p_u", par))
            for k in range(NKT):
                s.emit("tensor", lambda t, i=i, k=k, par=par: t.matmul(
                    p_g[par][:], wg_sb[:, par * 2048 + k * 128:par * 2048 + (k + 1) * 128],
                    xlnT[:, k * 512:(k + 1) * 512], start=(k == 0), stop=(k == NKT - 1)))
            t_mm2 = None
            for k in range(NKT):
                t_mm2 = s.emit("tensor", lambda t, i=i, k=k, par=par: t.matmul(
                    p_u[par][:], wu_sb[:, par * 2048 + k * 128:par * 2048 + (k + 1) * 128],
                    xlnT[:, k * 512:(k + 1) * 512], start=(k == 0), stop=(k == NKT - 1)))
            s.use(("wg", par), t_mm2)
            s.use(("wu", par), t_mm2)
            if i >= 62:
                for k in range(NKT):
                    s.use(("xlnT", k), t_mm2)
            s.wait("scalar", t_mm2)
            s.guard("scalar", ("gs", par))
            t_gs = s.emit("scalar", lambda sc, i=i, par=par: sc.activation(
                out=gs_sb[:, par * 512:(par + 1) * 512], in_=p_g[par][:],
                func=AF.Silu, scale=gsc_sb[:, i:i + 1]))
            s.use(("p_g", par), t_gs)
            s.wait("vector", t_mm2)
            s.guard("vector", ("up", par))
            t_up = s.emit("vector", lambda v, i=i, par=par: v.tensor_scalar(
                out=up_sb[:, par * 512:(par + 1) * 512], in0=p_u[par][:],
                scalar1=usc_sb[:, i:i + 1], scalar2=None, op0=ALU.mult))
            s.use(("p_u", par), t_up)
            s.wait("vector", t_gs)
            s.guard("vector", ("zsb", par))
            t_z = s.emit("vector", lambda v, par=par: v.tensor_tensor(
                out=zsb[:, par * 512:(par + 1) * 512], in0=up_sb[:, par * 512:(par + 1) * 512],
                in1=gs_sb[:, par * 512:(par + 1) * 512], op=ALU.mult))
            s.use(("gs", par), t_z)
            s.use(("up", par), t_z)
            s.wait("gpsimd", t_z)
            t_zst = s.dma("gpsimd", z_d[i * 128:(i + 1) * 128, c * 512:(c + 1) * 512],
                          zsb[:, par * 512:(par + 1) * 512], ch="zst")
            s.use(("zsb", par), t_zst)
        t_z_all = s.chan_total("zst")
        if debug and c == 0:
            s.wait("gpsimd", t_z_all)
            s.dma("gpsimd", dbg_z[:, :], z_d[:, 0:512], ch="dbg")
            t_dt2 = s.chan_total("dbg")
            for e in ("sync", "scalar", "vector", "tensor"):
                s.wait(e, t_dt2)

        for oc in range(4):
            s.guard("sync", "wd_sb")
            t_wd = s.dma("sync",
                         wd_sb.rearrange("p (k f) -> p k f", k=16),
                         dTw[0:2048, oc * 512:(oc + 1) * 512].rearrange("(k p) f -> p k f", p=128),
                         ch="wdld")
            t_dn = {}
            for ig in range(4):
                if ig > 0:
                    s.guard("sync", "wd_sb")
                    t_wd = s.dma("sync",
                                 wd_sb.rearrange("p (k f) -> p k f", k=16),
                                 dTw[ig * 2048:(ig + 1) * 2048, oc * 512:(oc + 1) * 512]
                                 .rearrange("(k p) f -> p k f", p=128), ch="wdld")
                for ii in range(16):
                    i = ig * 16 + ii
                    zpar = ii % 2
                    s.guard("sync", ("zld", zpar))
                    s.wait("sync", t_z_all)
                    t_zl = s.dma("sync", zld[:, zpar * 512:(zpar + 1) * 512],
                                 z_d[i * 128:(i + 1) * 128, c * 512:(c + 1) * 512],
                                 ch=("zld", zpar))
                    s.wait("tensor", t_zl)
                    s.wait("tensor", t_wd)
                    t_last_z = None
                    for tt in range(4):
                        if ig == 0 and ii == 0:
                            s.guard("tensor", dn_keys[tt])
                        t_dn[tt] = s.emit("tensor", lambda t, tt=tt, ii=ii, zpar=zpar, ig=ig: t.matmul(
                            p_dn[tt][:], zld[:, zpar * 512 + tt * 128:zpar * 512 + (tt + 1) * 128],
                            wd_sb[:, ii * 512:(ii + 1) * 512],
                            start=(ig == 0 and ii == 0), stop=(ig == 3 and ii == 15)))
                    s.use(("zld", zpar), t_dn[3])
                s.use("wd_sb", t_dn[3])
            for tt in range(4):
                ppar = tt % 2
                s.wait("vector", t_dn[tt])
                s.guard("vector", ("x4s", ppar))
                t_x4 = s.emit("vector", lambda v, tt=tt, oc=oc, ppar=ppar: v.tensor_tensor(
                    out=x4s[:, ppar * 512:(ppar + 1) * 512], in0=p_dn[tt][:],
                    in1=x3t[:, tt * DM + oc * 512:tt * DM + (oc + 1) * 512], op=ALU.add))
                s.use(dn_keys[tt], t_x4)
                if oc == 3:
                    s.use(("x3rd", tt), t_x4)
                s.wait("gpsimd", t_x4)
                t_xo = s.dma("gpsimd",
                             xout[c * 512 + tt * 128:c * 512 + (tt + 1) * 128,
                                  oc * 512:(oc + 1) * 512],
                             x4s[:, ppar * 512:(ppar + 1) * 512], ch="xost")
                s.use(("x4s", ppar), t_xo)

    s.wait_self("gpsimd", s.chan_total("xost"))
    return nc, s



def build_prep():
    nc = bass.Bass("TRN2", target_bir_lowering=False)
    s = Sched()
    wup_r = nc.dram_tensor("wup_r", [1024, DM], F32, kind="ExternalInput").ap()
    wgt_r = nc.dram_tensor("wgt_r", [1024, DM], F32, kind="ExternalInput").ap()
    wdn_r = nc.dram_tensor("wdn_r", [256, INTER], F32, kind="ExternalInput").ap()
    gq_r = nc.dram_tensor("gq_r", [1024, DM], I32, kind="ExternalInput").ap()
    uq_r = nc.dram_tensor("uq_r", [1024, DM], I32, kind="ExternalInput").ap()
    dq_r = nc.dram_tensor("dq_r", [256, INTER], I32, kind="ExternalInput").ap()
    suug = nc.dram_tensor("suug", [128, 16], F32, kind="ExternalInput").ap()
    sudp = nc.dram_tensor("sudp", [128, 64], F32, kind="ExternalInput").ap()
    wsdp_c = nc.dram_tensor("wsdp_c", [128, 2], F32, kind="ExternalInput").ap()
    dsc = nc.dram_tensor("dsc", [128, 2], F32, kind="ExternalInput").ap()
    h128p = nc.dram_tensor("h128p", [128, 128], BF16, kind="ExternalInput").ap()
    h128n = nc.dram_tensor("h128n", [128, 128], BF16, kind="ExternalInput").ap()
    aw = {}
    AT_SHAPES = {"wq": (DM, KDIM), "wk0": (DM, KDIM), "wk1": (DM, KDIM),
                 "wv0": (DM, VDIM), "wv1": (DM, VDIM), "wg": (DM, VDIM),
                 "wo": (VDIM, DM), "wb01": (DM, 12), "wa": (DM, 6)}
    for nm, (r, cdim) in AT_SHAPES.items():
        aw[nm] = (nc.dram_tensor(nm + "_r", [r // 8, cdim], F32, kind="ExternalInput").ap(),
                  nc.dram_tensor(nm + "_o", [r // 8, cdim], BF16, kind="ExternalOutput").ap())
    w2up_o = nc.dram_tensor("w2up_o", [DM, 1024], BF16, kind="ExternalOutput").ap()
    w2gt_o = nc.dram_tensor("w2gt_o", [DM, 1024], BF16, kind="ExternalOutput").ap()
    w2dn_o = nc.dram_tensor("w2dn_o", [INTER, 256], BF16, kind="ExternalOutput").ap()
    gT_o = nc.dram_tensor("gT_o", [DM, 1024], BF16, kind="ExternalOutput").ap()
    uT_o = nc.dram_tensor("uT_o", [DM, 1024], BF16, kind="ExternalOutput").ap()
    dT_o = nc.dram_tensor("dT_o", [INTER, 256], BF16, kind="ExternalOutput").ap()
    upb_d = nc.dram_tensor("upb_d", [1024, DM], BF16, kind="Internal").ap()
    gtb_d = nc.dram_tensor("gtb_d", [1024, DM], BF16, kind="Internal").ap()
    dnb_d = nc.dram_tensor("dnb_d", [256, INTER], BF16, kind="Internal").ap()
    gqb_d = nc.dram_tensor("gqb_d", [1024, DM], BF16, kind="Internal").ap()
    uqb_d = nc.dram_tensor("uqb_d", [1024, DM], BF16, kind="Internal").ap()
    dqb_d = nc.dram_tensor("dqb_d", [256, INTER], BF16, kind="Internal").ap()

    sb = lambda n, shp, dt: nc.alloc_sbuf_tensor(n, shp, dt).ap()
    ps = lambda n, shp, dt: nc.alloc_psum_tensor(n, shp, dt).ap()
    hp = sb("hp", [128, 128], BF16)
    hn = sb("hn", [128, 128], BF16)
    su16 = sb("su16", [128, 16], F32)
    su64 = sb("su64", [128, 64], F32)
    wsd = sb("wsd", [128, 2], F32)
    dss = sb("dss", [128, 2], F32)
    ld32 = sb("ld32", [128, INTER], F32)
    ldi = sb("ldi", [128, INTER], BF16)
    wt = sb("wt", [128, 2 * 1024], BF16)
    wtd = sb("wtd", [128, 2 * 256], BF16)
    evb = sb("evb", [128, 2 * 512], BF16)
    p_t = [ps("p_t0", [128, 512], F32), ps("p_t1", [128, 512], F32)]

    s.dma("sync", hp[:], h128p[:], ch="consts")
    s.dma("sync", hn[:], h128n[:], ch="consts")
    s.dma("sync", su16[:], suug[:], ch="consts")
    s.dma("sync", su64[:], sudp[:], ch="consts")
    s.dma("sync", wsd[:], wsdp_c[:], ch="consts")
    s.dma("sync", dss[:], dsc[:], ch="consts")
    t_consts = s.chan_total("consts")
    H16 = hadamard(16)
    H64 = hadamard(64)

    for nm, (src, dst) in aw.items():
        s.dma("gpsimd", dst[:], src[:], ch="cast")

    s.dma("gpsimd", upb_d[:], wup_r[:], ch="cast")
    s.dma("gpsimd", gtb_d[:], wgt_r[:], ch="cast")
    for t2 in range(2):
        t_l = s.dma("sync", ld32[:], wdn_r[t2 * 128:(t2 + 1) * 128, :], ch="ld32")
        s.wait("vector", t_l)
        s.guard("vector", ("ldi", 0))
        t_cv = s.emit("vector", lambda v, t2=t2: v.tensor_scalar(
            out=ldi[:], in0=ld32[:], scalar1=wsd[:, t2:t2 + 1], scalar2=None, op0=ALU.mult))
        s.wait("vector", t_consts)
        s.wait("sync", t_cv)
        s.wait("gpsimd", t_cv)
        t_st = s.dma("gpsimd", dnb_d[t2 * 128:(t2 + 1) * 128, :], ldi[:], ch="dnst")
        s.use(("ldi", 0), t_st)
    t_ic = {}
    for nm, src, dstd in (("g", gq_r, gqb_d), ("u", uq_r, uqb_d)):
        t_ic[nm] = s.dma("gpsimd", dstd[:], src[:], ch="icast")
    t_icd = s.dma("gpsimd", dqb_d[:], dq_r[:], ch="icast")
    t_ic_all = s.chan_total("icast")
    for nm, dstd, outd in (("g", gqb_d, gT_o), ("u", uqb_d, uT_o)):
        for t2 in range(8):
            s.guard("sync", "ldi2")
            s.wait("sync", t_ic_all)
            t_l = s.dma("sync", ldi[:, 0:DM], dstd[t2 * 128:(t2 + 1) * 128, :], ch="ldi2c")
            s.wait("vector", t_l)
            t_sh = s.emit("vector", lambda v: v.tensor_scalar(
                out=ldi[:, 0:DM], in0=ldi[:, 0:DM], scalar1=128.0, scalar2=None, op0=ALU.subtract))
            s.use("ldi2", t_sh)
            s.wait("gpsimd", t_sh)
            t_st = s.dma("gpsimd", dstd[t2 * 128:(t2 + 1) * 128, :], ldi[:, 0:DM], ch="shst")
            s.use("ldi2", t_st)
    for t2 in range(2):
        s.guard("sync", "ldi2")
        s.wait("sync", t_ic_all)
        t_l = s.dma("sync", ldi[:], dqb_d[t2 * 128:(t2 + 1) * 128, :], ch="ldi2c")
        s.wait("vector", t_l)
        t_sh = s.emit("vector", lambda v, t2=t2: v.tensor_scalar(
            out=ldi[:], in0=ldi[:], scalar1=128.0, scalar2=dss[:, t2:t2 + 1],
            op0=ALU.subtract, op1=ALU.mult))
        s.use("ldi2", t_sh)
        s.wait("gpsimd", t_sh)
        t_st = s.dma("gpsimd", dqb_d[t2 * 128:(t2 + 1) * 128, :], ldi[:], ch="shst")
        s.use("ldi2", t_st)
    t_sh_all = s.chan_total("shst")
    t_cast_all = s.chan_total("cast")
    t_dn_all = s.chan_total("dnst")

    for dstd, outd in ((gqb_d, gT_o), (uqb_d, uT_o)):
        for hb in range(16):
            s.guard("scalar", ("wt", hb % 2))
            s.wait("scalar", t_sh_all)
            t_t = s.dma("scalar", wt[:, (hb % 2) * 1024:(hb % 2 + 1) * 1024],
                        dstd[:, hb * 128:(hb + 1) * 128], transpose=True, ch=("wt", hb % 2))
            s.wait("gpsimd", t_t)
            t_st = s.dma("gpsimd", outd[hb * 128:(hb + 1) * 128, :],
                         wt[:, (hb % 2) * 1024:(hb % 2 + 1) * 1024], ch="outst")
            s.use(("wt", hb % 2), t_st)
    for ib in range(64):
        s.guard("scalar", ("wtd", ib % 2))
        s.wait("scalar", t_sh_all)
        t_t = s.dma("scalar", wtd[:, (ib % 2) * 256:(ib % 2 + 1) * 256],
                    dqb_d[:, ib * 128:(ib + 1) * 128], transpose=True, ch=("wtd", ib % 2))
        s.wait("gpsimd", t_t)
        t_st = s.dma("gpsimd", dT_o[ib * 128:(ib + 1) * 128, :],
                     wtd[:, (ib % 2) * 256:(ib % 2 + 1) * 256], ch="outst")
        s.use(("wtd", ib % 2), t_st)

    wta = sb("wta", [128, 16 * 512], BF16)
    wtad = sb("wtad", [128, 64 * 256], BF16)
    for srcd, outd, su in ((upb_d, w2up_o, su16), (gtb_d, w2gt_o, su16)):
        for ch_ in range(2):
            for a_ in range(16):
                s.guard("scalar", ("wta", a_))
                s.wait("scalar", t_cast_all)
                s.dma("scalar", wta[:, a_ * 512:(a_ + 1) * 512],
                      srcd[ch_ * 512:(ch_ + 1) * 512, a_ * 128:(a_ + 1) * 128],
                      transpose=True, ch="wta")
            t_wta = s.chan_total("wta")
            for ap_ in range(16):
                ppar = ap_ % 2
                s.guard("tensor", ("p_t", ppar))
                s.wait("tensor", t_wta)
                s.wait("tensor", t_consts)
                t_mm = None
                for a_ in range(16):
                    sgn = H16[ap_, a_]
                    t_mm = s.emit("tensor", lambda t, ppar=ppar, sgn=sgn, a_=a_: t.matmul(
                        p_t[ppar][:], hp[:] if sgn > 0 else hn[:],
                        wta[:, a_ * 512:(a_ + 1) * 512],
                        start=(a_ == 0), stop=(a_ == 15)))
                if ap_ == 15:
                    for a_ in range(16):
                        s.use(("wta", a_), t_mm)
                s.wait("vector", t_mm)
                s.guard("vector", ("evb", ppar))
                t_ev = s.emit("vector", lambda v, ap_=ap_, ppar=ppar, su=su: v.tensor_scalar(
                    out=evb[:, ppar * 512:(ppar + 1) * 512], in0=p_t[ppar][:],
                    scalar1=su[:, ap_:ap_ + 1], scalar2=None, op0=ALU.mult))
                s.use(("p_t", ppar), t_ev)
                s.wait("gpsimd", t_ev)
                t_st = s.dma("gpsimd", outd[ap_ * 128:(ap_ + 1) * 128, ch_ * 512:(ch_ + 1) * 512],
                             evb[:, ppar * 512:(ppar + 1) * 512], ch="outst")
                s.use(("evb", ppar), t_st)
    for a_ in range(64):
        s.wait("scalar", t_dn_all)
        s.dma("scalar", wtad[:, a_ * 256:(a_ + 1) * 256],
              dnb_d[:, a_ * 128:(a_ + 1) * 128], transpose=True, ch="wtad")
    t_wtad = s.chan_total("wtad")
    for ap_ in range(64):
        ppar = ap_ % 2
        s.guard("tensor", ("p_t", ppar))
        s.wait("tensor", t_wtad)
        t_mm = None
        for a_ in range(64):
            sgn = H64[ap_, a_]
            t_mm = s.emit("tensor", lambda t, ppar=ppar, sgn=sgn, a_=a_: t.matmul(
                p_t[ppar][:, 0:256], hp[:] if sgn > 0 else hn[:],
                wtad[:, a_ * 256:(a_ + 1) * 256],
                start=(a_ == 0), stop=(a_ == 63)))
        s.wait("vector", t_mm)
        s.guard("vector", ("evb", ppar))
        t_ev = s.emit("vector", lambda v, ap_=ap_, ppar=ppar: v.tensor_scalar(
            out=evb[:, ppar * 512:ppar * 512 + 256], in0=p_t[ppar][:, 0:256],
            scalar1=su64[:, ap_:ap_ + 1], scalar2=None, op0=ALU.mult))
        s.use(("p_t", ppar), t_ev)
        s.wait("gpsimd", t_ev)
        t_st = s.dma("gpsimd", w2dn_o[ap_ * 128:(ap_ + 1) * 128, :],
                     evb[:, ppar * 512:ppar * 512 + 256], ch="outst")
        s.use(("evb", ppar), t_st)

    s.wait_self("gpsimd", s.chan_total("outst"))
    s.wait_self("gpsimd", s.chan_total("cast"))
    return nc, s


def _stage_a_inputs(inputs):
    x = np.ascontiguousarray(np.asarray(inputs["L0_attn_x"], np.float32))
    W_delta = np.asarray(inputs["L0_attn_W_delta"], np.float32)
    W_dt = np.asarray(inputs["L0_attn_W_dt"], np.float32)
    b_dt = np.asarray(inputs["L0_attn_b_dt"], np.float32)
    W_B = np.asarray(inputs["L0_attn_W_B"], np.float32)
    W_C = np.asarray(inputs["L0_attn_W_C"], np.float32)
    A = -np.exp(np.asarray(inputs["L0_attn_A_log"], np.float32))
    Dp = np.asarray(inputs["L0_attn_D"], np.float32)

    selbc = np.zeros((32, 32 * 128), np.float32)
    for n in range(16):
        selbc[n, n * 128:(n + 1) * 128] = 1.0
        selbc[16 + n, (16 + n) * 128:(17 + n) * 128] = 1.0
    wbc = np.concatenate([W_B, W_C], axis=1).astype(np.float32)

    maps = []
    for c in range(NCORES):
        b, j = divmod(c, 2)
        perm = np.concatenate([np.arange(j * 1024, (j + 1) * 1024),
                               np.arange((1 - j) * 1024, (2 - j) * 1024)])
        own = perm[:1024]
        constsa = np.zeros((128, 4), np.float32)
        constsa[:, 0] = EPS_LN
        constsa[:, 1] = 1.0
        constsa[:, 2] = -0.5
        constsa[:, 3] = -1.0
        maps.append({
            "constsa": constsa,
            "xb": np.ascontiguousarray(x[b][:, perm]),
            "wdelta": np.ascontiguousarray(W_delta[perm, :]),
            "wdt": np.ascontiguousarray(W_dt[:, own]),
            "bdt": np.ascontiguousarray(b_dt[own].reshape(8, 128).T),
            "wbc": np.ascontiguousarray(wbc[perm, :]),
            "acols": np.ascontiguousarray(
                A[own].reshape(8, 128, 16).transpose(1, 0, 2).reshape(128, 8 * 16)),
            "dcols": np.ascontiguousarray(Dp[own].reshape(8, 128).T),
            "selbc": selbc,
        })
    return maps


def _stage_b_maps(inputs, yhiF, yloF, w2upT, w2gateT, w2downT):
    x = np.asarray(inputs["L0_attn_x"], np.float32)
    wsug = np.asarray(inputs["L0_ffn_Wscale_ug"], np.float32)
    scu = (wsug[:INTER] / np.sqrt(DM)).reshape(64, 128).T.astype(np.float32)
    scg = (wsug[INTER:] / np.sqrt(DM)).reshape(64, 128).T.astype(np.float32)
    constsb = np.zeros((128, 2), np.float32)
    constsb[:, 0] = EPS_LN
    constsb[:, 1] = -0.5
    maps = []
    for c in range(NCORES):
        b, j = divmod(c, 2)
        sl = slice(j * TOK, (j + 1) * TOK)
        maps.append({
            "xrows": np.ascontiguousarray(x[b, sl, :]),
            "yhi": np.ascontiguousarray(yhiF[b][:, sl]),
            "ylo": np.ascontiguousarray(yloF[b][:, sl]),
            "w2up": w2upT, "w2gate": w2gateT, "w2down": w2downT,
            "scu": scu, "scg": scg, "constsb": constsb,
        })
    return maps


def run_stage_b(inputs, yhiF, yloF, w2upT, w2gateT, w2downT):
    nc, s = build_stage_b()
    s.realize(nc)
    maps = _stage_b_maps(inputs, yhiF, yloF, w2upT, w2gateT, w2downT)
    res = run_bass_kernel_spmd(nc, maps, core_ids=list(range(NCORES)))
    x2 = np.zeros((B, L, DM), np.float32)
    for c in range(NCORES):
        b, j = divmod(c, 2)
        x2[b][j * TOK:(j + 1) * TOK, :] = res.results[c]["x2o"]
    return x2


def _stage_c_maps(inputs, x2, wbf):
    conv_q = np.asarray(inputs["L1_attn_conv_q_w"], np.float32)
    conv_k = np.asarray(inputs["L1_attn_conv_k_w"], np.float32)
    conv_v = np.asarray(inputs["L1_attn_conv_v_w"], np.float32)
    cw = np.zeros((128, 84 * 4), np.float32)
    fi = 0
    for src, n in ((conv_q, 12), (conv_k[0], 12), (conv_k[1], 12),
                   (conv_v[0], 24), (conv_v[1], 24)):
        for i in range(n):
            cw[:, fi * 4:(fi + 1) * 4] = src[i * 128:(i + 1) * 128, :]
            fi += 1
    alogdt = np.stack([np.asarray(inputs["L1_attn_A_log"], np.float32),
                       np.asarray(inputs["L1_attn_dt_bias"], np.float32)], axis=1)
    constsc = np.zeros((128, 6), np.float32)
    constsc[:, 0] = EPS_LN
    constsc[:, 1] = -0.5
    constsc[:, 2] = 1.0
    constsc[:, 3] = -1.0
    constsc[:, 4] = 1e-6
    ident = np.eye(128, dtype=np.float32)
    maps = []
    for c in range(NCORES):
        b, j = divmod(c, 2)
        x2rows = np.zeros((TOK + 16, DM), np.float32)
        x2rows[16:] = x2[b, j * TOK:(j + 1) * TOK, :]
        if j == 1:
            x2rows[:16] = x2[b, TOK - 16:TOK, :]
        maps.append({
            "x2rows": x2rows,
            "wq": wbf["wq"], "wk0": wbf["wk0"], "wk1": wbf["wk1"],
            "wv0": wbf["wv0"], "wv1": wbf["wv1"], "wb01": wbf["wb01"],
            "wa": wbf["wa"], "wg": wbf["wg"],
            "cw": cw, "alogdt": alogdt, "constsc": constsc, "identc": ident,
        })
    return maps


def run_stage_c(inputs, x2, wbf):
    nc, s = build_stage_c()
    s.realize(nc)
    maps = _stage_c_maps(inputs, x2, wbf)
    res = run_bass_kernel_spmd(nc, maps, core_ids=list(range(NCORES)))
    return [res.results[c] for c in range(NCORES)]


def _stage_d_consts():
    r = np.arange(128)
    mup = (1e9 * (r[None, :] > r[:, None])).astype(np.float32)
    mlo = (1e9 * (r[None, :] < r[:, None])).astype(np.float32)
    identf = np.eye(128, dtype=np.float32)
    identb = np.eye(128, dtype=BF)
    ihalf = np.zeros((128, 512), np.float32)
    ihalf[r, r] = 1.0
    ihalf[r, 256 + 128 + r] = 1.0
    onesd = np.ones((6, 128), np.float32)
    return {"mup": mup, "mlo": mlo, "identf": identf, "identb": identb,
            "ihalf": ihalf, "onesd": onesd}


def run_stage_d(coutс_or_list, sinits, skip_o=False):
    nc, s = build_stage_d(skip_o=skip_o)
    s.realize(nc)
    consts = _stage_d_consts()
    maps = []
    for c in range(NCORES):
        r = coutс_or_list[c]
        maps.append({
            "Ks": np.asarray(r["Ks"]), "Qs": np.asarray(r["Qs"]),
            "Vs": np.asarray(r["Vs"]), "bgz": np.asarray(r["bgz"]),
            "Sinit": sinits[c], **consts,
        })
    res = run_bass_kernel_spmd(nc, maps, core_ids=list(range(NCORES)))
    return [res.results[c] for c in range(NCORES)]


def _stage_e_maps(inputs, x2, outsD, outsC, webf):
    onw = np.asarray(inputs["L1_attn_o_norm_w"], np.float32)
    onwb = np.broadcast_to(np.tile(onw, 6)[None, :], (128, VDIM)).astype(BF)
    gsc_ = np.asarray(inputs["L1_ffn_gate_scale"], np.float32).reshape(64, 128).T.copy()
    usc_ = np.asarray(inputs["L1_ffn_up_scale"], np.float32).reshape(64, 128).T.copy()
    constse = np.zeros((128, 6), np.float32)
    constse[:, 0] = EPS_LN
    constse[:, 1] = -0.5
    constse[:, 2] = 1.0 / DV
    maps = []
    for c in range(NCORES):
        b, j = divmod(c, 2)
        sl = slice(j * TOK, (j + 1) * TOK)
        maps.append({
            "Od": np.asarray(outsD[c]["O"]),
            "gg": np.asarray(outsC[c]["gg"]),
            "x2rows": np.ascontiguousarray(x2[b, sl, :]),
            "wo": webf["wo"], "gT": webf["gT"], "uT": webf["uT"], "dT": webf["dT"],
            "onwb": onwb, "gsc": gsc_, "usc": usc_, "constse": constse,
        })
    return maps


def run_stage_e(inputs, x2, outsD, outsC, webf):
    nc, s = build_stage_e()
    s.realize(nc)
    maps = _stage_e_maps(inputs, x2, outsD, outsC, webf)
    res = run_bass_kernel_spmd(nc, maps, core_ids=list(range(NCORES)))
    out = np.zeros((B, L, DM), np.float32)
    for c in range(NCORES):
        b, j = divmod(c, 2)
        out[b][j * TOK:(j + 1) * TOK, :] = res.results[c]["xout"]
    return out


def run_prep(inputs):
    nc, s = build_prep()
    s.realize(nc)
    H128 = hadamard(128)
    h128p = H128.astype(BF)
    h128n = (-H128).astype(BF)
    SU_ug = np.asarray(inputs["L0_ffn_SU_ug"], np.float32)
    SU_dp = np.asarray(inputs["L0_ffn_SU_dp"], np.float32)
    wsdp = np.asarray(inputs["L0_ffn_Wscale_dp"], np.float32)
    ds_ = np.asarray(inputs["L1_ffn_down_scale"], np.float32)
    W_up = np.asarray(inputs["L0_ffn_W_up"], np.float32)
    W_gate = np.asarray(inputs["L0_ffn_W_gate"], np.float32)
    W_down = np.asarray(inputs["L0_ffn_W_down"], np.float32)
    gq = np.asarray(inputs["L1_ffn_gate_w"], np.int32)
    uq = np.asarray(inputs["L1_ffn_up_w"], np.int32)
    dq = np.asarray(inputs["L1_ffn_down_w"], np.int32)
    at_full = {
        "wq": np.asarray(inputs["L1_attn_Wq"], np.float32),
        "wk0": np.asarray(inputs["L1_attn_Wk"], np.float32)[0],
        "wk1": np.asarray(inputs["L1_attn_Wk"], np.float32)[1],
        "wv0": np.asarray(inputs["L1_attn_Wv"], np.float32)[0],
        "wv1": np.asarray(inputs["L1_attn_Wv"], np.float32)[1],
        "wg": np.asarray(inputs["L1_attn_Wg"], np.float32),
        "wo": np.asarray(inputs["L1_attn_Wo"], np.float32),
        "wb01": np.concatenate([np.asarray(inputs["L1_attn_Wb"], np.float32)[0],
                                np.asarray(inputs["L1_attn_Wb"], np.float32)[1]], axis=1),
        "wa": np.asarray(inputs["L1_attn_Wa"], np.float32),
    }
    maps = []
    for c in range(NCORES):
        i1 = slice(c * 1024, (c + 1) * 1024)
        o1 = slice(c * 256, (c + 1) * 256)
        wsd = np.zeros((128, 2), np.float32)
        wsd[:, 0] = wsdp[c * 256:c * 256 + 128] / np.sqrt(INTER)
        wsd[:, 1] = wsdp[c * 256 + 128:(c + 1) * 256] / np.sqrt(INTER)
        dsc_ = np.zeros((128, 2), np.float32)
        dsc_[:, 0] = ds_[c * 256:c * 256 + 128]
        dsc_[:, 1] = ds_[c * 256 + 128:(c + 1) * 256]
        m = {
            "wup_r": np.ascontiguousarray(W_up[i1]),
            "wgt_r": np.ascontiguousarray(W_gate[i1]),
            "wdn_r": np.ascontiguousarray(W_down[o1]),
            "gq_r": np.ascontiguousarray(gq[i1]),
            "uq_r": np.ascontiguousarray(uq[i1]),
            "dq_r": np.ascontiguousarray(dq[o1]),
            "suug": np.ascontiguousarray(SU_ug.reshape(16, 128).T),
            "sudp": np.ascontiguousarray(SU_dp.reshape(64, 128).T),
            "wsdp_c": wsd, "dsc": dsc_,
            "h128p": h128p, "h128n": h128n,
        }
        for nm, arr in at_full.items():
            rsl = arr.shape[0] // 8
            m[nm + "_r"] = np.ascontiguousarray(arr[c * rsl:(c + 1) * rsl])
        maps.append(m)
    res = run_bass_kernel_spmd(nc, maps, core_ids=list(range(NCORES)))
    rs = res.results
    W = {}
    W["w2up"] = np.concatenate([np.asarray(rs[c]["w2up_o"]) for c in range(NCORES)], axis=1)
    W["w2gate"] = np.concatenate([np.asarray(rs[c]["w2gt_o"]) for c in range(NCORES)], axis=1)
    W["w2down"] = np.concatenate([np.asarray(rs[c]["w2dn_o"]) for c in range(NCORES)], axis=1)
    W["gT"] = np.concatenate([np.asarray(rs[c]["gT_o"]) for c in range(NCORES)], axis=1)
    W["uT"] = np.concatenate([np.asarray(rs[c]["uT_o"]) for c in range(NCORES)], axis=1)
    W["dT"] = np.concatenate([np.asarray(rs[c]["dT_o"]) for c in range(NCORES)], axis=1)
    for nm in at_full:
        W[nm] = np.concatenate([np.asarray(rs[c][nm + "_o"]) for c in range(NCORES)], axis=0)
    return W


def kernel(**inputs):
    W = run_prep(inputs)
    yhi, ylo = run_stage_a(inputs)
    x2 = run_stage_b(inputs, yhi, ylo, W["w2up"], W["w2gate"], W["w2down"])
    wbf = {k: W[k] for k in ("wq", "wk0", "wk1", "wv0", "wv1", "wb01", "wa", "wg")}
    outsC = run_stage_c(inputs, x2, wbf)
    zero_s = np.zeros((KDIM, DV), np.float32)
    outsD1 = run_stage_d(outsC, [zero_s] * NCORES, skip_o=True)
    sinits = [np.asarray(outsD1[c - 1]["Sfin"]) if c % 2 == 1 else zero_s
              for c in range(NCORES)]
    outsD2 = run_stage_d(outsC, sinits)
    webf = {"wo": W["wo"], "gT": W["gT"], "uT": W["uT"], "dT": W["dT"]}
    out = run_stage_e(inputs, x2, outsD2, outsC, webf)
    return out


def run_stage_a(inputs):
    nc, s = build_stage_a()
    s.realize(nc)
    maps = _stage_a_inputs(inputs)
    res = run_bass_kernel_spmd(nc, maps, core_ids=list(range(NCORES)))
    yhi = np.zeros((B, DM, L), BF)
    ylo = np.zeros((B, DM, L), BF)
    for c in range(NCORES):
        b, j = divmod(c, 2)
        yhi[b, j * 1024:(j + 1) * 1024, :] = res.results[c]["yhi"]
        ylo[b, j * 1024:(j + 1) * 1024, :] = res.results[c]["ylo"]
    return yhi, ylo




# version 03
# speedup vs baseline: 1.2237x; 1.0234x over previous
import contextlib
import numpy as np
import ml_dtypes
import concourse.bass as bass
import concourse.mybir as mybir
from concourse.bass_utils import run_bass_kernel_spmd

F32 = mybir.dt.float32
BF16 = mybir.dt.bfloat16
I32 = mybir.dt.int32
AF = mybir.ActivationFunctionType
ALU = mybir.AluOpType
AX = mybir.AxisListType
BF = ml_dtypes.bfloat16

NCORES = 8
B, L, DM = 4, 4096, 2048
TOK = 2048
NKT = DM // 128
EPS_LN = 1e-5
H, DK, NH, KCONV, DV = 6, 256, 2, 4, 512
KDIM, VDIM = H * DK, H * DV
INTER = 8192

ENGINES = ("sync", "scalar", "vector", "tensor", "gpsimd")


class Sched:

    AUTOSYNC = ("vector", "scalar", "gpsimd")

    def __init__(self):
        self.ops = {e: [] for e in ENGINES}
        self.cnt_c = {e: 0 for e in ENGINES}
        self.dch = {}
        self.last_use = {}

    def emit(self, e, fn, inc=0):
        if e in self.AUTOSYNC and self.cnt_c[e] > 0:
            self.ops[e].append(("wait", ("C", e), self.cnt_c[e]))
        self.ops[e].append(("op", fn, e))
        self.cnt_c[e] += 1
        return (("C", e), self.cnt_c[e])

    def dma(self, e, out, in_, ch=None, **kw):
        ch = ch if ch is not None else ("eng", e)
        self.dch[ch] = self.dch.get(ch, 0) + 16
        self.ops[e].append(
            ("dma", lambda g, out=out, in_=in_, kw=kw: g.dma_start(out=out, in_=in_, **kw), ch))
        return (("D", ch), self.dch[ch])

    def wait(self, e, tok):
        if tok is None:
            return
        key, val = tok
        if key == ("C", e):
            return
        self.ops[e].append(("wait", key, val))

    def wait_self(self, e, tok):
        key, val = tok
        self.ops[e].append(("wait", key, val))

    def chan_total(self, ch):
        return (("D", ch), self.dch.get(ch, 0))

    def use(self, key, tok):
        if tok is not None:
            self.last_use[key] = tok

    def guard(self, e, key):
        self.wait(e, self.last_use.get(key))

    def realize(self, nc):
        sems = {}
        with contextlib.ExitStack() as stack:
            for e in ENGINES:
                sems[("C", e)] = stack.enter_context(nc.semaphore(f"c_{e}"))
            for i, ch in enumerate(self.dch):
                sems[("D", ch)] = stack.enter_context(nc.semaphore(f"dm{i}"))
            with nc.Block() as block:
                for e in ENGINES:
                    if not self.ops[e]:
                        continue

                    def make_body(eng_name, oplist):
                        def body(eng):
                            hi = {}
                            for item in oplist:
                                if item[0] == "wait":
                                    _, key, val = item
                                    if val <= 0 or hi.get(key, -1) >= val:
                                        continue
                                    hi[key] = val
                                    eng.wait_ge(sems[key], val)
                                elif item[0] == "dma":
                                    _, fn, ch = item
                                    fn(eng).then_inc(sems[("D", ch)], 16)
                                else:
                                    _, fn, en = item
                                    fn(eng).then_inc(sems[("C", en)], 1)
                        return body

                    getattr(block, e)(make_body(e, self.ops[e]))
        return sems


def hadamard(n):
    h = np.array([[1.0]], np.float32)
    while h.shape[0] < n:
        h = np.block([[h, h], [h, -h]])
    return h


def build_stage_a():
    nc = bass.Bass("TRN2", target_bir_lowering=False)
    s = Sched()
    xb = nc.dram_tensor("xb", [L, DM], F32, kind="ExternalInput").ap()
    wdelta = nc.dram_tensor("wdelta", [DM, 128], F32, kind="ExternalInput").ap()
    wdt = nc.dram_tensor("wdt", [128, 1024], F32, kind="ExternalInput").ap()
    bdt = nc.dram_tensor("bdt", [128, 8], F32, kind="ExternalInput").ap()
    wbc = nc.dram_tensor("wbc", [DM, 32], F32, kind="ExternalInput").ap()
    acols = nc.dram_tensor("acols", [128, 128], F32, kind="ExternalInput").ap()
    dcols = nc.dram_tensor("dcols", [128, 8], F32, kind="ExternalInput").ap()
    selbc = nc.dram_tensor("selbc", [32, 32 * 128], F32, kind="ExternalInput").ap()
    constsa = nc.dram_tensor("constsa", [128, 4], F32, kind="ExternalInput").ap()
    yhi = nc.dram_tensor("yhi", [1024, L], BF16, kind="ExternalOutput").ap()
    ylo = nc.dram_tensor("ylo", [1024, L], BF16, kind="ExternalOutput").ap()
    xlntm_d = nc.dram_tensor("xlntm", [L, DM], BF16, kind="Internal").ap()

    sb = lambda n, shp, dt: nc.alloc_sbuf_tensor(n, shp, dt).ap()
    ps = lambda n, shp, dt: nc.alloc_psum_tensor(n, shp, dt).ap()

    xin = sb("xin", [128, DM], F32)
    sq_junk = sb("sq_junk", [128, DM], BF16)
    xlnbf = sb("xlnbf", [128, DM], BF16)
    stats = sb("stats", [128, 8], F32)
    xlt = sb("xlt", [128, 2 * 8 * 512], BF16)
    xstream = sb("xstream", [128, 2 * 8 * 512], BF16)
    xhalf32 = sb("xhalf32", [128, 8 * 512], F32)
    wd_bf = sb("wd_bf", [128, NKT * 128], BF16)
    wdt_bf = sb("wdt_bf", [128, 1024], BF16)
    wbc_bf = sb("wbc_bf", [128, NKT * 32], BF16)
    wld = sb("wld", [128, 2048], F32)
    bdt_sb = sb("bdt_sb", [128, 8], F32)
    acol_sb = sb("acol_sb", [128, 128], F32)
    dcol_sb = sb("dcol_sb", [128, 8], F32)
    selbc_sb = sb("selbc_sb", [32, 32 * 128], F32)
    cca = sb("cca", [128, 4], F32)
    bpcp32 = sb("bpcp32", [32, 512], F32)
    dpT = sb("dpT", [128, 512], BF16)
    spz = sb("spz", [128, 8 * 512], F32)
    spa = sb("spa", [128, 512], F32)
    spe = sb("spe", [128, 512], F32)
    spl = sb("spl", [128, 512], F32)
    spr = sb("spr", [128, 512], F32)
    uT = sb("uT", [128, 8 * 512], F32)
    NSL = 3
    abuf = sb("abuf", [128, NSL * 512], F32)
    bbuf = sb("bbuf", [128, NSL * 512], F32)
    hbuf = sb("hbuf", [128, NSL * 512], F32)
    t1buf = sb("t1buf", [128, NSL * 512], F32)
    yacc = sb("yacc", [128, 8 * 512], F32)
    yout = sb("yout", [128, 8 * 512], F32)
    yhi_sb = sb("yhi_sb", [128, 512], BF16)
    yhi32 = sb("yhi32", [128, 512], F32)
    ylo_sb = sb("ylo_sb", [128, 512], BF16)
    state = sb("state", [128, 8 * 16], F32)

    p_dp = ps("p_dp", [128, 512], F32)
    p_z = [ps("p_z0", [128, 512], F32), ps("p_z1", [128, 512], F32)]
    p_bc = [ps("p_bc0", [128, 2 * 512], F32), ps("p_bc1", [128, 2 * 512], F32)]
    p_bpcp = ps("p_bpcp", [32, 512], F32)

    tok = None
    for k in range(NKT):
        tok = s.dma("sync", wld[:, k * 128:(k + 1) * 128], wdelta[k * 128:(k + 1) * 128, :], ch="wld")
    s.wait("vector", tok)
    t_wd = s.emit("vector", lambda v: v.tensor_copy(wd_bf[:], wld[:, 0:2048]), inc=1)
    s.guard("sync", "wld")
    s.use("wld", t_wd)
    s.wait("sync", t_wd)
    tok = s.dma("sync", wld[:, 0:1024], wdt[:], ch="wld")
    s.wait("vector", tok)
    t_wdt = s.emit("vector", lambda v: v.tensor_copy(wdt_bf[:], wld[:, 0:1024]), inc=1)
    s.wait("sync", t_wdt)
    for k in range(NKT):
        tok = s.dma("sync", wld[:, k * 32:(k + 1) * 32], wbc[k * 128:(k + 1) * 128, :], ch="wld")
    s.wait("vector", tok)
    t_wbc = s.emit("vector", lambda v: v.tensor_copy(wbc_bf[:], wld[:, 0:NKT * 32]), inc=1)
    s.dma("sync", bdt_sb[:], bdt[:], ch="consts")
    s.dma("sync", acol_sb[:], acols[:], ch="consts")
    s.dma("sync", dcol_sb[:], dcols[:], ch="consts")
    s.dma("sync", selbc_sb[:], selbc[:], ch="consts")
    s.dma("sync", cca[:], constsa[:], ch="consts")
    t_consts = t_cca = s.chan_total("consts")

    t_state0 = s.emit("vector", lambda v: v.memset(state[:], 0.0), inc=1)

    t_store = None
    t_ln = None
    for tt in range(L // 128):
        if t_ln is not None:
            s.wait("sync", t_ln)
            s.wait("sync", t_sq)
        t_ld = s.dma("sync", xin[:], xb[tt * 128:(tt + 1) * 128, :], ch="xin")
        s.wait("vector", t_ld)
        s.emit("vector", lambda v: v.reduce_sum(stats[:, 0:1], xin[:], axis=AX.X))
        s.wait("scalar", t_ld)
        t_sq = s.emit("scalar", lambda sc: sc.activation(
            out=sq_junk[:], in_=xin[:], func=AF.Square, accum_out=stats[:, 1:2]), inc=1)
        s.wait("vector", t_sq)
        s.emit("vector", lambda v: v.tensor_scalar(
            out=stats[:, 2:3], in0=stats[:, 0:1], scalar1=1.0 / DM, scalar2=None, op0=ALU.mult))
        s.emit("vector", lambda v: v.tensor_scalar(
            out=stats[:, 5:6], in0=stats[:, 1:2], scalar1=1.0 / DM, scalar2=None, op0=ALU.mult))
        s.emit("vector", lambda v: v.tensor_tensor(
            out=stats[:, 3:4], in0=stats[:, 2:3], in1=stats[:, 2:3], op=ALU.mult))
        t_var = s.emit("vector", lambda v: v.tensor_tensor(
            out=stats[:, 3:4], in0=stats[:, 5:6], in1=stats[:, 3:4], op=ALU.subtract), inc=1)
        s.wait("scalar", t_var)
        if tt == 0:
            s.wait("scalar", t_cca)
        s.emit("scalar", lambda sc: sc.activation(
            out=stats[:, 4:5], in_=stats[:, 3:4], func=AF.Ln, bias=cca[:, 0:1]))
        t_rstd = s.emit("scalar", lambda sc: sc.activation(
            out=stats[:, 4:5], in_=stats[:, 4:5], func=AF.Exp, scale=cca[:, 2:3]), inc=1)
        s.wait("vector", t_rstd)
        if t_store is not None:
            s.wait("vector", t_store)
        t_ln = s.emit("vector", lambda v: v.tensor_scalar(
            out=xlnbf[:], in0=xin[:], scalar1=stats[:, 2:3], scalar2=stats[:, 4:5],
            op0=ALU.subtract, op1=ALU.mult), inc=1)
        s.wait("gpsimd", t_ln)
        t_store = s.dma("gpsimd", xlntm_d[tt * 128:(tt + 1) * 128, :], xlnbf[:], ch="xlnst")

    s.wait("scalar", t_store)

    NCH = L // 512
    for c in range(NCH):
        par = c % 2

        def xl(k, c=c, par=par):
            if k < 8:
                return xlt[:, (par * 8 + k) * 512:(par * 8 + k + 1) * 512]
            return xstream[:, (par * 8 + (k - 8)) * 512:(par * 8 + (k - 8) + 1) * 512]

        t_xlt = None
        for k in range(8):
            s.guard("scalar", ("xlt", par, k))
            t_xlt = s.dma("scalar", xlt[:, (par * 8 + k) * 512:(par * 8 + k + 1) * 512],
                          xlntm_d[c * 512:(c + 1) * 512, k * 128:(k + 1) * 128],
                          transpose=True, ch=("xlt", par))
        t_xs = None
        for k in range(8):
            s.guard("scalar", ("xstream", par, k))
            t_xs = s.dma("scalar", xstream[:, (par * 8 + k) * 512:(par * 8 + k + 1) * 512],
                         xlntm_d[c * 512:(c + 1) * 512, (8 + k) * 128:(9 + k) * 128],
                         transpose=True, ch=("xs", par))

        s.wait("tensor", t_xlt)
        s.wait("tensor", t_xs)
        s.wait("tensor", t_wd)
        s.guard("tensor", "p_dp")
        t_mm = None
        for k in range(NKT):
            ap = xl(k)
            t_mm = s.emit("tensor", lambda t, k=k, ap=ap: t.matmul(
                p_dp[:], wd_bf[:, k * 128:(k + 1) * 128], ap,
                start=(k == 0), stop=(k == NKT - 1)), inc=(1 if k == NKT - 1 else 0))
        s.wait("vector", t_mm)
        s.guard("vector", "dpT")
        t_dp = s.emit("vector", lambda v: v.tensor_copy(dpT[:], p_dp[:]), inc=1)
        s.use("p_dp", t_dp)

        s.wait("tensor", t_wbc)
        s.guard("tensor", "p_bpcp")
        for k in range(NKT):
            ap = xl(k)
            t_mmb = s.emit("tensor", lambda t, k=k, ap=ap: t.matmul(
                p_bpcp[:], wbc_bf[:, k * 32:(k + 1) * 32], ap,
                start=(k == 0), stop=(k == NKT - 1)), inc=(1 if k == NKT - 1 else 0))
        s.wait("vector", t_mmb)
        s.guard("vector", "bpcp32")
        t_bpcp = s.emit("vector", lambda v: v.tensor_copy(bpcp32[:], p_bpcp[:]), inc=1)
        s.use("p_bpcp", t_bpcp)
        for k in range(8):
            s.use(("xstream", par, k), t_mmb)

        s.wait("vector", t_xlt)
        s.guard("vector", "xhalf32")
        t_xh = None
        for k in range(8):
            ap = xl(k)
            t_xh = s.emit("vector", lambda v, k=k, ap=ap: v.tensor_copy(
                xhalf32[:, k * 512:(k + 1) * 512], ap), inc=(1 if k == 7 else 0))
        for k in range(8):
            s.use(("xlt", par, k), t_xh)

        t_u = {}
        t_delta = {}
        for d in range(8):
            pz = p_z[d % 2]
            s.wait("tensor", t_dp)
            s.wait("tensor", t_wdt)
            s.guard("tensor", ("p_z", d % 2))
            t_mm = s.emit("tensor", lambda t, d=d, pz=pz: t.matmul(
                pz[:], wdt_bf[:, d * 128:(d + 1) * 128], dpT[:],
                start=True, stop=True), inc=1)
            if d == 7:
                s.use("dpT", t_mm)
            s.wait("vector", t_mm)
            s.guard("vector", ("spz", d))
            t_zb = s.emit("vector", lambda v, d=d, pz=pz: v.tensor_scalar(
                out=spz[:, d * 512:(d + 1) * 512], in0=pz[:], scalar1=bdt_sb[:, d:d + 1],
                scalar2=None, op0=ALU.add), inc=1)
            s.use(("p_z", d % 2), t_zb)
            s.wait("scalar", t_zb)
            s.emit("scalar", lambda sc, d=d: sc.activation(
                out=spa[:], in_=spz[:, d * 512:(d + 1) * 512], func=AF.Abs))
            s.emit("scalar", lambda sc: sc.activation(out=spe[:], in_=spa[:], func=AF.Exp, scale=cca[:, 3:4]))
            t_l = s.emit("scalar", lambda sc: sc.activation(out=spl[:], in_=spe[:], func=AF.Ln, bias=cca[:, 1:2]), inc=1)
            s.emit("vector", lambda v, d=d: v.tensor_scalar(
                out=spr[:], in0=spz[:, d * 512:(d + 1) * 512], scalar1=0.0, scalar2=None, op0=ALU.max))
            s.wait("vector", t_l)
            t_delta[d] = s.emit("vector", lambda v, d=d: v.tensor_tensor(
                out=spz[:, d * 512:(d + 1) * 512], in0=spr[:], in1=spl[:], op=ALU.add), inc=1)
            s.use("spl", t_delta[d])
            t_u[d] = s.emit("vector", lambda v, d=d: v.tensor_tensor(
                out=uT[:, d * 512:(d + 1) * 512], in0=spz[:, d * 512:(d + 1) * 512],
                in1=xhalf32[:, d * 512:(d + 1) * 512], op=ALU.mult), inc=1)

        s.guard("vector", "yacc")
        s.emit("vector", lambda v: v.memset(yacc[:], 0.0))
        if c == 0:
            s.wait("vector", t_state0)
            s.wait("scalar", t_consts)

        slot = 0
        for n in range(16):
            pbc = p_bc[n % 2]
            s.wait("tensor", t_bpcp)
            s.guard("tensor", ("p_bc", n % 2, 0))
            s.guard("tensor", ("p_bc", n % 2, 1))
            t_b1 = s.emit("tensor", lambda t, n=n, pbc=pbc: t.matmul(
                pbc[:, 0:512], selbc_sb[:, n * 128:(n + 1) * 128], bpcp32[:],
                start=True, stop=True), inc=1)
            t_b2 = s.emit("tensor", lambda t, n=n, pbc=pbc: t.matmul(
                pbc[:, 512:1024], selbc_sb[:, (16 + n) * 128:(17 + n) * 128], bpcp32[:],
                start=True, stop=True), inc=1)
            if n == 15:
                s.use("bpcp32", t_b2)
            for d in range(8):
                col = d * 16 + n
                bb = slot * 512
                s.wait("scalar", t_delta[d])
                s.guard("scalar", ("abuf", slot))
                t_ab = s.emit("scalar", lambda sc, d=d, col=col, bb=bb: sc.activation(
                    out=abuf[:, bb:bb + 512], in_=spz[:, d * 512:(d + 1) * 512],
                    func=AF.Exp, scale=acol_sb[:, col:col + 1]), inc=1)
                s.use(("spz", d), t_ab)
                s.wait("vector", t_b1)
                s.wait("vector", t_u[d])
                s.guard("vector", ("bbuf", slot))
                t_bb = s.emit("vector", lambda v, d=d, bb=bb, pbc=pbc: v.tensor_tensor(
                    out=bbuf[:, bb:bb + 512], in0=uT[:, d * 512:(d + 1) * 512],
                    in1=pbc[:, 0:512], op=ALU.mult), inc=1)
                s.wait("vector", t_ab)
                s.guard("vector", ("hbuf", slot))
                t_h = s.emit("vector", lambda v, col=col, bb=bb: v.tensor_tensor_scan(
                    out=hbuf[:, bb:bb + 512], data0=abuf[:, bb:bb + 512],
                    data1=bbuf[:, bb:bb + 512], initial=state[:, col:col + 1],
                    op0=ALU.mult, op1=ALU.add), inc=1)
                s.use(("abuf", slot), t_h)
                s.use(("bbuf", slot), t_h)
                s.emit("vector", lambda v, col=col, bb=bb: v.tensor_copy(
                    state[:, col:col + 1], hbuf[:, bb + 511:bb + 512]))
                s.wait("vector", t_b2)
                s.emit("vector", lambda v, bb=bb, pbc=pbc: v.tensor_tensor(
                    out=t1buf[:, bb:bb + 512], in0=hbuf[:, bb:bb + 512],
                    in1=pbc[:, 512:1024], op=ALU.mult))
                t_yy = s.emit("vector", lambda v, d=d, bb=bb: v.tensor_tensor(
                    out=yacc[:, d * 512:(d + 1) * 512], in0=yacc[:, d * 512:(d + 1) * 512],
                    in1=t1buf[:, bb:bb + 512], op=ALU.add), inc=1)
                s.use(("hbuf", slot), t_yy)
                if d == 7:
                    s.use(("p_bc", n % 2, 0), t_yy)
                    s.use(("p_bc", n % 2, 1), t_yy)
                slot = (slot + 1) % NSL
        for d in range(8):
            s.guard("vector", ("yout", d))
            t_yf = s.emit("vector", lambda v, d=d: v.scalar_tensor_tensor(
                out=yout[:, d * 512:(d + 1) * 512], in0=xhalf32[:, d * 512:(d + 1) * 512],
                scalar=dcol_sb[:, d:d + 1], in1=yacc[:, d * 512:(d + 1) * 512],
                op0=ALU.mult, op1=ALU.add), inc=1)
            if d == 7:
                s.use("xhalf32", t_yf)
                s.use("yacc", t_yf)
            s.guard("vector", "yhis")
            s.emit("vector", lambda v, d=d: v.tensor_copy(
                yhi_sb[:], yout[:, d * 512:(d + 1) * 512]))
            s.emit("vector", lambda v: v.tensor_copy(yhi32[:], yhi_sb[:]))
            t_lo = s.emit("vector", lambda v, d=d: v.tensor_tensor(
                out=ylo_sb[:], in0=yout[:, d * 512:(d + 1) * 512], in1=yhi32[:],
                op=ALU.subtract))
            s.wait("gpsimd", t_lo)
            t_s1 = s.dma("gpsimd", yhi[d * 128:(d + 1) * 128, c * 512:(c + 1) * 512], yhi_sb[:], ch="yst")
            t_s2 = s.dma("gpsimd", ylo[d * 128:(d + 1) * 128, c * 512:(c + 1) * 512], ylo_sb[:], ch="yst")
            s.use("yhis", t_s2)

    s.wait_self("gpsimd", s.chan_total(("eng", "gpsimd")))
    return nc, s



def build_stage_b(debug=False):
    nc = bass.Bass("TRN2", target_bir_lowering=False)
    s = Sched()
    xrows = nc.dram_tensor("xrows", [TOK, DM], F32, kind="ExternalInput").ap()
    yhi = nc.dram_tensor("yhi", [DM, TOK], BF16, kind="ExternalInput").ap()
    ylo = nc.dram_tensor("ylo", [DM, TOK], BF16, kind="ExternalInput").ap()
    w2up = nc.dram_tensor("w2up", [DM, INTER], BF16, kind="ExternalInput").ap()
    w2gate = nc.dram_tensor("w2gate", [DM, INTER], BF16, kind="ExternalInput").ap()
    w2down = nc.dram_tensor("w2down", [INTER, DM], BF16, kind="ExternalInput").ap()
    scu = nc.dram_tensor("scu", [128, 64], F32, kind="ExternalInput").ap()
    scg = nc.dram_tensor("scg", [128, 64], F32, kind="ExternalInput").ap()
    constsb = nc.dram_tensor("constsb", [128, 2], F32, kind="ExternalInput").ap()
    x2o_d = nc.dram_tensor("x2o", [TOK, DM], F32, kind="ExternalOutput").ap()
    xln_d = nc.dram_tensor("xln_sc", [TOK, DM], BF16, kind="Internal").ap()
    if debug:
        dbg_x1 = nc.dram_tensor("dbg_x1", [512, DM], F32, kind="ExternalOutput").ap()
        dbg_xln = nc.dram_tensor("dbg_xln", [512, DM], BF16, kind="ExternalOutput").ap()
        dbg_z = nc.dram_tensor("dbg_z", [INTER, 512], BF16, kind="ExternalOutput").ap()

    sb = lambda n, shp, dt: nc.alloc_sbuf_tensor(n, shp, dt).ap()
    ps = lambda n, shp, dt: nc.alloc_psum_tensor(n, shp, dt).ap()

    x1 = sb("x1", [128, 4 * DM], F32)
    yts = sb("yts", [128, 2 * DM], BF16)
    xlnbf = sb("xlnbf", [128, DM], BF16)
    sq_junk = sb("sq_junk", [128, DM], BF16)
    stats = sb("stats", [128, 8], F32)
    ccb128 = sb("ccb128", [128, 2], F32)
    xlnT = sb("xlnT", [128, 16 * 512], BF16)
    zbf = sb("zbf", [128, 64 * 512], BF16)
    wup_sb = sb("wup_sb", [128, 2 * 2048], BF16)
    wgt_sb = sb("wgt_sb", [128, 2 * 2048], BF16)
    wdn_sb = sb("wdn_sb", [128, 2 * 16 * 512], BF16)
    up_sb = sb("up_sb", [128, 2 * 512], F32)
    gs_sb = sb("gs_sb", [128, 2 * 512], F32)
    x2s = sb("x2s", [128, 2 * 512], F32)
    scu_sb = sb("scu_sb", [128, 64], F32)
    scg_sb = sb("scg_sb", [128, 64], F32)

    p_up = [ps("p_up0", [128, 512], F32), ps("p_up1", [128, 512], F32)]
    p_gt = [ps("p_gt0", [128, 512], F32), ps("p_gt1", [128, 512], F32)]
    p_dn = [p_up[0], p_up[1], p_gt[0], p_gt[1]]

    s.dma("sync", scu_sb[:], scu[:], ch="consts")
    s.dma("sync", scg_sb[:], scg[:], ch="consts")
    s.dma("sync", ccb128[:], constsb[:], ch="consts")
    t_sc = t_cc128 = s.chan_total("consts")

    NCH = TOK // 512
    for c in range(NCH):
        t_lnst = {}
        for tt in range(4):
            s.guard("sync", ("x1", tt))
            t_x = s.dma("sync", x1[:, tt * DM:(tt + 1) * DM],
                        xrows[c * 512 + tt * 128: c * 512 + (tt + 1) * 128, :], ch=("x1", tt))
            par = tt % 2
            s.guard("scalar", ("yts", par))
            t_yh = s.dma("scalar", yts[:, par * DM:(par + 1) * DM],
                         yhi[:, c * 512 + tt * 128: c * 512 + (tt + 1) * 128], transpose=True,
                         ch=("yts", par))
            s.wait("vector", t_x)
            s.wait("vector", t_yh)
            t_a1 = s.emit("vector", lambda v, tt=tt, par=par: v.tensor_tensor(
                out=x1[:, tt * DM:(tt + 1) * DM], in0=x1[:, tt * DM:(tt + 1) * DM],
                in1=yts[:, par * DM:(par + 1) * DM], op=ALU.add))
            s.use(("yts", par), t_a1)
            s.guard("scalar", ("yts", par))
            t_yl = s.dma("scalar", yts[:, par * DM:(par + 1) * DM],
                         ylo[:, c * 512 + tt * 128: c * 512 + (tt + 1) * 128], transpose=True,
                         ch=("yts", par))
            s.wait("vector", t_yl)
            t_a2 = s.emit("vector", lambda v, tt=tt, par=par: v.tensor_tensor(
                out=x1[:, tt * DM:(tt + 1) * DM], in0=x1[:, tt * DM:(tt + 1) * DM],
                in1=yts[:, par * DM:(par + 1) * DM], op=ALU.add))
            s.use(("yts", par), t_a2)
            s.emit("vector", lambda v, tt=tt: v.reduce_sum(
                stats[:, 0:1], x1[:, tt * DM:(tt + 1) * DM], axis=AX.X))
            s.wait("scalar", t_a2)
            t_sq = s.emit("scalar", lambda sc, tt=tt: sc.activation(
                out=sq_junk[:], in_=x1[:, tt * DM:(tt + 1) * DM], func=AF.Square,
                accum_out=stats[:, 1:2]))
            s.wait("vector", t_sq)
            s.emit("vector", lambda v: v.tensor_scalar(
                out=stats[:, 2:3], in0=stats[:, 0:1], scalar1=1.0 / DM, scalar2=None, op0=ALU.mult))
            s.emit("vector", lambda v: v.tensor_scalar(
                out=stats[:, 5:6], in0=stats[:, 1:2], scalar1=1.0 / DM, scalar2=None, op0=ALU.mult))
            s.emit("vector", lambda v: v.tensor_tensor(
                out=stats[:, 3:4], in0=stats[:, 2:3], in1=stats[:, 2:3], op=ALU.mult))
            t_var = s.emit("vector", lambda v: v.tensor_tensor(
                out=stats[:, 3:4], in0=stats[:, 5:6], in1=stats[:, 3:4], op=ALU.subtract))
            s.wait("scalar", t_var)
            if c == 0 and tt == 0:
                s.wait("scalar", t_cc128)
            s.emit("scalar", lambda sc: sc.activation(
                out=stats[:, 4:5], in_=stats[:, 3:4], func=AF.Ln, bias=ccb128[:, 0:1]))
            t_rstd = s.emit("scalar", lambda sc: sc.activation(
                out=stats[:, 4:5], in_=stats[:, 4:5], func=AF.Exp, scale=ccb128[:, 1:2]))
            s.wait("vector", t_rstd)
            s.guard("vector", "xlnbf")
            t_ln = s.emit("vector", lambda v, tt=tt: v.tensor_scalar(
                out=xlnbf[:], in0=x1[:, tt * DM:(tt + 1) * DM], scalar1=stats[:, 2:3],
                scalar2=stats[:, 4:5], op0=ALU.subtract, op1=ALU.mult))
            s.wait("gpsimd", t_ln)
            t_xst = s.dma("gpsimd", xln_d[c * 512 + tt * 128: c * 512 + (tt + 1) * 128, :], xlnbf[:],
                          ch="xlnst")
            s.use("xlnbf", t_xst)
            t_lnst[tt] = t_xst

        if debug and c == 0:
            s.wait("gpsimd", s.chan_total("xlnst"))
            for tt in range(4):
                s.dma("gpsimd", dbg_x1[tt * 128:(tt + 1) * 128, :], x1[:, tt * DM:(tt + 1) * DM], ch="dbg")
            t_dmp = s.dma("gpsimd", dbg_xln[:], xln_d[0:512, :], ch="dbg")
            t_dt = s.chan_total("dbg")
            for e in ("sync", "scalar", "vector", "tensor"):
                s.wait(e, t_dt)
        t_allst = s.chan_total("xlnst")
        for k in range(NKT):
            s.guard("scalar", ("xlnT", k))
            s.wait("scalar", t_allst)
            s.dma("scalar", xlnT[:, k * 512:(k + 1) * 512],
                  xln_d[c * 512:(c + 1) * 512, k * 128:(k + 1) * 128], transpose=True, ch="xlnT")
        t_xt_all = s.chan_total("xlnT")

        for i in range(64):
            par = i % 2
            s.guard("sync", ("wup", par))
            t_wu = s.dma("sync",
                         wup_sb[:, par * 2048:(par + 1) * 2048].rearrange("p (k f) -> p k f", k=16),
                         w2up[:, i * 128:(i + 1) * 128].rearrange("(k p) f -> p k f", p=128),
                         ch=("wup", par))
            s.guard("sync", ("wgt", par))
            t_wg = s.dma("sync",
                         wgt_sb[:, par * 2048:(par + 1) * 2048].rearrange("p (k f) -> p k f", k=16),
                         w2gate[:, i * 128:(i + 1) * 128].rearrange("(k p) f -> p k f", p=128),
                         ch=("wgt", par))
            s.wait("tensor", t_wu)
            s.wait("tensor", t_wg)
            s.guard("tensor", ("p_up", par))
            s.guard("tensor", ("p_gt", par))
            s.wait("tensor", t_xt_all)
            for k in range(NKT):
                s.emit("tensor", lambda t, i=i, k=k, par=par: t.matmul(
                    p_up[par][:], wup_sb[:, par * 2048 + k * 128: par * 2048 + (k + 1) * 128],
                    xlnT[:, k * 512:(k + 1) * 512], start=(k == 0), stop=(k == NKT - 1)))
            t_mm2 = None
            for k in range(NKT):
                t_mm2 = s.emit("tensor", lambda t, i=i, k=k, par=par: t.matmul(
                    p_gt[par][:], wgt_sb[:, par * 2048 + k * 128: par * 2048 + (k + 1) * 128],
                    xlnT[:, k * 512:(k + 1) * 512], start=(k == 0), stop=(k == NKT - 1)))
            s.use(("wup", par), t_mm2)
            s.use(("wgt", par), t_mm2)
            if i >= 62:
                for k in range(NKT):
                    s.use(("xlnT", k), t_mm2)
            s.wait("vector", t_mm2)
            s.wait("vector", t_sc)
            t_up = s.emit("vector", lambda v, i=i, par=par: v.tensor_scalar(
                out=up_sb[:, par * 512:(par + 1) * 512], in0=p_up[par][:],
                scalar1=scu_sb[:, i:i + 1], scalar2=None, op0=ALU.mult))
            s.use(("p_up", par), t_up)
            s.wait("scalar", t_mm2)
            t_gs = s.emit("scalar", lambda sc, i=i, par=par: sc.activation(
                out=gs_sb[:, par * 512:(par + 1) * 512], in_=p_gt[par][:],
                func=AF.Silu, scale=scg_sb[:, i:i + 1]))
            s.use(("p_gt", par), t_gs)
            s.wait("vector", t_gs)
            s.guard("vector", ("zrd", i))
            t_z = s.emit("vector", lambda v, i=i, par=par: v.tensor_tensor(
                out=zbf[:, i * 512:(i + 1) * 512], in0=up_sb[:, par * 512:(par + 1) * 512],
                in1=gs_sb[:, par * 512:(par + 1) * 512], op=ALU.mult))
            s.use(("zmm", i), t_z)

        if debug and c == 0:
            t_dz = None
            for i in range(64):
                s.wait("gpsimd", s.last_use.get(("zmm", i)))
                t_dz = s.dma("gpsimd", dbg_z[i * 128:(i + 1) * 128, :], zbf[:, i * 512:(i + 1) * 512])
            s.wait("vector", t_dz)
            s.wait("tensor", t_dz)
        for oc in range(4):
            t_dn = {}
            for ig in range(4):
                wpar = ig % 2
                s.guard("sync", ("wdn", wpar))
                t_wd = s.dma("sync",
                             wdn_sb[:, wpar * 8192:(wpar + 1) * 8192].rearrange("p (k f) -> p k f", k=16),
                             w2down[ig * 2048:(ig + 1) * 2048, oc * 512:(oc + 1) * 512]
                             .rearrange("(k p) f -> p k f", p=128))
                s.wait("tensor", t_wd)
                for tt in range(4):
                    if ig == 0:
                        s.guard("tensor", ("p_up", tt) if tt < 2 else ("p_gt", tt - 2))
                    for ii in range(16):
                        i = ig * 16 + ii
                        s.wait("tensor", s.last_use.get(("zmm", i)))
                        t_dn[tt] = s.emit("tensor", lambda t, tt=tt, i=i, ii=ii, wpar=wpar, ig=ig: t.matmul(
                            p_dn[tt][:], zbf[:, i * 512 + tt * 128: i * 512 + (tt + 1) * 128],
                            wdn_sb[:, wpar * 8192 + ii * 512:wpar * 8192 + (ii + 1) * 512],
                            start=(ig == 0 and ii == 0), stop=(ig == 3 and ii == 15)))
                s.use(("wdn", wpar), t_dn[3])
            for tt in range(4):
                if oc == 3:
                    for i in range(64):
                        s.use(("zrd", i), t_dn[3])
                s.wait("vector", t_dn[tt])
                ppar = tt % 2
                s.guard("vector", ("x2s", ppar))
                t_x2 = s.emit("vector", lambda v, tt=tt, oc=oc, ppar=ppar: v.tensor_tensor(
                    out=x2s[:, ppar * 512:(ppar + 1) * 512], in0=p_dn[tt][:],
                    in1=x1[:, tt * DM + oc * 512: tt * DM + (oc + 1) * 512], op=ALU.add))
                s.use(("p_up", tt) if tt < 2 else ("p_gt", tt - 2), t_x2)
                if oc == 3:
                    s.use(("x1", tt), t_x2)
                s.wait("gpsimd", t_x2)
                t_so = s.dma("gpsimd",
                             x2o_d[c * 512 + tt * 128: c * 512 + (tt + 1) * 128, oc * 512:(oc + 1) * 512],
                             x2s[:, ppar * 512:(ppar + 1) * 512], ch="x2st")
                s.use(("x2s", ppar), t_so)

    s.wait_self("gpsimd", s.chan_total("x2st"))
    return nc, s



def build_stage_c():
    nc = bass.Bass("TRN2", target_bir_lowering=False)
    s = Sched()
    x2r = nc.dram_tensor("x2rows", [TOK + 16, DM], F32, kind="ExternalInput").ap()
    wq = nc.dram_tensor("wq", [DM, KDIM], BF16, kind="ExternalInput").ap()
    wk0 = nc.dram_tensor("wk0", [DM, KDIM], BF16, kind="ExternalInput").ap()
    wk1 = nc.dram_tensor("wk1", [DM, KDIM], BF16, kind="ExternalInput").ap()
    wv0 = nc.dram_tensor("wv0", [DM, VDIM], BF16, kind="ExternalInput").ap()
    wv1 = nc.dram_tensor("wv1", [DM, VDIM], BF16, kind="ExternalInput").ap()
    wb01 = nc.dram_tensor("wb01", [DM, 12], BF16, kind="ExternalInput").ap()
    wa = nc.dram_tensor("wa", [DM, 6], BF16, kind="ExternalInput").ap()
    wg = nc.dram_tensor("wg", [DM, VDIM], BF16, kind="ExternalInput").ap()
    cw = nc.dram_tensor("cw", [128, 84 * 4], F32, kind="ExternalInput").ap()
    alogdt = nc.dram_tensor("alogdt", [6, 2], F32, kind="ExternalInput").ap()
    constsc = nc.dram_tensor("constsc", [128, 6], F32, kind="ExternalInput").ap()
    identc = nc.dram_tensor("identc", [128, 128], F32, kind="ExternalInput").ap()
    Ks = nc.dram_tensor("Ks", [2 * TOK, KDIM], BF16, kind="ExternalOutput").ap()
    Qs = nc.dram_tensor("Qs", [2 * TOK, KDIM], BF16, kind="ExternalOutput").ap()
    Vs = nc.dram_tensor("Vs", [2 * TOK, VDIM], BF16, kind="ExternalOutput").ap()
    bgz = nc.dram_tensor("bgz", [2 * TOK, 12], F32, kind="ExternalOutput").ap()
    Kse = Ks.rearrange("(t two) f -> two t f", two=2)
    Qse = Qs.rearrange("(t two) f -> two t f", two=2)
    Vse = Vs.rearrange("(t two) f -> two t f", two=2)
    bgze = bgz.rearrange("(t two) f -> two t f", two=2)
    gg_d = nc.dram_tensor("gg", [TOK, VDIM], BF16, kind="ExternalOutput").ap()
    xln_d = nc.dram_tensor("xln_sc", [TOK + 16, DM], BF16, kind="Internal").ap()
    qc_d = nc.dram_tensor("qc_sc", [KDIM, TOK], BF16, kind="Internal").ap()
    k0c_d = nc.dram_tensor("k0c_sc", [KDIM, TOK], BF16, kind="Internal").ap()
    k1c_d = nc.dram_tensor("k1c_sc", [KDIM, TOK], BF16, kind="Internal").ap()
    v0c_d = nc.dram_tensor("v0c_sc", [VDIM, TOK], BF16, kind="Internal").ap()
    v1c_d = nc.dram_tensor("v1c_sc", [VDIM, TOK], BF16, kind="Internal").ap()

    sb = lambda n, shp, dt: nc.alloc_sbuf_tensor(n, shp, dt).ap()
    ps = lambda n, shp, dt: nc.alloc_psum_tensor(n, shp, dt).ap()

    xin = sb("xin", [128, DM], F32)
    sq_junk = sb("sq_junk", [128, DM], BF16)
    xlnbf = sb("xlnbf", [128, DM], BF16)
    stats = sb("stats", [128, 8], F32)
    cc = sb("cc", [128, 6], F32)
    hT = sb("hT", [128, 16 * 512], BF16)
    wproj = sb("wproj", [128, 2 * 2048], BF16)
    wsmall = sb("wsmall", [128, 16 * 18], BF16)
    projc = sb("projc", [128, 2 * 520], F32)
    carry = sb("carry", [128, 84 * 4], F32)
    convo = sb("convo", [128, 2 * 512], F32)
    cw_sb = sb("cw_sb", [128, 84 * 4], F32)
    silub = sb("silub", [128, 2 * 512], BF16)
    wmov = sb("wmov", [128, 2 * 16 * 512], BF16)
    ggsb = sb("ggsb", [128, 2 * 512], BF16)
    negA = sb("negA", [6, 2], F32)
    bsp = sb("bsp", [128, 5 * 512], F32)
    bgt = sb("bgt", [128, 2 * 12], F32)
    zero6 = sb("zero6", [128, 6], F32)
    idc = sb("idc", [128, 128], F32)
    idcb = sb("idcb", [16, 16], BF16)
    tq = sb("tq", [128, KDIM], BF16)
    tk0 = sb("tk0", [128, KDIM], BF16)
    tk1 = sb("tk1", [128, KDIM], BF16)
    tv0 = sb("tv0", [128, VDIM], BF16)
    tv1 = sb("tv1", [128, VDIM], BF16)
    tmpsq = sb("tmpsq", [128, KDIM], F32)
    nsq = sb("nsq", [128, 18], F32)
    outq = sb("outq", [128, KDIM], BF16)
    outk = sb("outk", [128, KDIM], BF16)
    zeroq = sb("zeroq", [128, KDIM], BF16)

    p_pr = [ps("p_pr0", [128, 512], F32), ps("p_pr1", [128, 512], F32)]
    p_gg = [ps("p_gg0", [128, 512], F32), ps("p_gg1", [128, 512], F32)]
    p_ba = ps("p_ba", [128, 512], F32)
    p_aa = ps("p_aa", [128, 512], F32)
    p_bt = ps("p_bt", [128, 128], F32)
    p_btb = ps("p_btb", [128, 128], BF16)

    s.dma("sync", cw_sb[:], cw[:], ch="consts")
    s.dma("sync", cc[:], constsc[:], ch="consts")
    s.dma("sync", negA[:], alogdt[:], ch="consts")
    s.dma("sync", idc[:], identc[:], ch="consts")
    s.dma("gpsimd", idcb[:], identc[0:16, 0:16], ch="consts")
    t_consts = s.chan_total("consts")
    t_z6 = s.emit("vector", lambda v: v.memset(zero6[:], 0.0))
    t_zq = s.emit("vector", lambda v: v.memset(zeroq[:], 0.0))
    s.wait("scalar", t_consts)
    t_expA = s.emit("scalar", lambda sc: sc.activation(
        out=negA[0:6, 0:1], in_=negA[0:6, 0:1], func=AF.Exp))
    s.wait("vector", t_expA)
    t_negA = s.emit("vector", lambda v: v.tensor_scalar(
        out=negA[0:6, 0:1], in0=negA[0:6, 0:1], scalar1=-1.0, scalar2=None, op0=ALU.mult))
    for k in range(NKT):
        s.dma("sync", wsmall[:, k * 18:k * 18 + 12], wb01[k * 128:(k + 1) * 128, :], ch="wsm")
        s.dma("sync", wsmall[:, k * 18 + 12:k * 18 + 18], wa[k * 128:(k + 1) * 128, :], ch="wsm")
    t_wsm = s.chan_total("wsm")

    def ln_tile(rows, src_lo, dst_lo, first):
        nonlocal t_store, t_ln, t_sq
        if t_ln is not None:
            s.wait("sync", t_ln)
            s.wait("sync", t_sq)
        t_ld = s.dma("sync", xin[0:rows, :], x2r[src_lo:src_lo + rows, :], ch="xin")
        s.wait("vector", t_ld)
        s.emit("vector", lambda v: v.reduce_sum(stats[0:rows, 0:1], xin[0:rows, :], axis=AX.X))
        s.wait("scalar", t_ld)
        t_sq = s.emit("scalar", lambda sc: sc.activation(
            out=sq_junk[0:rows, :], in_=xin[0:rows, :], func=AF.Square, accum_out=stats[0:rows, 1:2]))
        s.wait("vector", t_sq)
        s.emit("vector", lambda v: v.tensor_scalar(
            out=stats[0:rows, 2:3], in0=stats[0:rows, 0:1], scalar1=1.0 / DM, scalar2=None, op0=ALU.mult))
        s.emit("vector", lambda v: v.tensor_scalar(
            out=stats[0:rows, 5:6], in0=stats[0:rows, 1:2], scalar1=1.0 / DM, scalar2=None, op0=ALU.mult))
        s.emit("vector", lambda v: v.tensor_tensor(
            out=stats[0:rows, 3:4], in0=stats[0:rows, 2:3], in1=stats[0:rows, 2:3], op=ALU.mult))
        t_var = s.emit("vector", lambda v: v.tensor_tensor(
            out=stats[0:rows, 3:4], in0=stats[0:rows, 5:6], in1=stats[0:rows, 3:4], op=ALU.subtract))
        s.wait("scalar", t_var)
        if first:
            s.wait("scalar", t_consts)
        s.emit("scalar", lambda sc: sc.activation(
            out=stats[0:rows, 4:5], in_=stats[0:rows, 3:4], func=AF.Ln, bias=cc[0:rows, 0:1]))
        t_rstd = s.emit("scalar", lambda sc: sc.activation(
            out=stats[0:rows, 4:5], in_=stats[0:rows, 4:5], func=AF.Exp, scale=cc[0:rows, 1:2]))
        s.wait("vector", t_rstd)
        if t_store is not None:
            s.wait("vector", t_store)
        t_ln = s.emit("vector", lambda v: v.tensor_scalar(
            out=xlnbf[0:rows, :], in0=xin[0:rows, :], scalar1=stats[0:rows, 2:3],
            scalar2=stats[0:rows, 4:5], op0=ALU.subtract, op1=ALU.mult))
        s.wait("gpsimd", t_ln)
        t_store = s.dma("gpsimd", xln_d[dst_lo:dst_lo + rows, :], xlnbf[0:rows, :], ch="xlnst")

    t_store = t_ln = t_sq = None
    for tt in range(16):
        ln_tile(128, 16 + tt * 128, 16 + tt * 128, tt == 0)
    ln_tile(16, 0, 0, False)
    t_allln = s.chan_total("xlnst")

    ftiles = []
    for i in range(12):
        ftiles.append((wq, qc_d, i))
    for i in range(12):
        ftiles.append((wk0, k0c_d, i))
    for i in range(12):
        ftiles.append((wk1, k1c_d, i))
    for i in range(24):
        ftiles.append((wv0, v0c_d, i))
    for i in range(24):
        ftiles.append((wv1, v1c_d, i))

    s.wait("sync", t_allln)
    t_hl = s.dma("sync", xlnbf[0:16, :], xln_d[0:16, :], ch="xin")
    t_hx_all = None
    for k in range(NKT):
        s.wait("tensor", t_hl)
        s.wait("tensor", t_consts)
        s.guard("tensor", "p_btb")
        t_tp = s.emit("tensor", lambda t, k=k: t.transpose(
            p_btb[:, 0:16], xlnbf[0:16, k * 128:(k + 1) * 128], idcb[0:16, 0:16]))
        s.wait("vector", t_tp)
        t_hx_all = s.emit("vector", lambda v, k=k: v.tensor_copy(
            hT[:, k * 512:k * 512 + 16], p_btb[:, 0:16]))
        s.use("p_btb", t_hx_all)
    for fi, (wap, scd_, fw) in enumerate(ftiles):
        par = fi % 2
        s.guard("sync", ("wpr", par))
        t_w = s.dma("sync",
                    wproj[:, par * 2048:(par + 1) * 2048].rearrange("p (k f) -> p k f", k=16),
                    wap[:, fw * 128:(fw + 1) * 128].rearrange("(k p) f -> p k f", p=128),
                    ch=("wpr", par))
        s.wait("tensor", t_w)
        s.wait("tensor", t_hx_all)
        s.guard("tensor", ("p_pr", par))
        t_mm = None
        for k in range(NKT):
            t_mm = s.emit("tensor", lambda t, k=k, par=par: t.matmul(
                p_pr[par][:, 0:16], wproj[:, par * 2048 + k * 128: par * 2048 + (k + 1) * 128],
                hT[:, k * 512:k * 512 + 16], start=(k == 0), stop=(k == NKT - 1)))
        s.use(("wpr", par), t_mm)
        for k in range(NKT):
            s.use(("hT", k), t_mm)
        s.wait("vector", t_mm)
        t_cv = s.emit("vector", lambda v, fi=fi, par=par: v.tensor_copy(
            carry[:, fi * 4:fi * 4 + 3], p_pr[par][:, 13:16]))
        s.use(("p_pr", par), t_cv)

    NCH = TOK // 512
    for c in range(NCH):
        for k in range(NKT):
            s.guard("scalar", ("hT", k))
            s.wait("scalar", t_allln)
            s.dma("scalar", hT[:, k * 512:(k + 1) * 512],
                  xln_d[16 + c * 512:16 + (c + 1) * 512, k * 128:(k + 1) * 128],
                  transpose=True, ch="hT")
        t_hT = s.chan_total("hT")

        for fi, (wap, scd_, fw) in enumerate(ftiles):
            par = fi % 2
            s.guard("sync", ("wpr", par))
            t_w = s.dma("sync",
                        wproj[:, par * 2048:(par + 1) * 2048].rearrange("p (k f) -> p k f", k=16),
                        wap[:, fw * 128:(fw + 1) * 128].rearrange("(k p) f -> p k f", p=128),
                        ch=("wpr", par))
            s.wait("tensor", t_w)
            s.wait("tensor", t_hT)
            s.guard("tensor", ("p_pr", par))
            t_mm = None
            for k in range(NKT):
                t_mm = s.emit("tensor", lambda t, k=k, par=par: t.matmul(
                    p_pr[par][:], wproj[:, par * 2048 + k * 128: par * 2048 + (k + 1) * 128],
                    hT[:, k * 512:(k + 1) * 512], start=(k == 0), stop=(k == NKT - 1)))
            s.use(("wpr", par), t_mm)
            s.wait("vector", t_mm)
            s.guard("vector", ("projc", par))
            s.emit("vector", lambda v, par=par: v.tensor_copy(
                projc[:, par * 520 + 3: par * 520 + 515], p_pr[par][:]))
            s.use(("p_pr", par), s.emit("vector", lambda v, fi=fi, par=par: v.tensor_copy(
                projc[:, par * 520: par * 520 + 3], carry[:, fi * 4:fi * 4 + 3])))
            s.emit("vector", lambda v, fi=fi, par=par: v.tensor_copy(
                carry[:, fi * 4:fi * 4 + 3], projc[:, par * 520 + 512: par * 520 + 515]))
            s.wait("vector", t_consts)
            s.emit("vector", lambda v, fi=fi, par=par: v.tensor_scalar(
                out=convo[:, par * 512:(par + 1) * 512],
                in0=projc[:, par * 520: par * 520 + 512],
                scalar1=cw_sb[:, fi * 4 + 0: fi * 4 + 1], scalar2=None, op0=ALU.mult))
            t_cv1 = None
            for j in range(1, 4):
                t_cv1 = s.emit("vector", lambda v, fi=fi, par=par, j=j: v.scalar_tensor_tensor(
                    out=convo[:, par * 512:(par + 1) * 512],
                    in0=projc[:, par * 520 + j: par * 520 + j + 512],
                    scalar=cw_sb[:, fi * 4 + j: fi * 4 + j + 1],
                    in1=convo[:, par * 512:(par + 1) * 512],
                    op0=ALU.mult, op1=ALU.add))
            s.use(("projc", par), t_cv1)
            s.wait("scalar", t_cv1)
            s.guard("scalar", ("silub", par))
            t_si = s.emit("scalar", lambda sc, par=par: sc.activation(
                out=silub[:, par * 512:(par + 1) * 512], in_=convo[:, par * 512:(par + 1) * 512],
                func=AF.Silu))
            s.wait("gpsimd", t_si)
            t_ss = s.dma("gpsimd", scd_[fw * 128:(fw + 1) * 128, c * 512:(c + 1) * 512],
                         silub[:, par * 512:(par + 1) * 512], ch="convst")
            s.use(("silub", par), t_ss)

        for fc in range(6):
            par = fc % 2
            s.guard("sync", ("wmov", par))
            t_wm = s.dma("sync", wmov[:, par * 8192:(par + 1) * 8192]
                         .rearrange("p (k f) -> p k f", k=16),
                         wg[:, fc * 512:(fc + 1) * 512].rearrange("(k p) f -> p k f", p=128),
                         ch=("wmov", par))
            for tt in range(4):
                ppar = tt % 2
                s.wait("tensor", t_wm)
                s.wait("tensor", t_hT)
                s.guard("tensor", ("p_gg", ppar))
                t_mm = None
                for k in range(NKT):
                    t_mm = s.emit("tensor", lambda t, k=k, tt=tt, par=par, ppar=ppar: t.matmul(
                        p_gg[ppar][:], hT[:, k * 512 + tt * 128: k * 512 + (tt + 1) * 128],
                        wmov[:, par * 8192 + k * 512:par * 8192 + (k + 1) * 512],
                        start=(k == 0), stop=(k == NKT - 1)))
                if tt == 3:
                    s.use(("wmov", par), t_mm)
                s.wait("vector", t_mm)
                s.guard("vector", ("ggsb", ppar))
                t_ge = s.emit("vector", lambda v, ppar=ppar: v.tensor_copy(
                    ggsb[:, ppar * 512:(ppar + 1) * 512], p_gg[ppar][:]))
                s.use(("p_gg", ppar), t_ge)
                s.wait("gpsimd", t_ge)
                t_gs = s.dma("gpsimd", gg_d[c * 512 + tt * 128: c * 512 + (tt + 1) * 128,
                                            fc * 512:(fc + 1) * 512],
                             ggsb[:, ppar * 512:(ppar + 1) * 512], ch="ggst")
                s.use(("ggsb", ppar), t_gs)

        s.wait("tensor", t_wsm)
        s.guard("tensor", "p_ba")
        t_mmb = None
        for k in range(NKT):
            t_mmb = s.emit("tensor", lambda t, k=k: t.matmul(
                p_ba[0:12, :], wsmall[:, k * 18:k * 18 + 12],
                hT[:, k * 512:(k + 1) * 512], start=(k == 0), stop=(k == NKT - 1)))
        s.guard("tensor", "p_aa")
        t_mma = None
        for k in range(NKT):
            t_mma = s.emit("tensor", lambda t, k=k: t.matmul(
                p_aa[0:6, :], wsmall[:, k * 18 + 12:k * 18 + 18],
                hT[:, k * 512:(k + 1) * 512], start=(k == 0), stop=(k == NKT - 1)))
        s.wait("scalar", t_mmb)
        s.guard("scalar", "bsp0")
        t_bs = s.emit("scalar", lambda sc: sc.activation(
            out=bsp[0:12, 0:512], in_=p_ba[0:12, :], func=AF.Sigmoid))
        s.use("p_ba", t_bs)
        s.wait("vector", t_mma)
        s.wait("vector", t_negA)
        t_a0 = s.emit("vector", lambda v: v.tensor_scalar(
            out=bsp[0:6, 512:1024], in0=p_aa[0:6, :], scalar1=negA[0:6, 1:2],
            scalar2=None, op0=ALU.add))
        s.use("p_aa", t_a0)
        s.wait("scalar", t_a0)
        s.emit("scalar", lambda sc: sc.activation(
            out=bsp[0:6, 1024:1536], in_=bsp[0:6, 512:1024], func=AF.Abs))
        s.emit("scalar", lambda sc: sc.activation(
            out=bsp[0:6, 1024:1536], in_=bsp[0:6, 1024:1536], func=AF.Exp, scale=cc[0:6, 3:4]))
        t_l = s.emit("scalar", lambda sc: sc.activation(
            out=bsp[0:6, 1024:1536], in_=bsp[0:6, 1024:1536], func=AF.Ln, bias=cc[0:6, 2:3]))
        s.emit("vector", lambda v: v.tensor_scalar(
            out=bsp[0:6, 1536:2048], in0=bsp[0:6, 512:1024], scalar1=0.0, scalar2=None,
            op0=ALU.max))
        s.wait("vector", t_l)
        s.emit("vector", lambda v: v.tensor_tensor(
            out=bsp[0:6, 1536:2048], in0=bsp[0:6, 1536:2048], in1=bsp[0:6, 1024:1536], op=ALU.add))
        t_g = s.emit("vector", lambda v: v.tensor_scalar(
            out=bsp[0:6, 1536:2048], in0=bsp[0:6, 1536:2048], scalar1=negA[0:6, 0:1],
            scalar2=None, op0=ALU.mult))
        for bb in range(4):
            s.wait("tensor", t_bs)
            s.wait("tensor", t_g)
            s.guard("tensor", "p_bt")
            t_t1 = s.emit("tensor", lambda t, bb=bb: t.transpose(
                p_bt[:, 0:12], bsp[0:12, bb * 128:(bb + 1) * 128], idc[0:12, 0:12]))
            s.wait("vector", t_t1)
            s.guard("vector", "bgeo")
            t_e1 = s.emit("vector", lambda v: v.tensor_copy(bgt[:, 0:12], p_bt[:, 0:12]))
            s.use("p_bt", t_e1)
            s.guard("tensor", "p_bt")
            t_t2 = s.emit("tensor", lambda t, bb=bb: t.transpose(
                p_bt[:, 0:6], bsp[0:6, 1536 + bb * 128:1536 + (bb + 1) * 128], idc[0:6, 0:6]))
            s.wait("vector", t_t2)
            t_e2 = s.emit("vector", lambda v: v.tensor_copy(bgt[:, 12:18], p_bt[:, 0:6]))
            s.use("p_bt", t_e2)
            s.guard("vector", "bgeo")
            s.emit("vector", lambda v: v.tensor_copy(bgt[:, 18:24], bgt[:, 6:12]))
            t_asm = s.emit("vector", lambda v: v.tensor_copy(bgt[:, 6:12], bgt[:, 12:18]))
            s.wait("gpsimd", t_asm)
            base = c * 512 + bb * 128
            t_w1 = s.dma("gpsimd", bgze[0, base:base + 128, 0:12], bgt[:, 0:12], ch="bgst")
            t_w2 = s.dma("gpsimd", bgze[1, base:base + 128, 0:6], bgt[:, 18:24], ch="bgst")
            s.wait("gpsimd", t_z6)
            t_w3 = s.dma("gpsimd", bgze[1, base:base + 128, 6:12], zero6[:, 0:6], ch="bgst")
            s.use("bgeo", t_w3)

    t_allconv = s.chan_total("convst")
    for tt2 in range(TOK // 128):
        t0 = tt2 * 128
        s.guard("scalar", "tq")
        s.wait("scalar", t_allconv)
        t_lq = s.dma("scalar", tq[:], qc_d[:, t0:t0 + 128], transpose=True, ch="tq")
        s.guard("scalar", "tk0")
        t_lk0 = s.dma("scalar", tk0[:], k0c_d[:, t0:t0 + 128], transpose=True, ch="tk0")
        s.guard("scalar", "tk1")
        t_lk1 = s.dma("scalar", tk1[:], k1c_d[:, t0:t0 + 128], transpose=True, ch="tk1")
        s.guard("scalar", "tv0")
        t_lv0 = s.dma("scalar", tv0[:], v0c_d[:, t0:t0 + 128], transpose=True, ch="tv0")
        s.guard("scalar", "tv1")
        t_lv1 = s.dma("scalar", tv1[:], v1c_d[:, t0:t0 + 128], transpose=True, ch="tv1")
        for h in range(6):
            s.wait("scalar", t_lq)
            s.emit("scalar", lambda sc, h=h: sc.activation(
                out=sq_junk[:, 0:256], in_=tq[:, h * 256:(h + 1) * 256], func=AF.Square,
                accum_out=nsq[:, h:h + 1]))
            s.wait("scalar", t_lk0)
            s.emit("scalar", lambda sc, h=h: sc.activation(
                out=sq_junk[:, 0:256], in_=tk0[:, h * 256:(h + 1) * 256], func=AF.Square,
                accum_out=nsq[:, 6 + h:7 + h]))
            s.wait("scalar", t_lk1)
            t_sqq = s.emit("scalar", lambda sc, h=h: sc.activation(
                out=sq_junk[:, 0:256], in_=tk1[:, h * 256:(h + 1) * 256], func=AF.Square,
                accum_out=nsq[:, 12 + h:13 + h]))
        s.wait("scalar", t_sqq)
        t_ln1 = s.emit("scalar", lambda sc: sc.activation(
            out=nsq[:, 0:18], in_=nsq[:, 0:18], func=AF.Ln, bias=cc[:, 4:5]))
        t_rn = s.emit("scalar", lambda sc: sc.activation(
            out=nsq[:, 0:18], in_=nsq[:, 0:18], func=AF.Exp, scale=cc[:, 1:2]))
        s.wait("vector", t_rn)
        t_qs = s.emit("vector", lambda v: v.tensor_scalar(
            out=nsq[:, 0:6], in0=nsq[:, 0:6], scalar1=float(DK) ** -0.5, scalar2=None,
            op0=ALU.mult))
        s.guard("vector", "outq")
        t_oq = None
        for h in range(6):
            t_oq = s.emit("vector", lambda v, h=h: v.tensor_scalar(
                out=outq[:, h * 256:(h + 1) * 256], in0=tq[:, h * 256:(h + 1) * 256],
                scalar1=nsq[:, h:h + 1], scalar2=None, op0=ALU.mult))
        s.guard("vector", "outk")
        for h in range(6):
            t_kk = s.emit("vector", lambda v, h=h: v.tensor_scalar(
                out=outk[:, h * 256:(h + 1) * 256], in0=tk0[:, h * 256:(h + 1) * 256],
                scalar1=nsq[:, 6 + h:7 + h], scalar2=None, op0=ALU.mult))
        s.wait("gpsimd", t_kk)
        t_sk0 = s.dma("gpsimd", Kse[0, t0:t0 + 128, :], outk[:], ch="kst")
        s.use("outk", t_sk0)
        s.guard("vector", "outk")
        for h in range(6):
            t_kk = s.emit("vector", lambda v, h=h: v.tensor_scalar(
                out=outk[:, h * 256:(h + 1) * 256], in0=tk1[:, h * 256:(h + 1) * 256],
                scalar1=nsq[:, 12 + h:13 + h], scalar2=None, op0=ALU.mult))
        s.wait("gpsimd", t_kk)
        t_sk1 = s.dma("gpsimd", Kse[1, t0:t0 + 128, :], outk[:], ch="kst")
        s.use("outk", t_sk1)
        s.wait("gpsimd", t_oq)
        t_sq1 = s.dma("gpsimd", Qse[1, t0:t0 + 128, :], outq[:], ch="qst")
        s.use("outq", t_sq1)
        s.wait("gpsimd", t_zq)
        s.dma("gpsimd", Qse[0, t0:t0 + 128, :], zeroq[:], ch="qst")
        s.wait("gpsimd", t_lv0)
        t_sv0 = s.dma("gpsimd", Vse[0, t0:t0 + 128, :], tv0[:], ch="vst")
        s.use("tv0", t_sv0)
        s.wait("gpsimd", t_lv1)
        t_sv1 = s.dma("gpsimd", Vse[1, t0:t0 + 128, :], tv1[:], ch="vst")
        s.use("tv1", t_sv1)
        s.use("tq", t_sq1)
        s.use("tk0", t_sk0)
        s.use("tk1", t_sk1)

    for chn in ("ggst", "bgst", "kst", "qst", "vst"):
        s.wait_self("gpsimd", s.chan_total(chn))
    return nc, s



def build_stage_d(nch=None, nh=6, upto=99, skip_o=False):
    nc = bass.Bass("TRN2", target_bir_lowering=False)
    s = Sched()
    Ks = nc.dram_tensor("Ks", [2 * TOK, KDIM], BF16, kind="ExternalInput").ap()
    Qs = nc.dram_tensor("Qs", [2 * TOK, KDIM], BF16, kind="ExternalInput").ap()
    Vs = nc.dram_tensor("Vs", [2 * TOK, VDIM], BF16, kind="ExternalInput").ap()
    bgz = nc.dram_tensor("bgz", [2 * TOK, 12], F32, kind="ExternalInput").ap()
    Sinit = nc.dram_tensor("Sinit", [KDIM, DV], F32, kind="ExternalInput").ap()
    mup = nc.dram_tensor("mup", [128, 128], F32, kind="ExternalInput").ap()
    mlo = nc.dram_tensor("mlo", [128, 128], F32, kind="ExternalInput").ap()
    identf = nc.dram_tensor("identf", [128, 128], F32, kind="ExternalInput").ap()
    identb = nc.dram_tensor("identb", [128, 128], BF16, kind="ExternalInput").ap()
    ihalf = nc.dram_tensor("ihalf", [128, 512], F32, kind="ExternalInput").ap()
    onesd = nc.dram_tensor("onesd", [6, 128], F32, kind="ExternalInput").ap()
    Sfin = nc.dram_tensor("Sfin", [KDIM, DV], F32, kind="ExternalOutput").ap()
    O_d = nc.dram_tensor("O", [2 * TOK, VDIM], BF16, kind="ExternalOutput").ap()

    sb = lambda n, shp, dt: nc.alloc_sbuf_tensor(n, shp, dt).ap()
    ps = lambda n, shp, dt: nc.alloc_psum_tensor(n, shp, dt).ap()

    Sb = [sb("Sb0", [128, 6 * 2 * 512], BF16), sb("Sb1", [128, 6 * 2 * 512], BF16)]
    Sld = sb("Sld", [128, 512], F32)
    bgld = sb("bgld", [128, 12], F32)
    grow = sb("grow", [6, 5 * 128], F32)
    onesr = sb("onesr", [6, 128], F32)
    gcols = sb("gcols", [128, 18], F32)
    lamC = sb("lamC", [128, 6], F32)
    Kt = sb("Kt", [128, 256], BF16)
    Vt = sb("Vt", [128, 512], BF16)
    Qt = sb("Qt", [128, 256], BF16)
    KTt = sb("KTt", [128, 256], BF16)
    QTt = sb("QTt", [128, 256], BF16)
    gcb = sb("gcb", [128, 128], F32)
    grb = sb("grb", [128, 128], F32)
    lrb = sb("lrb", [128, 128], F32)
    bcb = sb("bcb", [128, 128], F32)
    brb = sb("brb", [128, 128], F32)
    mupt = sb("mupt", [128, 128], F32)
    mlot = sb("mlot", [128, 128], F32)
    idf = sb("idf", [128, 128], F32)
    idb = sb("idb", [128, 128], BF16)
    ihf = sb("ihf", [128, 512], F32)
    dmi = sb("dmi", [128, 128], F32)
    dms = sb("dms", [128, 128], F32)
    dmt = sb("dmt", [128, 128], F32)
    dst_ = sb("dst_", [128, 128], F32)
    lsb = sb("lsb", [128, 128], F32)
    nsb = sb("nsb", [128, 128], BF16)
    ntsb = sb("ntsb", [128, 128], BF16)
    tmpa = sb("tmpa", [128, 128], F32)
    pw = sb("pw", [128, 2 * 128], BF16)
    qpowt = sb("qpowt", [128, 6 * 128], BF16)
    bvt = sb("bvt", [128, 512], BF16)
    bkt = sb("bkt", [128, 256], BF16)
    yw = sb("yw", [128, 512], BF16)
    yr = sb("yr", [128, 256], BF16)
    adt = sb("adt", [128, 128], BF16)
    aqkT = sb("aqkT", [128, 128], F32)
    ket = sb("ket", [128, 256], BF16)
    wet = sb("wet", [128, 512], BF16)
    gtt = sb("gtt", [128, 256], BF16)
    ftt = sb("ftt", [128, 2 * 256], BF16)
    ut = sb("ut", [128, 2 * 512], BF16)
    oev = sb("oev", [128, 512], BF16)
    scol = sb("scol", [128, 8], F32)
    sfo = sb("sfo", [128, 512], F32)

    p_a = ps("p_a", [128, 512], F32)
    p_b = ps("p_b", [128, 512], F32)
    p_w = ps("p_w", [128, 512], F32)
    p_r = ps("p_r", [128, 256], F32)
    p_o = ps("p_o", [128, 512], F32)
    p_s = [ps("p_s0", [128, 512], F32), ps("p_s1", [128, 512], F32)]
    p_g = ps("p_g", [128, 128], F32)

    s.dma("sync", mupt[:], mup[:], ch="consts")
    s.dma("sync", mlot[:], mlo[:], ch="consts")
    s.dma("sync", idf[:], identf[:], ch="consts")
    s.dma("sync", idb[:], identb[:], ch="consts")
    s.dma("sync", ihf[:], ihalf[:], ch="consts")
    s.dma("sync", onesr[:], onesd[:], ch="consts")
    t_consts = s.chan_total("consts")

    for h in range(6):
        for half in range(2):
            t_sl = s.dma("sync", Sld[:], Sinit[h * 256 + half * 128: h * 256 + (half + 1) * 128, :],
                         ch="sinit")
            s.wait("vector", t_sl)
            t_sc = s.emit("vector", lambda v, h=h, half=half: v.tensor_copy(
                Sb[1][:, (h * 2 + half) * 512:(h * 2 + half + 1) * 512], Sld[:]))
            s.wait("sync", t_sc)

    NCH = nch if nch is not None else (2 * TOK) // 128
    t_chain = {}
    for c in range(NCH):
        par = c % 2
        sold = Sb[(c + 1) % 2]
        snew = Sb[c % 2]
        s.guard("sync", "bgld_v")
        s.guard("sync", "bgld_t")
        t_bg = s.dma("sync", bgld[:], bgz[c * 128:(c + 1) * 128, :], ch="bgld")
        s.wait("tensor", t_bg)
        s.wait("vector", t_bg)
        s.wait("tensor", t_consts)
        s.guard("tensor", "p_g")
        t_tp = s.emit("tensor", lambda t: t.transpose(p_g[0:6, :], bgld[:, 6:12], idf[:]))
        s.use("bgld_t", t_tp)
        s.wait("vector", t_tp)
        s.guard("vector", "grow_s")
        s.guard("vector", "grow_t")
        t_ev = s.emit("vector", lambda v: v.tensor_copy(grow[:, 0:128], p_g[0:6, :]))
        s.use("p_g", t_ev)
        t_gam = s.emit("vector", lambda v: v.tensor_tensor_scan(
            out=grow[:, 128:256], data0=onesr[:], data1=grow[:, 0:128], initial=0.0,
            op0=ALU.mult, op1=ALU.add))
        s.wait("scalar", t_gam)
        t_lam = s.emit("scalar", lambda sc: sc.activation(
            out=grow[:, 256:384], in_=grow[:, 128:256], func=AF.Exp))
        t_er = s.emit("scalar", lambda sc: sc.activation(
            out=grow[:, 384:512], in_=grow[:, 128:256], func=AF.Exp, scale=-1.0,
            bias=grow[:, 255:256]))
        s.use("grow_s", t_er)
        for ii, (lo, tok) in enumerate(((128, t_gam), (256, t_lam), (384, t_er))):
            s.wait("tensor", tok)
            s.guard("tensor", "p_g")
            t_tp2 = s.emit("tensor", lambda t, lo=lo: t.transpose(
                p_g[:, 0:6], grow[:, lo:lo + 128], idf[0:6, 0:6]))
            s.use("grow_t", t_tp2)
            s.wait("vector", t_tp2)
            s.guard("vector", ("gcols", ii))
            t_gc = s.emit("vector", lambda v, ii=ii: v.tensor_copy(
                gcols[:, ii * 6:(ii + 1) * 6], p_g[:, 0:6]))
            s.use("p_g", t_gc)
        t_lc = s.emit("vector", lambda v: v.tensor_tensor(
            out=lamC[:], in0=gcols[:, 6:12], in1=gcols[:, 12:18], op=ALU.mult))

        for h in range(nh):
            s.guard("sync", "Kt")
            t_K = s.dma("sync", Kt[:], Ks[c * 128:(c + 1) * 128, h * 256:(h + 1) * 256], ch="ldK")
            s.guard("sync", "Vt")
            t_V = s.dma("sync", Vt[:], Vs[c * 128:(c + 1) * 128, h * 512:(h + 1) * 512], ch="ldV")
            s.guard("scalar", "KTt")
            t_KT = None
            for half in range(2):
                t_KT = s.dma("scalar", KTt[:, half * 128:(half + 1) * 128],
                             Ks[c * 128:(c + 1) * 128, h * 256 + half * 128: h * 256 + (half + 1) * 128],
                             transpose=True, ch="ldKT")
            t_QT = None
            if not skip_o:
                s.guard("scalar", "QTt")
                for half in range(2):
                    t_QT = s.dma("scalar", QTt[:, half * 128:(half + 1) * 128],
                                 Qs[c * 128:(c + 1) * 128, h * 256 + half * 128: h * 256 + (half + 1) * 128],
                                 transpose=True, ch="ldQT")
            if upto < 1:
                continue
            s.wait("vector", t_gc)
            s.guard("vector", "gcb")
            t_gcb = s.emit("vector", lambda v, h=h: v.tensor_scalar(
                out=gcb[:], in0=mupt[:], scalar1=0.0, scalar2=gcols[:, h:h + 1],
                op0=ALU.mult, op1=ALU.add))
            s.wait("tensor", t_gcb)
            s.guard("tensor", "p_g")
            t_tgrb = s.emit("tensor", lambda t: t.transpose(p_g[:], gcb[:], idf[:]))
            s.wait("vector", t_tgrb)
            s.guard("vector", "grb")
            t_grb = s.emit("vector", lambda v: v.tensor_copy(grb[:], p_g[:]))
            s.use("p_g", t_grb)
            t_bcb = s.emit("vector", lambda v, h=h: v.tensor_scalar(
                out=bcb[:], in0=mupt[:], scalar1=0.0, scalar2=bgld[:, h:h + 1],
                op0=ALU.mult, op1=ALU.add))
            s.wait("tensor", t_bcb)
            s.guard("tensor", "p_g")
            t_tbrb = s.emit("tensor", lambda t: t.transpose(p_g[:], bcb[:], idf[:]))
            s.wait("vector", t_tbrb)
            s.guard("vector", "brb")
            t_brb = s.emit("vector", lambda v: v.tensor_copy(brb[:], p_g[:]))
            s.use("p_g", t_brb)
            t_lrb = None
            if not skip_o:
                s.wait("scalar", t_grb)
                s.guard("scalar", "lrb")
                t_lrb = s.emit("scalar", lambda sc: sc.activation(out=lrb[:], in_=grb[:], func=AF.Exp))
            if upto < 2:
                continue
            s.guard("vector", "dmi")
            t_v1 = s.emit("vector", lambda v, h=h: v.scalar_tensor_tensor(
                out=dmi[:], in0=grb[:], scalar=gcols[:, h:h + 1], in1=mupt[:],
                op0=ALU.subtract, op1=ALU.add))
            s.wait("scalar", t_v1)
            t_dmi = s.emit("scalar", lambda sc: sc.activation(
                out=dmi[:], in_=dmi[:], func=AF.Exp, scale=-1.0))
            s.wait("vector", t_dmi)
            t_dms = s.emit("vector", lambda v: v.tensor_tensor(
                out=dms[:], in0=dmi[:], in1=idf[:], op=ALU.subtract))
            s.guard("vector", "dmt")
            t_v2 = s.emit("vector", lambda v, h=h: v.scalar_tensor_tensor(
                out=dmt[:], in0=grb[:], scalar=gcols[:, h:h + 1], in1=mlot[:],
                op0=ALU.subtract, op1=ALU.subtract))
            s.wait("scalar", t_v2)
            t_dmt = s.emit("scalar", lambda sc: sc.activation(
                out=dmt[:], in_=dmt[:], func=AF.Exp))
            s.wait("vector", t_dmt)
            t_dst = s.emit("vector", lambda v: v.tensor_tensor(
                out=dst_[:], in0=dmt[:], in1=idf[:], op=ALU.subtract))
            if upto < 3:
                continue
            s.wait("tensor", t_KT)
            s.guard("tensor", "p_a")
            t_mL = None
            for half in range(2):
                t_mL = s.emit("tensor", lambda t, half=half: t.matmul(
                    p_a[:, 0:128], KTt[:, half * 128:(half + 1) * 128],
                    KTt[:, half * 128:(half + 1) * 128], start=(half == 0), stop=(half == 1)))
            if not skip_o:
                s.wait("tensor", t_QT)
                s.guard("tensor", "p_b")
                t_mA = None
                for half in range(2):
                    t_mA = s.emit("tensor", lambda t, half=half: t.matmul(
                        p_b[:, 0:128], KTt[:, half * 128:(half + 1) * 128],
                        QTt[:, half * 128:(half + 1) * 128], start=(half == 0), stop=(half == 1)))
                s.use("KTt", t_mA)
                s.use("QTt", t_mA)
            s.wait("vector", t_mL)
            s.guard("vector", "lsb")
            t_lsb = s.emit("vector", lambda v: v.tensor_copy(lsb[:], p_a[:, 0:128]))
            s.use("p_a", t_lsb)
            if not skip_o:
                s.wait("vector", t_mA)
                s.guard("vector", "aqkT")
                t_aqkT = s.emit("vector", lambda v: v.tensor_copy(aqkT[:], p_b[:, 0:128]))
                s.use("p_b", t_aqkT)
            if upto < 4:
                continue
            s.wait("vector", t_dms)
            s.guard("vector", "nsb")
            t_n = s.emit("vector", lambda v, h=h: v.scalar_tensor_tensor(
                out=nsb[:], in0=lsb[:], scalar=bgld[:, h:h + 1], in1=dms[:],
                op0=ALU.mult, op1=ALU.mult))
            s.wait("vector", t_dst)
            s.guard("vector", "tmpa")
            s.emit("vector", lambda v: v.tensor_tensor(
                out=tmpa[:], in0=lsb[:], in1=dst_[:], op=ALU.mult))
            s.guard("vector", "ntsb")
            t_nt = s.emit("vector", lambda v: v.tensor_tensor(
                out=ntsb[:], in0=tmpa[:], in1=brb[:], op=ALU.mult))
            t_adt = None
            if not skip_o:
                s.guard("vector", "adt")
                t_adt = s.emit("vector", lambda v: v.tensor_tensor(
                    out=adt[:], in0=aqkT[:], in1=dmt[:], op=ALU.mult))
            t_bl = s.emit("vector", lambda v, h=h: v.tensor_tensor(
                out=scol[:, 0:1], in0=bgld[:, h:h + 1], in1=gcols[:, 6 + h:7 + h], op=ALU.mult))
            s.wait("vector", t_V)
            s.guard("vector", "bvt")
            t_bv = s.emit("vector", lambda v, h=h: v.tensor_scalar(
                out=bvt[:], in0=Vt[:], scalar1=bgld[:, h:h + 1], scalar2=None, op0=ALU.mult))
            s.use("Vt", t_bv)
            if h == nh - 1:
                s.use("bgld_v", t_bv)
            s.wait("vector", t_K)
            s.guard("vector", "bkt")
            t_bk = s.emit("vector", lambda v: v.tensor_scalar(
                out=bkt[:], in0=Kt[:], scalar1=scol[:, 0:1], scalar2=None, op0=ALU.mult))
            s.guard("vector", "yw")
            t_yw = s.emit("vector", lambda v: v.tensor_copy(yw[:], bvt[:]))
            s.guard("vector", "yr")
            t_yr = s.emit("vector", lambda v: v.tensor_copy(yr[:], bkt[:]))
            if upto < 5:
                continue
            qpow = qpowt
            t_p_prev = s.emit("vector", lambda v: v.tensor_copy(pw[:, 0:128], nsb[:]))
            t_q_prev = t_nt
            t_qs = {0: t_q_prev}
            for lev in range(1, 7):
                s.wait("tensor", t_p_prev)
                s.wait("tensor", t_q_prev)
                s.guard("tensor", "p_b")
                qprev = ntsb if lev == 1 else qpow[:, (lev - 2) * 128:(lev - 1) * 128]
                t_mp = s.emit("tensor", lambda t, qprev=qprev: t.matmul(
                    p_b[:, 0:128], qprev, pw[:, 0:128], start=True, stop=True))
                t_mq = s.emit("tensor", lambda t, qprev=qprev: t.matmul(
                    p_b[:, 128:256], pw[:, 0:128], qprev, start=True, stop=True))
                s.wait("vector", t_mq)
                t_p_prev = s.emit("vector", lambda v: v.tensor_copy(pw[:, 0:128], p_b[:, 0:128]))
                t_q_prev = s.emit("vector", lambda v, lev=lev: v.tensor_copy(
                    qpow[:, (lev - 1) * 128:lev * 128], p_b[:, 128:256]))
                s.use("p_b", t_q_prev)
                t_qs[lev] = t_q_prev
            if upto < 6:
                continue
            for lev in range(6, 0, -1):
                s.wait("tensor", t_qs[lev])
                s.wait("tensor", t_yw)
                s.wait("tensor", t_yr)
                s.guard("tensor", "p_w")
                s.guard("tensor", "p_r")
                t_mw = s.emit("tensor", lambda t, lev=lev: t.matmul(
                    p_w[:], qpow[:, (lev - 1) * 128:lev * 128], yw[:], start=True, stop=False))
                t_mw = s.emit("tensor", lambda t: t.matmul(
                    p_w[:], idb[:], yw[:], start=False, stop=True))
                t_mr = s.emit("tensor", lambda t, lev=lev: t.matmul(
                    p_r[:], qpow[:, (lev - 1) * 128:lev * 128], yr[:], start=True, stop=False))
                t_mr = s.emit("tensor", lambda t: t.matmul(
                    p_r[:], idb[:], yr[:], start=False, stop=True))
                s.wait("vector", t_mw)
                t_yw = s.emit("vector", lambda v: v.tensor_copy(yw[:], p_w[:]))
                s.use("p_w", t_yw)
                s.wait("vector", t_mr)
                t_yr = s.emit("vector", lambda v: v.tensor_copy(yr[:], p_r[:]))
                s.use("p_r", t_yr)
            s.wait("tensor", t_n)
            s.wait("tensor", t_yw)
            s.wait("tensor", t_yr)
            s.guard("tensor", "p_w")
            s.guard("tensor", "p_r")
            t_mw = s.emit("tensor", lambda t: t.matmul(p_w[:], ntsb[:], yw[:], start=True, stop=True))
            t_mr = s.emit("tensor", lambda t: t.matmul(p_r[:], ntsb[:], yr[:], start=True, stop=True))
            s.use("ntsb", t_mr)
            s.use("nsb", t_mr)
            s.wait("vector", t_mw)
            t_yw = s.emit("vector", lambda v: v.tensor_tensor(
                out=yw[:], in0=yw[:], in1=p_w[:], op=ALU.subtract))
            s.use("p_w", t_yw)
            s.wait("vector", t_mr)
            t_yr = s.emit("vector", lambda v: v.tensor_tensor(
                out=yr[:], in0=yr[:], in1=p_r[:], op=ALU.subtract))
            s.use("p_r", t_yr)
            if upto < 7:
                continue
            s.guard("vector", "wet")
            t_we = s.emit("vector", lambda v, h=h: v.tensor_scalar(
                out=wet[:], in0=yw[:], scalar1=gcols[:, 12 + h:13 + h], scalar2=None, op0=ALU.mult))
            s.guard("vector", "ket")
            t_ke = s.emit("vector", lambda v, h=h: v.tensor_scalar(
                out=ket[:], in0=Kt[:], scalar1=gcols[:, 12 + h:13 + h], scalar2=None, op0=ALU.mult))
            if h == nh - 1:
                for ii2 in range(3):
                    s.use(("gcols", ii2), t_ke)
                s.use("lamC_rd", t_ke)
            for half in range(2):
                s.wait("tensor", t_we)
                s.guard("tensor", "p_w")
                t_mu = s.emit("tensor", lambda t, half=half: t.matmul(
                    p_w[:], Kt[:, half * 128:(half + 1) * 128], wet[:], start=True, stop=True))
                s.wait("vector", t_mu)
                t_u = s.emit("vector", lambda v, half=half: v.tensor_copy(
                    ut[:, half * 512:(half + 1) * 512], p_w[:]))
                s.use("p_w", t_u)
                if half == 1:
                    s.use("Kt", t_mu)
            for half in range(2):
                s.wait("tensor", t_ke)
                s.wait("tensor", t_yr)
                s.guard("tensor", "p_r")
                s.guard("tensor", "p_a")
                t_mf = s.emit("tensor", lambda t, half=half: t.matmul(
                    p_a[:, 0:256], yr[:, half * 128:(half + 1) * 128], ket[:],
                    start=True, stop=True))
                s.wait("vector", t_mf)
                s.wait("vector", t_lc)
                t_ft = s.emit("vector", lambda v, half=half, h=h: v.scalar_tensor_tensor(
                    out=ftt[:, half * 256:(half + 1) * 256], in0=ihf[:, half * 256:(half + 1) * 256],
                    scalar=lamC[:, h:h + 1], in1=p_a[:, 0:256], op0=ALU.mult, op1=ALU.subtract))
                s.use("p_a", t_ft)
            for half in (range(2) if not skip_o else ()):
                s.wait("tensor", t_adt)
                s.wait("tensor", t_yr)
                s.guard("tensor", "p_a")
                t_mg = s.emit("tensor", lambda t, half=half: t.matmul(
                    p_a[:, 0:128], yr[:, half * 128:(half + 1) * 128], adt[:],
                    start=True, stop=True))
                s.wait("vector", t_mg)
                s.wait("vector", t_lrb)
                s.guard("vector", "tmpa")
                s.emit("vector", lambda v, half=half: v.tensor_tensor(
                    out=tmpa[:], in0=QTt[:, half * 128:(half + 1) * 128], in1=lrb[:], op=ALU.mult))
                t_gt = s.emit("vector", lambda v, half=half: v.tensor_tensor(
                    out=gtt[:, half * 128:(half + 1) * 128], in0=tmpa[:], in1=p_a[:, 0:128],
                    op=ALU.subtract))
                s.use("p_a", t_gt)
                if half == 1:
                    s.use("QTt", t_gt)
            if upto < 8:
                continue
            if not skip_o:
                s.wait("tensor", t_adt)
                s.wait("tensor", t_yw)
                s.guard("tensor", "p_o")
                s.emit("tensor", lambda t: t.matmul(p_o[:], adt[:], yw[:], start=True, stop=False))
                t_chn_prev = t_chain.get((c - 1, h))
                if t_chn_prev is not None:
                    s.wait("tensor", t_chn_prev)
                s.wait("tensor", t_gt)
                t_mo = None
                for half in range(2):
                    t_mo = s.emit("tensor", lambda t, half=half, h=h, sold=sold: t.matmul(
                        p_o[:], gtt[:, half * 128:(half + 1) * 128],
                        sold[:, (h * 2 + half) * 512:(h * 2 + half + 1) * 512],
                        start=False, stop=(half == 1)))
                s.wait("vector", t_mo)
                s.guard("vector", "oev")
                t_oev = s.emit("vector", lambda v: v.tensor_copy(oev[:], p_o[:]))
                s.use("p_o", t_oev)
                s.wait("gpsimd", t_oev)
                t_ost = s.dma("gpsimd", O_d[c * 128:(c + 1) * 128, h * 512:(h + 1) * 512], oev[:],
                              ch="ost")
                s.use("oev", t_ost)
                if upto < 9:
                    t_chain[(c, h)] = t_chain.get((c - 1, h))
                    continue
            if skip_o:
                t_chn_prev = t_chain.get((c - 1, h))
                if t_chn_prev is not None:
                    s.wait("tensor", t_chn_prev)
            import os as _os
            _cv = int(_os.environ.get("CHAINVAR", "0"))
            for ihalf_ in range(2):
                pS = p_s[ihalf_]
                s.guard("tensor", ("p_s", ihalf_))
                s.wait("tensor", t_ft)
                t_mc = None
                for khalf in range(2):
                    t_mc = s.emit("tensor", lambda t, ihalf_=ihalf_, khalf=khalf, h=h, pS=pS,
                                  sold=sold: t.matmul(
                        pS[:], ftt[:, khalf * 256 + ihalf_ * 128: khalf * 256 + (ihalf_ + 1) * 128],
                        sold[:, (h * 2 + khalf) * 512:(h * 2 + khalf + 1) * 512],
                        start=(khalf == 0), stop=(_cv == 1)))
                if _cv != 1:
                    s.wait("tensor", t_u)
                    t_mc = s.emit("tensor", lambda t, ihalf_=ihalf_, pS=pS: t.matmul(
                        pS[:], idb[:], ut[:, ihalf_ * 512:(ihalf_ + 1) * 512], start=False, stop=True))
                s.wait("vector", t_mc)
                t_sn = s.emit("vector", lambda v, ihalf_=ihalf_, h=h, pS=pS, snew=snew: v.tensor_copy(
                    snew[:, (h * 2 + ihalf_) * 512:(h * 2 + ihalf_ + 1) * 512], pS[:]))
                s.use(("p_s", ihalf_), t_sn)
                t_chain[(c, h)] = t_sn

    import os as _os2
    for h in range(nh if _os2.environ.get("NOSF") != "1" else 0):
        for half in range(2):
            if t_chain.get((NCH - 1, h)) is None:
                continue
            s.wait("vector", t_chain[(NCH - 1, h)])
            s.guard("vector", "sfo")
            t_sf = s.emit("vector", lambda v, h=h, half=half, Sb=Sb, NCH=NCH: v.tensor_copy(
                sfo[:], Sb[(NCH - 1) % 2][:, (h * 2 + half) * 512:(h * 2 + half + 1) * 512]))
            s.wait("gpsimd", t_sf)
            t_sfd = s.dma("gpsimd", Sfin[h * 256 + half * 128: h * 256 + (half + 1) * 128, :], sfo[:],
                  ch="sfst")
            s.use("sfo", t_sfd)
    s.wait_self("gpsimd", s.chan_total("ost"))
    s.wait_self("gpsimd", s.chan_total("sfst"))
    return nc, s



def build_stage_e(debug=False):
    nc = bass.Bass("TRN2", target_bir_lowering=False)
    s = Sched()
    Od = nc.dram_tensor("Od", [2 * TOK, VDIM], BF16, kind="ExternalInput").ap()
    gg_d = nc.dram_tensor("gg", [TOK, VDIM], BF16, kind="ExternalInput").ap()
    x2r = nc.dram_tensor("x2rows", [TOK, DM], F32, kind="ExternalInput").ap()
    wo = nc.dram_tensor("wo", [VDIM, DM], BF16, kind="ExternalInput").ap()
    onwb = nc.dram_tensor("onwb", [128, VDIM], BF16, kind="ExternalInput").ap()
    gTw = nc.dram_tensor("gT", [DM, INTER], BF16, kind="ExternalInput").ap()
    uTw = nc.dram_tensor("uT", [DM, INTER], BF16, kind="ExternalInput").ap()
    dTw = nc.dram_tensor("dT", [INTER, DM], BF16, kind="ExternalInput").ap()
    gsc = nc.dram_tensor("gsc", [128, 64], F32, kind="ExternalInput").ap()
    usc = nc.dram_tensor("usc", [128, 64], F32, kind="ExternalInput").ap()
    constse = nc.dram_tensor("constse", [128, 6], F32, kind="ExternalInput").ap()
    xout = nc.dram_tensor("xout", [TOK, DM], F32, kind="ExternalOutput").ap()
    gat_d = nc.dram_tensor("gat_sc", [TOK, VDIM], BF16, kind="Internal").ap()
    if debug:
        dbg_x3 = nc.dram_tensor("dbg_x3", [512, DM], F32, kind="ExternalOutput").ap()
        dbg_z = nc.dram_tensor("dbg_z", [INTER, 512], BF16, kind="ExternalOutput").ap()
        dbg_gat = nc.dram_tensor("dbg_gat", [TOK, VDIM], BF16, kind="ExternalOutput").ap()
    xln_d = nc.dram_tensor("xln3_sc", [TOK, DM], BF16, kind="Internal").ap()
    z_d = nc.dram_tensor("z_sc", [INTER, TOK], BF16, kind="Internal").ap()
    Ode = Od.rearrange("(t two) f -> two t f", two=2)

    sb = lambda n, shp, dt: nc.alloc_sbuf_tensor(n, shp, dt).ap()
    ps = lambda n, shp, dt: nc.alloc_psum_tensor(n, shp, dt).ap()

    ot = sb("ot", [128, VDIM], BF16)
    ggt = sb("ggt", [128, VDIM], BF16)
    sgg = sb("sgg", [128, VDIM], BF16)
    onw_sb = sb("onw_sb", [128, VDIM], BF16)
    sqj = sb("sqj", [128, 512], BF16)
    nsq = sb("nsq", [128, 8], F32)
    cc = sb("cc", [128, 6], F32)
    gatb = sb("gatb", [128, VDIM], BF16)
    x2t = sb("x2t", [128, 2 * 512], F32)
    x3t = sb("x3t", [128, 4 * DM], F32)
    gatT = sb("gatT", [128, 24 * 512], BF16)
    wo_sb = sb("wo_sb", [128, 24 * 512], BF16)
    xlnbf = sb("xlnbf", [128, DM], BF16)
    sq_junk = sb("sq_junk", [128, DM], BF16)
    stats = sb("stats", [128, 8], F32)
    xlnT = sb("xlnT", [128, 16 * 512], BF16)
    zsb = sb("zsb", [128, 2 * 512], BF16)
    zld = sb("zld", [128, 2 * 512], BF16)
    wg_sb = sb("wg_sb", [128, 2 * 2048], BF16)
    wu_sb = sb("wu_sb", [128, 2 * 2048], BF16)
    wd_sb = sb("wd_sb", [128, 16 * 512], BF16)
    up_sb = sb("up_sb", [128, 2 * 512], F32)
    gs_sb = sb("gs_sb", [128, 2 * 512], F32)
    gsc_sb = sb("gsc_sb", [128, 64], F32)
    usc_sb = sb("usc_sb", [128, 64], F32)
    x4s = sb("x4s", [128, 2 * 512], F32)

    p_wo = [ps("p_wo0", [128, 512], F32), ps("p_wo1", [128, 512], F32)]
    p_g = [ps("p_gA", [128, 512], F32), ps("p_gB", [128, 512], F32)]
    p_u = [ps("p_uA", [128, 512], F32), ps("p_uB", [128, 512], F32)]
    p_dn = [p_g[0], p_g[1], p_u[0], p_u[1]]
    dn_keys = [("p_g", 0), ("p_g", 1), ("p_u", 0), ("p_u", 1)]

    s.dma("sync", onw_sb[:], onwb[:], ch="consts")
    s.dma("sync", cc[:], constse[:], ch="consts")
    s.dma("sync", gsc_sb[:], gsc[:], ch="consts")
    s.dma("sync", usc_sb[:], usc[:], ch="consts")
    t_consts = s.chan_total("consts")

    NCH = TOK // 512
    for c in range(NCH):
        for tt in range(4):
            t0 = c * 512 + tt * 128
            s.guard("sync", "ot")
            t_o = s.dma("sync", ot[:], Ode[1, t0:t0 + 128, :], ch="oldc")
            s.guard("sync", "ggt")
            t_g = s.dma("sync", ggt[:], gg_d[t0:t0 + 128, :], ch="ggld")
            s.wait("scalar", t_o)
            for h in range(6):
                s.emit("scalar", lambda sc, h=h: sc.activation(
                    out=sqj[:, 0:512], in_=ot[:, h * 512:(h + 1) * 512], func=AF.Square,
                    accum_out=nsq[:, h:h + 1]))
            s.wait("scalar", t_consts)
            s.emit("scalar", lambda sc: sc.activation(
                out=nsq[:, 0:6], in_=nsq[:, 0:6], func=AF.Ln, scale=cc[:, 2:3], bias=cc[:, 0:1]))
            t_rms = s.emit("scalar", lambda sc: sc.activation(
                out=nsq[:, 0:6], in_=nsq[:, 0:6], func=AF.Exp, scale=cc[:, 1:2]))
            s.wait("scalar", t_g)
            s.guard("scalar", "sgg")
            t_sg = s.emit("scalar", lambda sc: sc.activation(
                out=sgg[:], in_=ggt[:], func=AF.Silu))
            s.use("ggt", t_sg)
            s.wait("vector", t_sg)
            s.wait("vector", t_consts)
            t_m1 = s.emit("vector", lambda v: v.tensor_tensor(
                out=sgg[:], in0=sgg[:], in1=onw_sb[:], op=ALU.mult))
            s.use("sgg", t_m1)
            s.wait("vector", t_rms)
            s.guard("vector", "gatb")
            for h in range(6):
                s.emit("vector", lambda v, h=h: v.tensor_scalar(
                    out=gatb[:, h * 512:(h + 1) * 512], in0=ot[:, h * 512:(h + 1) * 512],
                    scalar1=nsq[:, h:h + 1], scalar2=None, op0=ALU.mult))
            t_gb = s.emit("vector", lambda v: v.tensor_tensor(
                out=gatb[:], in0=gatb[:], in1=sgg[:], op=ALU.mult))
            s.use("ot", t_gb)
            s.wait("gpsimd", t_gb)
            t_gst = s.dma("gpsimd", gat_d[t0:t0 + 128, :], gatb[:], ch="gatst")
            s.use("gatb", t_gst)
    t_gat_all = s.chan_total("gatst")
    if debug:
        s.wait("gpsimd", t_gat_all)
        s.dma("gpsimd", dbg_gat[:], gat_d[:], ch="dbg")

    for c in range(NCH):
        for k in range(24):
            s.guard("scalar", ("gatT", k))
            s.wait("scalar", t_gat_all)
            s.dma("scalar", gatT[:, k * 512:(k + 1) * 512],
                  gat_d[c * 512:(c + 1) * 512, k * 128:(k + 1) * 128], transpose=True, ch="gatT")
        t_gT = s.chan_total("gatT")
        t_x3f = {}
        for oc in range(4):
            s.guard("sync", "wo_sb")
            t_wo = s.dma("sync",
                         wo_sb.rearrange("p (k f) -> p k f", k=24),
                         wo[:, oc * 512:(oc + 1) * 512].rearrange("(k p) f -> p k f", p=128),
                         ch="wold")
            t_wo_used = None
            for tt in range(4):
                ppar = tt % 2
                s.guard("sync", ("x2t", ppar))
                t_x2 = s.dma("sync", x2t[:, ppar * 512:(ppar + 1) * 512],
                             x2r[c * 512 + tt * 128:c * 512 + (tt + 1) * 128,
                                 oc * 512:(oc + 1) * 512], ch=("x2l", ppar))
                s.wait("tensor", t_wo)
                s.wait("tensor", t_gT)
                s.guard("tensor", ("p_wo", ppar))
                t_mm = None
                for k in range(24):
                    t_mm = s.emit("tensor", lambda t, k=k, tt=tt, ppar=ppar: t.matmul(
                        p_wo[ppar][:], gatT[:, k * 512 + tt * 128:k * 512 + (tt + 1) * 128],
                        wo_sb[:, k * 512:(k + 1) * 512], start=(k == 0), stop=(k == 23)))
                t_wo_used = t_mm
                if oc == 3 and tt == 3:
                    for k in range(24):
                        s.use(("gatT", k), t_mm)
                s.wait("vector", t_mm)
                s.wait("vector", t_x2)
                s.guard("vector", ("x3s", tt))
                s.guard("vector", ("x3rd", tt))
                t_x3 = s.emit("vector", lambda v, tt=tt, oc=oc, ppar=ppar: v.tensor_tensor(
                    out=x3t[:, tt * DM + oc * 512:tt * DM + (oc + 1) * 512],
                    in0=p_wo[ppar][:], in1=x2t[:, ppar * 512:(ppar + 1) * 512], op=ALU.add))
                s.use(("p_wo", ppar), t_x3)
                s.use(("x2t", ppar), t_x3)
                if oc == 3:
                    t_x3f[tt] = t_x3
            s.use("wo_sb", t_wo_used)
        if debug and c == 0:
            for tt in range(4):
                s.wait("gpsimd", t_x3f[tt])
                s.dma("gpsimd", dbg_x3[tt * 128:(tt + 1) * 128, :], x3t[:, tt * DM:(tt + 1) * DM],
                      ch="dbg")
            t_dt_ = s.chan_total("dbg")
            for e in ("sync", "scalar", "vector", "tensor"):
                s.wait(e, t_dt_)
        t_lnst = {}
        for tt in range(4):
            s.emit("vector", lambda v, tt=tt: v.reduce_sum(
                stats[:, 0:1], x3t[:, tt * DM:(tt + 1) * DM], axis=AX.X))
            s.wait("scalar", t_x3f[tt])
            t_sq = s.emit("scalar", lambda sc, tt=tt: sc.activation(
                out=sq_junk[:], in_=x3t[:, tt * DM:(tt + 1) * DM], func=AF.Square,
                accum_out=stats[:, 1:2]))
            s.use(("x3s", tt), t_sq)
            s.wait("vector", t_sq)
            s.emit("vector", lambda v: v.tensor_scalar(
                out=stats[:, 2:3], in0=stats[:, 0:1], scalar1=1.0 / DM, scalar2=None, op0=ALU.mult))
            s.emit("vector", lambda v: v.tensor_scalar(
                out=stats[:, 5:6], in0=stats[:, 1:2], scalar1=1.0 / DM, scalar2=None, op0=ALU.mult))
            s.emit("vector", lambda v: v.tensor_tensor(
                out=stats[:, 3:4], in0=stats[:, 2:3], in1=stats[:, 2:3], op=ALU.mult))
            t_var = s.emit("vector", lambda v: v.tensor_tensor(
                out=stats[:, 3:4], in0=stats[:, 5:6], in1=stats[:, 3:4], op=ALU.subtract))
            s.wait("scalar", t_var)
            s.emit("scalar", lambda sc: sc.activation(
                out=stats[:, 4:5], in_=stats[:, 3:4], func=AF.Ln, bias=cc[:, 0:1]))
            t_rstd = s.emit("scalar", lambda sc: sc.activation(
                out=stats[:, 4:5], in_=stats[:, 4:5], func=AF.Exp, scale=cc[:, 1:2]))
            s.wait("vector", t_rstd)
            s.guard("vector", "xlnbf")
            t_ln = s.emit("vector", lambda v, tt=tt: v.tensor_scalar(
                out=xlnbf[:], in0=x3t[:, tt * DM:(tt + 1) * DM], scalar1=stats[:, 2:3],
                scalar2=stats[:, 4:5], op0=ALU.subtract, op1=ALU.mult))
            s.wait("gpsimd", t_ln)
            t_xst = s.dma("gpsimd", xln_d[c * 512 + tt * 128:c * 512 + (tt + 1) * 128, :],
                          xlnbf[:], ch="xlnst")
            s.use("xlnbf", t_xst)
            t_lnst[tt] = t_xst
        t_allst = s.chan_total("xlnst")
        for k in range(NKT):
            s.guard("scalar", ("xlnT", k))
            s.wait("scalar", t_allst)
            s.dma("scalar", xlnT[:, k * 512:(k + 1) * 512],
                  xln_d[c * 512:(c + 1) * 512, k * 128:(k + 1) * 128], transpose=True, ch="xlnT")
        t_xt_all = s.chan_total("xlnT")

        for i in range(64):
            par = i % 2
            s.guard("sync", ("wg", par))
            t_wg = s.dma("sync",
                         wg_sb[:, par * 2048:(par + 1) * 2048].rearrange("p (k f) -> p k f", k=16),
                         gTw[:, i * 128:(i + 1) * 128].rearrange("(k p) f -> p k f", p=128),
                         ch=("wg", par))
            s.guard("sync", ("wu", par))
            t_wu = s.dma("sync",
                         wu_sb[:, par * 2048:(par + 1) * 2048].rearrange("p (k f) -> p k f", k=16),
                         uTw[:, i * 128:(i + 1) * 128].rearrange("(k p) f -> p k f", p=128),
                         ch=("wu", par))
            s.wait("tensor", t_wg)
            s.wait("tensor", t_wu)
            s.wait("tensor", t_xt_all)
            s.guard("tensor", ("p_g", par))
            s.guard("tensor", ("p_u", par))
            for k in range(NKT):
                s.emit("tensor", lambda t, i=i, k=k, par=par: t.matmul(
                    p_g[par][:], wg_sb[:, par * 2048 + k * 128:par * 2048 + (k + 1) * 128],
                    xlnT[:, k * 512:(k + 1) * 512], start=(k == 0), stop=(k == NKT - 1)))
            t_mm2 = None
            for k in range(NKT):
                t_mm2 = s.emit("tensor", lambda t, i=i, k=k, par=par: t.matmul(
                    p_u[par][:], wu_sb[:, par * 2048 + k * 128:par * 2048 + (k + 1) * 128],
                    xlnT[:, k * 512:(k + 1) * 512], start=(k == 0), stop=(k == NKT - 1)))
            s.use(("wg", par), t_mm2)
            s.use(("wu", par), t_mm2)
            if i >= 62:
                for k in range(NKT):
                    s.use(("xlnT", k), t_mm2)
            s.wait("scalar", t_mm2)
            s.guard("scalar", ("gs", par))
            t_gs = s.emit("scalar", lambda sc, i=i, par=par: sc.activation(
                out=gs_sb[:, par * 512:(par + 1) * 512], in_=p_g[par][:],
                func=AF.Silu, scale=gsc_sb[:, i:i + 1]))
            s.use(("p_g", par), t_gs)
            s.wait("vector", t_mm2)
            s.guard("vector", ("up", par))
            t_up = s.emit("vector", lambda v, i=i, par=par: v.tensor_scalar(
                out=up_sb[:, par * 512:(par + 1) * 512], in0=p_u[par][:],
                scalar1=usc_sb[:, i:i + 1], scalar2=None, op0=ALU.mult))
            s.use(("p_u", par), t_up)
            s.wait("vector", t_gs)
            s.guard("vector", ("zsb", par))
            t_z = s.emit("vector", lambda v, par=par: v.tensor_tensor(
                out=zsb[:, par * 512:(par + 1) * 512], in0=up_sb[:, par * 512:(par + 1) * 512],
                in1=gs_sb[:, par * 512:(par + 1) * 512], op=ALU.mult))
            s.use(("gs", par), t_z)
            s.use(("up", par), t_z)
            s.wait("gpsimd", t_z)
            t_zst = s.dma("gpsimd", z_d[i * 128:(i + 1) * 128, c * 512:(c + 1) * 512],
                          zsb[:, par * 512:(par + 1) * 512], ch="zst")
            s.use(("zsb", par), t_zst)
        t_z_all = s.chan_total("zst")
        if debug and c == 0:
            s.wait("gpsimd", t_z_all)
            s.dma("gpsimd", dbg_z[:, :], z_d[:, 0:512], ch="dbg")
            t_dt2 = s.chan_total("dbg")
            for e in ("sync", "scalar", "vector", "tensor"):
                s.wait(e, t_dt2)

        for oc in range(4):
            s.guard("sync", "wd_sb")
            t_wd = s.dma("sync",
                         wd_sb.rearrange("p (k f) -> p k f", k=16),
                         dTw[0:2048, oc * 512:(oc + 1) * 512].rearrange("(k p) f -> p k f", p=128),
                         ch="wdld")
            t_dn = {}
            for ig in range(4):
                if ig > 0:
                    s.guard("sync", "wd_sb")
                    t_wd = s.dma("sync",
                                 wd_sb.rearrange("p (k f) -> p k f", k=16),
                                 dTw[ig * 2048:(ig + 1) * 2048, oc * 512:(oc + 1) * 512]
                                 .rearrange("(k p) f -> p k f", p=128), ch="wdld")
                for ii in range(16):
                    i = ig * 16 + ii
                    zpar = ii % 2
                    s.guard("sync", ("zld", zpar))
                    s.wait("sync", t_z_all)
                    t_zl = s.dma("sync", zld[:, zpar * 512:(zpar + 1) * 512],
                                 z_d[i * 128:(i + 1) * 128, c * 512:(c + 1) * 512],
                                 ch=("zld", zpar))
                    s.wait("tensor", t_zl)
                    s.wait("tensor", t_wd)
                    t_last_z = None
                    for tt in range(4):
                        if ig == 0 and ii == 0:
                            s.guard("tensor", dn_keys[tt])
                        t_dn[tt] = s.emit("tensor", lambda t, tt=tt, ii=ii, zpar=zpar, ig=ig: t.matmul(
                            p_dn[tt][:], zld[:, zpar * 512 + tt * 128:zpar * 512 + (tt + 1) * 128],
                            wd_sb[:, ii * 512:(ii + 1) * 512],
                            start=(ig == 0 and ii == 0), stop=(ig == 3 and ii == 15)))
                    s.use(("zld", zpar), t_dn[3])
                s.use("wd_sb", t_dn[3])
            for tt in range(4):
                ppar = tt % 2
                s.wait("vector", t_dn[tt])
                s.guard("vector", ("x4s", ppar))
                t_x4 = s.emit("vector", lambda v, tt=tt, oc=oc, ppar=ppar: v.tensor_tensor(
                    out=x4s[:, ppar * 512:(ppar + 1) * 512], in0=p_dn[tt][:],
                    in1=x3t[:, tt * DM + oc * 512:tt * DM + (oc + 1) * 512], op=ALU.add))
                s.use(dn_keys[tt], t_x4)
                if oc == 3:
                    s.use(("x3rd", tt), t_x4)
                s.wait("gpsimd", t_x4)
                t_xo = s.dma("gpsimd",
                             xout[c * 512 + tt * 128:c * 512 + (tt + 1) * 128,
                                  oc * 512:(oc + 1) * 512],
                             x4s[:, ppar * 512:(ppar + 1) * 512], ch="xost")
                s.use(("x4s", ppar), t_xo)

    s.wait_self("gpsimd", s.chan_total("xost"))
    return nc, s



def build_prep():
    nc = bass.Bass("TRN2", target_bir_lowering=False)
    s = Sched()
    wup_r = nc.dram_tensor("wup_r", [1024, DM], F32, kind="ExternalInput").ap()
    wgt_r = nc.dram_tensor("wgt_r", [1024, DM], F32, kind="ExternalInput").ap()
    wdn_r = nc.dram_tensor("wdn_r", [256, INTER], F32, kind="ExternalInput").ap()
    gq_r = nc.dram_tensor("gq_r", [1024, DM], I32, kind="ExternalInput").ap()
    uq_r = nc.dram_tensor("uq_r", [1024, DM], I32, kind="ExternalInput").ap()
    dq_r = nc.dram_tensor("dq_r", [256, INTER], I32, kind="ExternalInput").ap()
    suug = nc.dram_tensor("suug", [128, 16], F32, kind="ExternalInput").ap()
    sudp = nc.dram_tensor("sudp", [128, 64], F32, kind="ExternalInput").ap()
    wsdp_c = nc.dram_tensor("wsdp_c", [128, 2], F32, kind="ExternalInput").ap()
    dsc = nc.dram_tensor("dsc", [128, 2], F32, kind="ExternalInput").ap()
    h128p = nc.dram_tensor("h128p", [128, 128], BF16, kind="ExternalInput").ap()
    h128n = nc.dram_tensor("h128n", [128, 128], BF16, kind="ExternalInput").ap()
    aw = {}
    AT_SHAPES = {"wq": (DM, KDIM), "wk0": (DM, KDIM), "wk1": (DM, KDIM),
                 "wv0": (DM, VDIM), "wv1": (DM, VDIM), "wg": (DM, VDIM),
                 "wo": (VDIM, DM), "wb01": (DM, 12), "wa": (DM, 6)}
    for nm, (r, cdim) in AT_SHAPES.items():
        aw[nm] = (nc.dram_tensor(nm + "_r", [r // 8, cdim], F32, kind="ExternalInput").ap(),
                  nc.dram_tensor(nm + "_o", [r // 8, cdim], BF16, kind="ExternalOutput").ap())
    w2up_o = nc.dram_tensor("w2up_o", [DM, 1024], BF16, kind="ExternalOutput").ap()
    w2gt_o = nc.dram_tensor("w2gt_o", [DM, 1024], BF16, kind="ExternalOutput").ap()
    w2dn_o = nc.dram_tensor("w2dn_o", [INTER, 256], BF16, kind="ExternalOutput").ap()
    gT_o = nc.dram_tensor("gT_o", [DM, 1024], BF16, kind="ExternalOutput").ap()
    uT_o = nc.dram_tensor("uT_o", [DM, 1024], BF16, kind="ExternalOutput").ap()
    dT_o = nc.dram_tensor("dT_o", [INTER, 256], BF16, kind="ExternalOutput").ap()
    upb_d = nc.dram_tensor("upb_d", [1024, DM], BF16, kind="Internal").ap()
    gtb_d = nc.dram_tensor("gtb_d", [1024, DM], BF16, kind="Internal").ap()
    dnb_d = nc.dram_tensor("dnb_d", [256, INTER], BF16, kind="Internal").ap()
    gqb_d = nc.dram_tensor("gqb_d", [1024, DM], BF16, kind="Internal").ap()
    uqb_d = nc.dram_tensor("uqb_d", [1024, DM], BF16, kind="Internal").ap()
    dqb_d = nc.dram_tensor("dqb_d", [256, INTER], BF16, kind="Internal").ap()

    sb = lambda n, shp, dt: nc.alloc_sbuf_tensor(n, shp, dt).ap()
    ps = lambda n, shp, dt: nc.alloc_psum_tensor(n, shp, dt).ap()
    hp = sb("hp", [128, 128], BF16)
    hn = sb("hn", [128, 128], BF16)
    su16 = sb("su16", [128, 16], F32)
    su64 = sb("su64", [128, 64], F32)
    wsd = sb("wsd", [128, 2], F32)
    dss = sb("dss", [128, 2], F32)
    ld32 = sb("ld32", [128, INTER], F32)
    ldi = sb("ldi", [128, INTER], BF16)
    wt = sb("wt", [128, 2 * 1024], BF16)
    wtd = sb("wtd", [128, 2 * 256], BF16)
    evb = sb("evb", [128, 2 * 512], BF16)
    p_t = [ps("p_t0", [128, 512], F32), ps("p_t1", [128, 512], F32)]

    s.dma("sync", hp[:], h128p[:], ch="consts")
    s.dma("sync", hn[:], h128n[:], ch="consts")
    s.dma("sync", su16[:], suug[:], ch="consts")
    s.dma("sync", su64[:], sudp[:], ch="consts")
    s.dma("sync", wsd[:], wsdp_c[:], ch="consts")
    s.dma("sync", dss[:], dsc[:], ch="consts")
    t_consts = s.chan_total("consts")
    H16 = hadamard(16)
    H64 = hadamard(64)

    for nm, (src, dst) in aw.items():
        s.dma("gpsimd", dst[:], src[:], ch="cast")

    s.dma("gpsimd", upb_d[:], wup_r[:], ch="cast")
    s.dma("gpsimd", gtb_d[:], wgt_r[:], ch="cast")
    for t2 in range(2):
        t_l = s.dma("sync", ld32[:], wdn_r[t2 * 128:(t2 + 1) * 128, :], ch="ld32")
        s.wait("vector", t_l)
        s.guard("vector", ("ldi", 0))
        t_cv = s.emit("vector", lambda v, t2=t2: v.tensor_scalar(
            out=ldi[:], in0=ld32[:], scalar1=wsd[:, t2:t2 + 1], scalar2=None, op0=ALU.mult))
        s.wait("vector", t_consts)
        s.wait("sync", t_cv)
        s.wait("gpsimd", t_cv)
        t_st = s.dma("gpsimd", dnb_d[t2 * 128:(t2 + 1) * 128, :], ldi[:], ch="dnst")
        s.use(("ldi", 0), t_st)
    t_ic = {}
    for nm, src, dstd in (("g", gq_r, gqb_d), ("u", uq_r, uqb_d)):
        t_ic[nm] = s.dma("gpsimd", dstd[:], src[:], ch="icast")
    t_icd = s.dma("gpsimd", dqb_d[:], dq_r[:], ch="icast")
    t_ic_all = s.chan_total("icast")
    for nm, dstd, outd in (("g", gqb_d, gT_o), ("u", uqb_d, uT_o)):
        for t2 in range(8):
            s.guard("sync", "ldi2")
            s.wait("sync", t_ic_all)
            t_l = s.dma("sync", ldi[:, 0:DM], dstd[t2 * 128:(t2 + 1) * 128, :], ch="ldi2c")
            s.wait("vector", t_l)
            t_sh = s.emit("vector", lambda v: v.tensor_scalar(
                out=ldi[:, 0:DM], in0=ldi[:, 0:DM], scalar1=128.0, scalar2=None, op0=ALU.subtract))
            s.use("ldi2", t_sh)
            s.wait("gpsimd", t_sh)
            t_st = s.dma("gpsimd", dstd[t2 * 128:(t2 + 1) * 128, :], ldi[:, 0:DM], ch="shst")
            s.use("ldi2", t_st)
    for t2 in range(2):
        s.guard("sync", "ldi2")
        s.wait("sync", t_ic_all)
        t_l = s.dma("sync", ldi[:], dqb_d[t2 * 128:(t2 + 1) * 128, :], ch="ldi2c")
        s.wait("vector", t_l)
        t_sh = s.emit("vector", lambda v, t2=t2: v.tensor_scalar(
            out=ldi[:], in0=ldi[:], scalar1=128.0, scalar2=dss[:, t2:t2 + 1],
            op0=ALU.subtract, op1=ALU.mult))
        s.use("ldi2", t_sh)
        s.wait("gpsimd", t_sh)
        t_st = s.dma("gpsimd", dqb_d[t2 * 128:(t2 + 1) * 128, :], ldi[:], ch="shst")
        s.use("ldi2", t_st)
    t_sh_all = s.chan_total("shst")
    t_cast_all = s.chan_total("cast")
    t_dn_all = s.chan_total("dnst")

    for dstd, outd in ((gqb_d, gT_o), (uqb_d, uT_o)):
        for hb in range(16):
            s.guard("scalar", ("wt", hb % 2))
            s.wait("scalar", t_sh_all)
            t_t = s.dma("scalar", wt[:, (hb % 2) * 1024:(hb % 2 + 1) * 1024],
                        dstd[:, hb * 128:(hb + 1) * 128], transpose=True, ch=("wt", hb % 2))
            s.wait("gpsimd", t_t)
            t_st = s.dma("gpsimd", outd[hb * 128:(hb + 1) * 128, :],
                         wt[:, (hb % 2) * 1024:(hb % 2 + 1) * 1024], ch="outst")
            s.use(("wt", hb % 2), t_st)
    for ib in range(64):
        s.guard("scalar", ("wtd", ib % 2))
        s.wait("scalar", t_sh_all)
        t_t = s.dma("scalar", wtd[:, (ib % 2) * 256:(ib % 2 + 1) * 256],
                    dqb_d[:, ib * 128:(ib + 1) * 128], transpose=True, ch=("wtd", ib % 2))
        s.wait("gpsimd", t_t)
        t_st = s.dma("gpsimd", dT_o[ib * 128:(ib + 1) * 128, :],
                     wtd[:, (ib % 2) * 256:(ib % 2 + 1) * 256], ch="outst")
        s.use(("wtd", ib % 2), t_st)

    wta = sb("wta", [128, 16 * 512], BF16)
    wtad = sb("wtad", [128, 64 * 256], BF16)
    for srcd, outd, su in ((upb_d, w2up_o, su16), (gtb_d, w2gt_o, su16)):
        for ch_ in range(2):
            for a_ in range(16):
                s.guard("scalar", ("wta", a_))
                s.wait("scalar", t_cast_all)
                s.dma("scalar", wta[:, a_ * 512:(a_ + 1) * 512],
                      srcd[ch_ * 512:(ch_ + 1) * 512, a_ * 128:(a_ + 1) * 128],
                      transpose=True, ch="wta")
            t_wta = s.chan_total("wta")
            for ap_ in range(16):
                ppar = ap_ % 2
                s.guard("tensor", ("p_t", ppar))
                s.wait("tensor", t_wta)
                s.wait("tensor", t_consts)
                t_mm = None
                for a_ in range(16):
                    sgn = H16[ap_, a_]
                    t_mm = s.emit("tensor", lambda t, ppar=ppar, sgn=sgn, a_=a_: t.matmul(
                        p_t[ppar][:], hp[:] if sgn > 0 else hn[:],
                        wta[:, a_ * 512:(a_ + 1) * 512],
                        start=(a_ == 0), stop=(a_ == 15)))
                if ap_ == 15:
                    for a_ in range(16):
                        s.use(("wta", a_), t_mm)
                s.wait("vector", t_mm)
                s.guard("vector", ("evb", ppar))
                t_ev = s.emit("vector", lambda v, ap_=ap_, ppar=ppar, su=su: v.tensor_scalar(
                    out=evb[:, ppar * 512:(ppar + 1) * 512], in0=p_t[ppar][:],
                    scalar1=su[:, ap_:ap_ + 1], scalar2=None, op0=ALU.mult))
                s.use(("p_t", ppar), t_ev)
                s.wait("gpsimd", t_ev)
                t_st = s.dma("gpsimd", outd[ap_ * 128:(ap_ + 1) * 128, ch_ * 512:(ch_ + 1) * 512],
                             evb[:, ppar * 512:(ppar + 1) * 512], ch="outst")
                s.use(("evb", ppar), t_st)
    for a_ in range(64):
        s.wait("scalar", t_dn_all)
        s.dma("scalar", wtad[:, a_ * 256:(a_ + 1) * 256],
              dnb_d[:, a_ * 128:(a_ + 1) * 128], transpose=True, ch="wtad")
    t_wtad = s.chan_total("wtad")
    for ap_ in range(64):
        ppar = ap_ % 2
        s.guard("tensor", ("p_t", ppar))
        s.wait("tensor", t_wtad)
        t_mm = None
        for a_ in range(64):
            sgn = H64[ap_, a_]
            t_mm = s.emit("tensor", lambda t, ppar=ppar, sgn=sgn, a_=a_: t.matmul(
                p_t[ppar][:, 0:256], hp[:] if sgn > 0 else hn[:],
                wtad[:, a_ * 256:(a_ + 1) * 256],
                start=(a_ == 0), stop=(a_ == 63)))
        s.wait("vector", t_mm)
        s.guard("vector", ("evb", ppar))
        t_ev = s.emit("vector", lambda v, ap_=ap_, ppar=ppar: v.tensor_scalar(
            out=evb[:, ppar * 512:ppar * 512 + 256], in0=p_t[ppar][:, 0:256],
            scalar1=su64[:, ap_:ap_ + 1], scalar2=None, op0=ALU.mult))
        s.use(("p_t", ppar), t_ev)
        s.wait("gpsimd", t_ev)
        t_st = s.dma("gpsimd", w2dn_o[ap_ * 128:(ap_ + 1) * 128, :],
                     evb[:, ppar * 512:ppar * 512 + 256], ch="outst")
        s.use(("evb", ppar), t_st)

    s.wait_self("gpsimd", s.chan_total("outst"))
    s.wait_self("gpsimd", s.chan_total("cast"))
    return nc, s


def _stage_a_inputs(inputs):
    x = np.ascontiguousarray(np.asarray(inputs["L0_attn_x"], np.float32))
    W_delta = np.asarray(inputs["L0_attn_W_delta"], np.float32)
    W_dt = np.asarray(inputs["L0_attn_W_dt"], np.float32)
    b_dt = np.asarray(inputs["L0_attn_b_dt"], np.float32)
    W_B = np.asarray(inputs["L0_attn_W_B"], np.float32)
    W_C = np.asarray(inputs["L0_attn_W_C"], np.float32)
    A = -np.exp(np.asarray(inputs["L0_attn_A_log"], np.float32))
    Dp = np.asarray(inputs["L0_attn_D"], np.float32)

    selbc = np.zeros((32, 32 * 128), np.float32)
    for n in range(16):
        selbc[n, n * 128:(n + 1) * 128] = 1.0
        selbc[16 + n, (16 + n) * 128:(17 + n) * 128] = 1.0
    wbc = np.concatenate([W_B, W_C], axis=1).astype(np.float32)

    maps = []
    for c in range(NCORES):
        b, j = divmod(c, 2)
        perm = np.concatenate([np.arange(j * 1024, (j + 1) * 1024),
                               np.arange((1 - j) * 1024, (2 - j) * 1024)])
        own = perm[:1024]
        constsa = np.zeros((128, 4), np.float32)
        constsa[:, 0] = EPS_LN
        constsa[:, 1] = 1.0
        constsa[:, 2] = -0.5
        constsa[:, 3] = -1.0
        maps.append({
            "constsa": constsa,
            "xb": np.ascontiguousarray(x[b][:, perm]),
            "wdelta": np.ascontiguousarray(W_delta[perm, :]),
            "wdt": np.ascontiguousarray(W_dt[:, own]),
            "bdt": np.ascontiguousarray(b_dt[own].reshape(8, 128).T),
            "wbc": np.ascontiguousarray(wbc[perm, :]),
            "acols": np.ascontiguousarray(
                A[own].reshape(8, 128, 16).transpose(1, 0, 2).reshape(128, 8 * 16)),
            "dcols": np.ascontiguousarray(Dp[own].reshape(8, 128).T),
            "selbc": selbc,
        })
    return maps


def _stage_b_maps(inputs, yhiF, yloF, w2upT, w2gateT, w2downT):
    x = np.asarray(inputs["L0_attn_x"], np.float32)
    wsug = np.asarray(inputs["L0_ffn_Wscale_ug"], np.float32)
    scu = (wsug[:INTER] / np.sqrt(DM)).reshape(64, 128).T.astype(np.float32)
    scg = (wsug[INTER:] / np.sqrt(DM)).reshape(64, 128).T.astype(np.float32)
    constsb = np.zeros((128, 2), np.float32)
    constsb[:, 0] = EPS_LN
    constsb[:, 1] = -0.5
    maps = []
    for c in range(NCORES):
        b, j = divmod(c, 2)
        sl = slice(j * TOK, (j + 1) * TOK)
        maps.append({
            "xrows": np.ascontiguousarray(x[b, sl, :]),
            "yhi": np.ascontiguousarray(yhiF[b][:, sl]),
            "ylo": np.ascontiguousarray(yloF[b][:, sl]),
            "w2up": w2upT, "w2gate": w2gateT, "w2down": w2downT,
            "scu": scu, "scg": scg, "constsb": constsb,
        })
    return maps


def run_stage_b(inputs, yhiF, yloF, w2upT, w2gateT, w2downT):
    nc, s = build_stage_b()
    s.realize(nc)
    maps = _stage_b_maps(inputs, yhiF, yloF, w2upT, w2gateT, w2downT)
    res = run_bass_kernel_spmd(nc, maps, core_ids=list(range(NCORES)))
    x2 = np.zeros((B, L, DM), np.float32)
    for c in range(NCORES):
        b, j = divmod(c, 2)
        x2[b][j * TOK:(j + 1) * TOK, :] = res.results[c]["x2o"]
    return x2


def _stage_c_maps(inputs, x2, wbf):
    conv_q = np.asarray(inputs["L1_attn_conv_q_w"], np.float32)
    conv_k = np.asarray(inputs["L1_attn_conv_k_w"], np.float32)
    conv_v = np.asarray(inputs["L1_attn_conv_v_w"], np.float32)
    cw = np.zeros((128, 84 * 4), np.float32)
    fi = 0
    for src, n in ((conv_q, 12), (conv_k[0], 12), (conv_k[1], 12),
                   (conv_v[0], 24), (conv_v[1], 24)):
        for i in range(n):
            cw[:, fi * 4:(fi + 1) * 4] = src[i * 128:(i + 1) * 128, :]
            fi += 1
    alogdt = np.stack([np.asarray(inputs["L1_attn_A_log"], np.float32),
                       np.asarray(inputs["L1_attn_dt_bias"], np.float32)], axis=1)
    constsc = np.zeros((128, 6), np.float32)
    constsc[:, 0] = EPS_LN
    constsc[:, 1] = -0.5
    constsc[:, 2] = 1.0
    constsc[:, 3] = -1.0
    constsc[:, 4] = 1e-6
    ident = np.eye(128, dtype=np.float32)
    maps = []
    for c in range(NCORES):
        b, j = divmod(c, 2)
        x2rows = np.zeros((TOK + 16, DM), np.float32)
        x2rows[16:] = x2[b, j * TOK:(j + 1) * TOK, :]
        if j == 1:
            x2rows[:16] = x2[b, TOK - 16:TOK, :]
        maps.append({
            "x2rows": x2rows,
            "wq": wbf["wq"], "wk0": wbf["wk0"], "wk1": wbf["wk1"],
            "wv0": wbf["wv0"], "wv1": wbf["wv1"], "wb01": wbf["wb01"],
            "wa": wbf["wa"], "wg": wbf["wg"],
            "cw": cw, "alogdt": alogdt, "constsc": constsc, "identc": ident,
        })
    return maps


def run_stage_c(inputs, x2, wbf):
    nc, s = build_stage_c()
    s.realize(nc)
    maps = _stage_c_maps(inputs, x2, wbf)
    res = run_bass_kernel_spmd(nc, maps, core_ids=list(range(NCORES)))
    return [res.results[c] for c in range(NCORES)]


def _stage_d_consts():
    r = np.arange(128)
    mup = (1e9 * (r[None, :] > r[:, None])).astype(np.float32)
    mlo = (1e9 * (r[None, :] < r[:, None])).astype(np.float32)
    identf = np.eye(128, dtype=np.float32)
    identb = np.eye(128, dtype=BF)
    ihalf = np.zeros((128, 512), np.float32)
    ihalf[r, r] = 1.0
    ihalf[r, 256 + 128 + r] = 1.0
    onesd = np.ones((6, 128), np.float32)
    return {"mup": mup, "mlo": mlo, "identf": identf, "identb": identb,
            "ihalf": ihalf, "onesd": onesd}


def run_stage_d(coutс_or_list, sinits, skip_o=False):
    nc, s = build_stage_d(skip_o=skip_o)
    s.realize(nc)
    consts = _stage_d_consts()
    maps = []
    for c in range(NCORES):
        r = coutс_or_list[c]
        maps.append({
            "Ks": np.asarray(r["Ks"]), "Qs": np.asarray(r["Qs"]),
            "Vs": np.asarray(r["Vs"]), "bgz": np.asarray(r["bgz"]),
            "Sinit": sinits[c], **consts,
        })
    res = run_bass_kernel_spmd(nc, maps, core_ids=list(range(NCORES)))
    return [res.results[c] for c in range(NCORES)]


def _stage_e_maps(inputs, x2, outsD, outsC, webf):
    onw = np.asarray(inputs["L1_attn_o_norm_w"], np.float32)
    onwb = np.broadcast_to(np.tile(onw, 6)[None, :], (128, VDIM)).astype(BF)
    gsc_ = np.asarray(inputs["L1_ffn_gate_scale"], np.float32).reshape(64, 128).T.copy()
    usc_ = np.asarray(inputs["L1_ffn_up_scale"], np.float32).reshape(64, 128).T.copy()
    constse = np.zeros((128, 6), np.float32)
    constse[:, 0] = EPS_LN
    constse[:, 1] = -0.5
    constse[:, 2] = 1.0 / DV
    maps = []
    for c in range(NCORES):
        b, j = divmod(c, 2)
        sl = slice(j * TOK, (j + 1) * TOK)
        maps.append({
            "Od": np.asarray(outsD[c]["O"]),
            "gg": np.asarray(outsC[c]["gg"]),
            "x2rows": np.ascontiguousarray(x2[b, sl, :]),
            "wo": webf["wo"], "gT": webf["gT"], "uT": webf["uT"], "dT": webf["dT"],
            "onwb": onwb, "gsc": gsc_, "usc": usc_, "constse": constse,
        })
    return maps


def run_stage_e(inputs, x2, outsD, outsC, webf):
    nc, s = build_stage_e()
    s.realize(nc)
    maps = _stage_e_maps(inputs, x2, outsD, outsC, webf)
    res = run_bass_kernel_spmd(nc, maps, core_ids=list(range(NCORES)))
    out = np.zeros((B, L, DM), np.float32)
    for c in range(NCORES):
        b, j = divmod(c, 2)
        out[b][j * TOK:(j + 1) * TOK, :] = res.results[c]["xout"]
    return out


def run_prep(inputs):
    nc, s = build_prep()
    s.realize(nc)
    H128 = hadamard(128)
    h128p = H128.astype(BF)
    h128n = (-H128).astype(BF)
    SU_ug = np.asarray(inputs["L0_ffn_SU_ug"], np.float32)
    SU_dp = np.asarray(inputs["L0_ffn_SU_dp"], np.float32)
    wsdp = np.asarray(inputs["L0_ffn_Wscale_dp"], np.float32)
    ds_ = np.asarray(inputs["L1_ffn_down_scale"], np.float32)
    W_up = np.asarray(inputs["L0_ffn_W_up"], np.float32)
    W_gate = np.asarray(inputs["L0_ffn_W_gate"], np.float32)
    W_down = np.asarray(inputs["L0_ffn_W_down"], np.float32)
    gq = np.asarray(inputs["L1_ffn_gate_w"], np.int32)
    uq = np.asarray(inputs["L1_ffn_up_w"], np.int32)
    dq = np.asarray(inputs["L1_ffn_down_w"], np.int32)
    at_full = {
        "wq": np.asarray(inputs["L1_attn_Wq"], np.float32),
        "wk0": np.asarray(inputs["L1_attn_Wk"], np.float32)[0],
        "wk1": np.asarray(inputs["L1_attn_Wk"], np.float32)[1],
        "wv0": np.asarray(inputs["L1_attn_Wv"], np.float32)[0],
        "wv1": np.asarray(inputs["L1_attn_Wv"], np.float32)[1],
        "wg": np.asarray(inputs["L1_attn_Wg"], np.float32),
        "wo": np.asarray(inputs["L1_attn_Wo"], np.float32),
        "wb01": np.concatenate([np.asarray(inputs["L1_attn_Wb"], np.float32)[0],
                                np.asarray(inputs["L1_attn_Wb"], np.float32)[1]], axis=1),
        "wa": np.asarray(inputs["L1_attn_Wa"], np.float32),
    }
    maps = []
    for c in range(NCORES):
        i1 = slice(c * 1024, (c + 1) * 1024)
        o1 = slice(c * 256, (c + 1) * 256)
        wsd = np.zeros((128, 2), np.float32)
        wsd[:, 0] = wsdp[c * 256:c * 256 + 128] / np.sqrt(INTER)
        wsd[:, 1] = wsdp[c * 256 + 128:(c + 1) * 256] / np.sqrt(INTER)
        dsc_ = np.zeros((128, 2), np.float32)
        dsc_[:, 0] = ds_[c * 256:c * 256 + 128]
        dsc_[:, 1] = ds_[c * 256 + 128:(c + 1) * 256]
        m = {
            "wup_r": np.ascontiguousarray(W_up[i1]),
            "wgt_r": np.ascontiguousarray(W_gate[i1]),
            "wdn_r": np.ascontiguousarray(W_down[o1]),
            "gq_r": np.ascontiguousarray(gq[i1]),
            "uq_r": np.ascontiguousarray(uq[i1]),
            "dq_r": np.ascontiguousarray(dq[o1]),
            "suug": np.ascontiguousarray(SU_ug.reshape(16, 128).T),
            "sudp": np.ascontiguousarray(SU_dp.reshape(64, 128).T),
            "wsdp_c": wsd, "dsc": dsc_,
            "h128p": h128p, "h128n": h128n,
        }
        for nm, arr in at_full.items():
            rsl = arr.shape[0] // 8
            m[nm + "_r"] = np.ascontiguousarray(arr[c * rsl:(c + 1) * rsl])
        maps.append(m)
    res = run_bass_kernel_spmd(nc, maps, core_ids=list(range(NCORES)))
    rs = res.results
    W = {}
    W["w2up"] = np.concatenate([np.asarray(rs[c]["w2up_o"]) for c in range(NCORES)], axis=1)
    W["w2gate"] = np.concatenate([np.asarray(rs[c]["w2gt_o"]) for c in range(NCORES)], axis=1)
    W["w2down"] = np.concatenate([np.asarray(rs[c]["w2dn_o"]) for c in range(NCORES)], axis=1)
    W["gT"] = np.concatenate([np.asarray(rs[c]["gT_o"]) for c in range(NCORES)], axis=1)
    W["uT"] = np.concatenate([np.asarray(rs[c]["uT_o"]) for c in range(NCORES)], axis=1)
    W["dT"] = np.concatenate([np.asarray(rs[c]["dT_o"]) for c in range(NCORES)], axis=1)
    for nm in at_full:
        W[nm] = np.concatenate([np.asarray(rs[c][nm + "_o"]) for c in range(NCORES)], axis=0)
    return W


def kernel(**inputs):
    W = run_prep(inputs)
    yhi, ylo = run_stage_a(inputs)
    x2 = run_stage_b(inputs, yhi, ylo, W["w2up"], W["w2gate"], W["w2down"])
    wbf = {k: W[k] for k in ("wq", "wk0", "wk1", "wv0", "wv1", "wb01", "wa", "wg")}
    outsC = run_stage_c(inputs, x2, wbf)
    zero_s = np.zeros((KDIM, DV), np.float32)
    outsD1 = run_stage_d(outsC, [zero_s] * NCORES, skip_o=True)
    sinits = [np.asarray(outsD1[c - 1]["Sfin"]) if c % 2 == 1 else zero_s
              for c in range(NCORES)]
    outsD2 = run_stage_d(outsC, sinits)
    webf = {"wo": W["wo"], "gT": W["gT"], "uT": W["uT"], "dT": W["dT"]}
    out = run_stage_e(inputs, x2, outsD2, outsC, webf)
    return out


def run_stage_a(inputs):
    nc, s = build_stage_a()
    s.realize(nc)
    maps = _stage_a_inputs(inputs)
    res = run_bass_kernel_spmd(nc, maps, core_ids=list(range(NCORES)))
    yhi = np.zeros((B, DM, L), BF)
    ylo = np.zeros((B, DM, L), BF)
    for c in range(NCORES):
        b, j = divmod(c, 2)
        yhi[b, j * 1024:(j + 1) * 1024, :] = res.results[c]["yhi"]
        ylo[b, j * 1024:(j + 1) * 1024, :] = res.results[c]["ylo"]
    return yhi, ylo


